# version 03
# baseline (speedup 1.0000x reference)
import sys
import numpy as np
import ml_dtypes
from contextlib import ExitStack
import concourse.bass as bass
import concourse.mybir as mybir
from concourse.bass_utils import run_bass_kernel_spmd

F32 = mybir.dt.float32
BF16 = mybir.dt.bfloat16
I32 = mybir.dt.int32
U8 = mybir.dt.uint8
AF = mybir.ActivationFunctionType
ALU = mybir.AluOpType
AX = mybir.AxisListType
BF = ml_dtypes.bfloat16

D = 2048
T = 2048
H = 16
FF = 8192
NT = 16
ISC = float(1.0 / np.sqrt(np.float64(128.0)))
NEGBIG = -1.0e9
LNEPS = 1e-5
QEPS = 1e-5
WDEN = float(D) * float(FF)
KB = 1024

_STAGE = 4
_DBG_ZB = False
_DBG_HOOKS = False
_DBG_NH = None
_DBG_NOOUT = False
_DBG_NH_B = None
_DBG_NOOUT_B = False
_DBG_NSEG = None
_DBG_NOSIG = False
_DBG_RACE = False


class PL:
    ENGS = ("sync", "scalar", "vector", "tensor", "gpsimd")
    SEMS = ("qC", "qB", "qX0", "qX1", "qL0", "qL1", "qP", "qT", "qZ", "qSt",
            "qG0", "qG1", "sPE", "sACT", "sDVE", "sGP")

    def __init__(self):
        self.ops = {e: [] for e in self.ENGS}
        self.cnt = {s: 0 for s in self.SEMS}
        self.ni = 0

    def w(self, eng, sem, val):
        if val is None or val <= 0:
            return
        assert val <= self.cnt[sem], (eng, sem, val, self.cnt[sem])
        self.ops[eng].append(("w", sem, val))

    def wd(self, eng, dep):
        if dep is None:
            return
        if isinstance(dep, tuple):
            dep = [dep]
        for s, v in dep:
            self.w(eng, s, v)

    def o(self, eng, fn, sem=None):
        v = None
        if sem is not None:
            self.cnt[sem] += 1
            v = self.cnt[sem]
        self.ops[eng].append(("o", fn, sem))
        self.ni += 1
        return v

    def d(self, eng, out, in_, sem):
        self.cnt[sem] += 16
        v = self.cnt[sem]
        self.ops[eng].append(("d", out, in_, sem))
        self.ni += 1
        return v


def _emit(P, nc, sems):
    with nc.Block() as block:
        def runner(name):
            def run(eng):
                for op in P.ops[name]:
                    if op[0] == "w":
                        eng.wait_ge(sems[op[1]], op[2])
                    elif op[0] == "d":
                        eng.dma_start(out=op[1], in_=op[2]).then_inc(sems[op[3]], 16)
                    else:
                        ins = op[1](eng)
                        if op[2] is not None:
                            ins.then_inc(sems[op[2]], 1)
            return run
        block.sync(runner("sync"))
        block.scalar(runner("scalar"))
        block.vector(runner("vector"))
        block.tensor(runner("tensor"))
        block.gpsimd(runner("gpsimd"))


def build_program(stage=_STAGE):
    nc = bass.Bass("TRN2", target_bir_lowering=False, detect_race_conditions=_DBG_RACE)

    DT = {}
    def din(name, shape, dt=F32):
        DT[name] = nc.dram_tensor(name, shape, dt, kind="ExternalInput").ap()
    din("x0", [T, D]); din("wq_a", [D, D]); din("wk_a", [D, D]); din("wv_a", [D, D])
    din("wo_a", [D, D]); din("bq_in", [128, H]); din("bk_in", [128, H])
    din("bv_in", [128, H]); din("bo_bc", [128, D])
    din("wq_b", [D, D]); din("wk_b", [D, D]); din("wv_b", [D, D]); din("wo_b", [D, D])
    din("beta_in", [128, D])
    din("wgT", [D, FF]); din("wuT", [D, FF]); din("wdT", [FF, D])
    din("ident_in", [128, 128], BF16); din("mask_in", [128, 128]); din("ones_in", [128, 128])
    y = nc.dram_tensor("y", [T, D], F32, kind="ExternalOutput").ap()
    xr = nc.dram_tensor("xr", [T, D], F32, kind="Internal").ap()
    zkind = "ExternalOutput" if _DBG_ZB else "Internal"
    zbank = nc.dram_tensor("zbank", [H * 128, T], BF16, kind=zkind).ap()
    abank = nc.dram_tensor("abank", [H * 128, T], BF16, kind=zkind).ap()
    if _DBG_HOOKS:
        dbgA = nc.dram_tensor("dbgA", [128, 32 * KB], BF16, kind="ExternalOutput").ap()
        dbgq = nc.dram_tensor("dbgq", [128, T], BF16, kind="ExternalOutput").ap()
        dbgk = nc.dram_tensor("dbgk", [128, T], BF16, kind="ExternalOutput").ap()
        dbgv = nc.dram_tensor("dbgv", [128, T], BF16, kind="ExternalOutput").ap()
        dbgp = nc.dram_tensor("dbgp", [128, T], BF16, kind="ExternalOutput").ap()
    tg = nc.dram_tensor("tg", [D, FF], BF16, kind="Internal").ap()
    tu = nc.dram_tensor("tu", [D, FF], BF16, kind="Internal").ap()
    td = nc.dram_tensor("td", [FF, D], BF16, kind="Internal").ap()

    base = (nc.sbuf_base + 31) & ~31
    assert nc.sbuf_top - base >= 206 * KB, (base, nc.sbuf_top)

    def at(name, off_b, shape, dt):
        assert off_b % 32 == 0, (name, off_b)
        h = nc.alloc_sbuf_tensor_at(name, shape, dt, offset=base + off_b)
        return h

    P2 = 64 * KB
    EO = 172 * KB
    SM = EO + 32 * KB
    A = at("A", 0, [128, 32 * KB], BF16)
    WOQ = [at("WOQ0", P2, [128, 8 * KB], BF16), at("WOQ1", P2 + 16 * KB, [128, 8 * KB], BF16)]
    Qt = at("Qt", P2 + 32 * KB, [128, 2 * KB], BF16)
    Kt = at("Kt", P2 + 36 * KB, [128, 2 * KB], BF16)
    VtT = at("VtT", P2 + 40 * KB, [128, 2 * KB], BF16)
    Vtok = at("Vtok", P2 + 44 * KB, [128, 2 * KB], BF16)
    Pbuf = at("Pbuf", P2 + 48 * KB, [128, 2 * KB], BF16)
    Ptb = at("Ptb", P2 + 52 * KB, [128, 2 * KB], BF16)
    Ztb = at("Ztb", P2 + 56 * KB, [128, 2 * KB], BF16)
    ZSL = [at("zsl0", P2 + 76 * KB, [128, 2 * KB], BF16), at("zsl1", P2 + 80 * KB, [128, 2 * KB], BF16)]
    bObc = at("bObc", P2 + 68 * KB, [128, 2 * KB], F32)
    Wqh = at("Wqh", P2 + 76 * KB, [128, 2 * KB], BF16)
    Wkh = at("Wkh", P2 + 80 * KB, [128, 2 * KB], BF16)
    Wvh = at("Wvh", P2 + 84 * KB, [128, 2 * KB], BF16)
    PRES = at("pres", P2 + 88 * KB, [128, 2 * KB], F32)
    PRELO = at("prelo", P2 + 96 * KB, [128, 2 * KB], BF16)
    PREOUT = at("preout", P2 + 100 * KB, [128, 2 * KB], BF16)
    YTO = [at(f"yto{i}", P2 + (32 + 8 * i) * KB, [128, 2 * KB], F32) for i in range(4)]
    HSL = [at(f"hsl{i}", P2 + 16 * i * KB, [128, 8 * KB], BF16) for i in range(4)]
    TGU = [at("tgu0", P2 + 64 * KB, [128, 8 * KB], BF16), at("tgu1", P2 + 80 * KB, [128, 8 * KB], BF16)]
    TDQ = [at("tdq0", P2 + 96 * KB, [128, 2 * KB], BF16), at("tdq1", P2 + 100 * KB, [128, 2 * KB], BF16)]
    GSCR = at("gscr", P2 + 104 * KB, [128, 256], F32)
    RHC = at("rhc", P2 + 105 * KB, [128, 1 * KB], BF16)
    YTF = [at("ytf0", P2 + 64 * KB, [128, 2 * KB], F32), at("ytf1", P2 + 72 * KB, [128, 2 * KB], F32),
           at("ytf2", P2 + 80 * KB, [128, 2 * KB], F32), at("ytf3", P2 + 88 * KB, [128, 2 * KB], F32)]
    SqT = at("SqT", P2 + 48 * KB, [128, 2 * KB], BF16)
    SkT = at("SkT", P2 + 88 * KB, [128, 2 * KB], BF16)
    ATTH = at("ATTH", P2 + 92 * KB, [128, 2 * KB], BF16)
    SS0 = at("ss0", P2 + 96 * KB, [128, 512], F32)
    SS1 = at("ss1", P2 + 98 * KB, [128, 512], F32)
    CMB = at("cmb", P2 + 100 * KB, [128, 256], F32)
    PbB = at("pbB", P2 + 101 * KB, [128, 512], BF16)
    PtB = at("ptB", P2 + 102 * KB, [128, 512], BF16)
    MEMF = at("memf", P2 + 103 * KB, [128, 128], F32)
    MEMZ = at("memz", P2 + 103 * KB + 512, [128, 132], BF16)
    NGD = at("ngd", P2 + 103 * KB + 800, [128, 128], BF16)
    SKTOK = at("sktok", P2 + 103 * KB + 1056, [128, 128], BF16)
    BETA = at("beta", P2 + 68 * KB, [128, 2 * KB], F32)
    XO = [at("xo0", EO, [128, 2 * KB], F32), at("xo1", EO + 8 * KB, [128, 2 * KB], F32)]
    YTE = at("ytE", EO + 16 * KB, [128, 2 * KB], F32)
    I32S = at("i32s", EO + 24 * KB, [128, 1 * KB], I32)
    XNB = at("xnb", EO + 28 * KB, [128, 2 * KB], BF16)
    MASK = at("maskT", SM, [128, 128], F32)
    IDT = at("idT", SM + 512, [128, 128], BF16)
    ONESF = at("onesT", SM + 768, [128, 128], F32)
    BQC = at("bqc", SM + 1280, [128, H], F32)
    BKC = at("bkc", SM + 1344, [128, H], F32)
    BVC = at("bvc", SM + 1408, [128, H], F32)
    SX = at("sx", SM + 1472, [128, NT], F32)
    STAT = at("stat", SM + 1536, [128, 32], F32)
    WSUM = at("wsum", SM + 1664, [128, 4], F32)
    WSC = at("wsc", SM + 1696, [128, 4], F32)
    THR = at("thr", SM + 1728, [128, 4], F32)
    NTHR = at("nthr", SM + 1760, [128, 4], F32)
    SHD = at("shd", SM + 1792, [128, 4], F32)
    SGU = at("sgu", SM + 1824, [128, 8], F32)
    HMX = at("hmx", SM + 1856, [128, 4], F32)
    ZINC = at("zinc", SM + 1888, [128, 4], F32)
    CEPS = at("ceps", SM + 1920, [128, 4], F32)
    DUMP = at("dump", SM + 1952, [128, 64], F32)
    HMXW = at("hmxw", SM + 2208, [128, 128], F32)

    pctx = ExitStack()
    psA = pctx.enter_context(nc.psum_tensor("psA", [128, 2048], F32))
    psB = pctx.enter_context(nc.psum_tensor("psB", [128, 1024], F32))
    psT = pctx.enter_context(nc.psum_tensor("psT", [128, 1024], BF16))
    psU = pctx.enter_context(nc.psum_tensor("psU", [128, 1024], BF16))
    sctx = ExitStack()
    sems = {s: sctx.enter_context(nc.semaphore(s)) for s in PL.SEMS}

    P = PL()
    K = {
        "bkg": {b: None for b in
                ("A0", "A1", "A2", "A3", "B0", "B1")},
        "bkw": {b: None for b in
                ("A0", "A1", "A2", "A3", "B0", "B1")},
        "psT_i": 0,
        "psT_g": [None, None],
        "xo_g": [None, None],
        "a_rdy": [None] * NT,
        "sx_rdy": [None] * NT,
        "spair_i": 0,
        "tern_st": {m: [0] * 32 for m in range(2)},
        "td_all": 0,
        "thr_rdy": None,
        "zb_st": 0,
        "ab_st": 0,
        "xnb_g": None,
    }

    def bank_ap(name):
        if name.startswith("A"):
            i = int(name[1])
            return psA[:, i * 512:(i + 1) * 512]
        if name.startswith("B"):
            i = int(name[1])
            return psB[:, i * 512:(i + 1) * 512]
        raise KeyError(name)

    def bank_wait_w(eng, names):
        for n in names:
            P.wd(eng, K["bkg"][n])

    def bank_set_w(names, val):
        for n in names:
            K["bkw"][n] = ("sPE", val)

    def bank_wait_r(eng, names):
        for n in names:
            P.wd(eng, K["bkw"][n])

    def bank_set_r(names, dep):
        for n in names:
            old = K["bkg"][n]
            deps = [dep] if isinstance(dep, tuple) else list(dep)
            if old:
                deps = deps + ([old] if isinstance(old, tuple) else list(old))
            best = {}
            for s, v in deps:
                best[s] = max(best.get(s, 0), v)
            K["bkg"][n] = [(s, v) for s, v in best.items()]

    def transpose_to(src_ap, dst_ap, pe_dep=None, act_dep=None):
        i = K["psT_i"] % 2
        K["psT_i"] += 1
        slot = (psT if i == 0 else psU)[:, 0:128]
        P.wd("tensor", K["psT_g"][i])
        P.wd("tensor", pe_dep)
        pe = P.o("tensor", lambda t, s=slot, a=src_ap: t.transpose(s, a, IDT[:]), "sPE")
        P.w("scalar", "sPE", pe)
        P.wd("scalar", act_dep)
        ac = P.o("scalar", lambda s, d=dst_ap, sl=slot: s.copy(d, sl), "sACT")
        K["psT_g"][i] = ("sACT", ac)
        return ac

    ident = DT["ident_in"]; maskin = DT["mask_in"]; onesin = DT["ones_in"]

    cl = P.d("sync", IDT[:], ident, "qC")
    cl = P.d("sync", MASK[:], maskin, "qC")
    cl = P.d("sync", ONESF[:], onesin, "qC")
    cl = P.d("sync", BQC[:], DT["bq_in"], "qC")
    cl = P.d("sync", BKC[:], DT["bk_in"], "qC")
    cl = P.d("sync", BVC[:], DT["bv_in"], "qC")
    consts_rdy = cl
    P.o("vector", lambda v: v.memset(CEPS[:, 0:1], 1e-5))
    P.o("vector", lambda v: v.memset(CEPS[:, 1:2], 5e-6))
    cinit = P.o("vector", lambda v: v.memset(CEPS[:, 2:3], -5e-6), "sDVE")
    P.w("scalar", "sDVE", cinit)

    def sep(eng, n=1):
        for _ in range(n):
            if eng == "scalar":
                P.o("scalar", lambda s: s.activation(DUMP[:, 0:64], MASK[:, 0:64], AF.Copy))
            elif eng == "vector":
                P.o("vector", lambda v: v.tensor_copy(DUMP[:, 0:64], MASK[:, 0:64]))
            else:
                P.o("gpsimd", lambda g: g.tensor_copy(DUMP[:, 0:64], MASK[:, 0:64]))

    def finish_tile(t, yth, yt_dep, xo, dst_tile, prep, scratch_f32):
        if yth is not None:
            P.wd("vector", yt_dep)
            v1 = P.o("vector", lambda v, a=yth, b=xo: v.tensor_tensor(b[:], a[:], b[:], ALU.add), "sDVE")
        else:
            P.wd("vector", yt_dep)
            v1 = None
        if dst_tile is not None:
            if v1:
                P.w("sync", "sDVE", v1)
            st = P.d("sync", dst_tile, xo[:], "qSt")
        else:
            st = None
        v2 = P.o("vector", lambda v, b=xo: v.tensor_reduce(STAT[:, 0:1], b[:], AX.X, ALU.add), "sDVE")
        P.w("gpsimd", "sDVE", v2)
        g1 = P.o("gpsimd", lambda g: g.tensor_single_scalar(
            STAT[:, 1:2], STAT[:, 0:1], -1.0 / D, ALU.mult), "sGP")
        P.w("scalar", "sGP", g1)
        P.o("scalar", lambda s, b=xo, d=scratch_f32: s.activation(
            d[:], b[:], AF.Square, bias=STAT[:, 1:2], accum_out=STAT[:, 2:3]))
        sep("scalar", 2)
        a3 = P.o("scalar", lambda s: s.activation(STAT[:, 7:8], STAT[:, 2:3], AF.Sqrt,
                                                  scale=1.0 / D, bias=CEPS[:, 0:1]), "sACT")
        P.w("vector", "sACT", a3)
        P.o("vector", lambda v: v.reciprocal(STAT[:, 3:4], STAT[:, 7:8]))
        sep("vector", 2)
        rdr = [("sACT", a3)]
        if prep == "xnT":
            P.wd("vector", K["xnb_g"])
            v3 = P.o("vector", lambda v, b=xo: v.tensor_scalar(
                XNB[:], b[:], STAT[:, 1:2], STAT[:, 3:4], ALU.add, ALU.mult), "sDVE")
            rdr.append(("sDVE", v3))
            last = None
            for c in range(16):
                last = transpose_to(XNB[:, c * 128:(c + 1) * 128],
                                    A[:, c * 2048 + t * 128: c * 2048 + (t + 1) * 128],
                                    pe_dep=("sDVE", v3) if c == 0 else None)
            K["a_rdy"][t] = ("sACT", last)
            K["xnb_g"] = ("sACT", last)
        elif prep == "R":
            P.wd("vector", K["xnb_g"])
            v3 = P.o("vector", lambda v, b=xo, d=scratch_f32: v.tensor_scalar(
                d[:], b[:], STAT[:, 1:2], STAT[:, 3:4], ALU.add, ALU.mult), "sDVE")
            v4 = P.o("vector", lambda v, d=scratch_f32: v.tensor_reduce(
                STAT[:, 4:5], d[:], AX.X, ALU.max, apply_absolute_value=True), "sDVE")
            sep("vector", 2)
            P.o("vector", lambda v: v.tensor_single_scalar(
                STAT[:, 5:6], STAT[:, 4:5], QEPS, ALU.max))
            sep("vector", 2)
            rc = P.o("vector", lambda v: v.reciprocal(STAT[:, 8:9], STAT[:, 5:6]), "sDVE")
            P.w("gpsimd", "sDVE", rc)
            v5 = P.o("gpsimd", lambda g: g.tensor_single_scalar(
                STAT[:, 6:7], STAT[:, 8:9], 127.0, ALU.mult), "sGP")
            rdr.append(("sDVE", v4))
            P.w("scalar", "sGP", v5)
            sxr = P.o("scalar", lambda s, tt=t: s.activation(SX[:, tt:tt + 1], STAT[:, 5:6],
                                                             AF.Copy, scale=1.0 / 127.0), "sACT")
            K["sx_rdy"][t] = ("sACT", sxr)
            last = None
            for half in range(2):
                P.o("scalar", lambda s, d=scratch_f32, q=half: s.activation(
                    I32S[:], d[:, q * 1024:(q + 1) * 1024], AF.Copy, scale=STAT[:, 6:7]))
                ra = P.o("scalar", lambda s, q=half: s.activation(
                    XNB[:, q * 1024:(q + 1) * 1024], I32S[:], AF.Copy), "sACT")
                for cc in range(8):
                    c = half * 8 + cc
                    last = transpose_to(XNB[:, c * 128:(c + 1) * 128],
                                        A[:, c * 2048 + t * 128: c * 2048 + (t + 1) * 128],
                                        pe_dep=("sACT", ra) if cc == 0 else None)
            K["a_rdy"][t] = ("sACT", last)
            K["xnb_g"] = ("sACT", last)
        rdr2 = list(rdr)
        if st is not None:
            rdr2.append(("qSt", st))
        return st, rdr2

    def prologue():
        for e in ("tensor", "scalar", "vector", "gpsimd"):
            P.w(e, "qC", consts_rdy)
        for t in range(NT):
            par = t % 2
            P.wd("sync", K["xo_g"][par])
            qx = "qX0" if par == 0 else "qX1"
            ld = P.d("sync", XO[par][:], DT["x0"][t * 128:(t + 1) * 128, :], qx)
            _, rdr = finish_tile(t, None, [(qx, ld)], XO[par], None, "xnT", YTE)
            K["xo_g"][par] = rdr

    WSRC = [DT["wgT"], DT["wuT"], DT["wdT"]]
    TDST = [tg, tu, td]

    def w_tile_aps(m, db, fb):
        src = WSRC[m][db * 1024:(db + 1) * 1024, fb * 256:(fb + 1) * 256]
        src = src.rearrange("(c p) f -> p c f", p=128)
        dst = TDST[m][db * 1024:(db + 1) * 1024, fb * 256:(fb + 1) * 256]
        dst = dst.rearrange("(c p) f -> p c f", p=128)
        return src, dst

    pp_state = {"stage_g": None, "lo_g": None, "out_g": None, "wsum_last": [None] * 3}

    def mean_units():
        units = []
        for m in range(3):
            nb = (2, 32) if m < 2 else (8, 8)
            zi = P.o("gpsimd", lambda g, mm=m: g.memset(WSUM[:, mm:mm + 1], 0.0), "sGP")
            pp_state["wsum_last"][m] = ("sGP", zi)
            for db in range(nb[0]):
                for fb in range(nb[1]):
                    units.append((m, db, fb))
        def emit(u):
            m, db, fb = u
            src, _ = w_tile_aps(m, db, fb)
            P.wd("sync", pp_state["stage_g"])
            ld = P.d("sync", PRES[:].rearrange("p (c f) -> p c f", f=256), src, "qP")
            P.w("scalar", "qP", ld)
            P.wd("scalar", pp_state.get("gv_g"))
            aa = P.o("scalar", lambda s: s.activation(PRELO[:], PRES[:], AF.Abs,
                                                      accum_out=STAT[:, 25:26]), "sACT")
            P.w("gpsimd", "sACT", aa)
            gv = P.o("gpsimd", lambda g, mm=m: g.tensor_tensor(
                WSUM[:, mm:mm + 1], WSUM[:, mm:mm + 1], STAT[:, 25:26], ALU.add), "sGP")
            pp_state["gv_g"] = ("sGP", gv)
            pp_state["stage_g"] = [("sACT", aa)]
            pp_state["lo_g"] = [("sACT", aa)]
            pp_state["wsum_last"][m] = ("sGP", gv)
        return units, emit

    def wscale_finalize():
        last = None
        for m in range(3):
            bank_wait_w("tensor", ["B1"])
            P.wd("tensor", pp_state["wsum_last"][m])
            pe = P.o("tensor", lambda t, mm=m: t.matmul(
                psB[:, 512 + mm:512 + mm + 1], ONESF[:], WSUM[:, mm:mm + 1],
                start=True, stop=True), "sPE")
            bank_set_w(["B1"], pe)
            P.w("scalar", "sPE", pe)
            P.o("scalar", lambda s, mm=m: s.activation(WSC[:, mm:mm + 1],
                psB[:, 512 + mm:512 + mm + 1], AF.Identity, scale=1.0 / WDEN, bias=CEPS[:, 0:1]))
            P.o("scalar", lambda s, mm=m: s.activation(THR[:, mm:mm + 1],
                psB[:, 512 + mm:512 + mm + 1], AF.Identity, scale=0.5 / WDEN, bias=CEPS[:, 1:2]))
            last = P.o("scalar", lambda s, mm=m: s.activation(NTHR[:, mm:mm + 1],
                psB[:, 512 + mm:512 + mm + 1], AF.Identity, scale=-0.5 / WDEN,
                bias=CEPS[:, 2:3]), "sACT")
            bank_set_r(["B1"], ("sACT", last))
        K["thr_rdy"] = ("sACT", last)

    def tern_units():
        units = []
        for m in range(3):
            nb = (2, 32) if m < 2 else (8, 8)
            for fb in range(nb[1]):
                for db in range(nb[0]):
                    units.append((m, db, fb, db == nb[0] - 1))
        def emit(u):
            m, db, fb, is_last_db = u
            src, dst = w_tile_aps(m, db, fb)
            P.wd("sync", pp_state["stage_g"])
            ld = P.d("sync", PRES[:].rearrange("p (c f) -> p c f", f=256), src, "qP")
            P.w("gpsimd", "qP", ld)
            P.wd("gpsimd", K["thr_rdy"])
            P.wd("gpsimd", pp_state["lo_g"])
            g1 = P.o("gpsimd", lambda g, mm=m: g.tensor_single_scalar(
                PRELO[:], PRES[:], NTHR[:, mm:mm + 1], ALU.is_le), "sGP")
            P.w("vector", "sGP", g1)
            P.wd("vector", pp_state["out_g"])
            v1 = P.o("vector", lambda v, mm=m: v.scalar_tensor_tensor(
                PREOUT[:], PRES[:], THR[:, mm:mm + 1], PRELO[:],
                ALU.is_ge, ALU.subtract), "sDVE")
            P.w("sync", "sDVE", v1)
            st = P.d("sync", dst, PREOUT[:].rearrange("p (c f) -> p c f", f=256), "qT")
            pp_state["stage_g"] = [("sGP", g1), ("sDVE", v1)]
            pp_state["lo_g"] = [("sDVE", v1)]
            pp_state["out_g"] = [("qT", st)]
            if m < 2 and is_last_db:
                K["tern_st"][m][fb] = st
            if m == 2 and is_last_db and fb == 7:
                K["td_all"] = st
        return units, emit

    def load_head_weights(h, wsrcs, bufs, guard_pe):
        outs = []
        for src, buf in zip(wsrcs, bufs):
            P.wd("gpsimd", guard_pe)
            sap = src.rearrange("(c p) e -> p c e", p=128)[:, :, h * 128:(h + 1) * 128]
            dap = buf[:].rearrange("p (c e) -> p c e", e=128)
            outs.append(P.d("gpsimd", dap, sap, "qG0"))
        return outs[-1]

    def proj4(wbuf, out_sb, bias_col, wdep, sig_out=None, zinc=False):
        rdy = None
        sig_rdy = None
        for tc in range(4):
            bk = "B0" if tc % 2 == 0 else "B1"
            bank_wait_w("tensor", [bk])
            if tc == 0:
                P.w("tensor", "qG0", wdep)
            P.wd("tensor", K["a_rdy"][tc * 4 + 3])
            P.wd("tensor", K["a_rdy"][tc * 4 + 2])
            P.wd("tensor", K["a_rdy"][tc * 4 + 1])
            P.wd("tensor", K["a_rdy"][tc * 4 + 0])
            pe = None
            for c in range(16):
                pe = P.o("tensor", lambda t, cc=c, w=wbuf, b=bk, tcc=tc: t.matmul(
                    bank_ap(b), w[:, cc * 128:(cc + 1) * 128],
                    A[:, cc * 2048 + tcc * 512: cc * 2048 + (tcc + 1) * 512],
                    start=(cc == 0), stop=(cc == 15)), "sPE" if c == 15 else None)
            bank_set_w([bk], pe)
            P.w("scalar", "sPE", pe)
            if bias_col is not None:
                ac = P.o("scalar", lambda s, o=out_sb, b=bk, tcc=tc, bc=bias_col: s.activation(
                    o[:, tcc * 512:(tcc + 1) * 512], bank_ap(b), AF.Identity, bias=bc), "sACT")
            else:
                ac = P.o("scalar", lambda s, o=out_sb, b=bk, tcc=tc: s.copy(
                    o[:, tcc * 512:(tcc + 1) * 512], bank_ap(b)), "sACT")
            deps = [("sACT", ac)]
            if sig_out is not None:
                P.w("vector", "sPE", pe)
                P.w("vector", "sACT", ac)
                sm0 = P.o("vector", lambda v, b=bk: v.tensor_single_scalar(
                    SS0[:, 0:512], bank_ap(b), 0.0, ALU.min), "sDVE")
                P.w("scalar", "sDVE", sm0)
                e1 = P.o("scalar", lambda s: s.activation(SS1[:, 0:512], SS0[:, 0:512],
                                                          AF.Exp), "sACT")
                P.w("vector", "sACT", e1)
                if zinc:
                    sv = P.o("vector", lambda v, o=sig_out, b=bk, tcc=tc: v.scalar_tensor_tensor(
                        o[:, tcc * 512:(tcc + 1) * 512], bank_ap(b), 0.0, SS1[:, 0:512],
                        ALU.max, ALU.add, accum_out=ZINC[:, tcc:tcc + 1]), "sDVE")
                else:
                    sv = P.o("vector", lambda v, o=sig_out, b=bk, tcc=tc: v.scalar_tensor_tensor(
                        o[:, tcc * 512:(tcc + 1) * 512], bank_ap(b), 0.0, SS1[:, 0:512],
                        ALU.max, ALU.add), "sDVE")
                deps.append(("sDVE", sv))
                sig_rdy = sv
            bank_set_r([bk], deps)
            rdy = ac
        return rdy, sig_rdy

    def vproj(wbuf, wdep, bias_col):
        vr, _ = proj4(wbuf, VtT, bias_col, wdep)
        last = None
        for tt in range(NT):
            last = transpose_to(VtT[:, tt * 128:(tt + 1) * 128],
                                Vtok[:, tt * 128:(tt + 1) * 128],
                                pe_dep=("sACT", vr) if tt == 0 else None)
        return last

    def attn_a(pp_iter):
        bol = P.d("sync", bObc[:], DT["bo_bc"], "qC")
        woq_guard = [None, None]
        zsl_guard = [None, None]
        wq_guard = None
        qkv_rdy = None
        for h in range(_DBG_NH if _DBG_NH is not None else H):
            wdep = load_head_weights(h, (DT["wq_a"], DT["wk_a"], DT["wv_a"]),
                                     (Wqh, Wkh, Wvh), wq_guard)
            qr, _ = proj4(Wqh, Qt, BQC[:, h:h + 1], wdep)
            kr, _ = proj4(Wkh, Kt, BKC[:, h:h + 1], wdep)
            vr = vproj(Wvh, wdep, BVC[:, h:h + 1])
            wq_guard = ("sPE", P.cnt["sPE"])
            if _DBG_HOOKS and h == 0:
                P.w("sync", "sACT", qr)
                P.w("sync", "sACT", kr)
                P.w("sync", "sACT", vr)
                P.d("sync", dbgA, A[:], "qSt")
                P.d("sync", dbgq, Qt[:], "qSt")
                P.d("sync", dbgk, Kt[:], "qSt")
                P.d("sync", dbgv, Vtok[:], "qSt")
            zlast = None
            for i in range(NT):
                klen = (i + 1) * 128
                chunks = [(0, min(klen, 1024))]
                if klen > 1024:
                    chunks.append((1024, klen - 1024))
                nch = len(chunks)
                exp_cols = []
                for ci, (c0, wdt) in enumerate(chunks):
                    pair = K["spair_i"] % 2
                    K["spair_i"] += 1
                    pbanks = ["A0", "A1"] if pair == 0 else ["A2", "A3"]
                    pb0 = pair * 1024
                    bank_wait_w("tensor", pbanks)
                    if i == 0 and ci == 0:
                        P.w("tensor", "sACT", qr)
                        P.w("tensor", "sACT", kr)
                        P.w("tensor", "sACT", vr)
                    pe = None
                    nmm = (wdt + 511) // 512
                    for mi in range(nmm):
                        mw = min(512, wdt - mi * 512)
                        pe = P.o("tensor", lambda t, ii=i, b0=pb0, m=mi, w=mw, cc0=c0: t.matmul(
                            psA[:, b0 + m * 512: b0 + m * 512 + w],
                            Qt[:, ii * 128:(ii + 1) * 128],
                            Kt[:, cc0 + m * 512: cc0 + m * 512 + w],
                            start=True, stop=True), "sPE" if mi == nmm - 1 else None)
                    bank_set_w(pbanks, pe)
                    P.w("vector", "sPE", pe)
                    if c0 <= i * 128 < c0 + wdt:
                        off = pb0 + i * 128 - c0
                        P.o("vector", lambda v, o=off: v.tensor_tensor(
                            psA[:, o:o + 128], psA[:, o:o + 128], MASK[:], ALU.add))
                    nmcol = 8 + ci
                    P.o("vector", lambda v, b0=pb0, w=wdt, nm=nmcol: v.tensor_reduce(
                        STAT[:, nm:nm + 1], psA[:, b0:b0 + w], AX.X, ALU.max, negate=True),
                        "sDVE")
                    vv = P.cnt["sDVE"]
                    P.w("gpsimd", "sDVE", vv)
                    gb = P.o("gpsimd", lambda g, nm=nmcol, bm=10 + ci: g.tensor_single_scalar(
                        STAT[:, bm:bm + 1], STAT[:, nm:nm + 1], ISC, ALU.mult), "sGP")
                    P.w("scalar", "sGP", gb)
                    ex = P.o("scalar", lambda s, b0=pb0, w=wdt, cc0=c0, bm=10 + ci, lc=12 + ci:
                             s.activation(Pbuf[:, cc0:cc0 + w], psA[:, b0:b0 + w], AF.Exp,
                                          bias=STAT[:, bm:bm + 1], scale=ISC,
                                          accum_out=STAT[:, lc:lc + 1]), "sACT")
                    bank_set_r(pbanks, [("sACT", ex), ("sDVE", vv)])
                    exp_cols.append((c0, wdt, ex))
                if nch == 1:
                    P.w("vector", "sACT", exp_cols[0][2])
                    P.o("vector", lambda v: v.reciprocal(STAT[:, 14:15], STAT[:, 12:13]))
                    sep("vector", 2)
                    pn = P.o("vector", lambda v, w=klen: v.tensor_scalar(
                        Pbuf[:, 0:w], Pbuf[:, 0:w], STAT[:, 14:15], None, ALU.mult),
                        "sDVE")
                else:
                    P.w("vector", "sACT", exp_cols[1][2])
                    sep("vector", 2)
                    P.o("vector", lambda v: v.tensor_tensor(STAT[:, 15:16], STAT[:, 8:9],
                                                            STAT[:, 9:10], ALU.min))
                    sep("vector", 2)
                    P.o("vector", lambda v: v.tensor_tensor(STAT[:, 16:17], STAT[:, 15:16],
                                                            STAT[:, 8:9], ALU.subtract))
                    vm = P.o("vector", lambda v: v.tensor_tensor(STAT[:, 17:18], STAT[:, 15:16],
                                                                 STAT[:, 9:10], ALU.subtract),
                             "sDVE")
                    P.w("scalar", "sDVE", vm)
                    P.o("scalar", lambda s: s.activation(STAT[:, 18:19], STAT[:, 16:17],
                                                         AF.Exp, scale=ISC))
                    am = P.o("scalar", lambda s: s.activation(STAT[:, 19:20], STAT[:, 17:18],
                                                              AF.Exp, scale=ISC), "sACT")
                    P.w("gpsimd", "sACT", am)
                    P.o("gpsimd", lambda g: g.tensor_tensor(STAT[:, 20:21], STAT[:, 12:13],
                                                            STAT[:, 18:19], ALU.mult))
                    gu = P.o("gpsimd", lambda g: g.tensor_tensor(STAT[:, 21:22], STAT[:, 13:14],
                                                                 STAT[:, 19:20], ALU.mult), "sGP")
                    P.w("vector", "sGP", gu)
                    P.o("vector", lambda v: v.tensor_tensor(STAT[:, 22:23], STAT[:, 20:21],
                                                            STAT[:, 21:22], ALU.add))
                    sep("vector", 2)
                    rlm = P.o("vector", lambda v: v.reciprocal(STAT[:, 14:15], STAT[:, 22:23]),
                              "sDVE")
                    P.w("gpsimd", "sDVE", rlm)
                    P.o("gpsimd", lambda g: g.tensor_tensor(STAT[:, 23:24], STAT[:, 18:19],
                                                            STAT[:, 14:15], ALU.mult))
                    gf = P.o("gpsimd", lambda g: g.tensor_tensor(STAT[:, 24:25], STAT[:, 19:20],
                                                                 STAT[:, 14:15], ALU.mult), "sGP")
                    P.w("vector", "sGP", gf)
                    pn = None
                    for ci, (c0, wdt, _) in enumerate(exp_cols):
                        pn = P.o("vector", lambda v, cc0=c0, w=wdt, fc=23 + ci: v.tensor_scalar(
                            Pbuf[:, cc0:cc0 + w], Pbuf[:, cc0:cc0 + w], STAT[:, fc:fc + 1],
                            None, ALU.mult), "sDVE")
                q4 = i % 4
                avdone = None
                for j in range(i + 1):
                    pt = transpose_to(Pbuf[:, j * 128:(j + 1) * 128],
                                      Ptb[:, j * 128:(j + 1) * 128],
                                      pe_dep=("sDVE", pn) if j == 0 else None)
                    if j == 0:
                        bank_wait_w("tensor", ["B0"])
                    P.w("tensor", "sACT", pt)
                    avdone = P.o("tensor", lambda t, jj=j, q=q4, ii=i: t.matmul(
                        psB[:, q * 128: (q + 1) * 128],
                        Vtok[:, jj * 128:(jj + 1) * 128],
                        Ptb[:, jj * 128:(jj + 1) * 128],
                        start=(jj == 0), stop=(jj == ii)),
                        "sPE" if j == i else None)
                bank_set_w(["B0"], avdone)
                P.w("scalar", "sPE", avdone)
                if i == 0:
                    P.w("scalar", "qZ", K["zb_st"])
                zlast = P.o("scalar", lambda s, ii=i, q=q4: s.copy(
                    Ztb[:, ii * 128:(ii + 1) * 128],
                    psB[:, q * 128: (q + 1) * 128]), "sACT")
                bank_set_r(["B0"], ("sACT", zlast))
            if _DBG_HOOKS and h == 0:
                P.w("sync", "sDVE", P.cnt["sDVE"])
                P.d("sync", dbgp, Pbuf[:], "qSt")
            P.w("sync", "sACT", zlast)
            K["zb_st"] = P.d("sync", zbank[h * 128:(h + 1) * 128, :], Ztb[:], "qZ")
            for _ in range(pp_iter.get("per_head", 0)):
                if pp_iter["units"]:
                    pp_iter["emit"](pp_iter["units"].pop(0))
                elif pp_iter.get("next"):
                    nxt = pp_iter.pop("next")
                    nxt[0]()
                    pp_iter["units"], pp_iter["emit"] = nxt[1]
        if pp_iter.get("next"):
            nxt = pp_iter.pop("next")
            nxt[0]()
            pp_iter["units"], pp_iter["emit"] = nxt[1]
        while pp_iter["units"]:
            pp_iter["emit"](pp_iter["units"].pop(0))

        if _DBG_NOOUT:
            return
        obank = ["A0", "A1", "A2", "A3"]
        yq_rdy = [[None] * 4 for _ in range(4)]
        yto_g = [None] * 4
        for g in range(4):
            for dq in range(4):
                wpar = dq % 2
                P.wd("gpsimd", woq_guard[wpar])
                sap = DT["wo_a"].rearrange("(hh p) d -> p hh d", p=128)[:, :, dq * 512:(dq + 1) * 512]
                dap = WOQ[wpar][:].rearrange("p (hh d) -> p hh d", d=512)
                qg = "qG0" if wpar == 0 else "qG1"
                wl = P.d("gpsimd", dap, sap, qg)
                for ti in range(4):
                    t = g * 4 + ti
                    zpar = t % 2
                    P.wd("sync", zsl_guard[zpar])
                    P.w("sync", "qZ", K["zb_st"])
                    zap = zbank.rearrange("(hh p) t2 -> p hh t2", p=128)[:, :, t * 128:(t + 1) * 128]
                    ql = "qL0" if zpar == 0 else "qL1"
                    zl = P.d("sync", ZSL[zpar][:].rearrange("p (hh q) -> p hh q", q=128), zap, ql)
                    bk = obank[ti]
                    bank_wait_w("tensor", [bk])
                    P.w("tensor", qg, wl)
                    P.w("tensor", ql, zl)
                    pe = None
                    for h2 in range(16):
                        pe = P.o("tensor", lambda t_, hh=h2, zp=zpar, wp=wpar, b=bk: t_.matmul(
                            bank_ap(b), ZSL[zp][:, hh * 128:(hh + 1) * 128],
                            WOQ[wp][:, hh * 512:(hh + 1) * 512],
                            start=(hh == 0), stop=(hh == 15)), "sPE" if h2 == 15 else None)
                    bank_set_w([bk], pe)
                    zsl_guard[zpar] = ("sPE", pe)
                    woq_guard[wpar] = ("sPE", pe)
                    P.w("scalar", "sPE", pe)
                    P.wd("scalar", yto_g[ti])
                    ac = P.o("scalar", lambda s, ti_=ti, dq_=dq, b=bk: s.copy(
                        YTO[ti_][:, dq_ * 512:(dq_ + 1) * 512], bank_ap(b)), "sACT")
                    bank_set_r([bk], ("sACT", ac))
                    yq_rdy[ti][dq] = ac
            for ti in range(4):
                t = g * 4 + ti
                par = t % 2
                P.wd("sync", K["xo_g"][par])
                qx = "qX0" if par == 0 else "qX1"
                xl = P.d("sync", XO[par][:], DT["x0"][t * 128:(t + 1) * 128, :], qx)
                P.w("vector", "sACT", yq_rdy[ti][3])
                P.w("vector", "qC", bol)
                bv = P.o("vector", lambda v, ti_=ti: v.tensor_tensor(
                    YTO[ti_][:], YTO[ti_][:], bObc[:], ALU.add), "sDVE")
                dsta = (xr if stage > 1 else y)
                _, rdr = finish_tile(t, YTO[ti], [("sDVE", bv), (qx, xl)], XO[par],
                                     dsta[t * 128:(t + 1) * 128, :],
                                     "R" if stage > 1 else None, YTE)
                K["xo_g"][par] = rdr
                yto_g[ti] = ("sDVE", P.cnt["sDVE"])

    def ffn(idx, xsrc, xdst, prep_next):
        P.w("sync", "qT", P.cnt["qT"])
        init_dep = [("sDVE", P.cnt["sDVE"]), ("sPE", P.cnt["sPE"]),
                    ("sACT", P.cnt["sACT"])]
        tgu_guard = [list(init_dep), list(init_dep)]
        tdq_guard = [list(init_dep), list(init_dep)]
        ytf_guard = [None] * 4
        gscr_g = None
        last_ga = None
        shd_g = [None] * 4
        for S in range(4):
            sg_rdy = None
            for tloc in range(4):
                t = S * 4 + tloc
                P.wd("vector", K["sx_rdy"][t])
                P.wd("vector", K["thr_rdy"])
                if last_ga is not None and tloc == 0:
                    P.w("vector", "sACT", last_ga)
                P.o("vector", lambda v, tl=tloc, tt=t: v.tensor_tensor(
                    SGU[:, tl:tl + 1], SX[:, tt:tt + 1], WSC[:, 0:1], ALU.mult))
                sg_rdy = P.o("vector", lambda v, tl=tloc, tt=t: v.tensor_tensor(
                    SGU[:, 4 + tl:5 + tl], SX[:, tt:tt + 1], WSC[:, 1:2], ALU.mult), "sDVE")
            hslot_wr = [[] for _ in range(4)]
            for fc in range(32):
                par = fc % 2
                ql = "qL0" if par == 0 else "qL1"
                P.wd("sync", tgu_guard[par])
                gdst = TGU[par][:].rearrange("p (c f) -> p c f", f=512)
                gap = tg.rearrange("(c p) f -> p c f", p=128)[:, :, fc * 256:(fc + 1) * 256]
                uap = tu.rearrange("(c p) f -> p c f", p=128)[:, :, fc * 256:(fc + 1) * 256]
                l1 = P.d("sync", gdst[:, :, 0:256], gap, ql)
                l2 = P.d("sync", gdst[:, :, 256:512], uap, ql)
                for tloc in range(4):
                    t = S * 4 + tloc
                    bk = "B0" if tloc % 2 == 0 else "B1"
                    bank_wait_w("tensor", [bk])
                    if tloc == 0:
                        P.w("tensor", ql, l2)
                    if fc == 0:
                        P.wd("tensor", K["a_rdy"][t])
                    pe = None
                    for c in range(16):
                        pe = P.o("tensor", lambda t_, cc=c, tt=t, p_=par, b=bk: t_.matmul(
                            bank_ap(b), A[:, cc * 2048 + tt * 128: cc * 2048 + (tt + 1) * 128],
                            TGU[p_][:, cc * 512:(cc + 1) * 512],
                            start=(cc == 0), stop=(cc == 15)), "sPE" if c == 15 else None)
                    bank_set_w([bk], pe)
                    tgu_guard[par] = ("sPE", pe)
                    P.w("scalar", "sPE", pe)
                    if sg_rdy:
                        P.w("scalar", "sDVE", sg_rdy)
                    P.wd("scalar", gscr_g)
                    ga = P.o("scalar", lambda s, b=bk, tl=tloc: s.activation(
                        GSCR[:, 0:256], bank_ap(b)[:, 0:256], AF.Sigmoid,
                        scale=SGU[:, tl:tl + 1]), "sACT")
                    last_ga = ga
                    P.w("vector", "sACT", ga)
                    hv = P.o("vector", lambda v, b=bk, tl=tloc, f=fc: v.scalar_tensor_tensor(
                        HSL[tl][:, f * 256:(f + 1) * 256], bank_ap(b)[:, 256:512],
                        SGU[:, 4 + tl:5 + tl], GSCR[:, 0:256], ALU.mult, ALU.mult), "sDVE")
                    hm = P.o("vector", lambda v, tl=tloc, f=fc: v.tensor_reduce(
                        HMXW[:, tl * 32 + f: tl * 32 + f + 1],
                        HSL[tl][:, f * 256:(f + 1) * 256], AX.X,
                        ALU.max, apply_absolute_value=True), "sDVE")
                    gscr_g = ("sDVE", hv)
                    bank_set_r([bk], [("sACT", ga), ("sDVE", hv)])
                    hslot_wr[tloc].append(("sDVE", hm))
            rh_rdy = [None] * 4
            last_ra = None
            for tloc in range(4):
                if hslot_wr[tloc]:
                    P.wd("vector", hslot_wr[tloc][-1])
                if last_ra is not None:
                    P.w("vector", "sACT", last_ra)
                P.o("vector", lambda v, tl=tloc: v.tensor_reduce(
                    STAT[:, 9:10], HMXW[:, tl * 32:(tl + 1) * 32], AX.X, ALU.max))
                sep("vector", 2)
                P.o("vector", lambda v: v.tensor_single_scalar(
                    STAT[:, 27:28], STAT[:, 9:10], QEPS, ALU.max))
                sep("vector", 2)
                rc = P.o("vector", lambda v: v.reciprocal(STAT[:, 26:27], STAT[:, 27:28]),
                         "sDVE")
                P.w("gpsimd", "sDVE", rc)
                v5 = P.o("gpsimd", lambda g: g.tensor_single_scalar(
                    STAT[:, 28:29], STAT[:, 26:27], 127.0, ALU.mult), "sGP")
                P.w("scalar", "sGP", v5)
                sh1 = P.o("scalar", lambda s: s.activation(STAT[:, 29:30], STAT[:, 27:28],
                                                           AF.Copy, scale=1.0 / 127.0), "sACT")
                P.w("vector", "sACT", sh1)
                P.wd("vector", shd_g[tloc])
                shv = P.o("vector", lambda v, tl=tloc: v.tensor_tensor(
                    SHD[:, tl:tl + 1], STAT[:, 29:30], WSC[:, 2:3], ALU.mult), "sDVE")
                shd_w = shv
                last = None
                for fc2 in range(8):
                    P.o("scalar", lambda s, tl=tloc, f2=fc2: s.activation(
                        I32S[:], HSL[tl][:, f2 * 1024:(f2 + 1) * 1024], AF.Copy,
                        scale=STAT[:, 28:29]))
                    ra = P.o("scalar", lambda s: s.activation(RHC[:], I32S[:], AF.Copy),
                             "sACT")
                    last_ra = ra
                    for j in range(8):
                        f = fc2 * 8 + j
                        last = transpose_to(RHC[:, j * 128:(j + 1) * 128],
                                            HSL[tloc][:, f * 128:(f + 1) * 128],
                                            pe_dep=("sACT", ra) if j == 0 else None)
                rh_rdy[tloc] = ("sACT", last)
            shd_rdy = P.cnt["sDVE"]
            dn_pe = [[None] * 4 for _ in range(4)]
            dbank = lambda tl, dc: ["A0", "A1", "A2", "A3"][tl]
            for dc in range(4):
                for fq in range(16):
                    qpar = fq % 2
                    qd = "qL0" if qpar == 0 else "qL1"
                    P.wd("sync", tdq_guard[qpar])
                    tap = td[fq * 512:(fq + 1) * 512, dc * 512:(dc + 1) * 512]
                    tap = tap.rearrange("(q p) d -> p q d", p=128)
                    tl_ = P.d("sync", TDQ[qpar][:].rearrange("p (q d) -> p q d", d=512),
                              tap, qd)
                    for q4 in range(4):
                        f = fq * 4 + q4
                        for tloc in range(4):
                            bk = dbank(tloc, dc)
                            if f == 0:
                                bank_wait_w("tensor", [bk])
                                P.wd("tensor", rh_rdy[tloc])
                            if q4 == 0 and tloc == 0:
                                P.w("tensor", qd, tl_)
                            pe = P.o("tensor", lambda t_, ff=f, tl=tloc, qp=qpar, qq=q4, b=bk:
                                     t_.matmul(bank_ap(b),
                                               HSL[tl][:, ff * 128:(ff + 1) * 128],
                                               TDQ[qp][:, qq * 512:(qq + 1) * 512],
                                               start=(ff == 0), stop=(ff == 63)),
                                     "sPE" if (f == 63 or (q4 == 3 and tloc == 3))
                                     else None)
                            if f == 63:
                                bank_set_w([bk], pe)
                                dn_pe[tloc][dc] = pe
                    tdq_guard[qpar] = ("sPE", P.cnt["sPE"])
                for tloc in range(4):
                    bk = dbank(tloc, dc)
                    P.w("scalar", "sPE", dn_pe[tloc][dc])
                    P.w("scalar", "sDVE", shd_rdy)
                    P.wd("scalar", ytf_guard[tloc])
                    ac = P.o("scalar", lambda s, tl=tloc, d_=dc, b=bk: s.activation(
                        YTF[tl][:, d_ * 512:(d_ + 1) * 512], bank_ap(b), AF.Copy,
                        scale=SHD[:, tl:tl + 1]), "sACT")
                    bank_set_r([bk], ("sACT", ac))
                    shd_g[tloc] = ("sACT", ac)
                    dn_pe[tloc][dc] = ac
            for tloc in range(4):
                t = S * 4 + tloc
                par = t % 2
                P.wd("sync", K["xo_g"][par])
                qx = "qX0" if par == 0 else "qX1"
                xl = P.d("sync", XO[par][:], xsrc[t * 128:(t + 1) * 128, :], qx)
                ydep = [("sACT", dn_pe[tloc][3]), (qx, xl),
                        ("sACT", dn_pe[tloc][0]), ("sACT", dn_pe[tloc][1]),
                        ("sACT", dn_pe[tloc][2])]
                st, rdr = finish_tile(t, YTF[tloc], ydep, XO[par],
                                      xdst[t * 128:(t + 1) * 128, :], prep_next, YTE)
                K["xo_g"][par] = rdr
                ytf_guard[tloc] = [("sDVE", P.cnt["sDVE"])]
            tgu_guard[0] = [("sDVE", P.cnt["sDVE"])]
            tgu_guard[1] = [("sDVE", P.cnt["sDVE"])]

    def attn_b(xsrc, xdst, prep_next, nh=None, noout=False):
        bl = P.d("sync", BETA[:], DT["beta_in"], "qB")
        P.w("scalar", "qB", bl)
        brdy = P.o("scalar", lambda s: s.activation(BETA[:], BETA[:], AF.Sigmoid), "sACT")
        wq_guard = [("sDVE", P.cnt["sDVE"]), ("sPE", P.cnt["sPE"]),
                    ("sACT", P.cnt["sACT"])]
        woq_guard = [None, None]
        zsl_guard = [None, None]
        for h in range(nh if nh is not None else H):
            wdep = load_head_weights(h, (DT["wq_b"], DT["wk_b"], DT["wv_b"]),
                                     (Wqh, Wkh, Wvh), wq_guard)
            qr, sq_rdy = proj4(Wqh, Qt, None, wdep,
                               sig_out=None if _DBG_NOSIG else SqT)
            kr, sk_rdy = proj4(Wkh, Kt, None, wdep,
                               sig_out=None if _DBG_NOSIG else SkT, zinc=True)
            vr = vproj(Wvh, wdep, None)
            wq_guard = ("sPE", P.cnt["sPE"])
            P.o("vector", lambda v: v.memset(MEMF[:], 0.0))
            P.o("vector", lambda v: v.memset(STAT[:, 30:31], 1.0 / 128.0))
            P.o("vector", lambda v: v.memset(MEMZ[:, 0:128], 0.0))
            mz = P.o("vector", lambda v: v.memset(MEMZ[:, 128:129], 1.0 / 128.0), "sDVE")
            mz_rdy = ("sDVE", mz)
            atlast = None
            for s_ in range(_DBG_NSEG if _DBG_NSEG is not None else 4):
                c0 = s_ * 512
                for qi in range(4):
                    qq = K["spair_i"] % 4
                    K["spair_i"] += 1
                    bk = ["A0", "A1", "A2", "A3"][qq]
                    bank_wait_w("tensor", [bk])
                    if s_ == 0 and qi == 0:
                        P.w("tensor", "sACT", qr)
                        P.w("tensor", "sACT", kr)
                        P.w("tensor", "sACT", vr)
                        P.wd("tensor", [("sDVE", sq_rdy), ("sDVE", sk_rdy)])
                    pe = P.o("tensor", lambda t_, ii=qi, cc=c0, b=bk: t_.matmul(
                        bank_ap(b)[:, 0:(ii + 1) * 128], Qt[:, cc + ii * 128: cc + (ii + 1) * 128],
                        Kt[:, cc: cc + (ii + 1) * 128], start=True, stop=True), "sPE")
                    bank_set_w([bk], pe)
                    P.w("vector", "sPE", pe)
                    P.o("vector", lambda v, ii=qi, b=bk: v.tensor_tensor(
                        bank_ap(b)[:, ii * 128:(ii + 1) * 128],
                        bank_ap(b)[:, ii * 128:(ii + 1) * 128], MASK[:], ALU.add))
                    nv = P.o("vector", lambda v, ii=qi, b=bk: v.tensor_reduce(
                        STAT[:, 8:9], bank_ap(b)[:, 0:(ii + 1) * 128], AX.X, ALU.max,
                        negate=True), "sDVE")
                    P.w("gpsimd", "sDVE", nv)
                    gb = P.o("gpsimd", lambda g: g.tensor_single_scalar(
                        STAT[:, 10:11], STAT[:, 8:9], ISC, ALU.mult), "sGP")
                    P.w("scalar", "sGP", gb)
                    ex = P.o("scalar", lambda s, ii=qi, b=bk: s.activation(
                        PbB[:, 0:(ii + 1) * 128], bank_ap(b)[:, 0:(ii + 1) * 128], AF.Exp,
                        bias=STAT[:, 10:11], scale=ISC, accum_out=STAT[:, 12:13]), "sACT")
                    bank_set_r([bk], ("sACT", ex))
                    P.w("vector", "sACT", ex)
                    P.o("vector", lambda v: v.reciprocal(STAT[:, 14:15], STAT[:, 12:13]))
                    sep("vector", 2)
                    pn = P.o("vector", lambda v, ii=qi: v.tensor_scalar(
                        PbB[:, 0:(ii + 1) * 128], PbB[:, 0:(ii + 1) * 128], STAT[:, 14:15],
                        None, ALU.mult), "sDVE")
                    adq = psB[:, qq * 128:(qq + 1) * 128]
                    addone = None
                    for j in range(qi + 1):
                        pt = transpose_to(PbB[:, j * 128:(j + 1) * 128],
                                          PtB[:, j * 128:(j + 1) * 128],
                                          pe_dep=("sDVE", pn) if j == 0 else None)
                        if j == 0:
                            bank_wait_w("tensor", ["B0"])
                        P.w("tensor", "sACT", pt)
                        addone = P.o("tensor", lambda t_, jj=j, ss=s_, ii=qi, a=adq: t_.matmul(
                            a, PtB[:, jj * 128:(jj + 1) * 128],
                            Vtok[:, (ss * 4 + jj) * 128:(ss * 4 + jj + 1) * 128],
                            start=(jj == 0), stop=(jj == ii)), "sPE" if j == qi else None)
                    bank_set_w(["B0"], addone)
                    amb = psB[:, 512:641]
                    bank_wait_w("tensor", ["B1"])
                    P.wd("tensor", mz_rdy)
                    am = P.o("tensor", lambda t_, cc=c0, ii=qi, a=amb: t_.matmul(
                        a, SqT[:, cc + ii * 128: cc + (ii + 1) * 128], MEMZ[:, 0:129],
                        start=True, stop=True), "sPE")
                    bank_set_w(["B1"], am)
                    P.w("vector", "sPE", am)
                    rd = P.o("vector", lambda v: v.reciprocal(STAT[:, 31:32], psB[:, 640:641]),
                             "sDVE")
                    P.w("scalar", "sDVE", rd)
                    amv0 = P.o("scalar", lambda s: s.activation(
                        CMB[:, 0:128], psB[:, 512:640], AF.Identity, scale=STAT[:, 31:32]),
                        "sACT")
                    P.w("vector", "sACT", amv0)
                    amv = P.o("vector", lambda v, a=adq: v.tensor_tensor(
                        CMB[:, 128:256], CMB[:, 0:128], a, ALU.subtract), "sDVE")
                    bank_set_r(["B1"], [("sACT", amv0), ("sDVE", rd)])
                    P.w("gpsimd", "sACT", brdy)
                    P.w("gpsimd", "sDVE", amv)
                    gu2 = P.o("gpsimd", lambda g, hh=h: g.tensor_tensor(
                        CMB[:, 0:128], BETA[:, hh * 128:(hh + 1) * 128], CMB[:, 128:256],
                        ALU.mult), "sGP")
                    P.w("vector", "sGP", gu2)
                    if atlast is not None:
                        P.w("vector", "sACT", atlast)
                    at = P.o("vector", lambda v, a=adq: v.tensor_tensor(
                        XNB[:, 0:128], CMB[:, 0:128], a, ALU.add), "sDVE")
                    bank_set_r(["B0"], ("sDVE", at))
                    atlast = transpose_to(XNB[:, 0:128],
                                          ATTH[:, c0 + qi * 128: c0 + (qi + 1) * 128],
                                          pe_dep=("sDVE", at),
                                          act_dep=(("qZ", K["ab_st"])
                                                   if (s_ == 0 and qi == 0) else None))
                up = psB[:, 0:128]
                upfin = None
                prev_up = None
                for st_ in range(4):
                    smb = psB[:, 512:641]
                    bank_wait_w("tensor", ["B1"])
                    P.wd("tensor", mz_rdy)
                    sm = P.o("tensor", lambda t_, cc=c0, ii=st_, a=smb: t_.matmul(
                        a, SkT[:, cc + ii * 128: cc + (ii + 1) * 128], MEMZ[:, 0:129],
                        start=True, stop=True), "sPE")
                    bank_set_w(["B1"], sm)
                    P.w("vector", "sPE", sm)
                    rk = P.o("vector", lambda v: v.reciprocal(STAT[:, 31:32], psB[:, 640:641]),
                             "sDVE")
                    P.w("scalar", "sDVE", rk)
                    m1 = P.o("scalar", lambda s: s.activation(
                        SS0[:, 0:128], psB[:, 512:640], AF.Identity, scale=STAT[:, 31:32]),
                        "sACT")
                    P.w("vector", "sACT", m1)
                    if prev_up is not None:
                        P.w("vector", "sPE", prev_up)
                    ng = P.o("vector", lambda v, ss=s_, ii=st_: v.tensor_tensor(
                        NGD[:], SS0[:, 0:128],
                        Vtok[:, (ss * 4 + ii) * 128:(ss * 4 + ii + 1) * 128],
                        ALU.subtract), "sDVE")
                    bank_set_r(["B1"], [("sACT", m1), ("sDVE", rk)])
                    stk = transpose_to(SkT[:, c0 + st_ * 128: c0 + (st_ + 1) * 128],
                                       SKTOK[:], pe_dep=None)
                    if st_ == 0:
                        bank_wait_w("tensor", ["B0"])
                    P.w("tensor", "sACT", stk)
                    P.w("tensor", "sDVE", ng)
                    upfin = P.o("tensor", lambda t_, ii=st_, a=up: t_.matmul(
                        a, SKTOK[:], NGD[:], start=(ii == 0), stop=(ii == 3)), "sPE")
                    prev_up = upfin
                bank_set_w(["B0"], upfin)
                P.w("vector", "sPE", upfin)
                P.o("vector", lambda v, a=up: v.tensor_tensor(MEMF[:], MEMF[:], a,
                                                              ALU.subtract))
                zv = P.o("vector", lambda v, ss=s_: v.tensor_tensor(
                    STAT[:, 30:31], STAT[:, 30:31], ZINC[:, ss:ss + 1], ALU.add), "sDVE")
                bank_set_r(["B0"], ("sDVE", zv))
                P.w("scalar", "sDVE", zv)
                P.o("scalar", lambda s: s.copy(MEMZ[:, 0:128], MEMF[:]))
                mzv = P.o("scalar", lambda s: s.copy(MEMZ[:, 128:129], STAT[:, 30:31]),
                          "sACT")
                mz_rdy = ("sACT", mzv)
            P.w("sync", "sACT", atlast)
            K["ab_st"] = P.d("sync", abank[h * 128:(h + 1) * 128, :], ATTH[:], "qZ")
        if noout:
            return
        obank = ["A0", "A1", "A2", "A3"]
        yq_rdy = [[None] * 4 for _ in range(4)]
        yto_g = [None] * 4
        for g in range(4):
            for dq in range(4):
                wpar = dq % 2
                P.wd("gpsimd", woq_guard[wpar])
                sap = DT["wo_b"].rearrange("(hh p) d -> p hh d", p=128)[:, :, dq * 512:(dq + 1) * 512]
                dap = WOQ[wpar][:].rearrange("p (hh d) -> p hh d", d=512)
                qg = "qG0" if wpar == 0 else "qG1"
                wl = P.d("gpsimd", dap, sap, qg)
                for ti in range(4):
                    t = g * 4 + ti
                    zpar = t % 2
                    P.wd("sync", zsl_guard[zpar])
                    P.w("sync", "qZ", K["ab_st"])
                    h0 = (t % 4) * 4
                    seg = t // 4
                    aap = abank.rearrange("(hh p) t2 -> p hh t2", p=128)[
                        :, h0:h0 + 4, seg * 512:(seg + 1) * 512]
                    ql = "qL0" if zpar == 0 else "qL1"
                    zl = P.d("sync", ZSL[zpar][:].rearrange("p (a q) -> p a q", q=512),
                             aap, ql)
                    bk = obank[ti]
                    bank_wait_w("tensor", [bk])
                    P.w("tensor", qg, wl)
                    P.w("tensor", ql, zl)
                    pe = None
                    gat = ZSL[zpar][:].rearrange("p (a b c) -> p a b c", a=4, b=32, c=16)
                    for h2 in range(16):
                        pe = P.o("tensor", lambda t_, hh=h2, wp=wpar, b=bk, gg=gat: t_.matmul(
                            bank_ap(b), gg[:, :, :, hh],
                            WOQ[wp][:, hh * 512:(hh + 1) * 512],
                            start=(hh == 0), stop=(hh == 15)), "sPE" if h2 == 15 else None)
                    bank_set_w([bk], pe)
                    zsl_guard[zpar] = ("sPE", pe)
                    woq_guard[wpar] = ("sPE", pe)
                    P.w("scalar", "sPE", pe)
                    P.wd("scalar", yto_g[ti])
                    ac = P.o("scalar", lambda s, ti_=ti, dq_=dq, b=bk: s.copy(
                        YTO[ti_][:, dq_ * 512:(dq_ + 1) * 512], bank_ap(b)), "sACT")
                    bank_set_r([bk], ("sACT", ac))
                    yq_rdy[ti][dq] = ac
            for ti in range(4):
                t = g * 4 + ti
                par = t % 2
                P.wd("sync", K["xo_g"][par])
                qx = "qX0" if par == 0 else "qX1"
                xl = P.d("sync", XO[par][:], xsrc[t * 128:(t + 1) * 128, :], qx)
                ydep = [("sACT", yq_rdy[ti][3]), (qx, xl),
                        ("sACT", yq_rdy[ti][0]), ("sACT", yq_rdy[ti][1]),
                        ("sACT", yq_rdy[ti][2])]
                _, rdr = finish_tile(t, YTO[ti], ydep, XO[par],
                                     xdst[t * 128:(t + 1) * 128, :], prep_next, YTE)
                K["xo_g"][par] = rdr
                yto_g[ti] = ("sDVE", P.cnt["sDVE"])

    prologue()
    mu, me = mean_units()
    tu_, te = tern_units()
    if stage > 1:
        pp = {"units": mu, "emit": me, "per_head": 26,
              "next": (wscale_finalize, (tu_, te))}
    else:
        pp = {"units": [], "emit": me, "per_head": 0}
    attn_a(pp)
    if stage > 1:
        ffn(0, xr, xr if stage > 2 else y,
            ("xnT" if stage > 2 else None))
    if stage > 2:
        attn_b(xr, xr if stage > 3 else y, ("R" if stage > 3 else None),
               nh=_DBG_NH_B, noout=_DBG_NOOUT_B)
    if stage > 3:
        ffn(1, xr, y, None)
    for s in ("qSt", "qZ", "qT", "qC", "qB", "qX0", "qX1", "qL0", "qL1", "qP",
              "qG0", "qG1"):
        P.w("sync", s, P.cnt[s])

    print(f"[kernel build] planned instructions ~{P.ni}, sems {P.cnt}", file=sys.stderr)
    _emit(P, nc, sems)
    pctx.close()
    sctx.close()
    return nc


def _host_inputs(inputs, b):
    x = np.ascontiguousarray(inputs["L0_attn_a_x"][b])
    wq_a = np.ascontiguousarray(inputs["L0_attn_a_W_Q"].transpose(1, 0, 2).reshape(D, D))
    wk_a = np.ascontiguousarray(inputs["L0_attn_a_W_K"].transpose(1, 0, 2).reshape(D, D))
    wv_a = np.ascontiguousarray(inputs["L0_attn_a_W_V"].transpose(1, 0, 2).reshape(D, D))
    wo_a = np.ascontiguousarray(inputs["L0_attn_a_W_O"].reshape(D, D))
    bq = np.ascontiguousarray(inputs["L0_attn_a_b_Q"].T)
    bk = np.ascontiguousarray(inputs["L0_attn_a_b_K"].T)
    bv = np.ascontiguousarray(inputs["L0_attn_a_b_V"].T)
    bo = np.ascontiguousarray(np.broadcast_to(inputs["L0_attn_a_b_O"].reshape(1, D),
                                              (128, D)))
    beta = np.ascontiguousarray(np.broadcast_to(
        inputs["L0_attn_b_betas"].reshape(1, D), (128, D)))
    wgT = np.ascontiguousarray(inputs["L0_ffn_w_gate"].T)
    wuT = np.ascontiguousarray(inputs["L0_ffn_w_up"].T)
    wdT = np.ascontiguousarray(inputs["L0_ffn_w_down"].T)
    ident = np.eye(128, dtype=BF)
    mask = np.where(np.tril(np.ones((128, 128), bool)), np.float32(0.0),
                    np.float32(NEGBIG)).astype(np.float32)
    ones = np.ones((128, 128), np.float32)
    return {
        "x0": x, "wq_a": wq_a, "wk_a": wk_a, "wv_a": wv_a, "wo_a": wo_a,
        "bq_in": bq, "bk_in": bk, "bv_in": bv, "bo_bc": bo,
        "wq_b": np.ascontiguousarray(inputs["L0_attn_b_Wq"]),
        "wk_b": np.ascontiguousarray(inputs["L0_attn_b_Wk"]),
        "wv_b": np.ascontiguousarray(inputs["L0_attn_b_Wv"]),
        "wo_b": np.ascontiguousarray(inputs["L0_attn_b_Wo"]),
        "beta_in": beta, "wgT": wgT, "wuT": wuT, "wdT": wdT,
        "ident_in": ident, "mask_in": mask, "ones_in": ones,
    }


def kernel(**inputs):
    nc = build_program(_STAGE)
    in_maps = [_host_inputs(inputs, b) for b in range(8)]
    res = run_bass_kernel_spmd(nc, in_maps, core_ids=list(range(8)))
    out = np.stack([np.asarray(r["y"]) for r in res.results], axis=0)
    return out.astype(np.float32)


# version 04
# speedup vs baseline: 1.2680x; 1.2680x over previous
import sys
import numpy as np
import ml_dtypes
from contextlib import ExitStack
import concourse.bass as bass
import concourse.mybir as mybir
from concourse.bass_utils import run_bass_kernel_spmd

F32 = mybir.dt.float32
BF16 = mybir.dt.bfloat16
I32 = mybir.dt.int32
U8 = mybir.dt.uint8
AF = mybir.ActivationFunctionType
ALU = mybir.AluOpType
AX = mybir.AxisListType
BF = ml_dtypes.bfloat16

D = 2048
T = 2048
H = 16
FF = 8192
NT = 16
ISC = float(1.0 / np.sqrt(np.float64(128.0)))
NEGBIG = -1.0e9
LNEPS = 1e-5
QEPS = 1e-5
WDEN = float(D) * float(FF)
KB = 1024

_STAGE = 4
_DBG_ZB = False
_DBG_HOOKS = False
_DBG_NH = None
_DBG_NOOUT = False
_DBG_NH_B = None
_DBG_NOOUT_B = False
_DBG_NSEG = None
_DBG_NOSIG = False
_DBG_RACE = False


class PL:
    ENGS = ("sync", "scalar", "vector", "tensor", "gpsimd")
    SEMS = ("qC", "qB", "qX0", "qX1", "qL0", "qL1", "qP", "qT", "qZ", "qSt",
            "qG0", "qG1", "sPE", "sACT", "sDVE", "sGP")

    def __init__(self):
        self.ops = {e: [] for e in self.ENGS}
        self.cnt = {s: 0 for s in self.SEMS}
        self.ni = 0

    def w(self, eng, sem, val):
        if val is None or val <= 0:
            return
        assert val <= self.cnt[sem], (eng, sem, val, self.cnt[sem])
        self.ops[eng].append(("w", sem, val))

    def wd(self, eng, dep):
        if dep is None:
            return
        if isinstance(dep, tuple):
            dep = [dep]
        for s, v in dep:
            self.w(eng, s, v)

    def o(self, eng, fn, sem=None):
        v = None
        if sem is not None:
            self.cnt[sem] += 1
            v = self.cnt[sem]
        self.ops[eng].append(("o", fn, sem))
        self.ni += 1
        return v

    def d(self, eng, out, in_, sem):
        self.cnt[sem] += 16
        v = self.cnt[sem]
        self.ops[eng].append(("d", out, in_, sem))
        self.ni += 1
        return v


def _emit(P, nc, sems):
    with nc.Block() as block:
        def runner(name):
            def run(eng):
                for op in P.ops[name]:
                    if op[0] == "w":
                        eng.wait_ge(sems[op[1]], op[2])
                    elif op[0] == "d":
                        eng.dma_start(out=op[1], in_=op[2]).then_inc(sems[op[3]], 16)
                    else:
                        ins = op[1](eng)
                        if op[2] is not None:
                            ins.then_inc(sems[op[2]], 1)
            return run
        block.sync(runner("sync"))
        block.scalar(runner("scalar"))
        block.vector(runner("vector"))
        block.tensor(runner("tensor"))
        block.gpsimd(runner("gpsimd"))


def build_program(stage=_STAGE):
    nc = bass.Bass("TRN2", target_bir_lowering=False, detect_race_conditions=_DBG_RACE)

    DT = {}
    def din(name, shape, dt=F32):
        DT[name] = nc.dram_tensor(name, shape, dt, kind="ExternalInput").ap()
    din("x0", [T, D]); din("wq_a", [D, D]); din("wk_a", [D, D]); din("wv_a", [D, D])
    din("wo_a", [D, D]); din("bq_in", [128, H]); din("bk_in", [128, H])
    din("bv_in", [128, H]); din("bo_bc", [128, D])
    din("wq_b", [D, D]); din("wk_b", [D, D]); din("wv_b", [D, D]); din("wo_b", [D, D])
    din("beta_in", [128, D])
    din("wgT", [D, FF]); din("wuT", [D, FF]); din("wdT", [FF, D])
    din("ident_in", [128, 128], BF16); din("mask_in", [128, 128]); din("ones_in", [128, 128])
    y = nc.dram_tensor("y", [T, D], F32, kind="ExternalOutput").ap()
    xr = nc.dram_tensor("xr", [T, D], F32, kind="Internal").ap()
    zkind = "ExternalOutput" if _DBG_ZB else "Internal"
    zbank = nc.dram_tensor("zbank", [H * 128, T], BF16, kind=zkind).ap()
    abank = nc.dram_tensor("abank", [H * 128, T], BF16, kind=zkind).ap()
    if _DBG_HOOKS:
        dbgA = nc.dram_tensor("dbgA", [128, 32 * KB], BF16, kind="ExternalOutput").ap()
        dbgq = nc.dram_tensor("dbgq", [128, T], BF16, kind="ExternalOutput").ap()
        dbgk = nc.dram_tensor("dbgk", [128, T], BF16, kind="ExternalOutput").ap()
        dbgv = nc.dram_tensor("dbgv", [128, T], BF16, kind="ExternalOutput").ap()
        dbgp = nc.dram_tensor("dbgp", [128, T], BF16, kind="ExternalOutput").ap()
    tg = nc.dram_tensor("tg", [D, FF], BF16, kind="Internal").ap()
    tu = nc.dram_tensor("tu", [D, FF], BF16, kind="Internal").ap()
    td = nc.dram_tensor("td", [FF, D], BF16, kind="Internal").ap()

    base = (nc.sbuf_base + 31) & ~31
    assert nc.sbuf_top - base >= 206 * KB, (base, nc.sbuf_top)

    def at(name, off_b, shape, dt):
        assert off_b % 32 == 0, (name, off_b)
        h = nc.alloc_sbuf_tensor_at(name, shape, dt, offset=base + off_b)
        return h

    P2 = 64 * KB
    EO = 172 * KB
    SM = EO + 32 * KB
    A = at("A", 0, [128, 32 * KB], BF16)
    WOQ = [at("WOQ0", P2, [128, 8 * KB], BF16), at("WOQ1", P2 + 16 * KB, [128, 8 * KB], BF16)]
    Qt = at("Qt", P2 + 32 * KB, [128, 2 * KB], BF16)
    Kt = at("Kt", P2 + 36 * KB, [128, 2 * KB], BF16)
    VtT = at("VtT", P2 + 40 * KB, [128, 2 * KB], BF16)
    Vtok = at("Vtok", P2 + 44 * KB, [128, 2 * KB], BF16)
    Pbuf = at("Pbuf", P2 + 48 * KB, [128, 2 * KB], BF16)
    Ptb = at("Ptb", P2 + 52 * KB, [128, 2 * KB], BF16)
    Ztb = at("Ztb", P2 + 56 * KB, [128, 2 * KB], BF16)
    ZSL = [at("zsl0", P2 + 76 * KB, [128, 2 * KB], BF16), at("zsl1", P2 + 80 * KB, [128, 2 * KB], BF16)]
    bObc = at("bObc", P2 + 68 * KB, [128, 2 * KB], F32)
    Wqh = at("Wqh", P2 + 76 * KB, [128, 2 * KB], BF16)
    Wkh = at("Wkh", P2 + 80 * KB, [128, 2 * KB], BF16)
    Wvh = at("Wvh", P2 + 84 * KB, [128, 2 * KB], BF16)
    PRES = at("pres", P2 + 88 * KB, [128, 2 * KB], F32)
    PRELO = at("prelo", P2 + 96 * KB, [128, 2 * KB], BF16)
    PREOUT = at("preout", P2 + 100 * KB, [128, 2 * KB], BF16)
    YTO = [at(f"yto{i}", P2 + (32 + 8 * i) * KB, [128, 2 * KB], F32) for i in range(4)]
    HSL = [at(f"hsl{i}", P2 + 16 * i * KB, [128, 8 * KB], BF16) for i in range(4)]
    TGU = [at("tgu0", P2 + 64 * KB, [128, 8 * KB], BF16), at("tgu1", P2 + 80 * KB, [128, 8 * KB], BF16)]
    TDQ = [at("tdq0", P2 + 96 * KB, [128, 2 * KB], BF16), at("tdq1", P2 + 100 * KB, [128, 2 * KB], BF16)]
    GSCR = at("gscr", P2 + 104 * KB, [128, 256], F32)
    RHC = at("rhc", P2 + 105 * KB, [128, 1 * KB], BF16)
    YTF = [at("ytf0", P2 + 64 * KB, [128, 2 * KB], F32), at("ytf1", P2 + 72 * KB, [128, 2 * KB], F32),
           at("ytf2", P2 + 80 * KB, [128, 2 * KB], F32), at("ytf3", P2 + 88 * KB, [128, 2 * KB], F32)]
    SqT = at("SqT", P2 + 48 * KB, [128, 2 * KB], BF16)
    SkT = at("SkT", P2 + 88 * KB, [128, 2 * KB], BF16)
    ATTH = at("ATTH", P2 + 92 * KB, [128, 2 * KB], BF16)
    SS0 = at("ss0", P2 + 96 * KB, [128, 512], F32)
    SS1 = at("ss1", P2 + 98 * KB, [128, 512], F32)
    CMB = at("cmb", P2 + 100 * KB, [128, 256], F32)
    PbB = at("pbB", P2 + 101 * KB, [128, 512], BF16)
    PtB = at("ptB", P2 + 102 * KB, [128, 512], BF16)
    MEMF = at("memf", P2 + 103 * KB, [128, 128], F32)
    MEMZ = at("memz", P2 + 103 * KB + 512, [128, 132], BF16)
    NGD = at("ngd", P2 + 103 * KB + 800, [128, 128], BF16)
    SKTOK = at("sktok", P2 + 103 * KB + 1056, [128, 128], BF16)
    BETA = at("beta", P2 + 68 * KB, [128, 2 * KB], F32)
    XO = [at("xo0", EO, [128, 2 * KB], F32), at("xo1", EO + 8 * KB, [128, 2 * KB], F32)]
    YTE = at("ytE", EO + 16 * KB, [128, 2 * KB], F32)
    I32S = at("i32s", EO + 24 * KB, [128, 1 * KB], I32)
    XNB = at("xnb", EO + 28 * KB, [128, 2 * KB], BF16)
    MASK = at("maskT", SM, [128, 128], F32)
    IDT = at("idT", SM + 512, [128, 128], BF16)
    ONESF = at("onesT", SM + 768, [128, 128], F32)
    BQC = at("bqc", SM + 1280, [128, H], F32)
    BKC = at("bkc", SM + 1344, [128, H], F32)
    BVC = at("bvc", SM + 1408, [128, H], F32)
    SX = at("sx", SM + 1472, [128, NT], F32)
    STAT = at("stat", SM + 1536, [128, 32], F32)
    WSUM = at("wsum", SM + 1664, [128, 4], F32)
    WSC = at("wsc", SM + 1696, [128, 4], F32)
    THR = at("thr", SM + 1728, [128, 4], F32)
    NTHR = at("nthr", SM + 1760, [128, 4], F32)
    SHD = at("shd", SM + 1792, [128, 4], F32)
    SGU = at("sgu", SM + 1824, [128, 8], F32)
    HMX = at("hmx", SM + 1856, [128, 4], F32)
    ZINC = at("zinc", SM + 1888, [128, 4], F32)
    CEPS = at("ceps", SM + 1920, [128, 4], F32)
    DUMP = at("dump", SM + 1952, [128, 64], F32)
    HMXW = at("hmxw", SM + 2208, [128, 128], F32)

    pctx = ExitStack()
    psA = pctx.enter_context(nc.psum_tensor("psA", [128, 2048], F32))
    psB = pctx.enter_context(nc.psum_tensor("psB", [128, 1024], F32))
    psT = pctx.enter_context(nc.psum_tensor("psT", [128, 1024], BF16))
    psU = pctx.enter_context(nc.psum_tensor("psU", [128, 1024], BF16))
    sctx = ExitStack()
    sems = {s: sctx.enter_context(nc.semaphore(s)) for s in PL.SEMS}

    P = PL()
    K = {
        "bkg": {b: None for b in
                ("A0", "A1", "A2", "A3", "B0", "B1")},
        "bkw": {b: None for b in
                ("A0", "A1", "A2", "A3", "B0", "B1")},
        "psT_i": 0,
        "psT_g": [None, None],
        "xo_g": [None, None],
        "a_rdy": [None] * NT,
        "sx_rdy": [None] * NT,
        "spair_i": 0,
        "tern_st": {m: [0] * 32 for m in range(2)},
        "td_all": 0,
        "thr_rdy": None,
        "zb_st": 0,
        "ab_st": 0,
        "xnb_g": None,
    }

    def bank_ap(name):
        if name.startswith("A"):
            i = int(name[1])
            return psA[:, i * 512:(i + 1) * 512]
        if name.startswith("B"):
            i = int(name[1])
            return psB[:, i * 512:(i + 1) * 512]
        raise KeyError(name)

    def bank_wait_w(eng, names):
        for n in names:
            P.wd(eng, K["bkg"][n])

    def bank_set_w(names, val):
        for n in names:
            K["bkw"][n] = ("sPE", val)

    def bank_wait_r(eng, names):
        for n in names:
            P.wd(eng, K["bkw"][n])

    def bank_set_r(names, dep):
        for n in names:
            old = K["bkg"][n]
            deps = [dep] if isinstance(dep, tuple) else list(dep)
            if old:
                deps = deps + ([old] if isinstance(old, tuple) else list(old))
            best = {}
            for s, v in deps:
                best[s] = max(best.get(s, 0), v)
            K["bkg"][n] = [(s, v) for s, v in best.items()]

    def transpose_to(src_ap, dst_ap, pe_dep=None, act_dep=None):
        i = K["psT_i"] % 2
        K["psT_i"] += 1
        slot = (psT if i == 0 else psU)[:, 0:128]
        P.wd("tensor", K["psT_g"][i])
        P.wd("tensor", pe_dep)
        pe = P.o("tensor", lambda t, s=slot, a=src_ap: t.transpose(s, a, IDT[:]), "sPE")
        P.w("scalar", "sPE", pe)
        P.wd("scalar", act_dep)
        ac = P.o("scalar", lambda s, d=dst_ap, sl=slot: s.copy(d, sl), "sACT")
        K["psT_g"][i] = ("sACT", ac)
        return ac

    ident = DT["ident_in"]; maskin = DT["mask_in"]; onesin = DT["ones_in"]

    cl = P.d("sync", IDT[:], ident, "qC")
    cl = P.d("sync", MASK[:], maskin, "qC")
    cl = P.d("sync", ONESF[:], onesin, "qC")
    cl = P.d("sync", BQC[:], DT["bq_in"], "qC")
    cl = P.d("sync", BKC[:], DT["bk_in"], "qC")
    cl = P.d("sync", BVC[:], DT["bv_in"], "qC")
    consts_rdy = cl
    P.o("vector", lambda v: v.memset(CEPS[:, 0:1], 1e-5))
    P.o("vector", lambda v: v.memset(CEPS[:, 1:2], 5e-6))
    cinit = P.o("vector", lambda v: v.memset(CEPS[:, 2:3], -5e-6), "sDVE")
    P.w("scalar", "sDVE", cinit)

    def sep(eng, n=1):
        for _ in range(n):
            if eng == "scalar":
                P.o("scalar", lambda s: s.activation(DUMP[:, 0:64], MASK[:, 0:64], AF.Copy))
            elif eng == "vector":
                P.o("vector", lambda v: v.tensor_copy(DUMP[:, 0:64], MASK[:, 0:64]))
            else:
                P.o("gpsimd", lambda g: g.tensor_copy(DUMP[:, 0:64], MASK[:, 0:64]))

    def finish_tile(t, yth, yt_dep, xo, dst_tile, prep, scratch_f32):
        if yth is not None:
            P.wd("vector", yt_dep)
            v1 = P.o("vector", lambda v, a=yth, b=xo: v.tensor_tensor(b[:], a[:], b[:], ALU.add), "sDVE")
        else:
            P.wd("vector", yt_dep)
            v1 = None
        if dst_tile is not None:
            if v1:
                P.w("sync", "sDVE", v1)
            st = P.d("sync", dst_tile, xo[:], "qSt")
        else:
            st = None
        v2 = P.o("vector", lambda v, b=xo: v.tensor_reduce(STAT[:, 0:1], b[:], AX.X, ALU.add), "sDVE")
        P.w("gpsimd", "sDVE", v2)
        g1 = P.o("gpsimd", lambda g: g.tensor_single_scalar(
            STAT[:, 1:2], STAT[:, 0:1], -1.0 / D, ALU.mult), "sGP")
        P.w("scalar", "sGP", g1)
        P.o("scalar", lambda s, b=xo, d=scratch_f32: s.activation(
            d[:], b[:], AF.Square, bias=STAT[:, 1:2], accum_out=STAT[:, 2:3]))
        sep("scalar", 2)
        a3 = P.o("scalar", lambda s: s.activation(STAT[:, 7:8], STAT[:, 2:3], AF.Sqrt,
                                                  scale=1.0 / D, bias=CEPS[:, 0:1]), "sACT")
        P.w("vector", "sACT", a3)
        P.o("vector", lambda v: v.reciprocal(STAT[:, 3:4], STAT[:, 7:8]))
        sep("vector", 2)
        rdr = [("sACT", a3)]
        if prep == "xnT":
            P.wd("vector", K["xnb_g"])
            v3 = P.o("vector", lambda v, b=xo: v.tensor_scalar(
                XNB[:], b[:], STAT[:, 1:2], STAT[:, 3:4], ALU.add, ALU.mult), "sDVE")
            rdr.append(("sDVE", v3))
            last = None
            for c in range(16):
                last = transpose_to(XNB[:, c * 128:(c + 1) * 128],
                                    A[:, c * 2048 + t * 128: c * 2048 + (t + 1) * 128],
                                    pe_dep=("sDVE", v3) if c == 0 else None)
            K["a_rdy"][t] = ("sACT", last)
            K["xnb_g"] = ("sACT", last)
        elif prep == "R":
            P.wd("vector", K["xnb_g"])
            v3 = P.o("vector", lambda v, b=xo, d=scratch_f32: v.tensor_scalar(
                d[:], b[:], STAT[:, 1:2], STAT[:, 3:4], ALU.add, ALU.mult), "sDVE")
            v4 = P.o("vector", lambda v, d=scratch_f32: v.tensor_reduce(
                STAT[:, 4:5], d[:], AX.X, ALU.max, apply_absolute_value=True), "sDVE")
            sep("vector", 2)
            P.o("vector", lambda v: v.tensor_single_scalar(
                STAT[:, 5:6], STAT[:, 4:5], QEPS, ALU.max))
            sep("vector", 2)
            rc = P.o("vector", lambda v: v.reciprocal(STAT[:, 8:9], STAT[:, 5:6]), "sDVE")
            P.w("gpsimd", "sDVE", rc)
            v5 = P.o("gpsimd", lambda g: g.tensor_single_scalar(
                STAT[:, 6:7], STAT[:, 8:9], 127.0, ALU.mult), "sGP")
            rdr.append(("sDVE", v4))
            P.w("scalar", "sGP", v5)
            sxr = P.o("scalar", lambda s, tt=t: s.activation(SX[:, tt:tt + 1], STAT[:, 5:6],
                                                             AF.Copy, scale=1.0 / 127.0), "sACT")
            K["sx_rdy"][t] = ("sACT", sxr)
            last = None
            for half in range(2):
                P.o("scalar", lambda s, d=scratch_f32, q=half: s.activation(
                    I32S[:], d[:, q * 1024:(q + 1) * 1024], AF.Copy, scale=STAT[:, 6:7]))
                ra = P.o("scalar", lambda s, q=half: s.activation(
                    XNB[:, q * 1024:(q + 1) * 1024], I32S[:], AF.Copy), "sACT")
                for cc in range(8):
                    c = half * 8 + cc
                    last = transpose_to(XNB[:, c * 128:(c + 1) * 128],
                                        A[:, c * 2048 + t * 128: c * 2048 + (t + 1) * 128],
                                        pe_dep=("sACT", ra) if cc == 0 else None)
            K["a_rdy"][t] = ("sACT", last)
            K["xnb_g"] = ("sACT", last)
        rdr2 = list(rdr)
        if st is not None:
            rdr2.append(("qSt", st))
        return st, rdr2

    def prologue():
        for e in ("tensor", "scalar", "vector", "gpsimd"):
            P.w(e, "qC", consts_rdy)
        for t in range(NT):
            par = t % 2
            P.wd("sync", K["xo_g"][par])
            qx = "qX0" if par == 0 else "qX1"
            ld = P.d("sync", XO[par][:], DT["x0"][t * 128:(t + 1) * 128, :], qx)
            _, rdr = finish_tile(t, None, [(qx, ld)], XO[par], None, "xnT", YTE)
            K["xo_g"][par] = rdr

    WSRC = [DT["wgT"], DT["wuT"], DT["wdT"]]
    TDST = [tg, tu, td]

    def w_tile_aps(m, db, fb):
        src = WSRC[m][db * 1024:(db + 1) * 1024, fb * 256:(fb + 1) * 256]
        src = src.rearrange("(c p) f -> p c f", p=128)
        dst = TDST[m][db * 1024:(db + 1) * 1024, fb * 256:(fb + 1) * 256]
        dst = dst.rearrange("(c p) f -> p c f", p=128)
        return src, dst

    pp_state = {"stage_g": None, "lo_g": None, "out_g": None, "wsum_last": [None] * 3}

    def mean_units():
        units = []
        for m in range(3):
            nb = (2, 32) if m < 2 else (8, 8)
            zi = P.o("gpsimd", lambda g, mm=m: g.memset(WSUM[:, mm:mm + 1], 0.0), "sGP")
            pp_state["wsum_last"][m] = ("sGP", zi)
            for db in range(nb[0]):
                for fb in range(nb[1]):
                    units.append((m, db, fb))
        def emit(u):
            m, db, fb = u
            src, _ = w_tile_aps(m, db, fb)
            P.wd("sync", pp_state["stage_g"])
            ld = P.d("sync", PRES[:].rearrange("p (c f) -> p c f", f=256), src, "qP")
            P.w("scalar", "qP", ld)
            P.wd("scalar", pp_state.get("gv_g"))
            aa = P.o("scalar", lambda s: s.activation(PRELO[:], PRES[:], AF.Abs,
                                                      accum_out=STAT[:, 25:26]), "sACT")
            P.w("gpsimd", "sACT", aa)
            gv = P.o("gpsimd", lambda g, mm=m: g.tensor_tensor(
                WSUM[:, mm:mm + 1], WSUM[:, mm:mm + 1], STAT[:, 25:26], ALU.add), "sGP")
            pp_state["gv_g"] = ("sGP", gv)
            pp_state["stage_g"] = [("sACT", aa)]
            pp_state["lo_g"] = [("sACT", aa)]
            pp_state["wsum_last"][m] = ("sGP", gv)
        return units, emit

    def wscale_finalize():
        last = None
        for m in range(3):
            bank_wait_w("tensor", ["B1"])
            P.wd("tensor", pp_state["wsum_last"][m])
            pe = P.o("tensor", lambda t, mm=m: t.matmul(
                psB[:, 512 + mm:512 + mm + 1], ONESF[:], WSUM[:, mm:mm + 1],
                start=True, stop=True), "sPE")
            bank_set_w(["B1"], pe)
            P.w("scalar", "sPE", pe)
            P.o("scalar", lambda s, mm=m: s.activation(WSC[:, mm:mm + 1],
                psB[:, 512 + mm:512 + mm + 1], AF.Identity, scale=1.0 / WDEN, bias=CEPS[:, 0:1]))
            P.o("scalar", lambda s, mm=m: s.activation(THR[:, mm:mm + 1],
                psB[:, 512 + mm:512 + mm + 1], AF.Identity, scale=0.5 / WDEN, bias=CEPS[:, 1:2]))
            last = P.o("scalar", lambda s, mm=m: s.activation(NTHR[:, mm:mm + 1],
                psB[:, 512 + mm:512 + mm + 1], AF.Identity, scale=-0.5 / WDEN,
                bias=CEPS[:, 2:3]), "sACT")
            bank_set_r(["B1"], ("sACT", last))
        K["thr_rdy"] = ("sACT", last)

    def tern_units():
        units = []
        for m in range(3):
            nb = (2, 32) if m < 2 else (8, 8)
            for fb in range(nb[1]):
                for db in range(nb[0]):
                    units.append((m, db, fb, db == nb[0] - 1))
        def emit(u):
            m, db, fb, is_last_db = u
            src, dst = w_tile_aps(m, db, fb)
            P.wd("sync", pp_state["stage_g"])
            ld = P.d("sync", PRES[:].rearrange("p (c f) -> p c f", f=256), src, "qP")
            P.w("vector", "qP", ld)
            P.wd("vector", K["thr_rdy"])
            P.wd("vector", pp_state["out_g"])
            P.o("vector", lambda v, mm=m: v.tensor_single_scalar(
                PRELO[:], PRES[:], NTHR[:, mm:mm + 1], ALU.is_le))
            v1 = P.o("vector", lambda v, mm=m: v.scalar_tensor_tensor(
                PREOUT[:], PRES[:], THR[:, mm:mm + 1], PRELO[:],
                ALU.is_ge, ALU.subtract), "sDVE")
            P.w("sync", "sDVE", v1)
            st = P.d("sync", dst, PREOUT[:].rearrange("p (c f) -> p c f", f=256), "qT")
            pp_state["stage_g"] = [("sDVE", v1)]
            pp_state["out_g"] = [("qT", st)]
            if m < 2 and is_last_db:
                K["tern_st"][m][fb] = st
            if m == 2 and is_last_db and fb == 7:
                K["td_all"] = st
        return units, emit

    def load_head_weights(h, wsrcs, bufs, guard_pe):
        outs = []
        for src, buf in zip(wsrcs, bufs):
            P.wd("gpsimd", guard_pe)
            sap = src.rearrange("(c p) e -> p c e", p=128)[:, :, h * 128:(h + 1) * 128]
            dap = buf[:].rearrange("p (c e) -> p c e", e=128)
            outs.append(P.d("gpsimd", dap, sap, "qG0"))
        return outs[-1]

    def proj4(wbuf, out_sb, bias_col, wdep, sig_out=None, zinc=False):
        rdy = None
        sig_rdy = None
        for tc in range(4):
            bk = "B0" if tc % 2 == 0 else "B1"
            bank_wait_w("tensor", [bk])
            if tc == 0:
                P.w("tensor", "qG0", wdep)
            P.wd("tensor", K["a_rdy"][tc * 4 + 3])
            P.wd("tensor", K["a_rdy"][tc * 4 + 2])
            P.wd("tensor", K["a_rdy"][tc * 4 + 1])
            P.wd("tensor", K["a_rdy"][tc * 4 + 0])
            pe = None
            for c in range(16):
                pe = P.o("tensor", lambda t, cc=c, w=wbuf, b=bk, tcc=tc: t.matmul(
                    bank_ap(b), w[:, cc * 128:(cc + 1) * 128],
                    A[:, cc * 2048 + tcc * 512: cc * 2048 + (tcc + 1) * 512],
                    start=(cc == 0), stop=(cc == 15)), "sPE" if c == 15 else None)
            bank_set_w([bk], pe)
            P.w("scalar", "sPE", pe)
            if bias_col is not None:
                ac = P.o("scalar", lambda s, o=out_sb, b=bk, tcc=tc, bc=bias_col: s.activation(
                    o[:, tcc * 512:(tcc + 1) * 512], bank_ap(b), AF.Identity, bias=bc), "sACT")
            else:
                ac = P.o("scalar", lambda s, o=out_sb, b=bk, tcc=tc: s.copy(
                    o[:, tcc * 512:(tcc + 1) * 512], bank_ap(b)), "sACT")
            deps = [("sACT", ac)]
            if sig_out is not None:
                P.w("vector", "sPE", pe)
                P.w("vector", "sACT", ac)
                sm0 = P.o("vector", lambda v, b=bk: v.tensor_single_scalar(
                    SS0[:, 0:512], bank_ap(b), 0.0, ALU.min), "sDVE")
                P.w("scalar", "sDVE", sm0)
                e1 = P.o("scalar", lambda s: s.activation(SS1[:, 0:512], SS0[:, 0:512],
                                                          AF.Exp), "sACT")
                P.w("vector", "sACT", e1)
                if zinc:
                    sv = P.o("vector", lambda v, o=sig_out, b=bk, tcc=tc: v.scalar_tensor_tensor(
                        o[:, tcc * 512:(tcc + 1) * 512], bank_ap(b), 0.0, SS1[:, 0:512],
                        ALU.max, ALU.add, accum_out=ZINC[:, tcc:tcc + 1]), "sDVE")
                else:
                    sv = P.o("vector", lambda v, o=sig_out, b=bk, tcc=tc: v.scalar_tensor_tensor(
                        o[:, tcc * 512:(tcc + 1) * 512], bank_ap(b), 0.0, SS1[:, 0:512],
                        ALU.max, ALU.add), "sDVE")
                deps.append(("sDVE", sv))
                sig_rdy = sv
            bank_set_r([bk], deps)
            rdy = ac
        return rdy, sig_rdy

    def vproj(wbuf, wdep, bias_col):
        vr, _ = proj4(wbuf, VtT, bias_col, wdep)
        last = None
        for tt in range(NT):
            last = transpose_to(VtT[:, tt * 128:(tt + 1) * 128],
                                Vtok[:, tt * 128:(tt + 1) * 128],
                                pe_dep=("sACT", vr) if tt == 0 else None)
        return last

    def attn_a(pp_iter):
        bol = P.d("sync", bObc[:], DT["bo_bc"], "qC")
        woq_guard = [None, None]
        zsl_guard = [None, None]
        wq_guard = None
        qkv_rdy = None
        for h in range(_DBG_NH if _DBG_NH is not None else H):
            wdep = load_head_weights(h, (DT["wq_a"], DT["wk_a"], DT["wv_a"]),
                                     (Wqh, Wkh, Wvh), wq_guard)
            qr, _ = proj4(Wqh, Qt, BQC[:, h:h + 1], wdep)
            kr, _ = proj4(Wkh, Kt, BKC[:, h:h + 1], wdep)
            vr = vproj(Wvh, wdep, BVC[:, h:h + 1])
            wq_guard = ("sPE", P.cnt["sPE"])
            if _DBG_HOOKS and h == 0:
                P.w("sync", "sACT", qr)
                P.w("sync", "sACT", kr)
                P.w("sync", "sACT", vr)
                P.d("sync", dbgA, A[:], "qSt")
                P.d("sync", dbgq, Qt[:], "qSt")
                P.d("sync", dbgk, Kt[:], "qSt")
                P.d("sync", dbgv, Vtok[:], "qSt")
            zlast = None
            for i in range(NT):
                klen = (i + 1) * 128
                chunks = [(0, min(klen, 1024))]
                if klen > 1024:
                    chunks.append((1024, klen - 1024))
                nch = len(chunks)
                exp_cols = []
                for ci, (c0, wdt) in enumerate(chunks):
                    pair = K["spair_i"] % 2
                    K["spair_i"] += 1
                    pbanks = ["A0", "A1"] if pair == 0 else ["A2", "A3"]
                    pb0 = pair * 1024
                    bank_wait_w("tensor", pbanks)
                    if i == 0 and ci == 0:
                        P.w("tensor", "sACT", qr)
                        P.w("tensor", "sACT", kr)
                        P.w("tensor", "sACT", vr)
                    pe = None
                    nmm = (wdt + 511) // 512
                    for mi in range(nmm):
                        mw = min(512, wdt - mi * 512)
                        pe = P.o("tensor", lambda t, ii=i, b0=pb0, m=mi, w=mw, cc0=c0: t.matmul(
                            psA[:, b0 + m * 512: b0 + m * 512 + w],
                            Qt[:, ii * 128:(ii + 1) * 128],
                            Kt[:, cc0 + m * 512: cc0 + m * 512 + w],
                            start=True, stop=True), "sPE" if mi == nmm - 1 else None)
                    bank_set_w(pbanks, pe)
                    P.w("vector", "sPE", pe)
                    if c0 <= i * 128 < c0 + wdt:
                        off = pb0 + i * 128 - c0
                        P.o("vector", lambda v, o=off: v.tensor_tensor(
                            psA[:, o:o + 128], psA[:, o:o + 128], MASK[:], ALU.add))
                    nmcol = 8 + ci
                    P.o("vector", lambda v, b0=pb0, w=wdt, nm=nmcol: v.tensor_reduce(
                        STAT[:, nm:nm + 1], psA[:, b0:b0 + w], AX.X, ALU.max, negate=True),
                        "sDVE")
                    vv = P.cnt["sDVE"]
                    P.w("gpsimd", "sDVE", vv)
                    gb = P.o("gpsimd", lambda g, nm=nmcol, bm=10 + ci: g.tensor_single_scalar(
                        STAT[:, bm:bm + 1], STAT[:, nm:nm + 1], ISC, ALU.mult), "sGP")
                    P.w("scalar", "sGP", gb)
                    ex = P.o("scalar", lambda s, b0=pb0, w=wdt, cc0=c0, bm=10 + ci, lc=12 + ci:
                             s.activation(Pbuf[:, cc0:cc0 + w], psA[:, b0:b0 + w], AF.Exp,
                                          bias=STAT[:, bm:bm + 1], scale=ISC,
                                          accum_out=STAT[:, lc:lc + 1]), "sACT")
                    bank_set_r(pbanks, [("sACT", ex), ("sDVE", vv)])
                    exp_cols.append((c0, wdt, ex))
                if nch == 1:
                    P.w("vector", "sACT", exp_cols[0][2])
                    P.o("vector", lambda v: v.reciprocal(STAT[:, 14:15], STAT[:, 12:13]))
                    sep("vector", 2)
                    pn = P.o("vector", lambda v, w=klen: v.tensor_scalar(
                        Pbuf[:, 0:w], Pbuf[:, 0:w], STAT[:, 14:15], None, ALU.mult),
                        "sDVE")
                else:
                    P.w("vector", "sACT", exp_cols[1][2])
                    sep("vector", 2)
                    P.o("vector", lambda v: v.tensor_tensor(STAT[:, 15:16], STAT[:, 8:9],
                                                            STAT[:, 9:10], ALU.min))
                    sep("vector", 2)
                    P.o("vector", lambda v: v.tensor_tensor(STAT[:, 16:17], STAT[:, 15:16],
                                                            STAT[:, 8:9], ALU.subtract))
                    vm = P.o("vector", lambda v: v.tensor_tensor(STAT[:, 17:18], STAT[:, 15:16],
                                                                 STAT[:, 9:10], ALU.subtract),
                             "sDVE")
                    P.w("scalar", "sDVE", vm)
                    P.o("scalar", lambda s: s.activation(STAT[:, 18:19], STAT[:, 16:17],
                                                         AF.Exp, scale=ISC))
                    am = P.o("scalar", lambda s: s.activation(STAT[:, 19:20], STAT[:, 17:18],
                                                              AF.Exp, scale=ISC), "sACT")
                    P.w("gpsimd", "sACT", am)
                    P.o("gpsimd", lambda g: g.tensor_tensor(STAT[:, 20:21], STAT[:, 12:13],
                                                            STAT[:, 18:19], ALU.mult))
                    gu = P.o("gpsimd", lambda g: g.tensor_tensor(STAT[:, 21:22], STAT[:, 13:14],
                                                                 STAT[:, 19:20], ALU.mult), "sGP")
                    P.w("vector", "sGP", gu)
                    P.o("vector", lambda v: v.tensor_tensor(STAT[:, 22:23], STAT[:, 20:21],
                                                            STAT[:, 21:22], ALU.add))
                    sep("vector", 2)
                    rlm = P.o("vector", lambda v: v.reciprocal(STAT[:, 14:15], STAT[:, 22:23]),
                              "sDVE")
                    P.w("gpsimd", "sDVE", rlm)
                    P.o("gpsimd", lambda g: g.tensor_tensor(STAT[:, 23:24], STAT[:, 18:19],
                                                            STAT[:, 14:15], ALU.mult))
                    gf = P.o("gpsimd", lambda g: g.tensor_tensor(STAT[:, 24:25], STAT[:, 19:20],
                                                                 STAT[:, 14:15], ALU.mult), "sGP")
                    P.w("vector", "sGP", gf)
                    pn = None
                    for ci, (c0, wdt, _) in enumerate(exp_cols):
                        pn = P.o("vector", lambda v, cc0=c0, w=wdt, fc=23 + ci: v.tensor_scalar(
                            Pbuf[:, cc0:cc0 + w], Pbuf[:, cc0:cc0 + w], STAT[:, fc:fc + 1],
                            None, ALU.mult), "sDVE")
                q4 = i % 4
                avdone = None
                for j in range(i + 1):
                    pt = transpose_to(Pbuf[:, j * 128:(j + 1) * 128],
                                      Ptb[:, j * 128:(j + 1) * 128],
                                      pe_dep=("sDVE", pn) if j == 0 else None)
                    if j == 0:
                        bank_wait_w("tensor", ["B0"])
                    P.w("tensor", "sACT", pt)
                    avdone = P.o("tensor", lambda t, jj=j, q=q4, ii=i: t.matmul(
                        psB[:, q * 128: (q + 1) * 128],
                        Vtok[:, jj * 128:(jj + 1) * 128],
                        Ptb[:, jj * 128:(jj + 1) * 128],
                        start=(jj == 0), stop=(jj == ii)),
                        "sPE" if j == i else None)
                bank_set_w(["B0"], avdone)
                P.w("scalar", "sPE", avdone)
                if i == 0:
                    P.w("scalar", "qZ", K["zb_st"])
                zlast = P.o("scalar", lambda s, ii=i, q=q4: s.copy(
                    Ztb[:, ii * 128:(ii + 1) * 128],
                    psB[:, q * 128: (q + 1) * 128]), "sACT")
                bank_set_r(["B0"], ("sACT", zlast))
            if _DBG_HOOKS and h == 0:
                P.w("sync", "sDVE", P.cnt["sDVE"])
                P.d("sync", dbgp, Pbuf[:], "qSt")
            P.w("sync", "sACT", zlast)
            K["zb_st"] = P.d("sync", zbank[h * 128:(h + 1) * 128, :], Ztb[:], "qZ")
            for _ in range(pp_iter.get("per_head", 0)):
                if pp_iter["units"]:
                    pp_iter["emit"](pp_iter["units"].pop(0))
                elif pp_iter.get("next"):
                    nxt = pp_iter.pop("next")
                    nxt[0]()
                    pp_iter["units"], pp_iter["emit"] = nxt[1]
        if pp_iter.get("next"):
            nxt = pp_iter.pop("next")
            nxt[0]()
            pp_iter["units"], pp_iter["emit"] = nxt[1]
        while pp_iter["units"]:
            pp_iter["emit"](pp_iter["units"].pop(0))

        if _DBG_NOOUT:
            return
        obank = ["A0", "A1", "A2", "A3"]
        yq_rdy = [[None] * 4 for _ in range(4)]
        yto_g = [None] * 4
        wls = {}
        zls = {}
        def load_woq(g, dq):
            wpar = dq % 2
            P.wd("gpsimd", woq_guard[wpar])
            sap = DT["wo_a"].rearrange("(hh p) d -> p hh d", p=128)[:, :, dq * 512:(dq + 1) * 512]
            dap = WOQ[wpar][:].rearrange("p (hh d) -> p hh d", d=512)
            qg = "qG0" if wpar == 0 else "qG1"
            wls[(g, dq)] = (qg, P.d("gpsimd", dap, sap, qg))
        def load_zsl(g, dq, ti):
            t = g * 4 + ti
            zpar = t % 2
            P.wd("sync", zsl_guard[zpar])
            P.w("sync", "qZ", K["zb_st"])
            zap = zbank.rearrange("(hh p) t2 -> p hh t2", p=128)[:, :, t * 128:(t + 1) * 128]
            ql = "qL0" if zpar == 0 else "qL1"
            zls[(g, dq, ti)] = (ql, P.d("sync", ZSL[zpar][:].rearrange(
                "p (hh q) -> p hh q", q=128), zap, ql))
        steps = [(g, dq) for g in range(4) for dq in range(4)]
        load_woq(*steps[0])
        load_zsl(*steps[0], 0)
        zseq = [(g, dq, ti) for (g, dq) in steps for ti in range(4)]
        zi_next = 1
        for si, (g, dq) in enumerate(steps):
            if si + 1 < len(steps):
                load_woq(*steps[si + 1])
            for ti in range(4):
                t = g * 4 + ti
                while zi_next < len(zseq) and zi_next <= si * 4 + ti + 1:
                    load_zsl(*zseq[zi_next]); zi_next += 1
                qg, wl = wls[(g, dq)]
                ql, zl = zls[(g, dq, ti)]
                zpar = t % 2
                wpar = dq % 2
                bk = obank[ti]
                bank_wait_w("tensor", [bk])
                P.w("tensor", qg, wl)
                P.w("tensor", ql, zl)
                pe = None
                for h2 in range(16):
                    pe = P.o("tensor", lambda t_, hh=h2, zp=zpar, wp=wpar, b=bk: t_.matmul(
                        bank_ap(b), ZSL[zp][:, hh * 128:(hh + 1) * 128],
                        WOQ[wp][:, hh * 512:(hh + 1) * 512],
                        start=(hh == 0), stop=(hh == 15)), "sPE" if h2 == 15 else None)
                bank_set_w([bk], pe)
                zsl_guard[zpar] = ("sPE", pe)
                woq_guard[wpar] = ("sPE", pe)
                P.w("scalar", "sPE", pe)
                P.wd("scalar", yto_g[ti])
                ac = P.o("scalar", lambda s, ti_=ti, dq_=dq, b=bk: s.copy(
                    YTO[ti_][:, dq_ * 512:(dq_ + 1) * 512], bank_ap(b)), "sACT")
                bank_set_r([bk], ("sACT", ac))
                yq_rdy[ti][dq] = ac
            if dq == 3:
                for ti in range(4):
                    t = g * 4 + ti
                    par = t % 2
                    P.wd("sync", K["xo_g"][par])
                    qx = "qX0" if par == 0 else "qX1"
                    xl = P.d("sync", XO[par][:], DT["x0"][t * 128:(t + 1) * 128, :], qx)
                    P.w("vector", "sACT", yq_rdy[ti][3])
                    P.w("vector", "qC", bol)
                    bv = P.o("vector", lambda v, ti_=ti: v.tensor_tensor(
                        YTO[ti_][:], YTO[ti_][:], bObc[:], ALU.add), "sDVE")
                    dsta = (xr if stage > 1 else y)
                    _, rdr = finish_tile(t, YTO[ti], [("sDVE", bv), (qx, xl)], XO[par],
                                         dsta[t * 128:(t + 1) * 128, :],
                                         "R" if stage > 1 else None, YTE)
                    K["xo_g"][par] = rdr
                    yto_g[ti] = ("sDVE", P.cnt["sDVE"])

    def ffn(idx, xsrc, xdst, prep_next):
        P.w("sync", "qT", P.cnt["qT"])
        init_dep = [("sDVE", P.cnt["sDVE"]), ("sPE", P.cnt["sPE"]),
                    ("sACT", P.cnt["sACT"])]
        tgu_guard = [list(init_dep), list(init_dep)]
        tdq_guard = [list(init_dep), list(init_dep)]
        ytf_guard = [None] * 4
        gscr_g = None
        last_ga = None
        shd_g = [None] * 4
        for S in range(4):
            sg_rdy = None
            for tloc in range(4):
                t = S * 4 + tloc
                P.wd("vector", K["sx_rdy"][t])
                P.wd("vector", K["thr_rdy"])
                if last_ga is not None and tloc == 0:
                    P.w("vector", "sACT", last_ga)
                P.o("vector", lambda v, tl=tloc, tt=t: v.tensor_tensor(
                    SGU[:, tl:tl + 1], SX[:, tt:tt + 1], WSC[:, 0:1], ALU.mult))
                sg_rdy = P.o("vector", lambda v, tl=tloc, tt=t: v.tensor_tensor(
                    SGU[:, 4 + tl:5 + tl], SX[:, tt:tt + 1], WSC[:, 1:2], ALU.mult), "sDVE")
            hslot_wr = [[] for _ in range(4)]
            def load_tgu(fc):
                par = fc % 2
                ql = "qL0" if par == 0 else "qL1"
                P.wd("sync", tgu_guard[par])
                gdst = TGU[par][:].rearrange("p (c f) -> p c f", f=512)
                gap = tg.rearrange("(c p) f -> p c f", p=128)[:, :, fc * 256:(fc + 1) * 256]
                uap = tu.rearrange("(c p) f -> p c f", p=128)[:, :, fc * 256:(fc + 1) * 256]
                P.d("sync", gdst[:, :, 0:256], gap, ql)
                return ql, P.d("sync", gdst[:, :, 256:512], uap, ql)
            tgul = {0: load_tgu(0)}
            for fc in range(32):
                par = fc % 2
                if fc + 1 < 32:
                    tgul[fc + 1] = load_tgu(fc + 1)
                ql, l2 = tgul.pop(fc)
                for tloc in range(4):
                    t = S * 4 + tloc
                    bk = "B0" if tloc % 2 == 0 else "B1"
                    bank_wait_w("tensor", [bk])
                    if tloc == 0:
                        P.w("tensor", ql, l2)
                    if fc == 0:
                        P.wd("tensor", K["a_rdy"][t])
                    pe = None
                    for c in range(16):
                        pe = P.o("tensor", lambda t_, cc=c, tt=t, p_=par, b=bk: t_.matmul(
                            bank_ap(b), A[:, cc * 2048 + tt * 128: cc * 2048 + (tt + 1) * 128],
                            TGU[p_][:, cc * 512:(cc + 1) * 512],
                            start=(cc == 0), stop=(cc == 15)), "sPE" if c == 15 else None)
                    bank_set_w([bk], pe)
                    tgu_guard[par] = ("sPE", pe)
                    P.w("scalar", "sPE", pe)
                    if sg_rdy:
                        P.w("scalar", "sDVE", sg_rdy)
                    P.wd("scalar", gscr_g)
                    ga = P.o("scalar", lambda s, b=bk, tl=tloc: s.activation(
                        GSCR[:, 0:256], bank_ap(b)[:, 0:256], AF.Sigmoid,
                        scale=SGU[:, tl:tl + 1]), "sACT")
                    last_ga = ga
                    P.w("vector", "sACT", ga)
                    hv = P.o("vector", lambda v, b=bk, tl=tloc, f=fc: v.scalar_tensor_tensor(
                        HSL[tl][:, f * 256:(f + 1) * 256], bank_ap(b)[:, 256:512],
                        SGU[:, 4 + tl:5 + tl], GSCR[:, 0:256], ALU.mult, ALU.mult), "sDVE")
                    hm = P.o("vector", lambda v, tl=tloc, f=fc: v.tensor_reduce(
                        HMXW[:, tl * 32 + f: tl * 32 + f + 1],
                        HSL[tl][:, f * 256:(f + 1) * 256], AX.X,
                        ALU.max, apply_absolute_value=True), "sDVE")
                    gscr_g = ("sDVE", hv)
                    bank_set_r([bk], [("sACT", ga), ("sDVE", hv)])
                    hslot_wr[tloc].append(("sDVE", hm))
            rh_rdy = [None] * 4
            last_ra = None
            for tloc in range(4):
                if hslot_wr[tloc]:
                    P.wd("vector", hslot_wr[tloc][-1])
                if last_ra is not None:
                    P.w("vector", "sACT", last_ra)
                P.o("vector", lambda v, tl=tloc: v.tensor_reduce(
                    STAT[:, 9:10], HMXW[:, tl * 32:(tl + 1) * 32], AX.X, ALU.max))
                sep("vector", 2)
                P.o("vector", lambda v: v.tensor_single_scalar(
                    STAT[:, 27:28], STAT[:, 9:10], QEPS, ALU.max))
                sep("vector", 2)
                rc = P.o("vector", lambda v: v.reciprocal(STAT[:, 26:27], STAT[:, 27:28]),
                         "sDVE")
                P.w("gpsimd", "sDVE", rc)
                v5 = P.o("gpsimd", lambda g: g.tensor_single_scalar(
                    STAT[:, 28:29], STAT[:, 26:27], 127.0, ALU.mult), "sGP")
                P.w("scalar", "sGP", v5)
                sh1 = P.o("scalar", lambda s: s.activation(STAT[:, 29:30], STAT[:, 27:28],
                                                           AF.Copy, scale=1.0 / 127.0), "sACT")
                P.w("vector", "sACT", sh1)
                P.wd("vector", shd_g[tloc])
                shv = P.o("vector", lambda v, tl=tloc: v.tensor_tensor(
                    SHD[:, tl:tl + 1], STAT[:, 29:30], WSC[:, 2:3], ALU.mult), "sDVE")
                shd_w = shv
                last = None
                for fc2 in range(8):
                    P.o("scalar", lambda s, tl=tloc, f2=fc2: s.activation(
                        I32S[:], HSL[tl][:, f2 * 1024:(f2 + 1) * 1024], AF.Copy,
                        scale=STAT[:, 28:29]))
                    ra = P.o("scalar", lambda s: s.activation(RHC[:], I32S[:], AF.Copy),
                             "sACT")
                    last_ra = ra
                    for j in range(8):
                        f = fc2 * 8 + j
                        last = transpose_to(RHC[:, j * 128:(j + 1) * 128],
                                            HSL[tloc][:, f * 128:(f + 1) * 128],
                                            pe_dep=("sACT", ra) if j == 0 else None)
                rh_rdy[tloc] = ("sACT", last)
            shd_rdy = P.cnt["sDVE"]
            dn_pe = [[None] * 4 for _ in range(4)]
            dbank = lambda tl, dc: ["A0", "A1", "A2", "A3"][tl]
            def load_tdq(dc, fq):
                qpar = fq % 2
                qd = "qL0" if qpar == 0 else "qL1"
                P.wd("sync", tdq_guard[qpar])
                tap = td[fq * 512:(fq + 1) * 512, dc * 512:(dc + 1) * 512]
                tap = tap.rearrange("(q p) d -> p q d", p=128)
                return qd, P.d("sync", TDQ[qpar][:].rearrange("p (q d) -> p q d", d=512),
                               tap, qd)
            dsteps = [(dc, fq) for dc in range(4) for fq in range(16)]
            tdql = {dsteps[0]: load_tdq(*dsteps[0])}
            for di, (dc, fq) in enumerate(dsteps):
                if di + 1 < len(dsteps):
                    tdql[dsteps[di + 1]] = load_tdq(*dsteps[di + 1])
                qd, tl_ = tdql.pop((dc, fq))
                if True:
                    qpar = fq % 2
                    for q4 in range(4):
                        f = fq * 4 + q4
                        for tloc in range(4):
                            bk = dbank(tloc, dc)
                            if f == 0:
                                bank_wait_w("tensor", [bk])
                                P.wd("tensor", rh_rdy[tloc])
                            if q4 == 0 and tloc == 0:
                                P.w("tensor", qd, tl_)
                            pe = P.o("tensor", lambda t_, ff=f, tl=tloc, qp=qpar, qq=q4, b=bk:
                                     t_.matmul(bank_ap(b),
                                               HSL[tl][:, ff * 128:(ff + 1) * 128],
                                               TDQ[qp][:, qq * 512:(qq + 1) * 512],
                                               start=(ff == 0), stop=(ff == 63)),
                                     "sPE" if (f == 63 or (q4 == 3 and tloc == 3))
                                     else None)
                            if f == 63:
                                bank_set_w([bk], pe)
                                dn_pe[tloc][dc] = pe
                    tdq_guard[qpar] = ("sPE", P.cnt["sPE"])
                if fq != 15:
                    continue
                for tloc in range(4):
                    bk = dbank(tloc, dc)
                    P.w("scalar", "sPE", dn_pe[tloc][dc])
                    P.w("scalar", "sDVE", shd_rdy)
                    P.wd("scalar", ytf_guard[tloc])
                    ac = P.o("scalar", lambda s, tl=tloc, d_=dc, b=bk: s.activation(
                        YTF[tl][:, d_ * 512:(d_ + 1) * 512], bank_ap(b), AF.Copy,
                        scale=SHD[:, tl:tl + 1]), "sACT")
                    bank_set_r([bk], ("sACT", ac))
                    shd_g[tloc] = ("sACT", ac)
                    dn_pe[tloc][dc] = ac
            for tloc in range(4):
                t = S * 4 + tloc
                par = t % 2
                P.wd("sync", K["xo_g"][par])
                qx = "qX0" if par == 0 else "qX1"
                xl = P.d("sync", XO[par][:], xsrc[t * 128:(t + 1) * 128, :], qx)
                ydep = [("sACT", dn_pe[tloc][3]), (qx, xl),
                        ("sACT", dn_pe[tloc][0]), ("sACT", dn_pe[tloc][1]),
                        ("sACT", dn_pe[tloc][2])]
                st, rdr = finish_tile(t, YTF[tloc], ydep, XO[par],
                                      xdst[t * 128:(t + 1) * 128, :], prep_next, YTE)
                K["xo_g"][par] = rdr
                ytf_guard[tloc] = [("sDVE", P.cnt["sDVE"])]
            tgu_guard[0] = [("sDVE", P.cnt["sDVE"])]
            tgu_guard[1] = [("sDVE", P.cnt["sDVE"])]

    def attn_b(xsrc, xdst, prep_next, nh=None, noout=False):
        bl = P.d("sync", BETA[:], DT["beta_in"], "qB")
        P.w("scalar", "qB", bl)
        brdy = P.o("scalar", lambda s: s.activation(BETA[:], BETA[:], AF.Sigmoid), "sACT")
        wq_guard = [("sDVE", P.cnt["sDVE"]), ("sPE", P.cnt["sPE"]),
                    ("sACT", P.cnt["sACT"])]
        woq_guard = [None, None]
        zsl_guard = [None, None]
        for h in range(nh if nh is not None else H):
            wdep = load_head_weights(h, (DT["wq_b"], DT["wk_b"], DT["wv_b"]),
                                     (Wqh, Wkh, Wvh), wq_guard)
            qr, sq_rdy = proj4(Wqh, Qt, None, wdep,
                               sig_out=None if _DBG_NOSIG else SqT)
            kr, sk_rdy = proj4(Wkh, Kt, None, wdep,
                               sig_out=None if _DBG_NOSIG else SkT, zinc=True)
            vr = vproj(Wvh, wdep, None)
            wq_guard = ("sPE", P.cnt["sPE"])
            P.o("vector", lambda v: v.memset(MEMF[:], 0.0))
            P.o("vector", lambda v: v.memset(STAT[:, 30:31], 1.0 / 128.0))
            P.o("vector", lambda v: v.memset(MEMZ[:, 0:128], 0.0))
            mz = P.o("vector", lambda v: v.memset(MEMZ[:, 128:129], 1.0 / 128.0), "sDVE")
            mz_rdy = ("sDVE", mz)
            atlast = None
            for s_ in range(_DBG_NSEG if _DBG_NSEG is not None else 4):
                c0 = s_ * 512
                for qi in range(4):
                    qq = K["spair_i"] % 4
                    K["spair_i"] += 1
                    bk = ["A0", "A1", "A2", "A3"][qq]
                    bank_wait_w("tensor", [bk])
                    if s_ == 0 and qi == 0:
                        P.w("tensor", "sACT", qr)
                        P.w("tensor", "sACT", kr)
                        P.w("tensor", "sACT", vr)
                        P.wd("tensor", [("sDVE", sq_rdy), ("sDVE", sk_rdy)])
                    pe = P.o("tensor", lambda t_, ii=qi, cc=c0, b=bk: t_.matmul(
                        bank_ap(b)[:, 0:(ii + 1) * 128], Qt[:, cc + ii * 128: cc + (ii + 1) * 128],
                        Kt[:, cc: cc + (ii + 1) * 128], start=True, stop=True), "sPE")
                    bank_set_w([bk], pe)
                    P.w("vector", "sPE", pe)
                    P.o("vector", lambda v, ii=qi, b=bk: v.tensor_tensor(
                        bank_ap(b)[:, ii * 128:(ii + 1) * 128],
                        bank_ap(b)[:, ii * 128:(ii + 1) * 128], MASK[:], ALU.add))
                    nv = P.o("vector", lambda v, ii=qi, b=bk: v.tensor_reduce(
                        STAT[:, 8:9], bank_ap(b)[:, 0:(ii + 1) * 128], AX.X, ALU.max,
                        negate=True), "sDVE")
                    P.w("gpsimd", "sDVE", nv)
                    gb = P.o("gpsimd", lambda g: g.tensor_single_scalar(
                        STAT[:, 10:11], STAT[:, 8:9], ISC, ALU.mult), "sGP")
                    P.w("scalar", "sGP", gb)
                    ex = P.o("scalar", lambda s, ii=qi, b=bk: s.activation(
                        PbB[:, 0:(ii + 1) * 128], bank_ap(b)[:, 0:(ii + 1) * 128], AF.Exp,
                        bias=STAT[:, 10:11], scale=ISC, accum_out=STAT[:, 12:13]), "sACT")
                    bank_set_r([bk], ("sACT", ex))
                    P.w("vector", "sACT", ex)
                    P.o("vector", lambda v: v.reciprocal(STAT[:, 14:15], STAT[:, 12:13]))
                    sep("vector", 2)
                    pn = P.o("vector", lambda v, ii=qi: v.tensor_scalar(
                        PbB[:, 0:(ii + 1) * 128], PbB[:, 0:(ii + 1) * 128], STAT[:, 14:15],
                        None, ALU.mult), "sDVE")
                    adq = psB[:, qq * 128:(qq + 1) * 128]
                    addone = None
                    for j in range(qi + 1):
                        pt = transpose_to(PbB[:, j * 128:(j + 1) * 128],
                                          PtB[:, j * 128:(j + 1) * 128],
                                          pe_dep=("sDVE", pn) if j == 0 else None)
                        if j == 0:
                            bank_wait_w("tensor", ["B0"])
                        P.w("tensor", "sACT", pt)
                        addone = P.o("tensor", lambda t_, jj=j, ss=s_, ii=qi, a=adq: t_.matmul(
                            a, PtB[:, jj * 128:(jj + 1) * 128],
                            Vtok[:, (ss * 4 + jj) * 128:(ss * 4 + jj + 1) * 128],
                            start=(jj == 0), stop=(jj == ii)), "sPE" if j == qi else None)
                    bank_set_w(["B0"], addone)
                    amb = psB[:, 512:641]
                    bank_wait_w("tensor", ["B1"])
                    P.wd("tensor", mz_rdy)
                    am = P.o("tensor", lambda t_, cc=c0, ii=qi, a=amb: t_.matmul(
                        a, SqT[:, cc + ii * 128: cc + (ii + 1) * 128], MEMZ[:, 0:129],
                        start=True, stop=True), "sPE")
                    bank_set_w(["B1"], am)
                    P.w("vector", "sPE", am)
                    rd = P.o("vector", lambda v: v.reciprocal(STAT[:, 31:32], psB[:, 640:641]),
                             "sDVE")
                    P.w("scalar", "sDVE", rd)
                    amv0 = P.o("scalar", lambda s: s.activation(
                        CMB[:, 0:128], psB[:, 512:640], AF.Identity, scale=STAT[:, 31:32]),
                        "sACT")
                    P.w("vector", "sACT", amv0)
                    amv = P.o("vector", lambda v, a=adq: v.tensor_tensor(
                        CMB[:, 128:256], CMB[:, 0:128], a, ALU.subtract), "sDVE")
                    bank_set_r(["B1"], [("sACT", amv0), ("sDVE", rd)])
                    P.w("gpsimd", "sACT", brdy)
                    P.w("gpsimd", "sDVE", amv)
                    gu2 = P.o("gpsimd", lambda g, hh=h: g.tensor_tensor(
                        CMB[:, 0:128], BETA[:, hh * 128:(hh + 1) * 128], CMB[:, 128:256],
                        ALU.mult), "sGP")
                    P.w("vector", "sGP", gu2)
                    if atlast is not None:
                        P.w("vector", "sACT", atlast)
                    at = P.o("vector", lambda v, a=adq: v.tensor_tensor(
                        XNB[:, 0:128], CMB[:, 0:128], a, ALU.add), "sDVE")
                    bank_set_r(["B0"], ("sDVE", at))
                    atlast = transpose_to(XNB[:, 0:128],
                                          ATTH[:, c0 + qi * 128: c0 + (qi + 1) * 128],
                                          pe_dep=("sDVE", at),
                                          act_dep=(("qZ", K["ab_st"])
                                                   if (s_ == 0 and qi == 0) else None))
                up = psB[:, 0:128]
                upfin = None
                prev_up = None
                for st_ in range(4):
                    smb = psB[:, 512:641]
                    bank_wait_w("tensor", ["B1"])
                    P.wd("tensor", mz_rdy)
                    sm = P.o("tensor", lambda t_, cc=c0, ii=st_, a=smb: t_.matmul(
                        a, SkT[:, cc + ii * 128: cc + (ii + 1) * 128], MEMZ[:, 0:129],
                        start=True, stop=True), "sPE")
                    bank_set_w(["B1"], sm)
                    P.w("vector", "sPE", sm)
                    rk = P.o("vector", lambda v: v.reciprocal(STAT[:, 31:32], psB[:, 640:641]),
                             "sDVE")
                    P.w("scalar", "sDVE", rk)
                    m1 = P.o("scalar", lambda s: s.activation(
                        SS0[:, 0:128], psB[:, 512:640], AF.Identity, scale=STAT[:, 31:32]),
                        "sACT")
                    P.w("vector", "sACT", m1)
                    if prev_up is not None:
                        P.w("vector", "sPE", prev_up)
                    ng = P.o("vector", lambda v, ss=s_, ii=st_: v.tensor_tensor(
                        NGD[:], SS0[:, 0:128],
                        Vtok[:, (ss * 4 + ii) * 128:(ss * 4 + ii + 1) * 128],
                        ALU.subtract), "sDVE")
                    bank_set_r(["B1"], [("sACT", m1), ("sDVE", rk)])
                    stk = transpose_to(SkT[:, c0 + st_ * 128: c0 + (st_ + 1) * 128],
                                       SKTOK[:], pe_dep=None)
                    if st_ == 0:
                        bank_wait_w("tensor", ["B0"])
                    P.w("tensor", "sACT", stk)
                    P.w("tensor", "sDVE", ng)
                    upfin = P.o("tensor", lambda t_, ii=st_, a=up: t_.matmul(
                        a, SKTOK[:], NGD[:], start=(ii == 0), stop=(ii == 3)), "sPE")
                    prev_up = upfin
                bank_set_w(["B0"], upfin)
                P.w("vector", "sPE", upfin)
                P.o("vector", lambda v, a=up: v.tensor_tensor(MEMF[:], MEMF[:], a,
                                                              ALU.subtract))
                zv = P.o("vector", lambda v, ss=s_: v.tensor_tensor(
                    STAT[:, 30:31], STAT[:, 30:31], ZINC[:, ss:ss + 1], ALU.add), "sDVE")
                bank_set_r(["B0"], ("sDVE", zv))
                P.w("scalar", "sDVE", zv)
                P.o("scalar", lambda s: s.copy(MEMZ[:, 0:128], MEMF[:]))
                mzv = P.o("scalar", lambda s: s.copy(MEMZ[:, 128:129], STAT[:, 30:31]),
                          "sACT")
                mz_rdy = ("sACT", mzv)
            P.w("sync", "sACT", atlast)
            K["ab_st"] = P.d("sync", abank[h * 128:(h + 1) * 128, :], ATTH[:], "qZ")
        if noout:
            return
        obank = ["A0", "A1", "A2", "A3"]
        yq_rdy = [[None] * 4 for _ in range(4)]
        yto_g = [None] * 4
        wls = {}
        zls = {}
        def load_woq(g, dq):
            wpar = dq % 2
            P.wd("gpsimd", woq_guard[wpar])
            sap = DT["wo_b"].rearrange("(hh p) d -> p hh d", p=128)[:, :, dq * 512:(dq + 1) * 512]
            dap = WOQ[wpar][:].rearrange("p (hh d) -> p hh d", d=512)
            qg = "qG0" if wpar == 0 else "qG1"
            wls[(g, dq)] = (qg, P.d("gpsimd", dap, sap, qg))
        def load_zsl(g, dq, ti):
            t = g * 4 + ti
            zpar = t % 2
            P.wd("sync", zsl_guard[zpar])
            P.w("sync", "qZ", K["ab_st"])
            h0 = (t % 4) * 4
            seg = t // 4
            aap = abank.rearrange("(hh p) t2 -> p hh t2", p=128)[
                :, h0:h0 + 4, seg * 512:(seg + 1) * 512]
            ql = "qL0" if zpar == 0 else "qL1"
            zls[(g, dq, ti)] = (ql, P.d("sync", ZSL[zpar][:].rearrange(
                "p (a q) -> p a q", q=512), aap, ql))
        steps = [(g, dq) for g in range(4) for dq in range(4)]
        load_woq(*steps[0])
        load_zsl(*steps[0], 0)
        zseq = [(g, dq, ti) for (g, dq) in steps for ti in range(4)]
        zi_next = 1
        for si, (g, dq) in enumerate(steps):
            if si + 1 < len(steps):
                load_woq(*steps[si + 1])
            for ti in range(4):
                t = g * 4 + ti
                while zi_next < len(zseq) and zi_next <= si * 4 + ti + 1:
                    load_zsl(*zseq[zi_next]); zi_next += 1
                qg, wl = wls[(g, dq)]
                ql, zl = zls[(g, dq, ti)]
                zpar = t % 2
                wpar = dq % 2
                bk = obank[ti]
                bank_wait_w("tensor", [bk])
                P.w("tensor", qg, wl)
                P.w("tensor", ql, zl)
                pe = None
                gat = ZSL[zpar][:].rearrange("p (a b c) -> p a b c", a=4, b=32, c=16)
                for h2 in range(16):
                    pe = P.o("tensor", lambda t_, hh=h2, wp=wpar, b=bk, gg=gat: t_.matmul(
                        bank_ap(b), gg[:, :, :, hh],
                        WOQ[wp][:, hh * 512:(hh + 1) * 512],
                        start=(hh == 0), stop=(hh == 15)), "sPE" if h2 == 15 else None)
                bank_set_w([bk], pe)
                zsl_guard[zpar] = ("sPE", pe)
                woq_guard[wpar] = ("sPE", pe)
                P.w("scalar", "sPE", pe)
                P.wd("scalar", yto_g[ti])
                ac = P.o("scalar", lambda s, ti_=ti, dq_=dq, b=bk: s.copy(
                    YTO[ti_][:, dq_ * 512:(dq_ + 1) * 512], bank_ap(b)), "sACT")
                bank_set_r([bk], ("sACT", ac))
                yq_rdy[ti][dq] = ac
            if dq == 3:
                for ti in range(4):
                    t = g * 4 + ti
                    par = t % 2
                    P.wd("sync", K["xo_g"][par])
                    qx = "qX0" if par == 0 else "qX1"
                    xl = P.d("sync", XO[par][:], xsrc[t * 128:(t + 1) * 128, :], qx)
                    ydep = [("sACT", yq_rdy[ti][3]), (qx, xl),
                            ("sACT", yq_rdy[ti][0]), ("sACT", yq_rdy[ti][1]),
                            ("sACT", yq_rdy[ti][2])]
                    _, rdr = finish_tile(t, YTO[ti], ydep, XO[par],
                                         xdst[t * 128:(t + 1) * 128, :], prep_next, YTE)
                    K["xo_g"][par] = rdr
                    yto_g[ti] = ("sDVE", P.cnt["sDVE"])

    prologue()
    mu, me = mean_units()
    tu_, te = tern_units()
    if stage > 1:
        pp = {"units": mu, "emit": me, "per_head": 26,
              "next": (wscale_finalize, (tu_, te))}
    else:
        pp = {"units": [], "emit": me, "per_head": 0}
    attn_a(pp)
    if stage > 1:
        ffn(0, xr, xr if stage > 2 else y,
            ("xnT" if stage > 2 else None))
    if stage > 2:
        attn_b(xr, xr if stage > 3 else y, ("R" if stage > 3 else None),
               nh=_DBG_NH_B, noout=_DBG_NOOUT_B)
    if stage > 3:
        ffn(1, xr, y, None)
    for s in ("qSt", "qZ", "qT", "qC", "qB", "qX0", "qX1", "qL0", "qL1", "qP",
              "qG0", "qG1"):
        P.w("sync", s, P.cnt[s])

    print(f"[kernel build] planned instructions ~{P.ni}, sems {P.cnt}", file=sys.stderr)
    _emit(P, nc, sems)
    pctx.close()
    sctx.close()
    return nc


def _host_inputs(inputs, b):
    x = np.ascontiguousarray(inputs["L0_attn_a_x"][b])
    wq_a = np.ascontiguousarray(inputs["L0_attn_a_W_Q"].transpose(1, 0, 2).reshape(D, D))
    wk_a = np.ascontiguousarray(inputs["L0_attn_a_W_K"].transpose(1, 0, 2).reshape(D, D))
    wv_a = np.ascontiguousarray(inputs["L0_attn_a_W_V"].transpose(1, 0, 2).reshape(D, D))
    wo_a = np.ascontiguousarray(inputs["L0_attn_a_W_O"].reshape(D, D))
    bq = np.ascontiguousarray(inputs["L0_attn_a_b_Q"].T)
    bk = np.ascontiguousarray(inputs["L0_attn_a_b_K"].T)
    bv = np.ascontiguousarray(inputs["L0_attn_a_b_V"].T)
    bo = np.ascontiguousarray(np.broadcast_to(inputs["L0_attn_a_b_O"].reshape(1, D),
                                              (128, D)))
    beta = np.ascontiguousarray(np.broadcast_to(
        inputs["L0_attn_b_betas"].reshape(1, D), (128, D)))
    wgT = np.ascontiguousarray(inputs["L0_ffn_w_gate"].T)
    wuT = np.ascontiguousarray(inputs["L0_ffn_w_up"].T)
    wdT = np.ascontiguousarray(inputs["L0_ffn_w_down"].T)
    ident = np.eye(128, dtype=BF)
    mask = np.where(np.tril(np.ones((128, 128), bool)), np.float32(0.0),
                    np.float32(NEGBIG)).astype(np.float32)
    ones = np.ones((128, 128), np.float32)
    return {
        "x0": x, "wq_a": wq_a, "wk_a": wk_a, "wv_a": wv_a, "wo_a": wo_a,
        "bq_in": bq, "bk_in": bk, "bv_in": bv, "bo_bc": bo,
        "wq_b": np.ascontiguousarray(inputs["L0_attn_b_Wq"]),
        "wk_b": np.ascontiguousarray(inputs["L0_attn_b_Wk"]),
        "wv_b": np.ascontiguousarray(inputs["L0_attn_b_Wv"]),
        "wo_b": np.ascontiguousarray(inputs["L0_attn_b_Wo"]),
        "beta_in": beta, "wgT": wgT, "wuT": wuT, "wdT": wdT,
        "ident_in": ident, "mask_in": mask, "ones_in": ones,
    }


def kernel(**inputs):
    nc = build_program(_STAGE)
    in_maps = [_host_inputs(inputs, b) for b in range(8)]
    res = run_bass_kernel_spmd(nc, in_maps, core_ids=list(range(8)))
    out = np.stack([np.asarray(r["y"]) for r in res.results], axis=0)
    return out.astype(np.float32)


# version 05
# speedup vs baseline: 1.4449x; 1.1395x over previous
import sys
import numpy as np
import ml_dtypes
from contextlib import ExitStack
import concourse.bass as bass
import concourse.mybir as mybir
from concourse.bass_utils import run_bass_kernel_spmd

F32 = mybir.dt.float32
BF16 = mybir.dt.bfloat16
I32 = mybir.dt.int32
U8 = mybir.dt.uint8
AF = mybir.ActivationFunctionType
ALU = mybir.AluOpType
AX = mybir.AxisListType
BF = ml_dtypes.bfloat16

D = 2048
T = 2048
H = 16
FF = 8192
NT = 16
ISC = float(1.0 / np.sqrt(np.float64(128.0)))
NEGBIG = -1.0e9
LNEPS = 1e-5
QEPS = 1e-5
WDEN = float(D) * float(FF)
KB = 1024

_STAGE = 4
_DBG_ZB = False
_DBG_HOOKS = False
_DBG_NH = None
_DBG_NOOUT = False
_DBG_NH_B = None
_DBG_NOOUT_B = False
_DBG_NSEG = None
_DBG_NOSIG = False
_DBG_RACE = False


class PL:
    ENGS = ("sync", "scalar", "vector", "tensor", "gpsimd")
    SEMS = ("qC", "qB", "qX0", "qX1", "qL0", "qL1", "qP", "qT", "qZ", "qSt",
            "qG0", "qG1", "sPE", "sACT", "sDVE", "sGP")

    def __init__(self):
        self.ops = {e: [] for e in self.ENGS}
        self.cnt = {s: 0 for s in self.SEMS}
        self.ni = 0

    def w(self, eng, sem, val):
        if val is None or val <= 0:
            return
        assert val <= self.cnt[sem], (eng, sem, val, self.cnt[sem])
        self.ops[eng].append(("w", sem, val))

    def wd(self, eng, dep):
        if dep is None:
            return
        if isinstance(dep, tuple):
            dep = [dep]
        for s, v in dep:
            self.w(eng, s, v)

    def o(self, eng, fn, sem=None):
        v = None
        if sem is not None:
            self.cnt[sem] += 1
            v = self.cnt[sem]
        self.ops[eng].append(("o", fn, sem))
        self.ni += 1
        return v

    def d(self, eng, out, in_, sem):
        self.cnt[sem] += 16
        v = self.cnt[sem]
        self.ops[eng].append(("d", out, in_, sem))
        self.ni += 1
        return v


def _emit(P, nc, sems):
    with nc.Block() as block:
        def runner(name):
            def run(eng):
                for op in P.ops[name]:
                    if op[0] == "w":
                        eng.wait_ge(sems[op[1]], op[2])
                    elif op[0] == "d":
                        eng.dma_start(out=op[1], in_=op[2]).then_inc(sems[op[3]], 16)
                    else:
                        ins = op[1](eng)
                        if op[2] is not None:
                            ins.then_inc(sems[op[2]], 1)
            return run
        block.sync(runner("sync"))
        block.scalar(runner("scalar"))
        block.vector(runner("vector"))
        block.tensor(runner("tensor"))
        block.gpsimd(runner("gpsimd"))


def build_program(stage=_STAGE):
    nc = bass.Bass("TRN2", target_bir_lowering=False, detect_race_conditions=_DBG_RACE)

    DT = {}
    def din(name, shape, dt=F32):
        DT[name] = nc.dram_tensor(name, shape, dt, kind="ExternalInput").ap()
    din("x0", [T, D]); din("wq_a", [D, D]); din("wk_a", [D, D]); din("wv_a", [D, D])
    din("wo_a", [D, D]); din("bq_in", [128, H]); din("bk_in", [128, H])
    din("bv_in", [128, H]); din("bo_bc", [128, D])
    din("wq_b", [D, D]); din("wk_b", [D, D]); din("wv_b", [D, D]); din("wo_b", [D, D])
    din("beta_in", [128, D])
    din("wgT", [D, FF]); din("wuT", [D, FF]); din("wdT", [FF, D])
    din("ident_in", [128, 128], BF16); din("mask_in", [128, 128]); din("ones_in", [128, 128])
    y = nc.dram_tensor("y", [T, D], F32, kind="ExternalOutput").ap()
    xr = nc.dram_tensor("xr", [T, D], F32, kind="Internal").ap()
    zkind = "ExternalOutput" if _DBG_ZB else "Internal"
    zbank = nc.dram_tensor("zbank", [H * 128, T], BF16, kind=zkind).ap()
    abank = nc.dram_tensor("abank", [H * 128, T], BF16, kind=zkind).ap()
    if _DBG_HOOKS:
        dbgA = nc.dram_tensor("dbgA", [128, 32 * KB], BF16, kind="ExternalOutput").ap()
        dbgq = nc.dram_tensor("dbgq", [128, T], BF16, kind="ExternalOutput").ap()
        dbgk = nc.dram_tensor("dbgk", [128, T], BF16, kind="ExternalOutput").ap()
        dbgv = nc.dram_tensor("dbgv", [128, T], BF16, kind="ExternalOutput").ap()
        dbgp = nc.dram_tensor("dbgp", [128, T], BF16, kind="ExternalOutput").ap()
    tg = nc.dram_tensor("tg", [D, FF], BF16, kind="Internal").ap()
    tu = nc.dram_tensor("tu", [D, FF], BF16, kind="Internal").ap()
    td = nc.dram_tensor("td", [FF, D], BF16, kind="Internal").ap()

    base = (nc.sbuf_base + 31) & ~31
    assert nc.sbuf_top - base >= 206 * KB, (base, nc.sbuf_top)

    def at(name, off_b, shape, dt):
        assert off_b % 32 == 0, (name, off_b)
        h = nc.alloc_sbuf_tensor_at(name, shape, dt, offset=base + off_b)
        return h

    P2 = 64 * KB
    EO = 172 * KB
    SM = EO + 32 * KB
    A = at("A", 0, [128, 32 * KB], BF16)
    WOQ = [at("WOQ0", P2, [128, 8 * KB], BF16), at("WOQ1", P2 + 16 * KB, [128, 8 * KB], BF16)]
    Qt = at("Qt", P2 + 32 * KB, [128, 2 * KB], BF16)
    Kt = at("Kt", P2 + 36 * KB, [128, 2 * KB], BF16)
    VtT = at("VtT", P2 + 40 * KB, [128, 2 * KB], BF16)
    Vtok = at("Vtok", P2 + 44 * KB, [128, 2 * KB], BF16)
    Pbuf = at("Pbuf", P2 + 48 * KB, [128, 2 * KB], BF16)
    PbufB = at("pbuf2", P2 + 104 * KB, [128, 2 * KB], BF16)
    Ptb = at("Ptb", P2 + 52 * KB, [128, 2 * KB], BF16)
    Ztb = at("Ztb", P2 + 56 * KB, [128, 2 * KB], BF16)
    ZSL = [at("zsl0", P2 + 76 * KB, [128, 2 * KB], BF16), at("zsl1", P2 + 80 * KB, [128, 2 * KB], BF16)]
    bObc = at("bObc", P2 + 68 * KB, [128, 2 * KB], F32)
    Wqh = at("Wqh", P2 + 76 * KB, [128, 2 * KB], BF16)
    Wkh = at("Wkh", P2 + 80 * KB, [128, 2 * KB], BF16)
    Wvh = at("Wvh", P2 + 84 * KB, [128, 2 * KB], BF16)
    PRES = at("pres", P2 + 88 * KB, [128, 2 * KB], F32)
    PRELO = at("prelo", P2 + 96 * KB, [128, 2 * KB], BF16)
    PREOUT = at("preout", P2 + 100 * KB, [128, 2 * KB], BF16)
    YTO = [at(f"yto{i}", P2 + (32 + 8 * i) * KB, [128, 2 * KB], F32) for i in range(4)]
    HSL = [at(f"hsl{i}", P2 + 16 * i * KB, [128, 8 * KB], BF16) for i in range(4)]
    TGU = [at("tgu0", P2 + 64 * KB, [128, 8 * KB], BF16), at("tgu1", P2 + 80 * KB, [128, 8 * KB], BF16)]
    TDQ = [at("tdq0", P2 + 96 * KB, [128, 2 * KB], BF16), at("tdq1", P2 + 100 * KB, [128, 2 * KB], BF16)]
    GSCR = at("gscr", P2 + 104 * KB, [128, 256], F32)
    RHC = at("rhc", P2 + 105 * KB, [128, 1 * KB], BF16)
    YTF = [at("ytf0", P2 + 64 * KB, [128, 2 * KB], F32), at("ytf1", P2 + 72 * KB, [128, 2 * KB], F32),
           at("ytf2", P2 + 80 * KB, [128, 2 * KB], F32), at("ytf3", P2 + 88 * KB, [128, 2 * KB], F32)]
    SqT = at("SqT", P2 + 48 * KB, [128, 2 * KB], BF16)
    SkT = at("SkT", P2 + 88 * KB, [128, 2 * KB], BF16)
    ATTH = at("ATTH", P2 + 92 * KB, [128, 2 * KB], BF16)
    SS0 = at("ss0", P2 + 96 * KB, [128, 512], F32)
    SS1 = at("ss1", P2 + 98 * KB, [128, 512], F32)
    CMB = at("cmb", P2 + 100 * KB, [128, 256], F32)
    PbB = at("pbB", P2 + 101 * KB, [128, 512], BF16)
    PtB = at("ptB", P2 + 102 * KB, [128, 512], BF16)
    MEMF = at("memf", P2 + 103 * KB, [128, 128], F32)
    MEMZ = at("memz", P2 + 103 * KB + 512, [128, 132], BF16)
    NGD = at("ngd", P2 + 103 * KB + 800, [128, 128], BF16)
    SKTOK = at("sktok", P2 + 103 * KB + 1056, [128, 128], BF16)
    BETA = at("beta", P2 + 68 * KB, [128, 2 * KB], F32)
    XO = [at("xo0", EO, [128, 2 * KB], F32), at("xo1", EO + 8 * KB, [128, 2 * KB], F32)]
    YTE = at("ytE", EO + 16 * KB, [128, 2 * KB], F32)
    I32S = at("i32s", EO + 24 * KB, [128, 1 * KB], I32)
    XNB = at("xnb", EO + 28 * KB, [128, 2 * KB], BF16)
    MASK = at("maskT", SM, [128, 128], F32)
    IDT = at("idT", SM + 512, [128, 128], BF16)
    ONESF = at("onesT", SM + 768, [128, 128], F32)
    BQC = at("bqc", SM + 1280, [128, H], F32)
    BKC = at("bkc", SM + 1344, [128, H], F32)
    BVC = at("bvc", SM + 1408, [128, H], F32)
    SX = at("sx", SM + 1472, [128, NT], F32)
    STAT = at("stat", SM + 1536, [128, 32], F32)
    WSUM = at("wsum", SM + 1664, [128, 4], F32)
    WSC = at("wsc", SM + 1696, [128, 4], F32)
    THR = at("thr", SM + 1728, [128, 4], F32)
    NTHR = at("nthr", SM + 1760, [128, 4], F32)
    SHD = at("shd", SM + 1792, [128, 4], F32)
    SGU = at("sgu", SM + 1824, [128, 8], F32)
    HMX = at("hmx", SM + 1856, [128, 4], F32)
    ZINC = at("zinc", SM + 1888, [128, 4], F32)
    CEPS = at("ceps", SM + 1920, [128, 4], F32)
    DUMP = at("dump", SM + 1952, [128, 64], F32)
    HMXW = at("hmxw", SM + 2208, [128, 128], F32)

    pctx = ExitStack()
    psA = pctx.enter_context(nc.psum_tensor("psA", [128, 2048], F32))
    psB = pctx.enter_context(nc.psum_tensor("psB", [128, 1024], F32))
    psT = pctx.enter_context(nc.psum_tensor("psT", [128, 1024], BF16))
    psU = pctx.enter_context(nc.psum_tensor("psU", [128, 1024], BF16))
    sctx = ExitStack()
    sems = {s: sctx.enter_context(nc.semaphore(s)) for s in PL.SEMS}

    P = PL()
    K = {
        "bkg": {b: None for b in
                ("A0", "A1", "A2", "A3", "B0", "B1")},
        "bkw": {b: None for b in
                ("A0", "A1", "A2", "A3", "B0", "B1")},
        "psT_i": 0,
        "psT_g": [None, None],
        "xo_g": [None, None],
        "a_rdy": [None] * NT,
        "sx_rdy": [None] * NT,
        "spair_i": 0,
        "tern_st": {m: [0] * 32 for m in range(2)},
        "td_all": 0,
        "thr_rdy": None,
        "zb_st": 0,
        "ab_st": 0,
        "xnb_g": None,
    }

    def bank_ap(name):
        if name.startswith("A"):
            i = int(name[1])
            return psA[:, i * 512:(i + 1) * 512]
        if name.startswith("B"):
            i = int(name[1])
            return psB[:, i * 512:(i + 1) * 512]
        raise KeyError(name)

    def bank_wait_w(eng, names):
        for n in names:
            P.wd(eng, K["bkg"][n])

    def bank_set_w(names, val):
        for n in names:
            K["bkw"][n] = ("sPE", val)

    def bank_wait_r(eng, names):
        for n in names:
            P.wd(eng, K["bkw"][n])

    def bank_set_r(names, dep):
        for n in names:
            old = K["bkg"][n]
            deps = [dep] if isinstance(dep, tuple) else list(dep)
            if old:
                deps = deps + ([old] if isinstance(old, tuple) else list(old))
            best = {}
            for s, v in deps:
                best[s] = max(best.get(s, 0), v)
            K["bkg"][n] = [(s, v) for s, v in best.items()]

    def transpose_to(src_ap, dst_ap, pe_dep=None, act_dep=None):
        i = K["psT_i"] % 2
        K["psT_i"] += 1
        slot = (psT if i == 0 else psU)[:, 0:128]
        P.wd("tensor", K["psT_g"][i])
        P.wd("tensor", pe_dep)
        pe = P.o("tensor", lambda t, s=slot, a=src_ap: t.transpose(s, a, IDT[:]), "sPE")
        P.w("scalar", "sPE", pe)
        P.wd("scalar", act_dep)
        ac = P.o("scalar", lambda s, d=dst_ap, sl=slot: s.copy(d, sl), "sACT")
        K["psT_g"][i] = ("sACT", ac)
        return ac

    def transpose_group(srcs, dst_ap, pe_dep=None, act_dep=None):
        n = len(srcs)
        i = K["psT_i"] % 2
        K["psT_i"] += 1
        bank = psT if i == 0 else psU
        P.wd("tensor", K["psT_g"][i])
        P.wd("tensor", pe_dep)
        pe = None
        for k, s_ap in enumerate(srcs):
            pe = P.o("tensor", lambda t, sl=bank[:, k * 128:(k + 1) * 128], a=s_ap:
                     t.transpose(sl, a, IDT[:]), "sPE" if k == n - 1 else None)
        P.w("scalar", "sPE", pe)
        P.wd("scalar", act_dep)
        ac = P.o("scalar", lambda s, d=dst_ap, sl=bank[:, 0:n * 128]: s.copy(d, sl),
                 "sACT")
        K["psT_g"][i] = ("sACT", ac)
        return ac

    ident = DT["ident_in"]; maskin = DT["mask_in"]; onesin = DT["ones_in"]

    cl = P.d("sync", IDT[:], ident, "qC")
    cl = P.d("sync", MASK[:], maskin, "qC")
    cl = P.d("sync", ONESF[:], onesin, "qC")
    cl = P.d("sync", BQC[:], DT["bq_in"], "qC")
    cl = P.d("sync", BKC[:], DT["bk_in"], "qC")
    cl = P.d("sync", BVC[:], DT["bv_in"], "qC")
    consts_rdy = cl
    P.o("vector", lambda v: v.memset(CEPS[:, 0:1], 1e-5))
    P.o("vector", lambda v: v.memset(CEPS[:, 1:2], 5e-6))
    cinit = P.o("vector", lambda v: v.memset(CEPS[:, 2:3], -5e-6), "sDVE")
    P.w("scalar", "sDVE", cinit)

    def sep(eng, n=1):
        for _ in range(n):
            if eng == "scalar":
                P.o("scalar", lambda s: s.activation(DUMP[:, 0:64], MASK[:, 0:64], AF.Copy))
            elif eng == "vector":
                P.o("vector", lambda v: v.tensor_copy(DUMP[:, 0:64], MASK[:, 0:64]))
            else:
                P.o("gpsimd", lambda g: g.tensor_copy(DUMP[:, 0:64], MASK[:, 0:64]))

    def finish_tile(t, yth, yt_dep, xo, dst_tile, prep, scratch_f32):
        if yth is not None:
            P.wd("vector", yt_dep)
            v1 = P.o("vector", lambda v, a=yth, b=xo: v.tensor_tensor(b[:], a[:], b[:], ALU.add), "sDVE")
        else:
            P.wd("vector", yt_dep)
            v1 = None
        if dst_tile is not None:
            if v1:
                P.w("sync", "sDVE", v1)
            st = P.d("sync", dst_tile, xo[:], "qSt")
        else:
            st = None
        v2 = P.o("vector", lambda v, b=xo: v.tensor_reduce(STAT[:, 0:1], b[:], AX.X, ALU.add), "sDVE")
        P.w("gpsimd", "sDVE", v2)
        g1 = P.o("gpsimd", lambda g: g.tensor_single_scalar(
            STAT[:, 1:2], STAT[:, 0:1], -1.0 / D, ALU.mult), "sGP")
        P.w("scalar", "sGP", g1)
        P.o("scalar", lambda s, b=xo, d=scratch_f32: s.activation(
            d[:], b[:], AF.Square, bias=STAT[:, 1:2], accum_out=STAT[:, 2:3]))
        sep("scalar", 2)
        a3 = P.o("scalar", lambda s: s.activation(STAT[:, 7:8], STAT[:, 2:3], AF.Sqrt,
                                                  scale=1.0 / D, bias=CEPS[:, 0:1]), "sACT")
        P.w("vector", "sACT", a3)
        P.o("vector", lambda v: v.reciprocal(STAT[:, 3:4], STAT[:, 7:8]))
        sep("vector", 2)
        rdr = [("sACT", a3)]
        if prep == "xnT":
            P.wd("vector", K["xnb_g"])
            v3 = P.o("vector", lambda v, b=xo: v.tensor_scalar(
                XNB[:], b[:], STAT[:, 1:2], STAT[:, 3:4], ALU.add, ALU.mult), "sDVE")
            rdr.append(("sDVE", v3))
            last = None
            Av = A[:].rearrange("p (c x) -> p c x", x=2048)
            for c0 in range(0, 16, 4):
                srcs = [XNB[:, c * 128:(c + 1) * 128] for c in range(c0, c0 + 4)]
                last = transpose_group(srcs, Av[:, c0:c0 + 4, t * 128:(t + 1) * 128],
                                       pe_dep=("sDVE", v3) if c0 == 0 else None)
            K["a_rdy"][t] = ("sACT", last)
            K["xnb_g"] = ("sACT", last)
        elif prep == "R":
            P.wd("vector", K["xnb_g"])
            v3 = P.o("vector", lambda v, b=xo, d=scratch_f32: v.tensor_scalar(
                d[:], b[:], STAT[:, 1:2], STAT[:, 3:4], ALU.add, ALU.mult), "sDVE")
            v4 = P.o("vector", lambda v, d=scratch_f32: v.tensor_reduce(
                STAT[:, 4:5], d[:], AX.X, ALU.max, apply_absolute_value=True), "sDVE")
            sep("vector", 2)
            P.o("vector", lambda v: v.tensor_single_scalar(
                STAT[:, 5:6], STAT[:, 4:5], QEPS, ALU.max))
            sep("vector", 2)
            rc = P.o("vector", lambda v: v.reciprocal(STAT[:, 8:9], STAT[:, 5:6]), "sDVE")
            P.w("gpsimd", "sDVE", rc)
            v5 = P.o("gpsimd", lambda g: g.tensor_single_scalar(
                STAT[:, 6:7], STAT[:, 8:9], 127.0, ALU.mult), "sGP")
            rdr.append(("sDVE", v4))
            P.w("scalar", "sGP", v5)
            sxr = P.o("scalar", lambda s, tt=t: s.activation(SX[:, tt:tt + 1], STAT[:, 5:6],
                                                             AF.Copy, scale=1.0 / 127.0), "sACT")
            K["sx_rdy"][t] = ("sACT", sxr)
            last = None
            Av = A[:].rearrange("p (c x) -> p c x", x=2048)
            for half in range(2):
                P.o("scalar", lambda s, d=scratch_f32, q=half: s.activation(
                    I32S[:], d[:, q * 1024:(q + 1) * 1024], AF.Copy, scale=STAT[:, 6:7]))
                ra = P.o("scalar", lambda s, q=half: s.activation(
                    XNB[:, q * 1024:(q + 1) * 1024], I32S[:], AF.Copy), "sACT")
                for cc in range(0, 8, 4):
                    c0 = half * 8 + cc
                    srcs = [XNB[:, c * 128:(c + 1) * 128] for c in range(c0, c0 + 4)]
                    last = transpose_group(srcs, Av[:, c0:c0 + 4, t * 128:(t + 1) * 128],
                                           pe_dep=("sACT", ra) if cc == 0 else None)
            K["a_rdy"][t] = ("sACT", last)
            K["xnb_g"] = ("sACT", last)
        rdr2 = list(rdr)
        if st is not None:
            rdr2.append(("qSt", st))
        return st, rdr2

    def prologue():
        for e in ("tensor", "scalar", "vector", "gpsimd"):
            P.w(e, "qC", consts_rdy)
        for t in range(NT):
            par = t % 2
            P.wd("sync", K["xo_g"][par])
            qx = "qX0" if par == 0 else "qX1"
            ld = P.d("sync", XO[par][:], DT["x0"][t * 128:(t + 1) * 128, :], qx)
            _, rdr = finish_tile(t, None, [(qx, ld)], XO[par], None, "xnT", YTE)
            K["xo_g"][par] = rdr

    WSRC = [DT["wgT"], DT["wuT"], DT["wdT"]]
    TDST = [tg, tu, td]

    def w_tile_aps(m, db, fb):
        src = WSRC[m][db * 1024:(db + 1) * 1024, fb * 256:(fb + 1) * 256]
        src = src.rearrange("(c p) f -> p c f", p=128)
        dst = TDST[m][db * 1024:(db + 1) * 1024, fb * 256:(fb + 1) * 256]
        dst = dst.rearrange("(c p) f -> p c f", p=128)
        return src, dst

    pp_state = {"stage_g": None, "lo_g": None, "out_g": None, "wsum_last": [None] * 3}

    def mean_units():
        units = []
        for m in range(3):
            nb = (2, 32) if m < 2 else (8, 8)
            zi = P.o("gpsimd", lambda g, mm=m: g.memset(WSUM[:, mm:mm + 1], 0.0), "sGP")
            pp_state["wsum_last"][m] = ("sGP", zi)
            for db in range(nb[0]):
                for fb in range(nb[1]):
                    units.append((m, db, fb))
        def emit(u):
            m, db, fb = u
            src, _ = w_tile_aps(m, db, fb)
            P.wd("sync", pp_state["stage_g"])
            ld = P.d("sync", PRES[:].rearrange("p (c f) -> p c f", f=256), src, "qP")
            P.w("scalar", "qP", ld)
            P.wd("scalar", pp_state.get("gv_g"))
            aa = P.o("scalar", lambda s: s.activation(PRELO[:], PRES[:], AF.Abs,
                                                      accum_out=STAT[:, 25:26]), "sACT")
            P.w("gpsimd", "sACT", aa)
            gv = P.o("gpsimd", lambda g, mm=m: g.tensor_tensor(
                WSUM[:, mm:mm + 1], WSUM[:, mm:mm + 1], STAT[:, 25:26], ALU.add), "sGP")
            pp_state["gv_g"] = ("sGP", gv)
            pp_state["stage_g"] = [("sACT", aa)]
            pp_state["lo_g"] = [("sACT", aa)]
            pp_state["wsum_last"][m] = ("sGP", gv)
        return units, emit

    def wscale_finalize():
        last = None
        for m in range(3):
            bank_wait_w("tensor", ["B1"])
            P.wd("tensor", pp_state["wsum_last"][m])
            pe = P.o("tensor", lambda t, mm=m: t.matmul(
                psB[:, 512 + mm:512 + mm + 1], ONESF[:], WSUM[:, mm:mm + 1],
                start=True, stop=True), "sPE")
            bank_set_w(["B1"], pe)
            P.w("scalar", "sPE", pe)
            P.o("scalar", lambda s, mm=m: s.activation(WSC[:, mm:mm + 1],
                psB[:, 512 + mm:512 + mm + 1], AF.Identity, scale=1.0 / WDEN, bias=CEPS[:, 0:1]))
            P.o("scalar", lambda s, mm=m: s.activation(THR[:, mm:mm + 1],
                psB[:, 512 + mm:512 + mm + 1], AF.Identity, scale=0.5 / WDEN, bias=CEPS[:, 1:2]))
            last = P.o("scalar", lambda s, mm=m: s.activation(NTHR[:, mm:mm + 1],
                psB[:, 512 + mm:512 + mm + 1], AF.Identity, scale=-0.5 / WDEN,
                bias=CEPS[:, 2:3]), "sACT")
            bank_set_r(["B1"], ("sACT", last))
        K["thr_rdy"] = ("sACT", last)

    def tern_units():
        units = []
        for m in range(3):
            nb = (2, 32) if m < 2 else (8, 8)
            for fb in range(nb[1]):
                for db in range(nb[0]):
                    units.append((m, db, fb, db == nb[0] - 1))
        def emit(u):
            m, db, fb, is_last_db = u
            src, dst = w_tile_aps(m, db, fb)
            P.wd("sync", pp_state["stage_g"])
            ld = P.d("sync", PRES[:].rearrange("p (c f) -> p c f", f=256), src, "qP")
            P.w("vector", "qP", ld)
            P.wd("vector", K["thr_rdy"])
            P.wd("vector", pp_state["out_g"])
            P.o("vector", lambda v, mm=m: v.tensor_single_scalar(
                PRELO[:], PRES[:], NTHR[:, mm:mm + 1], ALU.is_le))
            v1 = P.o("vector", lambda v, mm=m: v.scalar_tensor_tensor(
                PREOUT[:], PRES[:], THR[:, mm:mm + 1], PRELO[:],
                ALU.is_ge, ALU.subtract), "sDVE")
            P.w("sync", "sDVE", v1)
            st = P.d("sync", dst, PREOUT[:].rearrange("p (c f) -> p c f", f=256), "qT")
            pp_state["stage_g"] = [("sDVE", v1)]
            pp_state["out_g"] = [("qT", st)]
            if m < 2 and is_last_db:
                K["tern_st"][m][fb] = st
            if m == 2 and is_last_db and fb == 7:
                K["td_all"] = st
        return units, emit

    def load_head_weights(h, wsrcs, bufs, guard_pe):
        outs = []
        for src, buf in zip(wsrcs, bufs):
            P.wd("gpsimd", guard_pe)
            sap = src.rearrange("(c p) e -> p c e", p=128)[:, :, h * 128:(h + 1) * 128]
            dap = buf[:].rearrange("p (c e) -> p c e", e=128)
            outs.append(P.d("gpsimd", dap, sap, "qG0"))
        return outs[-1]

    def proj4(wbuf, out_sb, bias_col, wdep, sig_out=None, zinc=False):
        rdy = None
        sig_rdy = None
        for tc in range(4):
            bk = "B0" if tc % 2 == 0 else "B1"
            bank_wait_w("tensor", [bk])
            if tc == 0:
                P.w("tensor", "qG0", wdep)
            P.wd("tensor", K["a_rdy"][tc * 4 + 3])
            P.wd("tensor", K["a_rdy"][tc * 4 + 2])
            P.wd("tensor", K["a_rdy"][tc * 4 + 1])
            P.wd("tensor", K["a_rdy"][tc * 4 + 0])
            pe = None
            for c in range(16):
                pe = P.o("tensor", lambda t, cc=c, w=wbuf, b=bk, tcc=tc: t.matmul(
                    bank_ap(b), w[:, cc * 128:(cc + 1) * 128],
                    A[:, cc * 2048 + tcc * 512: cc * 2048 + (tcc + 1) * 512],
                    start=(cc == 0), stop=(cc == 15)), "sPE" if c == 15 else None)
            bank_set_w([bk], pe)
            P.w("scalar", "sPE", pe)
            if bias_col is not None:
                ac = P.o("scalar", lambda s, o=out_sb, b=bk, tcc=tc, bc=bias_col: s.activation(
                    o[:, tcc * 512:(tcc + 1) * 512], bank_ap(b), AF.Identity, bias=bc), "sACT")
            else:
                ac = P.o("scalar", lambda s, o=out_sb, b=bk, tcc=tc: s.copy(
                    o[:, tcc * 512:(tcc + 1) * 512], bank_ap(b)), "sACT")
            deps = [("sACT", ac)]
            if sig_out is not None:
                P.w("vector", "sPE", pe)
                P.w("vector", "sACT", ac)
                sm0 = P.o("vector", lambda v, b=bk: v.tensor_single_scalar(
                    SS0[:, 0:512], bank_ap(b), 0.0, ALU.min), "sDVE")
                P.w("scalar", "sDVE", sm0)
                e1 = P.o("scalar", lambda s: s.activation(SS1[:, 0:512], SS0[:, 0:512],
                                                          AF.Exp), "sACT")
                P.w("vector", "sACT", e1)
                if zinc:
                    sv = P.o("vector", lambda v, o=sig_out, b=bk, tcc=tc: v.scalar_tensor_tensor(
                        o[:, tcc * 512:(tcc + 1) * 512], bank_ap(b), 0.0, SS1[:, 0:512],
                        ALU.max, ALU.add, accum_out=ZINC[:, tcc:tcc + 1]), "sDVE")
                else:
                    sv = P.o("vector", lambda v, o=sig_out, b=bk, tcc=tc: v.scalar_tensor_tensor(
                        o[:, tcc * 512:(tcc + 1) * 512], bank_ap(b), 0.0, SS1[:, 0:512],
                        ALU.max, ALU.add), "sDVE")
                deps.append(("sDVE", sv))
                sig_rdy = sv
            bank_set_r([bk], deps)
            rdy = ac
        return rdy, sig_rdy

    def vproj(wbuf, wdep, bias_col):
        vr, _ = proj4(wbuf, VtT, bias_col, wdep)
        last = None
        for t0 in range(0, NT, 4):
            srcs = [VtT[:, tt * 128:(tt + 1) * 128] for tt in range(t0, t0 + 4)]
            last = transpose_group(srcs, Vtok[:, t0 * 128:(t0 + 4) * 128],
                                   pe_dep=("sACT", vr) if t0 == 0 else None)
        return last

    def attn_a(pp_iter):
        bol = P.d("sync", bObc[:], DT["bo_bc"], "qC")
        woq_guard = [None, None]
        zsl_guard = [None, None]
        wq_guard = None
        qkv_rdy = None
        for h in range(_DBG_NH if _DBG_NH is not None else H):
            wdep = load_head_weights(h, (DT["wq_a"], DT["wk_a"], DT["wv_a"]),
                                     (Wqh, Wkh, Wvh), wq_guard)
            qr, _ = proj4(Wqh, Qt, BQC[:, h:h + 1], wdep)
            kr, _ = proj4(Wkh, Kt, BKC[:, h:h + 1], wdep)
            vr = vproj(Wvh, wdep, BVC[:, h:h + 1])
            wq_guard = ("sPE", P.cnt["sPE"])
            if _DBG_HOOKS and h == 0:
                P.w("sync", "sACT", qr)
                P.w("sync", "sACT", kr)
                P.w("sync", "sACT", vr)
                P.d("sync", dbgA, A[:], "qSt")
                P.d("sync", dbgq, Qt[:], "qSt")
                P.d("sync", dbgk, Kt[:], "qSt")
                P.d("sync", dbgv, Vtok[:], "qSt")
            zlast = None

            def emit_scores_softmax(i):
                pb = Pbuf if i % 2 == 0 else PbufB
                klen = (i + 1) * 128
                chunks = [(0, min(klen, 1024))]
                if klen > 1024:
                    chunks.append((1024, klen - 1024))
                nch = len(chunks)
                exp_cols = []
                for ci, (c0, wdt) in enumerate(chunks):
                    pair = K["spair_i"] % 2
                    K["spair_i"] += 1
                    pbanks = ["A0", "A1"] if pair == 0 else ["A2", "A3"]
                    pb0 = pair * 1024
                    bank_wait_w("tensor", pbanks)
                    if i == 0 and ci == 0:
                        P.w("tensor", "sACT", qr)
                        P.w("tensor", "sACT", kr)
                        P.w("tensor", "sACT", vr)
                    pe = None
                    nmm = (wdt + 511) // 512
                    for mi in range(nmm):
                        mw = min(512, wdt - mi * 512)
                        pe = P.o("tensor", lambda t, ii=i, b0=pb0, m=mi, w=mw, cc0=c0: t.matmul(
                            psA[:, b0 + m * 512: b0 + m * 512 + w],
                            Qt[:, ii * 128:(ii + 1) * 128],
                            Kt[:, cc0 + m * 512: cc0 + m * 512 + w],
                            start=True, stop=True), "sPE" if mi == nmm - 1 else None)
                    bank_set_w(pbanks, pe)
                    P.w("vector", "sPE", pe)
                    if c0 <= i * 128 < c0 + wdt:
                        off = pb0 + i * 128 - c0
                        P.o("vector", lambda v, o=off: v.tensor_tensor(
                            psA[:, o:o + 128], psA[:, o:o + 128], MASK[:], ALU.add))
                    nmcol = 8 + ci
                    P.o("vector", lambda v, b0=pb0, w=wdt, nm=nmcol: v.tensor_reduce(
                        STAT[:, nm:nm + 1], psA[:, b0:b0 + w], AX.X, ALU.max, negate=True),
                        "sDVE")
                    vv = P.cnt["sDVE"]
                    P.w("gpsimd", "sDVE", vv)
                    gb = P.o("gpsimd", lambda g, nm=nmcol, bm=10 + ci: g.tensor_single_scalar(
                        STAT[:, bm:bm + 1], STAT[:, nm:nm + 1], ISC, ALU.mult), "sGP")
                    P.w("scalar", "sGP", gb)
                    ex = P.o("scalar", lambda s, b0=pb0, w=wdt, cc0=c0, bm=10 + ci,
                             lc=12 + ci, pb_=pb:
                             s.activation(pb_[:, cc0:cc0 + w], psA[:, b0:b0 + w], AF.Exp,
                                          bias=STAT[:, bm:bm + 1], scale=ISC,
                                          accum_out=STAT[:, lc:lc + 1]), "sACT")
                    bank_set_r(pbanks, [("sACT", ex), ("sDVE", vv)])
                    exp_cols.append((c0, wdt, ex))
                if nch == 1:
                    P.w("vector", "sACT", exp_cols[0][2])
                    P.o("vector", lambda v: v.reciprocal(STAT[:, 14:15], STAT[:, 12:13]))
                    sep("vector", 2)
                    pn = P.o("vector", lambda v, w=klen, pb_=pb: v.tensor_scalar(
                        pb_[:, 0:w], pb_[:, 0:w], STAT[:, 14:15], None, ALU.mult),
                        "sDVE")
                else:
                    P.w("vector", "sACT", exp_cols[1][2])
                    sep("vector", 2)
                    P.o("vector", lambda v: v.tensor_tensor(STAT[:, 15:16], STAT[:, 8:9],
                                                            STAT[:, 9:10], ALU.min))
                    sep("vector", 2)
                    P.o("vector", lambda v: v.tensor_tensor(STAT[:, 16:17], STAT[:, 15:16],
                                                            STAT[:, 8:9], ALU.subtract))
                    vm = P.o("vector", lambda v: v.tensor_tensor(STAT[:, 17:18], STAT[:, 15:16],
                                                                 STAT[:, 9:10], ALU.subtract),
                             "sDVE")
                    P.w("scalar", "sDVE", vm)
                    P.o("scalar", lambda s: s.activation(STAT[:, 18:19], STAT[:, 16:17],
                                                         AF.Exp, scale=ISC))
                    am = P.o("scalar", lambda s: s.activation(STAT[:, 19:20], STAT[:, 17:18],
                                                              AF.Exp, scale=ISC), "sACT")
                    P.w("gpsimd", "sACT", am)
                    P.o("gpsimd", lambda g: g.tensor_tensor(STAT[:, 20:21], STAT[:, 12:13],
                                                            STAT[:, 18:19], ALU.mult))
                    gu = P.o("gpsimd", lambda g: g.tensor_tensor(STAT[:, 21:22], STAT[:, 13:14],
                                                                 STAT[:, 19:20], ALU.mult), "sGP")
                    P.w("vector", "sGP", gu)
                    P.o("vector", lambda v: v.tensor_tensor(STAT[:, 22:23], STAT[:, 20:21],
                                                            STAT[:, 21:22], ALU.add))
                    sep("vector", 2)
                    rlm = P.o("vector", lambda v: v.reciprocal(STAT[:, 14:15], STAT[:, 22:23]),
                              "sDVE")
                    P.w("gpsimd", "sDVE", rlm)
                    P.o("gpsimd", lambda g: g.tensor_tensor(STAT[:, 23:24], STAT[:, 18:19],
                                                            STAT[:, 14:15], ALU.mult))
                    gf = P.o("gpsimd", lambda g: g.tensor_tensor(STAT[:, 24:25], STAT[:, 19:20],
                                                                 STAT[:, 14:15], ALU.mult), "sGP")
                    P.w("vector", "sGP", gf)
                    pn = None
                    for ci, (c0, wdt, _) in enumerate(exp_cols):
                        pn = P.o("vector", lambda v, cc0=c0, w=wdt, fc=23 + ci, pb_=pb:
                                 v.tensor_scalar(
                            pb_[:, cc0:cc0 + w], pb_[:, cc0:cc0 + w], STAT[:, fc:fc + 1],
                            None, ALU.mult), "sDVE")
                return pn, pb

            def emit_trans_av(i, pn, pb):
                nonlocal zlast
                q4 = i % 4
                for j0 in range(0, i + 1, 4):
                    n = min(4, i + 1 - j0)
                    srcs = [pb[:, j * 128:(j + 1) * 128] for j in range(j0, j0 + n)]
                    pt = transpose_group(srcs, Ptb[:, j0 * 128:(j0 + n) * 128],
                                         pe_dep=("sDVE", pn) if j0 == 0 else None)
                    if j0 == 0:
                        bank_wait_w("tensor", ["B0"])
                    P.w("tensor", "sACT", pt)
                    for j in range(j0, j0 + n):
                        P.o("tensor", lambda t, jj=j, q=q4, ii=i: t.matmul(
                            psB[:, q * 128: (q + 1) * 128],
                            Vtok[:, jj * 128:(jj + 1) * 128],
                            Ptb[:, jj * 128:(jj + 1) * 128],
                            start=(jj == 0), stop=(jj == ii)),
                            "sPE" if j == i else None)
                avdone = P.cnt["sPE"]
                bank_set_w(["B0"], avdone)
                P.w("scalar", "sPE", avdone)
                if i == 0:
                    P.w("scalar", "qZ", K["zb_st"])
                zlast = P.o("scalar", lambda s, ii=i, q=q4: s.copy(
                    Ztb[:, ii * 128:(ii + 1) * 128],
                    psB[:, q * 128: (q + 1) * 128]), "sACT")
                bank_set_r(["B0"], ("sACT", zlast))

            pend = None
            for i in range(NT):
                info = emit_scores_softmax(i)
                if pend is not None:
                    emit_trans_av(*pend)
                pend = (i, info[0], info[1])
            emit_trans_av(*pend)
            if _DBG_HOOKS and h == 0:
                P.w("sync", "sDVE", P.cnt["sDVE"])
                P.d("sync", dbgp, Pbuf[:], "qSt")
            P.w("sync", "sACT", zlast)
            K["zb_st"] = P.d("sync", zbank[h * 128:(h + 1) * 128, :], Ztb[:], "qZ")
            for _ in range(pp_iter.get("per_head", 0)):
                if pp_iter["units"]:
                    pp_iter["emit"](pp_iter["units"].pop(0))
                elif pp_iter.get("next"):
                    nxt = pp_iter.pop("next")
                    nxt[0]()
                    pp_iter["units"], pp_iter["emit"] = nxt[1]
        if pp_iter.get("next"):
            nxt = pp_iter.pop("next")
            nxt[0]()
            pp_iter["units"], pp_iter["emit"] = nxt[1]
        while pp_iter["units"]:
            pp_iter["emit"](pp_iter["units"].pop(0))

        if _DBG_NOOUT:
            return
        obank = ["A0", "A1", "A2", "A3"]
        yq_rdy = [[None] * 4 for _ in range(4)]
        yto_g = [None] * 4
        wls = {}
        zls = {}
        def load_woq(g, dq):
            wpar = dq % 2
            P.wd("gpsimd", woq_guard[wpar])
            sap = DT["wo_a"].rearrange("(hh p) d -> p hh d", p=128)[:, :, dq * 512:(dq + 1) * 512]
            dap = WOQ[wpar][:].rearrange("p (hh d) -> p hh d", d=512)
            qg = "qG0" if wpar == 0 else "qG1"
            wls[(g, dq)] = (qg, P.d("gpsimd", dap, sap, qg))
        def load_zsl(g, dq, ti):
            t = g * 4 + ti
            zpar = t % 2
            P.wd("sync", zsl_guard[zpar])
            P.w("sync", "qZ", K["zb_st"])
            zap = zbank.rearrange("(hh p) t2 -> p hh t2", p=128)[:, :, t * 128:(t + 1) * 128]
            ql = "qL0" if zpar == 0 else "qL1"
            zls[(g, dq, ti)] = (ql, P.d("sync", ZSL[zpar][:].rearrange(
                "p (hh q) -> p hh q", q=128), zap, ql))
        steps = [(g, dq) for g in range(4) for dq in range(4)]
        load_woq(*steps[0])
        load_zsl(*steps[0], 0)
        zseq = [(g, dq, ti) for (g, dq) in steps for ti in range(4)]
        zi_next = 1
        for si, (g, dq) in enumerate(steps):
            if si + 1 < len(steps):
                load_woq(*steps[si + 1])
            for ti in range(4):
                t = g * 4 + ti
                while zi_next < len(zseq) and zi_next <= si * 4 + ti + 1:
                    load_zsl(*zseq[zi_next]); zi_next += 1
                qg, wl = wls[(g, dq)]
                ql, zl = zls[(g, dq, ti)]
                zpar = t % 2
                wpar = dq % 2
                bk = obank[ti]
                bank_wait_w("tensor", [bk])
                P.w("tensor", qg, wl)
                P.w("tensor", ql, zl)
                pe = None
                for h2 in range(16):
                    pe = P.o("tensor", lambda t_, hh=h2, zp=zpar, wp=wpar, b=bk: t_.matmul(
                        bank_ap(b), ZSL[zp][:, hh * 128:(hh + 1) * 128],
                        WOQ[wp][:, hh * 512:(hh + 1) * 512],
                        start=(hh == 0), stop=(hh == 15)), "sPE" if h2 == 15 else None)
                bank_set_w([bk], pe)
                zsl_guard[zpar] = ("sPE", pe)
                woq_guard[wpar] = ("sPE", pe)
                P.w("scalar", "sPE", pe)
                P.wd("scalar", yto_g[ti])
                ac = P.o("scalar", lambda s, ti_=ti, dq_=dq, b=bk: s.copy(
                    YTO[ti_][:, dq_ * 512:(dq_ + 1) * 512], bank_ap(b)), "sACT")
                bank_set_r([bk], ("sACT", ac))
                yq_rdy[ti][dq] = ac
            if dq == 3:
                for ti in range(4):
                    t = g * 4 + ti
                    par = t % 2
                    P.wd("sync", K["xo_g"][par])
                    qx = "qX0" if par == 0 else "qX1"
                    xl = P.d("sync", XO[par][:], DT["x0"][t * 128:(t + 1) * 128, :], qx)
                    P.w("vector", "sACT", yq_rdy[ti][3])
                    P.w("vector", "qC", bol)
                    bv = P.o("vector", lambda v, ti_=ti: v.tensor_tensor(
                        YTO[ti_][:], YTO[ti_][:], bObc[:], ALU.add), "sDVE")
                    dsta = (xr if stage > 1 else y)
                    _, rdr = finish_tile(t, YTO[ti], [("sDVE", bv), (qx, xl)], XO[par],
                                         dsta[t * 128:(t + 1) * 128, :],
                                         "R" if stage > 1 else None, YTE)
                    K["xo_g"][par] = rdr
                    yto_g[ti] = ("sDVE", P.cnt["sDVE"])

    def ffn(idx, xsrc, xdst, prep_next):
        P.w("sync", "qT", P.cnt["qT"])
        init_dep = [("sDVE", P.cnt["sDVE"]), ("sPE", P.cnt["sPE"]),
                    ("sACT", P.cnt["sACT"])]
        tgu_guard = [list(init_dep), list(init_dep)]
        tdq_guard = [list(init_dep), list(init_dep)]
        ytf_guard = [None] * 4
        gscr_g = None
        last_ga = None
        shd_g = [None] * 4
        for S in range(4):
            sg_rdy = None
            for tloc in range(4):
                t = S * 4 + tloc
                P.wd("vector", K["sx_rdy"][t])
                P.wd("vector", K["thr_rdy"])
                if last_ga is not None and tloc == 0:
                    P.w("vector", "sACT", last_ga)
                P.o("vector", lambda v, tl=tloc, tt=t: v.tensor_tensor(
                    SGU[:, tl:tl + 1], SX[:, tt:tt + 1], WSC[:, 0:1], ALU.mult))
                sg_rdy = P.o("vector", lambda v, tl=tloc, tt=t: v.tensor_tensor(
                    SGU[:, 4 + tl:5 + tl], SX[:, tt:tt + 1], WSC[:, 1:2], ALU.mult), "sDVE")
            hslot_wr = [[] for _ in range(4)]
            def load_tgu(fc):
                par = fc % 2
                ql = "qL0" if par == 0 else "qL1"
                P.wd("sync", tgu_guard[par])
                gdst = TGU[par][:].rearrange("p (c f) -> p c f", f=512)
                gap = tg.rearrange("(c p) f -> p c f", p=128)[:, :, fc * 256:(fc + 1) * 256]
                uap = tu.rearrange("(c p) f -> p c f", p=128)[:, :, fc * 256:(fc + 1) * 256]
                P.d("sync", gdst[:, :, 0:256], gap, ql)
                return ql, P.d("sync", gdst[:, :, 256:512], uap, ql)
            tgul = {0: load_tgu(0)}
            for fc in range(32):
                par = fc % 2
                if fc + 1 < 32:
                    tgul[fc + 1] = load_tgu(fc + 1)
                ql, l2 = tgul.pop(fc)
                for tloc in range(4):
                    t = S * 4 + tloc
                    bk = "B0" if tloc % 2 == 0 else "B1"
                    bank_wait_w("tensor", [bk])
                    if tloc == 0:
                        P.w("tensor", ql, l2)
                    if fc == 0:
                        P.wd("tensor", K["a_rdy"][t])
                    pe = None
                    for c in range(16):
                        pe = P.o("tensor", lambda t_, cc=c, tt=t, p_=par, b=bk: t_.matmul(
                            bank_ap(b), A[:, cc * 2048 + tt * 128: cc * 2048 + (tt + 1) * 128],
                            TGU[p_][:, cc * 512:(cc + 1) * 512],
                            start=(cc == 0), stop=(cc == 15)), "sPE" if c == 15 else None)
                    bank_set_w([bk], pe)
                    tgu_guard[par] = ("sPE", pe)
                    P.w("scalar", "sPE", pe)
                    if sg_rdy:
                        P.w("scalar", "sDVE", sg_rdy)
                    P.wd("scalar", gscr_g)
                    ga = P.o("scalar", lambda s, b=bk, tl=tloc: s.activation(
                        GSCR[:, 0:256], bank_ap(b)[:, 0:256], AF.Sigmoid,
                        scale=SGU[:, tl:tl + 1]), "sACT")
                    last_ga = ga
                    P.w("vector", "sACT", ga)
                    hv = P.o("vector", lambda v, b=bk, tl=tloc, f=fc: v.scalar_tensor_tensor(
                        HSL[tl][:, f * 256:(f + 1) * 256], bank_ap(b)[:, 256:512],
                        SGU[:, 4 + tl:5 + tl], GSCR[:, 0:256], ALU.mult, ALU.mult), "sDVE")
                    hm = P.o("vector", lambda v, tl=tloc, f=fc: v.tensor_reduce(
                        HMXW[:, tl * 32 + f: tl * 32 + f + 1],
                        HSL[tl][:, f * 256:(f + 1) * 256], AX.X,
                        ALU.max, apply_absolute_value=True), "sDVE")
                    gscr_g = ("sDVE", hv)
                    bank_set_r([bk], [("sACT", ga), ("sDVE", hv)])
                    hslot_wr[tloc].append(("sDVE", hm))
            rh_rdy = [None] * 4
            last_ra = None
            for tloc in range(4):
                if hslot_wr[tloc]:
                    P.wd("vector", hslot_wr[tloc][-1])
                if last_ra is not None:
                    P.w("vector", "sACT", last_ra)
                P.o("vector", lambda v, tl=tloc: v.tensor_reduce(
                    STAT[:, 9:10], HMXW[:, tl * 32:(tl + 1) * 32], AX.X, ALU.max))
                sep("vector", 2)
                P.o("vector", lambda v: v.tensor_single_scalar(
                    STAT[:, 27:28], STAT[:, 9:10], QEPS, ALU.max))
                sep("vector", 2)
                rc = P.o("vector", lambda v: v.reciprocal(STAT[:, 26:27], STAT[:, 27:28]),
                         "sDVE")
                P.w("gpsimd", "sDVE", rc)
                v5 = P.o("gpsimd", lambda g: g.tensor_single_scalar(
                    STAT[:, 28:29], STAT[:, 26:27], 127.0, ALU.mult), "sGP")
                P.w("scalar", "sGP", v5)
                sh1 = P.o("scalar", lambda s: s.activation(STAT[:, 29:30], STAT[:, 27:28],
                                                           AF.Copy, scale=1.0 / 127.0), "sACT")
                P.w("vector", "sACT", sh1)
                P.wd("vector", shd_g[tloc])
                shv = P.o("vector", lambda v, tl=tloc: v.tensor_tensor(
                    SHD[:, tl:tl + 1], STAT[:, 29:30], WSC[:, 2:3], ALU.mult), "sDVE")
                shd_w = shv
                last = None
                for fc2 in range(8):
                    P.o("scalar", lambda s, tl=tloc, f2=fc2: s.activation(
                        I32S[:], HSL[tl][:, f2 * 1024:(f2 + 1) * 1024], AF.Copy,
                        scale=STAT[:, 28:29]))
                    ra = P.o("scalar", lambda s: s.activation(RHC[:], I32S[:], AF.Copy),
                             "sACT")
                    last_ra = ra
                    for j0 in range(0, 8, 4):
                        f0 = fc2 * 8 + j0
                        srcs = [RHC[:, j * 128:(j + 1) * 128] for j in range(j0, j0 + 4)]
                        last = transpose_group(srcs,
                                               HSL[tloc][:, f0 * 128:(f0 + 4) * 128],
                                               pe_dep=("sACT", ra) if j0 == 0 else None)
                rh_rdy[tloc] = ("sACT", last)
            shd_rdy = P.cnt["sDVE"]
            dn_pe = [[None] * 4 for _ in range(4)]
            dbank = lambda tl, dc: ["A0", "A1", "A2", "A3"][tl]
            def load_tdq(dc, fq):
                qpar = fq % 2
                qd = "qL0" if qpar == 0 else "qL1"
                P.wd("sync", tdq_guard[qpar])
                tap = td[fq * 512:(fq + 1) * 512, dc * 512:(dc + 1) * 512]
                tap = tap.rearrange("(q p) d -> p q d", p=128)
                return qd, P.d("sync", TDQ[qpar][:].rearrange("p (q d) -> p q d", d=512),
                               tap, qd)
            dsteps = [(dc, fq) for dc in range(4) for fq in range(16)]
            tdql = {dsteps[0]: load_tdq(*dsteps[0])}
            for di, (dc, fq) in enumerate(dsteps):
                if di + 1 < len(dsteps):
                    tdql[dsteps[di + 1]] = load_tdq(*dsteps[di + 1])
                qd, tl_ = tdql.pop((dc, fq))
                if True:
                    qpar = fq % 2
                    for q4 in range(4):
                        f = fq * 4 + q4
                        for tloc in range(4):
                            bk = dbank(tloc, dc)
                            if f == 0:
                                bank_wait_w("tensor", [bk])
                                P.wd("tensor", rh_rdy[tloc])
                            if q4 == 0 and tloc == 0:
                                P.w("tensor", qd, tl_)
                            pe = P.o("tensor", lambda t_, ff=f, tl=tloc, qp=qpar, qq=q4, b=bk:
                                     t_.matmul(bank_ap(b),
                                               HSL[tl][:, ff * 128:(ff + 1) * 128],
                                               TDQ[qp][:, qq * 512:(qq + 1) * 512],
                                               start=(ff == 0), stop=(ff == 63)),
                                     "sPE" if (f == 63 or (q4 == 3 and tloc == 3))
                                     else None)
                            if f == 63:
                                bank_set_w([bk], pe)
                                dn_pe[tloc][dc] = pe
                    tdq_guard[qpar] = ("sPE", P.cnt["sPE"])
                if fq != 15:
                    continue
                for tloc in range(4):
                    bk = dbank(tloc, dc)
                    P.w("scalar", "sPE", dn_pe[tloc][dc])
                    P.w("scalar", "sDVE", shd_rdy)
                    P.wd("scalar", ytf_guard[tloc])
                    ac = P.o("scalar", lambda s, tl=tloc, d_=dc, b=bk: s.activation(
                        YTF[tl][:, d_ * 512:(d_ + 1) * 512], bank_ap(b), AF.Copy,
                        scale=SHD[:, tl:tl + 1]), "sACT")
                    bank_set_r([bk], ("sACT", ac))
                    shd_g[tloc] = ("sACT", ac)
                    dn_pe[tloc][dc] = ac
            for tloc in range(4):
                t = S * 4 + tloc
                par = t % 2
                P.wd("sync", K["xo_g"][par])
                qx = "qX0" if par == 0 else "qX1"
                xl = P.d("sync", XO[par][:], xsrc[t * 128:(t + 1) * 128, :], qx)
                ydep = [("sACT", dn_pe[tloc][3]), (qx, xl),
                        ("sACT", dn_pe[tloc][0]), ("sACT", dn_pe[tloc][1]),
                        ("sACT", dn_pe[tloc][2])]
                st, rdr = finish_tile(t, YTF[tloc], ydep, XO[par],
                                      xdst[t * 128:(t + 1) * 128, :], prep_next, YTE)
                K["xo_g"][par] = rdr
                ytf_guard[tloc] = [("sDVE", P.cnt["sDVE"])]
            tgu_guard[0] = [("sDVE", P.cnt["sDVE"])]
            tgu_guard[1] = [("sDVE", P.cnt["sDVE"])]

    def attn_b(xsrc, xdst, prep_next, nh=None, noout=False):
        bl = P.d("sync", BETA[:], DT["beta_in"], "qB")
        P.w("scalar", "qB", bl)
        brdy = P.o("scalar", lambda s: s.activation(BETA[:], BETA[:], AF.Sigmoid), "sACT")
        wq_guard = [("sDVE", P.cnt["sDVE"]), ("sPE", P.cnt["sPE"]),
                    ("sACT", P.cnt["sACT"])]
        woq_guard = [None, None]
        zsl_guard = [None, None]
        for h in range(nh if nh is not None else H):
            wdep = load_head_weights(h, (DT["wq_b"], DT["wk_b"], DT["wv_b"]),
                                     (Wqh, Wkh, Wvh), wq_guard)
            qr, sq_rdy = proj4(Wqh, Qt, None, wdep,
                               sig_out=None if _DBG_NOSIG else SqT)
            kr, sk_rdy = proj4(Wkh, Kt, None, wdep,
                               sig_out=None if _DBG_NOSIG else SkT, zinc=True)
            vr = vproj(Wvh, wdep, None)
            wq_guard = ("sPE", P.cnt["sPE"])
            P.o("vector", lambda v: v.memset(MEMF[:], 0.0))
            P.o("vector", lambda v: v.memset(STAT[:, 30:31], 1.0 / 128.0))
            P.o("vector", lambda v: v.memset(MEMZ[:, 0:128], 0.0))
            mz = P.o("vector", lambda v: v.memset(MEMZ[:, 128:129], 1.0 / 128.0), "sDVE")
            mz_rdy = ("sDVE", mz)
            atlast = None
            for s_ in range(_DBG_NSEG if _DBG_NSEG is not None else 4):
                c0 = s_ * 512
                for qi in range(4):
                    qq = K["spair_i"] % 4
                    K["spair_i"] += 1
                    bk = ["A0", "A1", "A2", "A3"][qq]
                    bank_wait_w("tensor", [bk])
                    if s_ == 0 and qi == 0:
                        P.w("tensor", "sACT", qr)
                        P.w("tensor", "sACT", kr)
                        P.w("tensor", "sACT", vr)
                        P.wd("tensor", [("sDVE", sq_rdy), ("sDVE", sk_rdy)])
                    pe = P.o("tensor", lambda t_, ii=qi, cc=c0, b=bk: t_.matmul(
                        bank_ap(b)[:, 0:(ii + 1) * 128], Qt[:, cc + ii * 128: cc + (ii + 1) * 128],
                        Kt[:, cc: cc + (ii + 1) * 128], start=True, stop=True), "sPE")
                    bank_set_w([bk], pe)
                    P.w("vector", "sPE", pe)
                    P.o("vector", lambda v, ii=qi, b=bk: v.tensor_tensor(
                        bank_ap(b)[:, ii * 128:(ii + 1) * 128],
                        bank_ap(b)[:, ii * 128:(ii + 1) * 128], MASK[:], ALU.add))
                    nv = P.o("vector", lambda v, ii=qi, b=bk: v.tensor_reduce(
                        STAT[:, 8:9], bank_ap(b)[:, 0:(ii + 1) * 128], AX.X, ALU.max,
                        negate=True), "sDVE")
                    P.w("gpsimd", "sDVE", nv)
                    gb = P.o("gpsimd", lambda g: g.tensor_single_scalar(
                        STAT[:, 10:11], STAT[:, 8:9], ISC, ALU.mult), "sGP")
                    P.w("scalar", "sGP", gb)
                    ex = P.o("scalar", lambda s, ii=qi, b=bk: s.activation(
                        PbB[:, 0:(ii + 1) * 128], bank_ap(b)[:, 0:(ii + 1) * 128], AF.Exp,
                        bias=STAT[:, 10:11], scale=ISC, accum_out=STAT[:, 12:13]), "sACT")
                    bank_set_r([bk], ("sACT", ex))
                    P.w("vector", "sACT", ex)
                    P.o("vector", lambda v: v.reciprocal(STAT[:, 14:15], STAT[:, 12:13]))
                    sep("vector", 2)
                    pn = P.o("vector", lambda v, ii=qi: v.tensor_scalar(
                        PbB[:, 0:(ii + 1) * 128], PbB[:, 0:(ii + 1) * 128], STAT[:, 14:15],
                        None, ALU.mult), "sDVE")
                    adq = psB[:, qq * 128:(qq + 1) * 128]
                    addone = None
                    for j in range(qi + 1):
                        pt = transpose_to(PbB[:, j * 128:(j + 1) * 128],
                                          PtB[:, j * 128:(j + 1) * 128],
                                          pe_dep=("sDVE", pn) if j == 0 else None)
                        if j == 0:
                            bank_wait_w("tensor", ["B0"])
                        P.w("tensor", "sACT", pt)
                        addone = P.o("tensor", lambda t_, jj=j, ss=s_, ii=qi, a=adq: t_.matmul(
                            a, PtB[:, jj * 128:(jj + 1) * 128],
                            Vtok[:, (ss * 4 + jj) * 128:(ss * 4 + jj + 1) * 128],
                            start=(jj == 0), stop=(jj == ii)), "sPE" if j == qi else None)
                    bank_set_w(["B0"], addone)
                    amb = psB[:, 512:641]
                    bank_wait_w("tensor", ["B1"])
                    P.wd("tensor", mz_rdy)
                    am = P.o("tensor", lambda t_, cc=c0, ii=qi, a=amb: t_.matmul(
                        a, SqT[:, cc + ii * 128: cc + (ii + 1) * 128], MEMZ[:, 0:129],
                        start=True, stop=True), "sPE")
                    bank_set_w(["B1"], am)
                    P.w("vector", "sPE", am)
                    rd = P.o("vector", lambda v: v.reciprocal(STAT[:, 31:32], psB[:, 640:641]),
                             "sDVE")
                    P.w("scalar", "sDVE", rd)
                    amv0 = P.o("scalar", lambda s: s.activation(
                        CMB[:, 0:128], psB[:, 512:640], AF.Identity, scale=STAT[:, 31:32]),
                        "sACT")
                    P.w("vector", "sACT", amv0)
                    amv = P.o("vector", lambda v, a=adq: v.tensor_tensor(
                        CMB[:, 128:256], CMB[:, 0:128], a, ALU.subtract), "sDVE")
                    bank_set_r(["B1"], [("sACT", amv0), ("sDVE", rd)])
                    P.w("gpsimd", "sACT", brdy)
                    P.w("gpsimd", "sDVE", amv)
                    gu2 = P.o("gpsimd", lambda g, hh=h: g.tensor_tensor(
                        CMB[:, 0:128], BETA[:, hh * 128:(hh + 1) * 128], CMB[:, 128:256],
                        ALU.mult), "sGP")
                    P.w("vector", "sGP", gu2)
                    if atlast is not None:
                        P.w("vector", "sACT", atlast)
                    at = P.o("vector", lambda v, a=adq: v.tensor_tensor(
                        XNB[:, 0:128], CMB[:, 0:128], a, ALU.add), "sDVE")
                    bank_set_r(["B0"], ("sDVE", at))
                    atlast = transpose_to(XNB[:, 0:128],
                                          ATTH[:, c0 + qi * 128: c0 + (qi + 1) * 128],
                                          pe_dep=("sDVE", at),
                                          act_dep=(("qZ", K["ab_st"])
                                                   if (s_ == 0 and qi == 0) else None))
                up = psB[:, 0:128]
                upfin = None
                prev_up = None
                for st_ in range(4):
                    smb = psB[:, 512:641]
                    bank_wait_w("tensor", ["B1"])
                    P.wd("tensor", mz_rdy)
                    sm = P.o("tensor", lambda t_, cc=c0, ii=st_, a=smb: t_.matmul(
                        a, SkT[:, cc + ii * 128: cc + (ii + 1) * 128], MEMZ[:, 0:129],
                        start=True, stop=True), "sPE")
                    bank_set_w(["B1"], sm)
                    P.w("vector", "sPE", sm)
                    rk = P.o("vector", lambda v: v.reciprocal(STAT[:, 31:32], psB[:, 640:641]),
                             "sDVE")
                    P.w("scalar", "sDVE", rk)
                    m1 = P.o("scalar", lambda s: s.activation(
                        SS0[:, 0:128], psB[:, 512:640], AF.Identity, scale=STAT[:, 31:32]),
                        "sACT")
                    P.w("vector", "sACT", m1)
                    if prev_up is not None:
                        P.w("vector", "sPE", prev_up)
                    ng = P.o("vector", lambda v, ss=s_, ii=st_: v.tensor_tensor(
                        NGD[:], SS0[:, 0:128],
                        Vtok[:, (ss * 4 + ii) * 128:(ss * 4 + ii + 1) * 128],
                        ALU.subtract), "sDVE")
                    bank_set_r(["B1"], [("sACT", m1), ("sDVE", rk)])
                    stk = transpose_to(SkT[:, c0 + st_ * 128: c0 + (st_ + 1) * 128],
                                       SKTOK[:], pe_dep=None)
                    if st_ == 0:
                        bank_wait_w("tensor", ["B0"])
                    P.w("tensor", "sACT", stk)
                    P.w("tensor", "sDVE", ng)
                    upfin = P.o("tensor", lambda t_, ii=st_, a=up: t_.matmul(
                        a, SKTOK[:], NGD[:], start=(ii == 0), stop=(ii == 3)), "sPE")
                    prev_up = upfin
                bank_set_w(["B0"], upfin)
                P.w("vector", "sPE", upfin)
                P.o("vector", lambda v, a=up: v.tensor_tensor(MEMF[:], MEMF[:], a,
                                                              ALU.subtract))
                zv = P.o("vector", lambda v, ss=s_: v.tensor_tensor(
                    STAT[:, 30:31], STAT[:, 30:31], ZINC[:, ss:ss + 1], ALU.add), "sDVE")
                bank_set_r(["B0"], ("sDVE", zv))
                P.w("scalar", "sDVE", zv)
                P.o("scalar", lambda s: s.copy(MEMZ[:, 0:128], MEMF[:]))
                mzv = P.o("scalar", lambda s: s.copy(MEMZ[:, 128:129], STAT[:, 30:31]),
                          "sACT")
                mz_rdy = ("sACT", mzv)
            P.w("sync", "sACT", atlast)
            K["ab_st"] = P.d("sync", abank[h * 128:(h + 1) * 128, :], ATTH[:], "qZ")
        if noout:
            return
        obank = ["A0", "A1", "A2", "A3"]
        yq_rdy = [[None] * 4 for _ in range(4)]
        yto_g = [None] * 4
        wls = {}
        zls = {}
        def load_woq(g, dq):
            wpar = dq % 2
            P.wd("gpsimd", woq_guard[wpar])
            sap = DT["wo_b"].rearrange("(hh p) d -> p hh d", p=128)[:, :, dq * 512:(dq + 1) * 512]
            dap = WOQ[wpar][:].rearrange("p (hh d) -> p hh d", d=512)
            qg = "qG0" if wpar == 0 else "qG1"
            wls[(g, dq)] = (qg, P.d("gpsimd", dap, sap, qg))
        def load_zsl(g, dq, ti):
            t = g * 4 + ti
            zpar = t % 2
            P.wd("sync", zsl_guard[zpar])
            P.w("sync", "qZ", K["ab_st"])
            h0 = (t % 4) * 4
            seg = t // 4
            aap = abank.rearrange("(hh p) t2 -> p hh t2", p=128)[
                :, h0:h0 + 4, seg * 512:(seg + 1) * 512]
            ql = "qL0" if zpar == 0 else "qL1"
            zls[(g, dq, ti)] = (ql, P.d("sync", ZSL[zpar][:].rearrange(
                "p (a q) -> p a q", q=512), aap, ql))
        steps = [(g, dq) for g in range(4) for dq in range(4)]
        load_woq(*steps[0])
        load_zsl(*steps[0], 0)
        zseq = [(g, dq, ti) for (g, dq) in steps for ti in range(4)]
        zi_next = 1
        for si, (g, dq) in enumerate(steps):
            if si + 1 < len(steps):
                load_woq(*steps[si + 1])
            for ti in range(4):
                t = g * 4 + ti
                while zi_next < len(zseq) and zi_next <= si * 4 + ti + 1:
                    load_zsl(*zseq[zi_next]); zi_next += 1
                qg, wl = wls[(g, dq)]
                ql, zl = zls[(g, dq, ti)]
                zpar = t % 2
                wpar = dq % 2
                bk = obank[ti]
                bank_wait_w("tensor", [bk])
                P.w("tensor", qg, wl)
                P.w("tensor", ql, zl)
                pe = None
                gat = ZSL[zpar][:].rearrange("p (a b c) -> p a b c", a=4, b=32, c=16)
                for h2 in range(16):
                    pe = P.o("tensor", lambda t_, hh=h2, wp=wpar, b=bk, gg=gat: t_.matmul(
                        bank_ap(b), gg[:, :, :, hh],
                        WOQ[wp][:, hh * 512:(hh + 1) * 512],
                        start=(hh == 0), stop=(hh == 15)), "sPE" if h2 == 15 else None)
                bank_set_w([bk], pe)
                zsl_guard[zpar] = ("sPE", pe)
                woq_guard[wpar] = ("sPE", pe)
                P.w("scalar", "sPE", pe)
                P.wd("scalar", yto_g[ti])
                ac = P.o("scalar", lambda s, ti_=ti, dq_=dq, b=bk: s.copy(
                    YTO[ti_][:, dq_ * 512:(dq_ + 1) * 512], bank_ap(b)), "sACT")
                bank_set_r([bk], ("sACT", ac))
                yq_rdy[ti][dq] = ac
            if dq == 3:
                for ti in range(4):
                    t = g * 4 + ti
                    par = t % 2
                    P.wd("sync", K["xo_g"][par])
                    qx = "qX0" if par == 0 else "qX1"
                    xl = P.d("sync", XO[par][:], xsrc[t * 128:(t + 1) * 128, :], qx)
                    ydep = [("sACT", yq_rdy[ti][3]), (qx, xl),
                            ("sACT", yq_rdy[ti][0]), ("sACT", yq_rdy[ti][1]),
                            ("sACT", yq_rdy[ti][2])]
                    _, rdr = finish_tile(t, YTO[ti], ydep, XO[par],
                                         xdst[t * 128:(t + 1) * 128, :], prep_next, YTE)
                    K["xo_g"][par] = rdr
                    yto_g[ti] = ("sDVE", P.cnt["sDVE"])

    prologue()
    mu, me = mean_units()
    tu_, te = tern_units()
    if stage > 1:
        pp = {"units": mu, "emit": me, "per_head": 26,
              "next": (wscale_finalize, (tu_, te))}
    else:
        pp = {"units": [], "emit": me, "per_head": 0}
    attn_a(pp)
    if stage > 1:
        ffn(0, xr, xr if stage > 2 else y,
            ("xnT" if stage > 2 else None))
    if stage > 2:
        attn_b(xr, xr if stage > 3 else y, ("R" if stage > 3 else None),
               nh=_DBG_NH_B, noout=_DBG_NOOUT_B)
    if stage > 3:
        ffn(1, xr, y, None)
    for s in ("qSt", "qZ", "qT", "qC", "qB", "qX0", "qX1", "qL0", "qL1", "qP",
              "qG0", "qG1"):
        P.w("sync", s, P.cnt[s])

    print(f"[kernel build] planned instructions ~{P.ni}, sems {P.cnt}", file=sys.stderr)
    _emit(P, nc, sems)
    pctx.close()
    sctx.close()
    return nc


def _host_inputs(inputs, b):
    x = np.ascontiguousarray(inputs["L0_attn_a_x"][b])
    wq_a = np.ascontiguousarray(inputs["L0_attn_a_W_Q"].transpose(1, 0, 2).reshape(D, D))
    wk_a = np.ascontiguousarray(inputs["L0_attn_a_W_K"].transpose(1, 0, 2).reshape(D, D))
    wv_a = np.ascontiguousarray(inputs["L0_attn_a_W_V"].transpose(1, 0, 2).reshape(D, D))
    wo_a = np.ascontiguousarray(inputs["L0_attn_a_W_O"].reshape(D, D))
    bq = np.ascontiguousarray(inputs["L0_attn_a_b_Q"].T)
    bk = np.ascontiguousarray(inputs["L0_attn_a_b_K"].T)
    bv = np.ascontiguousarray(inputs["L0_attn_a_b_V"].T)
    bo = np.ascontiguousarray(np.broadcast_to(inputs["L0_attn_a_b_O"].reshape(1, D),
                                              (128, D)))
    beta = np.ascontiguousarray(np.broadcast_to(
        inputs["L0_attn_b_betas"].reshape(1, D), (128, D)))
    wgT = np.ascontiguousarray(inputs["L0_ffn_w_gate"].T)
    wuT = np.ascontiguousarray(inputs["L0_ffn_w_up"].T)
    wdT = np.ascontiguousarray(inputs["L0_ffn_w_down"].T)
    ident = np.eye(128, dtype=BF)
    mask = np.where(np.tril(np.ones((128, 128), bool)), np.float32(0.0),
                    np.float32(NEGBIG)).astype(np.float32)
    ones = np.ones((128, 128), np.float32)
    return {
        "x0": x, "wq_a": wq_a, "wk_a": wk_a, "wv_a": wv_a, "wo_a": wo_a,
        "bq_in": bq, "bk_in": bk, "bv_in": bv, "bo_bc": bo,
        "wq_b": np.ascontiguousarray(inputs["L0_attn_b_Wq"]),
        "wk_b": np.ascontiguousarray(inputs["L0_attn_b_Wk"]),
        "wv_b": np.ascontiguousarray(inputs["L0_attn_b_Wv"]),
        "wo_b": np.ascontiguousarray(inputs["L0_attn_b_Wo"]),
        "beta_in": beta, "wgT": wgT, "wuT": wuT, "wdT": wdT,
        "ident_in": ident, "mask_in": mask, "ones_in": ones,
    }


def kernel(**inputs):
    nc = build_program(_STAGE)
    in_maps = [_host_inputs(inputs, b) for b in range(8)]
    res = run_bass_kernel_spmd(nc, in_maps, core_ids=list(range(8)))
    out = np.stack([np.asarray(r["y"]) for r in res.results], axis=0)
    return out.astype(np.float32)


# version 06
# speedup vs baseline: 1.4486x; 1.0026x over previous
import sys
import numpy as np
import ml_dtypes
from contextlib import ExitStack
import concourse.bass as bass
import concourse.mybir as mybir
from concourse.bass_utils import run_bass_kernel_spmd

F32 = mybir.dt.float32
BF16 = mybir.dt.bfloat16
I32 = mybir.dt.int32
U8 = mybir.dt.uint8
AF = mybir.ActivationFunctionType
ALU = mybir.AluOpType
AX = mybir.AxisListType
BF = ml_dtypes.bfloat16

D = 2048
T = 2048
H = 16
FF = 8192
NT = 16
ISC = float(1.0 / np.sqrt(np.float64(128.0)))
NEGBIG = -1.0e9
LNEPS = 1e-5
QEPS = 1e-5
WDEN = float(D) * float(FF)
KB = 1024

_STAGE = 4
_DBG_ZB = False
_DBG_HOOKS = False
_DBG_NH = None
_DBG_NOOUT = False
_DBG_NH_B = None
_DBG_NOOUT_B = False
_DBG_NSEG = None
_DBG_NOSIG = False
_DBG_RACE = False


class PL:
    ENGS = ("sync", "scalar", "vector", "tensor", "gpsimd")
    SEMS = ("qC", "qB", "qX0", "qX1", "qL0", "qL1", "qP", "qT", "qZ", "qSt",
            "qG0", "qG1", "sPE", "sACT", "sDVE", "sGP")

    def __init__(self):
        self.ops = {e: [] for e in self.ENGS}
        self.cnt = {s: 0 for s in self.SEMS}
        self.ni = 0

    def w(self, eng, sem, val):
        if val is None or val <= 0:
            return
        assert val <= self.cnt[sem], (eng, sem, val, self.cnt[sem])
        self.ops[eng].append(("w", sem, val))

    def wd(self, eng, dep):
        if dep is None:
            return
        if isinstance(dep, tuple):
            dep = [dep]
        for s, v in dep:
            self.w(eng, s, v)

    def o(self, eng, fn, sem=None):
        v = None
        if sem is not None:
            self.cnt[sem] += 1
            v = self.cnt[sem]
        self.ops[eng].append(("o", fn, sem))
        self.ni += 1
        return v

    def d(self, eng, out, in_, sem):
        self.cnt[sem] += 16
        v = self.cnt[sem]
        self.ops[eng].append(("d", out, in_, sem))
        self.ni += 1
        return v


def _emit(P, nc, sems):
    with nc.Block() as block:
        def runner(name):
            def run(eng):
                for op in P.ops[name]:
                    if op[0] == "w":
                        eng.wait_ge(sems[op[1]], op[2])
                    elif op[0] == "d":
                        eng.dma_start(out=op[1], in_=op[2]).then_inc(sems[op[3]], 16)
                    else:
                        ins = op[1](eng)
                        if op[2] is not None:
                            ins.then_inc(sems[op[2]], 1)
            return run
        block.sync(runner("sync"))
        block.scalar(runner("scalar"))
        block.vector(runner("vector"))
        block.tensor(runner("tensor"))
        block.gpsimd(runner("gpsimd"))


def build_program(stage=_STAGE):
    nc = bass.Bass("TRN2", target_bir_lowering=False, detect_race_conditions=_DBG_RACE)

    DT = {}
    def din(name, shape, dt=F32):
        DT[name] = nc.dram_tensor(name, shape, dt, kind="ExternalInput").ap()
    din("x0", [T, D]); din("wq_a", [D, D]); din("wk_a", [D, D]); din("wv_a", [D, D])
    din("wo_a", [D, D]); din("bq_in", [128, H]); din("bk_in", [128, H])
    din("bv_in", [128, H]); din("bo_bc", [128, D])
    din("wq_b", [D, D]); din("wk_b", [D, D]); din("wv_b", [D, D]); din("wo_b", [D, D])
    din("beta_in", [128, D])
    din("wgT", [D, FF]); din("wuT", [D, FF]); din("wdT", [FF, D])
    din("ident_in", [128, 128], BF16); din("mask_in", [128, 128]); din("ones_in", [128, 128])
    y = nc.dram_tensor("y", [T, D], F32, kind="ExternalOutput").ap()
    xr = nc.dram_tensor("xr", [T, D], F32, kind="Internal").ap()
    zkind = "ExternalOutput" if _DBG_ZB else "Internal"
    zbank = nc.dram_tensor("zbank", [H * 128, T], BF16, kind=zkind).ap()
    abank = nc.dram_tensor("abank", [H * 128, T], BF16, kind=zkind).ap()
    if _DBG_HOOKS:
        dbgA = nc.dram_tensor("dbgA", [128, 32 * KB], BF16, kind="ExternalOutput").ap()
        dbgq = nc.dram_tensor("dbgq", [128, T], BF16, kind="ExternalOutput").ap()
        dbgk = nc.dram_tensor("dbgk", [128, T], BF16, kind="ExternalOutput").ap()
        dbgv = nc.dram_tensor("dbgv", [128, T], BF16, kind="ExternalOutput").ap()
        dbgp = nc.dram_tensor("dbgp", [128, T], BF16, kind="ExternalOutput").ap()
    tg = nc.dram_tensor("tg", [D, FF], BF16, kind="Internal").ap()
    tu = nc.dram_tensor("tu", [D, FF], BF16, kind="Internal").ap()
    td = nc.dram_tensor("td", [FF, D], BF16, kind="Internal").ap()

    base = (nc.sbuf_base + 31) & ~31
    assert nc.sbuf_top - base >= 206 * KB, (base, nc.sbuf_top)

    def at(name, off_b, shape, dt):
        assert off_b % 32 == 0, (name, off_b)
        h = nc.alloc_sbuf_tensor_at(name, shape, dt, offset=base + off_b)
        return h

    P2 = 64 * KB
    EO = 172 * KB
    SM = EO + 32 * KB
    A = at("A", 0, [128, 32 * KB], BF16)
    WOQ = [at("WOQ0", P2, [128, 8 * KB], BF16), at("WOQ1", P2 + 16 * KB, [128, 8 * KB], BF16)]
    Qt = at("Qt", P2 + 32 * KB, [128, 2 * KB], BF16)
    Kt = at("Kt", P2 + 36 * KB, [128, 2 * KB], BF16)
    VtT = at("VtT", P2 + 40 * KB, [128, 2 * KB], BF16)
    Vtok = at("Vtok", P2 + 44 * KB, [128, 2 * KB], BF16)
    Pbuf = at("Pbuf", P2 + 48 * KB, [128, 2 * KB], BF16)
    PbufB = at("pbuf2", P2 + 104 * KB, [128, 2 * KB], BF16)
    Ptb = at("Ptb", P2 + 52 * KB, [128, 2 * KB], BF16)
    Ztb = at("Ztb", P2 + 56 * KB, [128, 2 * KB], BF16)
    ZSL = [at("zsl0", P2 + 76 * KB, [128, 2 * KB], BF16), at("zsl1", P2 + 80 * KB, [128, 2 * KB], BF16)]
    bObc = at("bObc", P2 + 68 * KB, [128, 2 * KB], F32)
    Wqh = at("Wqh", P2 + 76 * KB, [128, 2 * KB], BF16)
    Wkh = at("Wkh", P2 + 80 * KB, [128, 2 * KB], BF16)
    Wvh = at("Wvh", P2 + 84 * KB, [128, 2 * KB], BF16)
    Wq2 = at("Wq2", P2 + 0 * KB, [128, 2 * KB], BF16)
    Wk2 = at("Wk2", P2 + 4 * KB, [128, 2 * KB], BF16)
    Wv2 = at("Wv2", P2 + 8 * KB, [128, 2 * KB], BF16)
    ZTOK = at("ztok", P2 + 60 * KB, [128, 256], BF16)
    PRES = at("pres", P2 + 88 * KB, [128, 2 * KB], F32)
    PRELO = at("prelo", P2 + 96 * KB, [128, 2 * KB], BF16)
    PREOUT = at("preout", P2 + 100 * KB, [128, 2 * KB], BF16)
    YTO = [at(f"yto{i}", P2 + (32 + 8 * i) * KB, [128, 2 * KB], F32) for i in range(4)]
    HSL = [at(f"hsl{i}", P2 + 16 * i * KB, [128, 8 * KB], BF16) for i in range(4)]
    TGU = [at("tgu0", P2 + 64 * KB, [128, 8 * KB], BF16), at("tgu1", P2 + 80 * KB, [128, 8 * KB], BF16)]
    TDQ = [at("tdq0", P2 + 96 * KB, [128, 2 * KB], BF16), at("tdq1", P2 + 100 * KB, [128, 2 * KB], BF16)]
    GSCR = at("gscr", P2 + 104 * KB, [128, 256], F32)
    RHC = at("rhc", P2 + 105 * KB, [128, 1 * KB], BF16)
    YTF = [at("ytf0", P2 + 64 * KB, [128, 2 * KB], F32), at("ytf1", P2 + 72 * KB, [128, 2 * KB], F32),
           at("ytf2", P2 + 80 * KB, [128, 2 * KB], F32), at("ytf3", P2 + 88 * KB, [128, 2 * KB], F32)]
    SqT = at("SqT", P2 + 48 * KB, [128, 2 * KB], BF16)
    SkT = at("SkT", P2 + 88 * KB, [128, 2 * KB], BF16)
    ATTH = at("ATTH", P2 + 92 * KB, [128, 2 * KB], BF16)
    SS0 = at("ss0", P2 + 96 * KB, [128, 512], F32)
    SS1 = at("ss1", P2 + 98 * KB, [128, 512], F32)
    CMB = at("cmb", P2 + 100 * KB, [128, 256], F32)
    PbB = at("pbB", P2 + 101 * KB, [128, 512], BF16)
    PtB = at("ptB", P2 + 102 * KB, [128, 512], BF16)
    MEMF = at("memf", P2 + 103 * KB, [128, 128], F32)
    MEMZ = at("memz", P2 + 103 * KB + 512, [128, 132], BF16)
    NGD = at("ngd", P2 + 103 * KB + 800, [128, 128], BF16)
    SKTOK = at("sktok", P2 + 103 * KB + 1056, [128, 128], BF16)
    BETA = at("beta", P2 + 68 * KB, [128, 2 * KB], F32)
    XO = [at("xo0", EO, [128, 2 * KB], F32), at("xo1", EO + 8 * KB, [128, 2 * KB], F32)]
    YTE = at("ytE", EO + 16 * KB, [128, 2 * KB], F32)
    I32S = at("i32s", EO + 24 * KB, [128, 1 * KB], I32)
    XNB = at("xnb", EO + 28 * KB, [128, 2 * KB], BF16)
    MASK = at("maskT", SM, [128, 128], F32)
    IDT = at("idT", SM + 512, [128, 128], BF16)
    ONESF = at("onesT", SM + 768, [128, 128], F32)
    BQC = at("bqc", SM + 1280, [128, H], F32)
    BKC = at("bkc", SM + 1344, [128, H], F32)
    BVC = at("bvc", SM + 1408, [128, H], F32)
    SX = at("sx", SM + 1472, [128, NT], F32)
    STAT = at("stat", SM + 1536, [128, 32], F32)
    WSUM = at("wsum", SM + 1664, [128, 4], F32)
    WSC = at("wsc", SM + 1696, [128, 4], F32)
    THR = at("thr", SM + 1728, [128, 4], F32)
    NTHR = at("nthr", SM + 1760, [128, 4], F32)
    SHD = at("shd", SM + 1792, [128, 4], F32)
    SGU = at("sgu", SM + 1824, [128, 8], F32)
    HMX = at("hmx", SM + 1856, [128, 4], F32)
    ZINC = at("zinc", SM + 1888, [128, 4], F32)
    CEPS = at("ceps", SM + 1920, [128, 4], F32)
    DUMP = at("dump", SM + 1952, [128, 64], F32)
    HMXW = at("hmxw", SM + 2208, [128, 128], F32)

    pctx = ExitStack()
    psA = pctx.enter_context(nc.psum_tensor("psA", [128, 2048], F32))
    psB = pctx.enter_context(nc.psum_tensor("psB", [128, 1024], F32))
    psT = pctx.enter_context(nc.psum_tensor("psT", [128, 1024], BF16))
    psU = pctx.enter_context(nc.psum_tensor("psU", [128, 1024], BF16))
    sctx = ExitStack()
    sems = {s: sctx.enter_context(nc.semaphore(s)) for s in PL.SEMS}

    P = PL()
    K = {
        "bkg": {b: None for b in
                ("A0", "A1", "A2", "A3", "B0", "B1")},
        "bkw": {b: None for b in
                ("A0", "A1", "A2", "A3", "B0", "B1")},
        "psT_i": 0,
        "psT_g": [None, None],
        "xo_g": [None, None],
        "a_rdy": [None] * NT,
        "sx_rdy": [None] * NT,
        "spair_i": 0,
        "tern_st": {m: [0] * 32 for m in range(2)},
        "td_all": 0,
        "thr_rdy": None,
        "zb_st": 0,
        "ab_st": 0,
        "xnb_g": None,
    }

    def bank_ap(name):
        if name.startswith("A"):
            i = int(name[1])
            return psA[:, i * 512:(i + 1) * 512]
        if name.startswith("B"):
            i = int(name[1])
            return psB[:, i * 512:(i + 1) * 512]
        raise KeyError(name)

    def bank_wait_w(eng, names):
        for n in names:
            P.wd(eng, K["bkg"][n])

    def bank_set_w(names, val):
        for n in names:
            K["bkw"][n] = ("sPE", val)

    def bank_wait_r(eng, names):
        for n in names:
            P.wd(eng, K["bkw"][n])

    def bank_set_r(names, dep):
        for n in names:
            old = K["bkg"][n]
            deps = [dep] if isinstance(dep, tuple) else list(dep)
            if old:
                deps = deps + ([old] if isinstance(old, tuple) else list(old))
            best = {}
            for s, v in deps:
                best[s] = max(best.get(s, 0), v)
            K["bkg"][n] = [(s, v) for s, v in best.items()]

    def transpose_to(src_ap, dst_ap, pe_dep=None, act_dep=None):
        i = K["psT_i"] % 2
        K["psT_i"] += 1
        slot = (psT if i == 0 else psU)[:, 0:128]
        P.wd("tensor", K["psT_g"][i])
        P.wd("tensor", pe_dep)
        pe = P.o("tensor", lambda t, s=slot, a=src_ap: t.transpose(s, a, IDT[:]), "sPE")
        P.w("scalar", "sPE", pe)
        P.wd("scalar", act_dep)
        ac = P.o("scalar", lambda s, d=dst_ap, sl=slot: s.copy(d, sl), "sACT")
        K["psT_g"][i] = ("sACT", ac)
        return ac

    def transpose_group(srcs, dst_ap, pe_dep=None, act_dep=None):
        n = len(srcs)
        i = K["psT_i"] % 2
        K["psT_i"] += 1
        bank = psT if i == 0 else psU
        P.wd("tensor", K["psT_g"][i])
        P.wd("tensor", pe_dep)
        pe = None
        for k, s_ap in enumerate(srcs):
            pe = P.o("tensor", lambda t, sl=bank[:, k * 128:(k + 1) * 128], a=s_ap:
                     t.transpose(sl, a, IDT[:]), "sPE" if k == n - 1 else None)
        P.w("scalar", "sPE", pe)
        P.wd("scalar", act_dep)
        ac = P.o("scalar", lambda s, d=dst_ap, sl=bank[:, 0:n * 128]: s.copy(d, sl),
                 "sACT")
        K["psT_g"][i] = ("sACT", ac)
        return ac

    ident = DT["ident_in"]; maskin = DT["mask_in"]; onesin = DT["ones_in"]

    cl = P.d("sync", IDT[:], ident, "qC")
    cl = P.d("sync", MASK[:], maskin, "qC")
    cl = P.d("sync", ONESF[:], onesin, "qC")
    cl = P.d("sync", BQC[:], DT["bq_in"], "qC")
    cl = P.d("sync", BKC[:], DT["bk_in"], "qC")
    cl = P.d("sync", BVC[:], DT["bv_in"], "qC")
    consts_rdy = cl
    P.o("vector", lambda v: v.memset(CEPS[:, 0:1], 1e-5))
    P.o("vector", lambda v: v.memset(CEPS[:, 1:2], 5e-6))
    cinit = P.o("vector", lambda v: v.memset(CEPS[:, 2:3], -5e-6), "sDVE")
    P.w("scalar", "sDVE", cinit)

    def sep(eng, n=1):
        for _ in range(n):
            if eng == "scalar":
                P.o("scalar", lambda s: s.activation(DUMP[:, 0:64], MASK[:, 0:64], AF.Copy))
            elif eng == "vector":
                P.o("vector", lambda v: v.tensor_copy(DUMP[:, 0:64], MASK[:, 0:64]))
            else:
                P.o("gpsimd", lambda g: g.tensor_copy(DUMP[:, 0:64], MASK[:, 0:64]))

    def finish_tile(t, yth, yt_dep, xo, dst_tile, prep, scratch_f32):
        if yth is not None:
            P.wd("vector", yt_dep)
            v1 = P.o("vector", lambda v, a=yth, b=xo: v.tensor_tensor(b[:], a[:], b[:], ALU.add), "sDVE")
        else:
            P.wd("vector", yt_dep)
            v1 = None
        if dst_tile is not None:
            if v1:
                P.w("sync", "sDVE", v1)
            st = P.d("sync", dst_tile, xo[:], "qSt")
        else:
            st = None
        v2 = P.o("vector", lambda v, b=xo: v.tensor_reduce(STAT[:, 0:1], b[:], AX.X, ALU.add), "sDVE")
        P.w("gpsimd", "sDVE", v2)
        g1 = P.o("gpsimd", lambda g: g.tensor_single_scalar(
            STAT[:, 1:2], STAT[:, 0:1], -1.0 / D, ALU.mult), "sGP")
        P.w("scalar", "sGP", g1)
        P.o("scalar", lambda s, b=xo, d=scratch_f32: s.activation(
            d[:], b[:], AF.Square, bias=STAT[:, 1:2], accum_out=STAT[:, 2:3]))
        sep("scalar", 2)
        a3 = P.o("scalar", lambda s: s.activation(STAT[:, 7:8], STAT[:, 2:3], AF.Sqrt,
                                                  scale=1.0 / D, bias=CEPS[:, 0:1]), "sACT")
        P.w("vector", "sACT", a3)
        P.o("vector", lambda v: v.reciprocal(STAT[:, 3:4], STAT[:, 7:8]))
        sep("vector", 2)
        rdr = [("sACT", a3)]
        if prep == "xnT":
            P.wd("vector", K["xnb_g"])
            v3 = P.o("vector", lambda v, b=xo: v.tensor_scalar(
                XNB[:], b[:], STAT[:, 1:2], STAT[:, 3:4], ALU.add, ALU.mult), "sDVE")
            rdr.append(("sDVE", v3))
            last = None
            Av = A[:].rearrange("p (c x) -> p c x", x=2048)
            for c0 in range(0, 16, 4):
                srcs = [XNB[:, c * 128:(c + 1) * 128] for c in range(c0, c0 + 4)]
                last = transpose_group(srcs, Av[:, c0:c0 + 4, t * 128:(t + 1) * 128],
                                       pe_dep=("sDVE", v3) if c0 == 0 else None)
            K["a_rdy"][t] = ("sACT", last)
            K["xnb_g"] = ("sACT", last)
        elif prep == "R":
            P.wd("vector", K["xnb_g"])
            v3 = P.o("vector", lambda v, b=xo, d=scratch_f32: v.tensor_scalar(
                d[:], b[:], STAT[:, 1:2], STAT[:, 3:4], ALU.add, ALU.mult), "sDVE")
            v4 = P.o("vector", lambda v, d=scratch_f32: v.tensor_reduce(
                STAT[:, 4:5], d[:], AX.X, ALU.max, apply_absolute_value=True), "sDVE")
            sep("vector", 2)
            P.o("vector", lambda v: v.tensor_single_scalar(
                STAT[:, 5:6], STAT[:, 4:5], QEPS, ALU.max))
            sep("vector", 2)
            rc = P.o("vector", lambda v: v.reciprocal(STAT[:, 8:9], STAT[:, 5:6]), "sDVE")
            P.w("gpsimd", "sDVE", rc)
            v5 = P.o("gpsimd", lambda g: g.tensor_single_scalar(
                STAT[:, 6:7], STAT[:, 8:9], 127.0, ALU.mult), "sGP")
            rdr.append(("sDVE", v4))
            P.w("scalar", "sGP", v5)
            sxr = P.o("scalar", lambda s, tt=t: s.activation(SX[:, tt:tt + 1], STAT[:, 5:6],
                                                             AF.Copy, scale=1.0 / 127.0), "sACT")
            K["sx_rdy"][t] = ("sACT", sxr)
            last = None
            Av = A[:].rearrange("p (c x) -> p c x", x=2048)
            for half in range(2):
                P.o("scalar", lambda s, d=scratch_f32, q=half: s.activation(
                    I32S[:], d[:, q * 1024:(q + 1) * 1024], AF.Copy, scale=STAT[:, 6:7]))
                ra = P.o("scalar", lambda s, q=half: s.activation(
                    XNB[:, q * 1024:(q + 1) * 1024], I32S[:], AF.Copy), "sACT")
                for cc in range(0, 8, 4):
                    c0 = half * 8 + cc
                    srcs = [XNB[:, c * 128:(c + 1) * 128] for c in range(c0, c0 + 4)]
                    last = transpose_group(srcs, Av[:, c0:c0 + 4, t * 128:(t + 1) * 128],
                                           pe_dep=("sACT", ra) if cc == 0 else None)
            K["a_rdy"][t] = ("sACT", last)
            K["xnb_g"] = ("sACT", last)
        rdr2 = list(rdr)
        if st is not None:
            rdr2.append(("qSt", st))
        return st, rdr2

    def prologue():
        for e in ("tensor", "scalar", "vector", "gpsimd"):
            P.w(e, "qC", consts_rdy)
        for t in range(NT):
            par = t % 2
            P.wd("sync", K["xo_g"][par])
            qx = "qX0" if par == 0 else "qX1"
            ld = P.d("sync", XO[par][:], DT["x0"][t * 128:(t + 1) * 128, :], qx)
            _, rdr = finish_tile(t, None, [(qx, ld)], XO[par], None, "xnT", YTE)
            K["xo_g"][par] = rdr

    WSRC = [DT["wgT"], DT["wuT"], DT["wdT"]]
    TDST = [tg, tu, td]

    def w_tile_aps(m, db, fb):
        src = WSRC[m][db * 1024:(db + 1) * 1024, fb * 256:(fb + 1) * 256]
        src = src.rearrange("(c p) f -> p c f", p=128)
        dst = TDST[m][db * 1024:(db + 1) * 1024, fb * 256:(fb + 1) * 256]
        dst = dst.rearrange("(c p) f -> p c f", p=128)
        return src, dst

    pp_state = {"stage_g": None, "lo_g": None, "out_g": None, "wsum_last": [None] * 3}

    def mean_units():
        units = []
        for m in range(3):
            nb = (2, 32) if m < 2 else (8, 8)
            zi = P.o("gpsimd", lambda g, mm=m: g.memset(WSUM[:, mm:mm + 1], 0.0), "sGP")
            pp_state["wsum_last"][m] = ("sGP", zi)
            for db in range(nb[0]):
                for fb in range(nb[1]):
                    units.append((m, db, fb))
        def emit(u):
            m, db, fb = u
            src, _ = w_tile_aps(m, db, fb)
            P.wd("sync", pp_state["stage_g"])
            ld = P.d("sync", PRES[:].rearrange("p (c f) -> p c f", f=256), src, "qP")
            P.w("scalar", "qP", ld)
            P.wd("scalar", pp_state.get("gv_g"))
            aa = P.o("scalar", lambda s: s.activation(PRELO[:], PRES[:], AF.Abs,
                                                      accum_out=STAT[:, 25:26]), "sACT")
            P.w("gpsimd", "sACT", aa)
            gv = P.o("gpsimd", lambda g, mm=m: g.tensor_tensor(
                WSUM[:, mm:mm + 1], WSUM[:, mm:mm + 1], STAT[:, 25:26], ALU.add), "sGP")
            pp_state["gv_g"] = ("sGP", gv)
            pp_state["stage_g"] = [("sACT", aa)]
            pp_state["lo_g"] = [("sACT", aa)]
            pp_state["wsum_last"][m] = ("sGP", gv)
        return units, emit

    def wscale_finalize():
        last = None
        for m in range(3):
            bank_wait_w("tensor", ["B1"])
            P.wd("tensor", pp_state["wsum_last"][m])
            pe = P.o("tensor", lambda t, mm=m: t.matmul(
                psB[:, 512 + mm:512 + mm + 1], ONESF[:], WSUM[:, mm:mm + 1],
                start=True, stop=True), "sPE")
            bank_set_w(["B1"], pe)
            P.w("scalar", "sPE", pe)
            P.o("scalar", lambda s, mm=m: s.activation(WSC[:, mm:mm + 1],
                psB[:, 512 + mm:512 + mm + 1], AF.Identity, scale=1.0 / WDEN, bias=CEPS[:, 0:1]))
            P.o("scalar", lambda s, mm=m: s.activation(THR[:, mm:mm + 1],
                psB[:, 512 + mm:512 + mm + 1], AF.Identity, scale=0.5 / WDEN, bias=CEPS[:, 1:2]))
            last = P.o("scalar", lambda s, mm=m: s.activation(NTHR[:, mm:mm + 1],
                psB[:, 512 + mm:512 + mm + 1], AF.Identity, scale=-0.5 / WDEN,
                bias=CEPS[:, 2:3]), "sACT")
            bank_set_r(["B1"], ("sACT", last))
        K["thr_rdy"] = ("sACT", last)

    def tern_units():
        units = []
        for m in range(3):
            nb = (2, 32) if m < 2 else (8, 8)
            for fb in range(nb[1]):
                for db in range(nb[0]):
                    units.append((m, db, fb, db == nb[0] - 1))
        def emit(u):
            m, db, fb, is_last_db = u
            src, dst = w_tile_aps(m, db, fb)
            P.wd("sync", pp_state["stage_g"])
            ld = P.d("sync", PRES[:].rearrange("p (c f) -> p c f", f=256), src, "qP")
            P.w("vector", "qP", ld)
            P.wd("vector", K["thr_rdy"])
            P.wd("vector", pp_state["out_g"])
            P.o("vector", lambda v, mm=m: v.tensor_single_scalar(
                PRELO[:], PRES[:], NTHR[:, mm:mm + 1], ALU.is_le))
            v1 = P.o("vector", lambda v, mm=m: v.scalar_tensor_tensor(
                PREOUT[:], PRES[:], THR[:, mm:mm + 1], PRELO[:],
                ALU.is_ge, ALU.subtract), "sDVE")
            P.w("sync", "sDVE", v1)
            st = P.d("sync", dst, PREOUT[:].rearrange("p (c f) -> p c f", f=256), "qT")
            pp_state["stage_g"] = [("sDVE", v1)]
            pp_state["out_g"] = [("qT", st)]
            if m < 2 and is_last_db:
                K["tern_st"][m][fb] = st
            if m == 2 and is_last_db and fb == 7:
                K["td_all"] = st
        return units, emit

    def load_head_weights(h, wsrcs, bufs, guard_pe):
        outs = []
        for src, buf in zip(wsrcs, bufs):
            P.wd("gpsimd", guard_pe)
            sap = src.rearrange("(c p) e -> p c e", p=128)[:, :, h * 128:(h + 1) * 128]
            dap = buf[:].rearrange("p (c e) -> p c e", e=128)
            outs.append(P.d("gpsimd", dap, sap, "qG0"))
        return outs[-1]

    def proj4(wbuf, out_sb, bias_col, wdep, sig_out=None, zinc=False):
        rdy = None
        sig_rdy = None
        for tc in range(4):
            bk = "B0" if tc % 2 == 0 else "B1"
            bank_wait_w("tensor", [bk])
            if tc == 0:
                P.w("tensor", "qG0", wdep)
            P.wd("tensor", K["a_rdy"][tc * 4 + 3])
            P.wd("tensor", K["a_rdy"][tc * 4 + 2])
            P.wd("tensor", K["a_rdy"][tc * 4 + 1])
            P.wd("tensor", K["a_rdy"][tc * 4 + 0])
            pe = None
            for c in range(16):
                pe = P.o("tensor", lambda t, cc=c, w=wbuf, b=bk, tcc=tc: t.matmul(
                    bank_ap(b), w[:, cc * 128:(cc + 1) * 128],
                    A[:, cc * 2048 + tcc * 512: cc * 2048 + (tcc + 1) * 512],
                    start=(cc == 0), stop=(cc == 15)), "sPE" if c == 15 else None)
            bank_set_w([bk], pe)
            P.w("scalar", "sPE", pe)
            if bias_col is not None:
                ac = P.o("scalar", lambda s, o=out_sb, b=bk, tcc=tc, bc=bias_col: s.activation(
                    o[:, tcc * 512:(tcc + 1) * 512], bank_ap(b), AF.Identity, bias=bc), "sACT")
            else:
                ac = P.o("scalar", lambda s, o=out_sb, b=bk, tcc=tc: s.copy(
                    o[:, tcc * 512:(tcc + 1) * 512], bank_ap(b)), "sACT")
            deps = [("sACT", ac)]
            if sig_out is not None:
                P.w("vector", "sPE", pe)
                P.w("vector", "sACT", ac)
                sm0 = P.o("vector", lambda v, b=bk: v.tensor_single_scalar(
                    SS0[:, 0:512], bank_ap(b), 0.0, ALU.min), "sDVE")
                P.w("scalar", "sDVE", sm0)
                e1 = P.o("scalar", lambda s: s.activation(SS1[:, 0:512], SS0[:, 0:512],
                                                          AF.Exp), "sACT")
                P.w("vector", "sACT", e1)
                if zinc:
                    sv = P.o("vector", lambda v, o=sig_out, b=bk, tcc=tc: v.scalar_tensor_tensor(
                        o[:, tcc * 512:(tcc + 1) * 512], bank_ap(b), 0.0, SS1[:, 0:512],
                        ALU.max, ALU.add, accum_out=ZINC[:, tcc:tcc + 1]), "sDVE")
                else:
                    sv = P.o("vector", lambda v, o=sig_out, b=bk, tcc=tc: v.scalar_tensor_tensor(
                        o[:, tcc * 512:(tcc + 1) * 512], bank_ap(b), 0.0, SS1[:, 0:512],
                        ALU.max, ALU.add), "sDVE")
                deps.append(("sDVE", sv))
                sig_rdy = sv
            bank_set_r([bk], deps)
            rdy = ac
        return rdy, sig_rdy

    def vproj(wbuf, wdep, bias_col):
        vr, _ = proj4(wbuf, VtT, bias_col, wdep)
        last = None
        for t0 in range(0, NT, 4):
            srcs = [VtT[:, tt * 128:(tt + 1) * 128] for tt in range(t0, t0 + 4)]
            last = transpose_group(srcs, Vtok[:, t0 * 128:(t0 + 4) * 128],
                                   pe_dep=("sACT", vr) if t0 == 0 else None)
        return last

    def attn_a(pp_iter):
        bol = P.d("sync", bObc[:], DT["bo_bc"], "qC")
        woq_guard = [None, None]
        zsl_guard = [None, None]
        wq_guard = None
        qkv_rdy = None
        nheads = _DBG_NH if _DBG_NH is not None else H
        wsrcs = (DT["wq_a"], DT["wk_a"], DT["wv_a"])
        wbufs = [(Wqh, Wkh, Wvh), (Wq2, Wk2, Wv2)]
        wdeps = {}
        K["zl_hist"] = [None, None]
        for h in range(nheads):
            if h == 0:
                wdeps[0] = load_head_weights(0, wsrcs, wbufs[0], None)
                if nheads > 1:
                    wdeps[1] = load_head_weights(1, wsrcs, wbufs[1], None)
            elif h + 1 < nheads:
                wdeps[h + 1] = load_head_weights(h + 1, wsrcs, wbufs[(h + 1) % 2],
                                                 wq_guard)
            wdep = wdeps.pop(h)
            wb = wbufs[h % 2]
            qr, _ = proj4(wb[0], Qt, BQC[:, h:h + 1], wdep)
            kr, _ = proj4(wb[1], Kt, BKC[:, h:h + 1], wdep)
            vr = vproj(wb[2], wdep, BVC[:, h:h + 1])
            wq_guard = ("sPE", P.cnt["sPE"])
            if _DBG_HOOKS and h == 0:
                P.w("sync", "sACT", qr)
                P.w("sync", "sACT", kr)
                P.w("sync", "sACT", vr)
                P.d("sync", dbgA, A[:], "qSt")
                P.d("sync", dbgq, Qt[:], "qSt")
                P.d("sync", dbgk, Kt[:], "qSt")
                P.d("sync", dbgv, Vtok[:], "qSt")
            zlast = None

            def emit_scores_softmax(i):
                pb = Pbuf if i % 2 == 0 else PbufB
                so = 8 + (i % 2) * 8
                klen = (i + 1) * 128
                chunks = [(0, min(klen, 1024))]
                if klen > 1024:
                    chunks.append((1024, klen - 1024))
                nch = len(chunks)
                if K["zl_hist"][i % 2] is not None:
                    P.w("vector", "sACT", K["zl_hist"][i % 2])
                cinfo = []
                for ci, (c0, wdt) in enumerate(chunks):
                    pair = K["spair_i"] % 2
                    K["spair_i"] += 1
                    pbanks = ["A0", "A1"] if pair == 0 else ["A2", "A3"]
                    pb0 = pair * 1024
                    bank_wait_w("tensor", pbanks)
                    if i == 0 and ci == 0:
                        P.w("tensor", "sACT", qr)
                        P.w("tensor", "sACT", kr)
                        P.w("tensor", "sACT", vr)
                    pe = None
                    nmm = (wdt + 511) // 512
                    for mi in range(nmm):
                        mw = min(512, wdt - mi * 512)
                        pe = P.o("tensor", lambda t, ii=i, b0=pb0, m=mi, w=mw, cc0=c0: t.matmul(
                            psA[:, b0 + m * 512: b0 + m * 512 + w],
                            Qt[:, ii * 128:(ii + 1) * 128],
                            Kt[:, cc0 + m * 512: cc0 + m * 512 + w],
                            start=True, stop=True), "sPE" if mi == nmm - 1 else None)
                    bank_set_w(pbanks, pe)
                    P.w("vector", "sPE", pe)
                    if c0 <= i * 128 < c0 + wdt:
                        off = pb0 + i * 128 - c0
                        P.o("vector", lambda v, o=off: v.tensor_tensor(
                            psA[:, o:o + 128], psA[:, o:o + 128], MASK[:], ALU.add))
                    vv = P.o("vector", lambda v, b0=pb0, w=wdt, nm=so + ci: v.tensor_reduce(
                        STAT[:, nm:nm + 1], psA[:, b0:b0 + w], AX.X, ALU.max, negate=True),
                        "sDVE")
                    cinfo.append((c0, wdt, pb0, pbanks, vv))
                if nch == 2:
                    sep("vector", 2)
                    gdep = P.o("vector", lambda v, s_=so: v.tensor_tensor(
                        STAT[:, s_ + 2:s_ + 3], STAT[:, s_:s_ + 1],
                        STAT[:, s_ + 1:s_ + 2], ALU.min), "sDVE")
                    gcol = so + 2
                else:
                    gdep = cinfo[0][4]
                    gcol = so
                P.w("gpsimd", "sDVE", gdep)
                gb = P.o("gpsimd", lambda g, gc=gcol, bm=so + 3: g.tensor_single_scalar(
                    STAT[:, bm:bm + 1], STAT[:, gc:gc + 1], ISC, ALU.mult), "sGP")
                P.w("scalar", "sGP", gb)
                exl = None
                for ci, (c0, wdt, pb0, pbanks, vv) in enumerate(cinfo):
                    exl = P.o("scalar", lambda s, b0=pb0, w=wdt, cc0=c0, bm=so + 3,
                              lc=so + 4 + ci, pb_=pb:
                              s.activation(pb_[:, cc0:cc0 + w], psA[:, b0:b0 + w], AF.Exp,
                                           bias=STAT[:, bm:bm + 1], scale=ISC,
                                           accum_out=STAT[:, lc:lc + 1]), "sACT")
                    bank_set_r(pbanks, [("sACT", exl), ("sDVE", vv)])
                P.w("vector", "sACT", exl)
                if nch == 2:
                    P.o("vector", lambda v, s_=so: v.tensor_tensor(
                        STAT[:, s_ + 6:s_ + 7], STAT[:, s_ + 4:s_ + 5],
                        STAT[:, s_ + 5:s_ + 6], ALU.add))
                    sep("vector", 2)
                    rc = P.o("vector", lambda v, s_=so: v.reciprocal(
                        STAT[:, s_ + 7:s_ + 8], STAT[:, s_ + 6:s_ + 7]), "sDVE")
                else:
                    rc = P.o("vector", lambda v, s_=so: v.reciprocal(
                        STAT[:, s_ + 7:s_ + 8], STAT[:, s_ + 4:s_ + 5]), "sDVE")
                return exl, pb, so + 7, rc

            def emit_ztrans(zp):
                nonlocal zlast
                zi, ev1 = zp
                zlast = transpose_to(ZTOK[:, (zi % 2) * 128:(zi % 2) * 128 + 128],
                                     Ztb[:, zi * 128:(zi + 1) * 128],
                                     pe_dep=("sACT", ev1),
                                     act_dep=("qZ", K["zb_st"]) if zi == 0 else None)

            def emit_trans_av(i, exl, pb, rcol, rc):
                q4 = i % 4
                if K["zpend"] is not None:
                    emit_ztrans(K["zpend"])
                    K["zpend"] = None
                for j0 in range(0, i + 1, 4):
                    n = min(4, i + 1 - j0)
                    srcs = [pb[:, j * 128:(j + 1) * 128] for j in range(j0, j0 + n)]
                    pt = transpose_group(srcs, Ptb[:, j0 * 128:(j0 + n) * 128],
                                         pe_dep=("sACT", exl) if j0 == 0 else None)
                    if j0 == 0:
                        bank_wait_w("tensor", ["B0"])
                    P.w("tensor", "sACT", pt)
                    for j in range(j0, j0 + n):
                        P.o("tensor", lambda t, jj=j, q=q4, ii=i: t.matmul(
                            psB[:, q * 128: (q + 1) * 128],
                            Ptb[:, jj * 128:(jj + 1) * 128],
                            Vtok[:, jj * 128:(jj + 1) * 128],
                            start=(jj == 0), stop=(jj == ii)),
                            "sPE" if j == i else None)
                avdone = P.cnt["sPE"]
                bank_set_w(["B0"], avdone)
                P.w("scalar", "sPE", avdone)
                P.w("scalar", "sDVE", rc)
                ev1 = P.o("scalar", lambda s, ii=i, q=q4, rcc=rcol: s.activation(
                    ZTOK[:, (ii % 2) * 128:(ii % 2) * 128 + 128],
                    psB[:, q * 128: (q + 1) * 128], AF.Identity,
                    scale=STAT[:, rcc:rcc + 1]), "sACT")
                bank_set_r(["B0"], ("sACT", ev1))
                K["zl_hist"][i % 2] = ev1
                K["zpend"] = (i, ev1)

            pend = None
            K["zpend"] = None
            for i in range(NT):
                info = emit_scores_softmax(i)
                if pend is not None:
                    emit_trans_av(*pend)
                pend = (i,) + info
            emit_trans_av(*pend)
            emit_ztrans(K["zpend"])
            K["zpend"] = None
            if _DBG_HOOKS and h == 0:
                P.w("sync", "sDVE", P.cnt["sDVE"])
                P.d("sync", dbgp, Pbuf[:], "qSt")
            P.w("sync", "sACT", zlast)
            K["zb_st"] = P.d("sync", zbank[h * 128:(h + 1) * 128, :], Ztb[:], "qZ")
            for _ in range(pp_iter.get("per_head", 0)):
                if pp_iter["units"]:
                    pp_iter["emit"](pp_iter["units"].pop(0))
                elif pp_iter.get("next"):
                    nxt = pp_iter.pop("next")
                    nxt[0]()
                    pp_iter["units"], pp_iter["emit"] = nxt[1]
        if pp_iter.get("next"):
            nxt = pp_iter.pop("next")
            nxt[0]()
            pp_iter["units"], pp_iter["emit"] = nxt[1]
        while pp_iter["units"]:
            pp_iter["emit"](pp_iter["units"].pop(0))

        if _DBG_NOOUT:
            return
        obank = ["A0", "A1", "A2", "A3"]
        yq_rdy = [[None] * 4 for _ in range(4)]
        yto_g = [None] * 4
        wls = {}
        zls = {}
        def load_woq(g, dq):
            wpar = dq % 2
            P.wd("gpsimd", woq_guard[wpar])
            sap = DT["wo_a"].rearrange("(hh p) d -> p hh d", p=128)[:, :, dq * 512:(dq + 1) * 512]
            dap = WOQ[wpar][:].rearrange("p (hh d) -> p hh d", d=512)
            qg = "qG0" if wpar == 0 else "qG1"
            wls[(g, dq)] = (qg, P.d("gpsimd", dap, sap, qg))
        def load_zsl(g, dq, ti):
            t = g * 4 + ti
            zpar = t % 2
            P.wd("sync", zsl_guard[zpar])
            P.w("sync", "qZ", K["zb_st"])
            zap = zbank.rearrange("(hh p) t2 -> p hh t2", p=128)[:, :, t * 128:(t + 1) * 128]
            ql = "qL0" if zpar == 0 else "qL1"
            zls[(g, dq, ti)] = (ql, P.d("sync", ZSL[zpar][:].rearrange(
                "p (hh q) -> p hh q", q=128), zap, ql))
        steps = [(g, dq) for g in range(4) for dq in range(4)]
        load_woq(*steps[0])
        load_zsl(*steps[0], 0)
        zseq = [(g, dq, ti) for (g, dq) in steps for ti in range(4)]
        zi_next = 1
        for si, (g, dq) in enumerate(steps):
            if si + 1 < len(steps):
                load_woq(*steps[si + 1])
            for ti in range(4):
                t = g * 4 + ti
                while zi_next < len(zseq) and zi_next <= si * 4 + ti + 1:
                    load_zsl(*zseq[zi_next]); zi_next += 1
                qg, wl = wls[(g, dq)]
                ql, zl = zls[(g, dq, ti)]
                zpar = t % 2
                wpar = dq % 2
                bk = obank[ti]
                bank_wait_w("tensor", [bk])
                P.w("tensor", qg, wl)
                P.w("tensor", ql, zl)
                pe = None
                for h2 in range(16):
                    pe = P.o("tensor", lambda t_, hh=h2, zp=zpar, wp=wpar, b=bk: t_.matmul(
                        bank_ap(b), ZSL[zp][:, hh * 128:(hh + 1) * 128],
                        WOQ[wp][:, hh * 512:(hh + 1) * 512],
                        start=(hh == 0), stop=(hh == 15)), "sPE" if h2 == 15 else None)
                bank_set_w([bk], pe)
                zsl_guard[zpar] = ("sPE", pe)
                woq_guard[wpar] = ("sPE", pe)
                P.w("scalar", "sPE", pe)
                P.wd("scalar", yto_g[ti])
                ac = P.o("scalar", lambda s, ti_=ti, dq_=dq, b=bk: s.copy(
                    YTO[ti_][:, dq_ * 512:(dq_ + 1) * 512], bank_ap(b)), "sACT")
                bank_set_r([bk], ("sACT", ac))
                yq_rdy[ti][dq] = ac
            if dq == 3:
                for ti in range(4):
                    t = g * 4 + ti
                    par = t % 2
                    P.wd("sync", K["xo_g"][par])
                    qx = "qX0" if par == 0 else "qX1"
                    xl = P.d("sync", XO[par][:], DT["x0"][t * 128:(t + 1) * 128, :], qx)
                    P.w("vector", "sACT", yq_rdy[ti][3])
                    P.w("vector", "qC", bol)
                    bv = P.o("vector", lambda v, ti_=ti: v.tensor_tensor(
                        YTO[ti_][:], YTO[ti_][:], bObc[:], ALU.add), "sDVE")
                    dsta = (xr if stage > 1 else y)
                    _, rdr = finish_tile(t, YTO[ti], [("sDVE", bv), (qx, xl)], XO[par],
                                         dsta[t * 128:(t + 1) * 128, :],
                                         "R" if stage > 1 else None, YTE)
                    K["xo_g"][par] = rdr
                    yto_g[ti] = ("sDVE", P.cnt["sDVE"])

    def ffn(idx, xsrc, xdst, prep_next):
        P.w("sync", "qT", P.cnt["qT"])
        init_dep = [("sDVE", P.cnt["sDVE"]), ("sPE", P.cnt["sPE"]),
                    ("sACT", P.cnt["sACT"])]
        tgu_guard = [list(init_dep), list(init_dep)]
        tdq_guard = [list(init_dep), list(init_dep)]
        ytf_guard = [None] * 4
        gscr_g = None
        last_ga = None
        shd_g = [None] * 4
        for S in range(4):
            sg_rdy = None
            for tloc in range(4):
                t = S * 4 + tloc
                P.wd("vector", K["sx_rdy"][t])
                P.wd("vector", K["thr_rdy"])
                if last_ga is not None and tloc == 0:
                    P.w("vector", "sACT", last_ga)
                P.o("vector", lambda v, tl=tloc, tt=t: v.tensor_tensor(
                    SGU[:, tl:tl + 1], SX[:, tt:tt + 1], WSC[:, 0:1], ALU.mult))
                sg_rdy = P.o("vector", lambda v, tl=tloc, tt=t: v.tensor_tensor(
                    SGU[:, 4 + tl:5 + tl], SX[:, tt:tt + 1], WSC[:, 1:2], ALU.mult), "sDVE")
            hslot_wr = [[] for _ in range(4)]
            def load_tgu(fc):
                par = fc % 2
                ql = "qL0" if par == 0 else "qL1"
                P.wd("sync", tgu_guard[par])
                gdst = TGU[par][:].rearrange("p (c f) -> p c f", f=512)
                gap = tg.rearrange("(c p) f -> p c f", p=128)[:, :, fc * 256:(fc + 1) * 256]
                uap = tu.rearrange("(c p) f -> p c f", p=128)[:, :, fc * 256:(fc + 1) * 256]
                P.d("sync", gdst[:, :, 0:256], gap, ql)
                return ql, P.d("sync", gdst[:, :, 256:512], uap, ql)
            tgul = {0: load_tgu(0)}
            for fc in range(32):
                par = fc % 2
                if fc + 1 < 32:
                    tgul[fc + 1] = load_tgu(fc + 1)
                ql, l2 = tgul.pop(fc)
                for tloc in range(4):
                    t = S * 4 + tloc
                    bk = "B0" if tloc % 2 == 0 else "B1"
                    bank_wait_w("tensor", [bk])
                    if tloc == 0:
                        P.w("tensor", ql, l2)
                    if fc == 0:
                        P.wd("tensor", K["a_rdy"][t])
                    pe = None
                    for c in range(16):
                        pe = P.o("tensor", lambda t_, cc=c, tt=t, p_=par, b=bk: t_.matmul(
                            bank_ap(b), A[:, cc * 2048 + tt * 128: cc * 2048 + (tt + 1) * 128],
                            TGU[p_][:, cc * 512:(cc + 1) * 512],
                            start=(cc == 0), stop=(cc == 15)), "sPE" if c == 15 else None)
                    bank_set_w([bk], pe)
                    tgu_guard[par] = ("sPE", pe)
                    P.w("scalar", "sPE", pe)
                    if sg_rdy:
                        P.w("scalar", "sDVE", sg_rdy)
                    P.wd("scalar", gscr_g)
                    ga = P.o("scalar", lambda s, b=bk, tl=tloc: s.activation(
                        GSCR[:, 0:256], bank_ap(b)[:, 0:256], AF.Sigmoid,
                        scale=SGU[:, tl:tl + 1]), "sACT")
                    last_ga = ga
                    P.w("vector", "sACT", ga)
                    hv = P.o("vector", lambda v, b=bk, tl=tloc, f=fc: v.scalar_tensor_tensor(
                        HSL[tl][:, f * 256:(f + 1) * 256], bank_ap(b)[:, 256:512],
                        SGU[:, 4 + tl:5 + tl], GSCR[:, 0:256], ALU.mult, ALU.mult), "sDVE")
                    hm = P.o("vector", lambda v, tl=tloc, f=fc: v.tensor_reduce(
                        HMXW[:, tl * 32 + f: tl * 32 + f + 1],
                        HSL[tl][:, f * 256:(f + 1) * 256], AX.X,
                        ALU.max, apply_absolute_value=True), "sDVE")
                    gscr_g = ("sDVE", hv)
                    bank_set_r([bk], [("sACT", ga), ("sDVE", hv)])
                    hslot_wr[tloc].append(("sDVE", hm))
            rh_rdy = [None] * 4
            last_ra = None
            for tloc in range(4):
                if hslot_wr[tloc]:
                    P.wd("vector", hslot_wr[tloc][-1])
                if last_ra is not None:
                    P.w("vector", "sACT", last_ra)
                P.o("vector", lambda v, tl=tloc: v.tensor_reduce(
                    STAT[:, 9:10], HMXW[:, tl * 32:(tl + 1) * 32], AX.X, ALU.max))
                sep("vector", 2)
                P.o("vector", lambda v: v.tensor_single_scalar(
                    STAT[:, 27:28], STAT[:, 9:10], QEPS, ALU.max))
                sep("vector", 2)
                rc = P.o("vector", lambda v: v.reciprocal(STAT[:, 26:27], STAT[:, 27:28]),
                         "sDVE")
                P.w("gpsimd", "sDVE", rc)
                v5 = P.o("gpsimd", lambda g: g.tensor_single_scalar(
                    STAT[:, 28:29], STAT[:, 26:27], 127.0, ALU.mult), "sGP")
                P.w("scalar", "sGP", v5)
                sh1 = P.o("scalar", lambda s: s.activation(STAT[:, 29:30], STAT[:, 27:28],
                                                           AF.Copy, scale=1.0 / 127.0), "sACT")
                P.w("vector", "sACT", sh1)
                P.wd("vector", shd_g[tloc])
                shv = P.o("vector", lambda v, tl=tloc: v.tensor_tensor(
                    SHD[:, tl:tl + 1], STAT[:, 29:30], WSC[:, 2:3], ALU.mult), "sDVE")
                shd_w = shv
                last = None
                for fc2 in range(8):
                    P.o("scalar", lambda s, tl=tloc, f2=fc2: s.activation(
                        I32S[:], HSL[tl][:, f2 * 1024:(f2 + 1) * 1024], AF.Copy,
                        scale=STAT[:, 28:29]))
                    ra = P.o("scalar", lambda s: s.activation(RHC[:], I32S[:], AF.Copy),
                             "sACT")
                    last_ra = ra
                    for j0 in range(0, 8, 4):
                        f0 = fc2 * 8 + j0
                        srcs = [RHC[:, j * 128:(j + 1) * 128] for j in range(j0, j0 + 4)]
                        last = transpose_group(srcs,
                                               HSL[tloc][:, f0 * 128:(f0 + 4) * 128],
                                               pe_dep=("sACT", ra) if j0 == 0 else None)
                rh_rdy[tloc] = ("sACT", last)
            shd_rdy = P.cnt["sDVE"]
            dn_pe = [[None] * 4 for _ in range(4)]
            dbank = lambda tl, dc: ["A0", "A1", "A2", "A3"][tl]
            def load_tdq(dc, fq):
                qpar = fq % 2
                qd = "qL0" if qpar == 0 else "qL1"
                P.wd("sync", tdq_guard[qpar])
                tap = td[fq * 512:(fq + 1) * 512, dc * 512:(dc + 1) * 512]
                tap = tap.rearrange("(q p) d -> p q d", p=128)
                return qd, P.d("sync", TDQ[qpar][:].rearrange("p (q d) -> p q d", d=512),
                               tap, qd)
            dsteps = [(dc, fq) for dc in range(4) for fq in range(16)]
            tdql = {dsteps[0]: load_tdq(*dsteps[0])}
            for di, (dc, fq) in enumerate(dsteps):
                if di + 1 < len(dsteps):
                    tdql[dsteps[di + 1]] = load_tdq(*dsteps[di + 1])
                qd, tl_ = tdql.pop((dc, fq))
                if True:
                    qpar = fq % 2
                    for q4 in range(4):
                        f = fq * 4 + q4
                        for tloc in range(4):
                            bk = dbank(tloc, dc)
                            if f == 0:
                                bank_wait_w("tensor", [bk])
                                P.wd("tensor", rh_rdy[tloc])
                            if q4 == 0 and tloc == 0:
                                P.w("tensor", qd, tl_)
                            pe = P.o("tensor", lambda t_, ff=f, tl=tloc, qp=qpar, qq=q4, b=bk:
                                     t_.matmul(bank_ap(b),
                                               HSL[tl][:, ff * 128:(ff + 1) * 128],
                                               TDQ[qp][:, qq * 512:(qq + 1) * 512],
                                               start=(ff == 0), stop=(ff == 63)),
                                     "sPE" if (f == 63 or (q4 == 3 and tloc == 3))
                                     else None)
                            if f == 63:
                                bank_set_w([bk], pe)
                                dn_pe[tloc][dc] = pe
                    tdq_guard[qpar] = ("sPE", P.cnt["sPE"])
                if fq != 15:
                    continue
                for tloc in range(4):
                    bk = dbank(tloc, dc)
                    P.w("scalar", "sPE", dn_pe[tloc][dc])
                    P.w("scalar", "sDVE", shd_rdy)
                    P.wd("scalar", ytf_guard[tloc])
                    ac = P.o("scalar", lambda s, tl=tloc, d_=dc, b=bk: s.activation(
                        YTF[tl][:, d_ * 512:(d_ + 1) * 512], bank_ap(b), AF.Copy,
                        scale=SHD[:, tl:tl + 1]), "sACT")
                    bank_set_r([bk], ("sACT", ac))
                    shd_g[tloc] = ("sACT", ac)
                    dn_pe[tloc][dc] = ac
            for tloc in range(4):
                t = S * 4 + tloc
                par = t % 2
                P.wd("sync", K["xo_g"][par])
                qx = "qX0" if par == 0 else "qX1"
                xl = P.d("sync", XO[par][:], xsrc[t * 128:(t + 1) * 128, :], qx)
                ydep = [("sACT", dn_pe[tloc][3]), (qx, xl),
                        ("sACT", dn_pe[tloc][0]), ("sACT", dn_pe[tloc][1]),
                        ("sACT", dn_pe[tloc][2])]
                st, rdr = finish_tile(t, YTF[tloc], ydep, XO[par],
                                      xdst[t * 128:(t + 1) * 128, :], prep_next, YTE)
                K["xo_g"][par] = rdr
                ytf_guard[tloc] = [("sDVE", P.cnt["sDVE"])]
            tgu_guard[0] = [("sDVE", P.cnt["sDVE"])]
            tgu_guard[1] = [("sDVE", P.cnt["sDVE"])]

    def attn_b(xsrc, xdst, prep_next, nh=None, noout=False):
        bl = P.d("sync", BETA[:], DT["beta_in"], "qB")
        P.w("scalar", "qB", bl)
        brdy = P.o("scalar", lambda s: s.activation(BETA[:], BETA[:], AF.Sigmoid), "sACT")
        wq_guard = [("sDVE", P.cnt["sDVE"]), ("sPE", P.cnt["sPE"]),
                    ("sACT", P.cnt["sACT"])]
        woq_guard = [None, None]
        zsl_guard = [None, None]
        for h in range(nh if nh is not None else H):
            wdep = load_head_weights(h, (DT["wq_b"], DT["wk_b"], DT["wv_b"]),
                                     (Wqh, Wkh, Wvh), wq_guard)
            qr, sq_rdy = proj4(Wqh, Qt, None, wdep,
                               sig_out=None if _DBG_NOSIG else SqT)
            kr, sk_rdy = proj4(Wkh, Kt, None, wdep,
                               sig_out=None if _DBG_NOSIG else SkT, zinc=True)
            vr = vproj(Wvh, wdep, None)
            wq_guard = ("sPE", P.cnt["sPE"])
            P.o("vector", lambda v: v.memset(MEMF[:], 0.0))
            P.o("vector", lambda v: v.memset(STAT[:, 30:31], 1.0 / 128.0))
            P.o("vector", lambda v: v.memset(MEMZ[:, 0:128], 0.0))
            mz = P.o("vector", lambda v: v.memset(MEMZ[:, 128:129], 1.0 / 128.0), "sDVE")
            mz_rdy = ("sDVE", mz)
            atlast = None
            for s_ in range(_DBG_NSEG if _DBG_NSEG is not None else 4):
                c0 = s_ * 512
                for qi in range(4):
                    qq = K["spair_i"] % 4
                    K["spair_i"] += 1
                    bk = ["A0", "A1", "A2", "A3"][qq]
                    bank_wait_w("tensor", [bk])
                    if s_ == 0 and qi == 0:
                        P.w("tensor", "sACT", qr)
                        P.w("tensor", "sACT", kr)
                        P.w("tensor", "sACT", vr)
                        P.wd("tensor", [("sDVE", sq_rdy), ("sDVE", sk_rdy)])
                    pe = P.o("tensor", lambda t_, ii=qi, cc=c0, b=bk: t_.matmul(
                        bank_ap(b)[:, 0:(ii + 1) * 128], Qt[:, cc + ii * 128: cc + (ii + 1) * 128],
                        Kt[:, cc: cc + (ii + 1) * 128], start=True, stop=True), "sPE")
                    bank_set_w([bk], pe)
                    P.w("vector", "sPE", pe)
                    P.o("vector", lambda v, ii=qi, b=bk: v.tensor_tensor(
                        bank_ap(b)[:, ii * 128:(ii + 1) * 128],
                        bank_ap(b)[:, ii * 128:(ii + 1) * 128], MASK[:], ALU.add))
                    nv = P.o("vector", lambda v, ii=qi, b=bk: v.tensor_reduce(
                        STAT[:, 8:9], bank_ap(b)[:, 0:(ii + 1) * 128], AX.X, ALU.max,
                        negate=True), "sDVE")
                    P.w("gpsimd", "sDVE", nv)
                    gb = P.o("gpsimd", lambda g: g.tensor_single_scalar(
                        STAT[:, 10:11], STAT[:, 8:9], ISC, ALU.mult), "sGP")
                    P.w("scalar", "sGP", gb)
                    ex = P.o("scalar", lambda s, ii=qi, b=bk: s.activation(
                        PbB[:, 0:(ii + 1) * 128], bank_ap(b)[:, 0:(ii + 1) * 128], AF.Exp,
                        bias=STAT[:, 10:11], scale=ISC, accum_out=STAT[:, 12:13]), "sACT")
                    bank_set_r([bk], ("sACT", ex))
                    P.w("vector", "sACT", ex)
                    P.o("vector", lambda v: v.reciprocal(STAT[:, 14:15], STAT[:, 12:13]))
                    sep("vector", 2)
                    pn = P.o("vector", lambda v, ii=qi: v.tensor_scalar(
                        PbB[:, 0:(ii + 1) * 128], PbB[:, 0:(ii + 1) * 128], STAT[:, 14:15],
                        None, ALU.mult), "sDVE")
                    adq = psB[:, qq * 128:(qq + 1) * 128]
                    addone = None
                    for j in range(qi + 1):
                        pt = transpose_to(PbB[:, j * 128:(j + 1) * 128],
                                          PtB[:, j * 128:(j + 1) * 128],
                                          pe_dep=("sDVE", pn) if j == 0 else None)
                        if j == 0:
                            bank_wait_w("tensor", ["B0"])
                        P.w("tensor", "sACT", pt)
                        addone = P.o("tensor", lambda t_, jj=j, ss=s_, ii=qi, a=adq: t_.matmul(
                            a, PtB[:, jj * 128:(jj + 1) * 128],
                            Vtok[:, (ss * 4 + jj) * 128:(ss * 4 + jj + 1) * 128],
                            start=(jj == 0), stop=(jj == ii)), "sPE" if j == qi else None)
                    bank_set_w(["B0"], addone)
                    amb = psB[:, 512:641]
                    bank_wait_w("tensor", ["B1"])
                    P.wd("tensor", mz_rdy)
                    am = P.o("tensor", lambda t_, cc=c0, ii=qi, a=amb: t_.matmul(
                        a, SqT[:, cc + ii * 128: cc + (ii + 1) * 128], MEMZ[:, 0:129],
                        start=True, stop=True), "sPE")
                    bank_set_w(["B1"], am)
                    P.w("vector", "sPE", am)
                    rd = P.o("vector", lambda v: v.reciprocal(STAT[:, 31:32], psB[:, 640:641]),
                             "sDVE")
                    P.w("scalar", "sDVE", rd)
                    amv0 = P.o("scalar", lambda s: s.activation(
                        CMB[:, 0:128], psB[:, 512:640], AF.Identity, scale=STAT[:, 31:32]),
                        "sACT")
                    P.w("vector", "sACT", amv0)
                    amv = P.o("vector", lambda v, a=adq: v.tensor_tensor(
                        CMB[:, 128:256], CMB[:, 0:128], a, ALU.subtract), "sDVE")
                    bank_set_r(["B1"], [("sACT", amv0), ("sDVE", rd)])
                    P.w("gpsimd", "sACT", brdy)
                    P.w("gpsimd", "sDVE", amv)
                    gu2 = P.o("gpsimd", lambda g, hh=h: g.tensor_tensor(
                        CMB[:, 0:128], BETA[:, hh * 128:(hh + 1) * 128], CMB[:, 128:256],
                        ALU.mult), "sGP")
                    P.w("vector", "sGP", gu2)
                    if atlast is not None:
                        P.w("vector", "sACT", atlast)
                    at = P.o("vector", lambda v, a=adq: v.tensor_tensor(
                        XNB[:, 0:128], CMB[:, 0:128], a, ALU.add), "sDVE")
                    bank_set_r(["B0"], ("sDVE", at))
                    atlast = transpose_to(XNB[:, 0:128],
                                          ATTH[:, c0 + qi * 128: c0 + (qi + 1) * 128],
                                          pe_dep=("sDVE", at),
                                          act_dep=(("qZ", K["ab_st"])
                                                   if (s_ == 0 and qi == 0) else None))
                up = psB[:, 0:128]
                upfin = None
                prev_up = None
                for st_ in range(4):
                    smb = psB[:, 512:641]
                    bank_wait_w("tensor", ["B1"])
                    P.wd("tensor", mz_rdy)
                    sm = P.o("tensor", lambda t_, cc=c0, ii=st_, a=smb: t_.matmul(
                        a, SkT[:, cc + ii * 128: cc + (ii + 1) * 128], MEMZ[:, 0:129],
                        start=True, stop=True), "sPE")
                    bank_set_w(["B1"], sm)
                    P.w("vector", "sPE", sm)
                    rk = P.o("vector", lambda v: v.reciprocal(STAT[:, 31:32], psB[:, 640:641]),
                             "sDVE")
                    P.w("scalar", "sDVE", rk)
                    m1 = P.o("scalar", lambda s: s.activation(
                        SS0[:, 0:128], psB[:, 512:640], AF.Identity, scale=STAT[:, 31:32]),
                        "sACT")
                    P.w("vector", "sACT", m1)
                    if prev_up is not None:
                        P.w("vector", "sPE", prev_up)
                    ng = P.o("vector", lambda v, ss=s_, ii=st_: v.tensor_tensor(
                        NGD[:], SS0[:, 0:128],
                        Vtok[:, (ss * 4 + ii) * 128:(ss * 4 + ii + 1) * 128],
                        ALU.subtract), "sDVE")
                    bank_set_r(["B1"], [("sACT", m1), ("sDVE", rk)])
                    stk = transpose_to(SkT[:, c0 + st_ * 128: c0 + (st_ + 1) * 128],
                                       SKTOK[:], pe_dep=None)
                    if st_ == 0:
                        bank_wait_w("tensor", ["B0"])
                    P.w("tensor", "sACT", stk)
                    P.w("tensor", "sDVE", ng)
                    upfin = P.o("tensor", lambda t_, ii=st_, a=up: t_.matmul(
                        a, SKTOK[:], NGD[:], start=(ii == 0), stop=(ii == 3)), "sPE")
                    prev_up = upfin
                bank_set_w(["B0"], upfin)
                P.w("vector", "sPE", upfin)
                P.o("vector", lambda v, a=up: v.tensor_tensor(MEMF[:], MEMF[:], a,
                                                              ALU.subtract))
                zv = P.o("vector", lambda v, ss=s_: v.tensor_tensor(
                    STAT[:, 30:31], STAT[:, 30:31], ZINC[:, ss:ss + 1], ALU.add), "sDVE")
                bank_set_r(["B0"], ("sDVE", zv))
                P.w("scalar", "sDVE", zv)
                P.o("scalar", lambda s: s.copy(MEMZ[:, 0:128], MEMF[:]))
                mzv = P.o("scalar", lambda s: s.copy(MEMZ[:, 128:129], STAT[:, 30:31]),
                          "sACT")
                mz_rdy = ("sACT", mzv)
            P.w("sync", "sACT", atlast)
            K["ab_st"] = P.d("sync", abank[h * 128:(h + 1) * 128, :], ATTH[:], "qZ")
        if noout:
            return
        obank = ["A0", "A1", "A2", "A3"]
        yq_rdy = [[None] * 4 for _ in range(4)]
        yto_g = [None] * 4
        wls = {}
        zls = {}
        def load_woq(g, dq):
            wpar = dq % 2
            P.wd("gpsimd", woq_guard[wpar])
            sap = DT["wo_b"].rearrange("(hh p) d -> p hh d", p=128)[:, :, dq * 512:(dq + 1) * 512]
            dap = WOQ[wpar][:].rearrange("p (hh d) -> p hh d", d=512)
            qg = "qG0" if wpar == 0 else "qG1"
            wls[(g, dq)] = (qg, P.d("gpsimd", dap, sap, qg))
        def load_zsl(g, dq, ti):
            t = g * 4 + ti
            zpar = t % 2
            P.wd("sync", zsl_guard[zpar])
            P.w("sync", "qZ", K["ab_st"])
            h0 = (t % 4) * 4
            seg = t // 4
            aap = abank.rearrange("(hh p) t2 -> p hh t2", p=128)[
                :, h0:h0 + 4, seg * 512:(seg + 1) * 512]
            ql = "qL0" if zpar == 0 else "qL1"
            zls[(g, dq, ti)] = (ql, P.d("sync", ZSL[zpar][:].rearrange(
                "p (a q) -> p a q", q=512), aap, ql))
        steps = [(g, dq) for g in range(4) for dq in range(4)]
        load_woq(*steps[0])
        load_zsl(*steps[0], 0)
        zseq = [(g, dq, ti) for (g, dq) in steps for ti in range(4)]
        zi_next = 1
        for si, (g, dq) in enumerate(steps):
            if si + 1 < len(steps):
                load_woq(*steps[si + 1])
            for ti in range(4):
                t = g * 4 + ti
                while zi_next < len(zseq) and zi_next <= si * 4 + ti + 1:
                    load_zsl(*zseq[zi_next]); zi_next += 1
                qg, wl = wls[(g, dq)]
                ql, zl = zls[(g, dq, ti)]
                zpar = t % 2
                wpar = dq % 2
                bk = obank[ti]
                bank_wait_w("tensor", [bk])
                P.w("tensor", qg, wl)
                P.w("tensor", ql, zl)
                pe = None
                gat = ZSL[zpar][:].rearrange("p (a b c) -> p a b c", a=4, b=32, c=16)
                for h2 in range(16):
                    pe = P.o("tensor", lambda t_, hh=h2, wp=wpar, b=bk, gg=gat: t_.matmul(
                        bank_ap(b), gg[:, :, :, hh],
                        WOQ[wp][:, hh * 512:(hh + 1) * 512],
                        start=(hh == 0), stop=(hh == 15)), "sPE" if h2 == 15 else None)
                bank_set_w([bk], pe)
                zsl_guard[zpar] = ("sPE", pe)
                woq_guard[wpar] = ("sPE", pe)
                P.w("scalar", "sPE", pe)
                P.wd("scalar", yto_g[ti])
                ac = P.o("scalar", lambda s, ti_=ti, dq_=dq, b=bk: s.copy(
                    YTO[ti_][:, dq_ * 512:(dq_ + 1) * 512], bank_ap(b)), "sACT")
                bank_set_r([bk], ("sACT", ac))
                yq_rdy[ti][dq] = ac
            if dq == 3:
                for ti in range(4):
                    t = g * 4 + ti
                    par = t % 2
                    P.wd("sync", K["xo_g"][par])
                    qx = "qX0" if par == 0 else "qX1"
                    xl = P.d("sync", XO[par][:], xsrc[t * 128:(t + 1) * 128, :], qx)
                    ydep = [("sACT", yq_rdy[ti][3]), (qx, xl),
                            ("sACT", yq_rdy[ti][0]), ("sACT", yq_rdy[ti][1]),
                            ("sACT", yq_rdy[ti][2])]
                    _, rdr = finish_tile(t, YTO[ti], ydep, XO[par],
                                         xdst[t * 128:(t + 1) * 128, :], prep_next, YTE)
                    K["xo_g"][par] = rdr
                    yto_g[ti] = ("sDVE", P.cnt["sDVE"])

    prologue()
    mu, me = mean_units()
    tu_, te = tern_units()
    if stage > 1:
        pp = {"units": mu, "emit": me, "per_head": 26,
              "next": (wscale_finalize, (tu_, te))}
    else:
        pp = {"units": [], "emit": me, "per_head": 0}
    attn_a(pp)
    if stage > 1:
        ffn(0, xr, xr if stage > 2 else y,
            ("xnT" if stage > 2 else None))
    if stage > 2:
        attn_b(xr, xr if stage > 3 else y, ("R" if stage > 3 else None),
               nh=_DBG_NH_B, noout=_DBG_NOOUT_B)
    if stage > 3:
        ffn(1, xr, y, None)
    for s in ("qSt", "qZ", "qT", "qC", "qB", "qX0", "qX1", "qL0", "qL1", "qP",
              "qG0", "qG1"):
        P.w("sync", s, P.cnt[s])

    print(f"[kernel build] planned instructions ~{P.ni}, sems {P.cnt}", file=sys.stderr)
    _emit(P, nc, sems)
    pctx.close()
    sctx.close()
    return nc


def _host_inputs(inputs, b):
    x = np.ascontiguousarray(inputs["L0_attn_a_x"][b])
    wq_a = np.ascontiguousarray(inputs["L0_attn_a_W_Q"].transpose(1, 0, 2).reshape(D, D))
    wk_a = np.ascontiguousarray(inputs["L0_attn_a_W_K"].transpose(1, 0, 2).reshape(D, D))
    wv_a = np.ascontiguousarray(inputs["L0_attn_a_W_V"].transpose(1, 0, 2).reshape(D, D))
    wo_a = np.ascontiguousarray(inputs["L0_attn_a_W_O"].reshape(D, D))
    bq = np.ascontiguousarray(inputs["L0_attn_a_b_Q"].T)
    bk = np.ascontiguousarray(inputs["L0_attn_a_b_K"].T)
    bv = np.ascontiguousarray(inputs["L0_attn_a_b_V"].T)
    bo = np.ascontiguousarray(np.broadcast_to(inputs["L0_attn_a_b_O"].reshape(1, D),
                                              (128, D)))
    beta = np.ascontiguousarray(np.broadcast_to(
        inputs["L0_attn_b_betas"].reshape(1, D), (128, D)))
    wgT = np.ascontiguousarray(inputs["L0_ffn_w_gate"].T)
    wuT = np.ascontiguousarray(inputs["L0_ffn_w_up"].T)
    wdT = np.ascontiguousarray(inputs["L0_ffn_w_down"].T)
    ident = np.eye(128, dtype=BF)
    mask = np.where(np.tril(np.ones((128, 128), bool)), np.float32(0.0),
                    np.float32(NEGBIG)).astype(np.float32)
    ones = np.ones((128, 128), np.float32)
    return {
        "x0": x, "wq_a": wq_a, "wk_a": wk_a, "wv_a": wv_a, "wo_a": wo_a,
        "bq_in": bq, "bk_in": bk, "bv_in": bv, "bo_bc": bo,
        "wq_b": np.ascontiguousarray(inputs["L0_attn_b_Wq"]),
        "wk_b": np.ascontiguousarray(inputs["L0_attn_b_Wk"]),
        "wv_b": np.ascontiguousarray(inputs["L0_attn_b_Wv"]),
        "wo_b": np.ascontiguousarray(inputs["L0_attn_b_Wo"]),
        "beta_in": beta, "wgT": wgT, "wuT": wuT, "wdT": wdT,
        "ident_in": ident, "mask_in": mask, "ones_in": ones,
    }


def kernel(**inputs):
    nc = build_program(_STAGE)
    in_maps = [_host_inputs(inputs, b) for b in range(8)]
    res = run_bass_kernel_spmd(nc, in_maps, core_ids=list(range(8)))
    out = np.stack([np.asarray(r["y"]) for r in res.results], axis=0)
    return out.astype(np.float32)


# version 08
# speedup vs baseline: 1.6372x; 1.1302x over previous
import sys
import numpy as np
import ml_dtypes
from contextlib import ExitStack
import concourse.bass as bass
import concourse.mybir as mybir
from concourse.bass_utils import run_bass_kernel_spmd

F32 = mybir.dt.float32
BF16 = mybir.dt.bfloat16
I32 = mybir.dt.int32
U8 = mybir.dt.uint8
AF = mybir.ActivationFunctionType
ALU = mybir.AluOpType
AX = mybir.AxisListType
BF = ml_dtypes.bfloat16

D = 2048
T = 2048
H = 16
FF = 8192
NT = 16
ISC = float(1.0 / np.sqrt(np.float64(128.0)))
NEGBIG = -1.0e9
LNEPS = 1e-5
QEPS = 1e-5
WDEN = float(D) * float(FF)
KB = 1024

_STAGE = 4
_DBG_ZB = False
_DBG_HOOKS = False
_DBG_NH = None
_DBG_NOOUT = False
_DBG_NH_B = None
_DBG_NOOUT_B = False
_DBG_NSEG = None
_DBG_NOSIG = False
_DBG_RACE = False


class PL:
    ENGS = ("sync", "scalar", "vector", "tensor", "gpsimd")
    SEMS = ("qC", "qB", "qX0", "qX1", "qL0", "qL1", "qP", "qT", "qZ", "qSt",
            "qG0", "qG1", "sPE", "sACT", "sDVE", "sGP")

    def __init__(self):
        self.ops = {e: [] for e in self.ENGS}
        self.cnt = {s: 0 for s in self.SEMS}
        self.ni = 0

    def w(self, eng, sem, val):
        if val is None or val <= 0:
            return
        assert val <= self.cnt[sem], (eng, sem, val, self.cnt[sem])
        self.ops[eng].append(("w", sem, val))

    def wd(self, eng, dep):
        if dep is None:
            return
        if isinstance(dep, tuple):
            dep = [dep]
        for s, v in dep:
            self.w(eng, s, v)

    def o(self, eng, fn, sem=None):
        v = None
        if sem is not None:
            self.cnt[sem] += 1
            v = self.cnt[sem]
        self.ops[eng].append(("o", fn, sem))
        self.ni += 1
        return v

    def d(self, eng, out, in_, sem):
        self.cnt[sem] += 16
        v = self.cnt[sem]
        self.ops[eng].append(("d", out, in_, sem))
        self.ni += 1
        return v


def _emit(P, nc, sems):
    with nc.Block() as block:
        def runner(name):
            def run(eng):
                for op in P.ops[name]:
                    if op[0] == "w":
                        eng.wait_ge(sems[op[1]], op[2])
                    elif op[0] == "d":
                        eng.dma_start(out=op[1], in_=op[2]).then_inc(sems[op[3]], 16)
                    else:
                        ins = op[1](eng)
                        if op[2] is not None:
                            ins.then_inc(sems[op[2]], 1)
            return run
        block.sync(runner("sync"))
        block.scalar(runner("scalar"))
        block.vector(runner("vector"))
        block.tensor(runner("tensor"))
        block.gpsimd(runner("gpsimd"))


def build_program(stage=_STAGE):
    nc = bass.Bass("TRN2", target_bir_lowering=False, detect_race_conditions=_DBG_RACE)

    DT = {}
    def din(name, shape, dt=F32):
        DT[name] = nc.dram_tensor(name, shape, dt, kind="ExternalInput").ap()
    din("x0", [T, D]); din("wq_a", [D, D]); din("wk_a", [D, D]); din("wv_a", [D, D])
    din("wo_a", [D, D]); din("bq_in", [128, H]); din("bk_in", [128, H])
    din("bv_in", [128, H]); din("bo_bc", [128, D])
    din("wq_b", [D, D]); din("wk_b", [D, D]); din("wv_b", [D, D]); din("wo_b", [D, D])
    din("beta_in", [128, D])
    din("wgT", [D, FF]); din("wuT", [D, FF]); din("wdT", [FF, D])
    din("ident_in", [128, 128], BF16); din("mask_in", [128, 128]); din("ones_in", [128, 128])
    y = nc.dram_tensor("y", [T, D], F32, kind="ExternalOutput").ap()
    xr = nc.dram_tensor("xr", [T, D], F32, kind="Internal").ap()
    zkind = "ExternalOutput" if _DBG_ZB else "Internal"
    zbank = nc.dram_tensor("zbank", [H * 128, T], BF16, kind=zkind).ap()
    abank = nc.dram_tensor("abank", [H * 128, T], BF16, kind=zkind).ap()
    if _DBG_HOOKS:
        dbgA = nc.dram_tensor("dbgA", [128, 32 * KB], BF16, kind="ExternalOutput").ap()
        dbgq = nc.dram_tensor("dbgq", [128, T], BF16, kind="ExternalOutput").ap()
        dbgk = nc.dram_tensor("dbgk", [128, T], BF16, kind="ExternalOutput").ap()
        dbgv = nc.dram_tensor("dbgv", [128, T], BF16, kind="ExternalOutput").ap()
        dbgp = nc.dram_tensor("dbgp", [128, T], BF16, kind="ExternalOutput").ap()
    tg = nc.dram_tensor("tg", [D, FF], BF16, kind="Internal").ap()
    tu = nc.dram_tensor("tu", [D, FF], BF16, kind="Internal").ap()
    td = nc.dram_tensor("td", [FF, D], BF16, kind="Internal").ap()

    base = (nc.sbuf_base + 31) & ~31
    assert nc.sbuf_top - base >= 206 * KB, (base, nc.sbuf_top)

    def at(name, off_b, shape, dt):
        assert off_b % 32 == 0, (name, off_b)
        h = nc.alloc_sbuf_tensor_at(name, shape, dt, offset=base + off_b)
        return h

    P2 = 64 * KB
    EO = 172 * KB
    SM = EO + 32 * KB
    A = at("A", 0, [128, 32 * KB], BF16)
    WOQ = [at("WOQ0", P2, [128, 8 * KB], BF16), at("WOQ1", P2 + 16 * KB, [128, 8 * KB], BF16)]
    Qt = at("Qt", P2 + 32 * KB, [128, 2 * KB], BF16)
    Kt = at("Kt", P2 + 36 * KB, [128, 2 * KB], BF16)
    VtT = at("VtT", P2 + 40 * KB, [128, 2 * KB], BF16)
    Vtok = at("Vtok", P2 + 44 * KB, [128, 2 * KB], BF16)
    Pbuf = at("Pbuf", P2 + 48 * KB, [128, 2 * KB], BF16)
    PbufB = at("pbuf2", P2 + 104 * KB, [128, 2 * KB], BF16)
    Ptb = at("Ptb", P2 + 52 * KB, [128, 2 * KB], BF16)
    Ztb = at("Ztb", P2 + 56 * KB, [128, 2 * KB], BF16)
    ZSL = [at("zsl0", P2 + 76 * KB, [128, 2 * KB], BF16), at("zsl1", P2 + 80 * KB, [128, 2 * KB], BF16)]
    bObc = at("bObc", P2 + 68 * KB, [128, 2 * KB], F32)
    Wqh = at("Wqh", P2 + 76 * KB, [128, 2 * KB], BF16)
    Wkh = at("Wkh", P2 + 80 * KB, [128, 2 * KB], BF16)
    Wvh = at("Wvh", P2 + 84 * KB, [128, 2 * KB], BF16)
    Wq2 = at("Wq2", P2 + 0 * KB, [128, 2 * KB], BF16)
    Wk2 = at("Wk2", P2 + 4 * KB, [128, 2 * KB], BF16)
    Wv2 = at("Wv2", P2 + 8 * KB, [128, 2 * KB], BF16)
    ZTOK = at("ztok", P2 + 60 * KB, [128, 256], BF16)
    PRES = at("pres", P2 + 88 * KB, [128, 2 * KB], F32)
    PRES2 = at("pres2", P2 + 16 * KB, [128, 2 * KB], F32)
    PRELO = at("prelo", P2 + 96 * KB, [128, 2 * KB], BF16)
    PREOUT = at("preout", P2 + 100 * KB, [128, 2 * KB], BF16)
    PREOUT2 = at("preout2", P2 + 12 * KB, [128, 2 * KB], BF16)
    YTO = [at(f"yto{i}", P2 + (32 + 8 * i) * KB, [128, 2 * KB], F32) for i in range(4)]
    HSL = [at(f"hsl{i}", P2 + 16 * i * KB, [128, 8 * KB], BF16) for i in range(4)]
    TGU = [at("tgu0", P2 + 64 * KB, [128, 8 * KB], BF16), at("tgu1", P2 + 80 * KB, [128, 8 * KB], BF16)]
    TDQ = [at("tdq0", P2 + 96 * KB, [128, 2 * KB], BF16), at("tdq1", P2 + 100 * KB, [128, 2 * KB], BF16)]
    GSCR = at("gscr", P2 + 104 * KB, [128, 256], F32)
    RHC = at("rhc", P2 + 105 * KB, [128, 1 * KB], BF16)
    YTF = [at("ytf0", P2 + 64 * KB, [128, 2 * KB], F32), at("ytf1", P2 + 72 * KB, [128, 2 * KB], F32),
           at("ytf2", P2 + 80 * KB, [128, 2 * KB], F32), at("ytf3", P2 + 88 * KB, [128, 2 * KB], F32)]
    SqT = at("SqT", P2 + 48 * KB, [128, 2 * KB], BF16)
    SkT = at("SkT", P2 + 88 * KB, [128, 2 * KB], BF16)
    ATTH = at("ATTH", P2 + 92 * KB, [128, 2 * KB], BF16)
    SS0 = at("ss0", P2 + 96 * KB, [128, 512], F32)
    SS1 = at("ss1", P2 + 98 * KB, [128, 512], F32)
    CMB = at("cmb", P2 + 100 * KB, [128, 256], F32)
    PbB = at("pbB", P2 + 101 * KB, [128, 512], BF16)
    PtB = at("ptB", P2 + 102 * KB, [128, 512], BF16)
    MEMF = at("memf", P2 + 103 * KB, [128, 128], F32)
    MEMZ = at("memz", P2 + 103 * KB + 512, [128, 132], BF16)
    NGD = at("ngd", P2 + 103 * KB + 800, [128, 128], BF16)
    SKTOK = at("sktok", P2 + 103 * KB + 1056, [128, 128], BF16)
    BETA = at("beta", P2 + 68 * KB, [128, 2 * KB], F32)
    XO = [at("xo0", EO, [128, 2 * KB], F32), at("xo1", EO + 8 * KB, [128, 2 * KB], F32)]
    YTE = at("ytE", EO + 16 * KB, [128, 2 * KB], F32)
    I32S = at("i32s", EO + 24 * KB, [128, 1 * KB], I32)
    XNB = at("xnb", EO + 28 * KB, [128, 2 * KB], BF16)
    MASK = at("maskT", SM, [128, 128], F32)
    IDT = at("idT", SM + 512, [128, 128], BF16)
    ONESF = at("onesT", SM + 768, [128, 128], F32)
    BQC = at("bqc", SM + 1280, [128, H], F32)
    BKC = at("bkc", SM + 1344, [128, H], F32)
    BVC = at("bvc", SM + 1408, [128, H], F32)
    SX = at("sx", SM + 1472, [128, NT], F32)
    STAT = at("stat", SM + 1536, [128, 32], F32)
    WSUM = at("wsum", SM + 1664, [128, 4], F32)
    WSC = at("wsc", SM + 1696, [128, 4], F32)
    THR = at("thr", SM + 1728, [128, 4], F32)
    NTHR = at("nthr", SM + 1760, [128, 4], F32)
    SHD = at("shd", SM + 1792, [128, 4], F32)
    SGU = at("sgu", SM + 1824, [128, 8], F32)
    HMX = at("hmx", SM + 1856, [128, 4], F32)
    ZINC = at("zinc", SM + 1888, [128, 4], F32)
    CEPS = at("ceps", SM + 1920, [128, 4], F32)
    DUMP = at("dump", SM + 1952, [128, 64], F32)
    HMXW = at("hmxw", SM + 2208, [128, 128], F32)

    pctx = ExitStack()
    psA = pctx.enter_context(nc.psum_tensor("psA", [128, 2048], F32))
    psB = pctx.enter_context(nc.psum_tensor("psB", [128, 1024], F32))
    psT = pctx.enter_context(nc.psum_tensor("psT", [128, 1024], BF16))
    psU = pctx.enter_context(nc.psum_tensor("psU", [128, 1024], BF16))
    sctx = ExitStack()
    sems = {s: sctx.enter_context(nc.semaphore(s)) for s in PL.SEMS}

    P = PL()
    K = {
        "bkg": {b: None for b in
                ("A0", "A1", "A2", "A3", "B0", "B1")},
        "bkw": {b: None for b in
                ("A0", "A1", "A2", "A3", "B0", "B1")},
        "psT_i": 0,
        "psT_g": [None, None],
        "xo_g": [None, None],
        "a_rdy": [None] * NT,
        "sx_rdy": [None] * NT,
        "spair_i": 0,
        "tern_st": {m: [0] * 32 for m in range(2)},
        "td_all": 0,
        "thr_rdy": None,
        "zb_st": 0,
        "ab_st": 0,
        "xnb_g": None,
    }

    def bank_ap(name):
        if name.startswith("A"):
            i = int(name[1])
            return psA[:, i * 512:(i + 1) * 512]
        if name.startswith("B"):
            i = int(name[1])
            return psB[:, i * 512:(i + 1) * 512]
        raise KeyError(name)

    def bank_wait_w(eng, names):
        for n in names:
            P.wd(eng, K["bkg"][n])

    def bank_set_w(names, val):
        for n in names:
            K["bkw"][n] = ("sPE", val)

    def bank_wait_r(eng, names):
        for n in names:
            P.wd(eng, K["bkw"][n])

    def bank_set_r(names, dep):
        for n in names:
            old = K["bkg"][n]
            deps = [dep] if isinstance(dep, tuple) else list(dep)
            if old:
                deps = deps + ([old] if isinstance(old, tuple) else list(old))
            best = {}
            for s, v in deps:
                best[s] = max(best.get(s, 0), v)
            K["bkg"][n] = [(s, v) for s, v in best.items()]

    def transpose_to(src_ap, dst_ap, pe_dep=None, act_dep=None):
        i = K["psT_i"] % 2
        K["psT_i"] += 1
        slot = (psT if i == 0 else psU)[:, 0:128]
        P.wd("tensor", K["psT_g"][i])
        P.wd("tensor", pe_dep)
        pe = P.o("tensor", lambda t, s=slot, a=src_ap: t.transpose(s, a, IDT[:]), "sPE")
        P.w("scalar", "sPE", pe)
        P.wd("scalar", act_dep)
        ac = P.o("scalar", lambda s, d=dst_ap, sl=slot: s.copy(d, sl), "sACT")
        K["psT_g"][i] = ("sACT", ac)
        return ac

    def transpose_group(srcs, dst_ap, pe_dep=None, act_dep=None, eng="scalar"):
        n = len(srcs)
        i = K["psT_i"] % 2
        K["psT_i"] += 1
        bank = psT if i == 0 else psU
        P.wd("tensor", K["psT_g"][i])
        P.wd("tensor", pe_dep)
        pe = None
        for k, s_ap in enumerate(srcs):
            pe = P.o("tensor", lambda t, sl=bank[:, k * 128:(k + 1) * 128], a=s_ap:
                     t.transpose(sl, a, IDT[:]), "sPE" if k == n - 1 else None)
        P.w(eng, "sPE", pe)
        P.wd(eng, act_dep)
        if eng == "scalar":
            ac = P.o("scalar", lambda s, d=dst_ap, sl=bank[:, 0:n * 128]: s.copy(d, sl),
                     "sACT")
            dep = ("sACT", ac)
        else:
            ac = P.o("vector", lambda v, d=dst_ap, sl=bank[:, 0:n * 128]:
                     v.tensor_copy(d, sl), "sDVE")
            dep = ("sDVE", ac)
        K["psT_g"][i] = dep
        return dep

    ident = DT["ident_in"]; maskin = DT["mask_in"]; onesin = DT["ones_in"]

    cl = P.d("sync", IDT[:], ident, "qC")
    cl = P.d("sync", MASK[:], maskin, "qC")
    cl = P.d("sync", ONESF[:], onesin, "qC")
    cl = P.d("sync", BQC[:], DT["bq_in"], "qC")
    cl = P.d("sync", BKC[:], DT["bk_in"], "qC")
    cl = P.d("sync", BVC[:], DT["bv_in"], "qC")
    consts_rdy = cl
    P.o("vector", lambda v: v.memset(CEPS[:, 0:1], 1e-5))
    P.o("vector", lambda v: v.memset(CEPS[:, 1:2], 5e-6))
    cinit = P.o("vector", lambda v: v.memset(CEPS[:, 2:3], -5e-6), "sDVE")
    P.w("scalar", "sDVE", cinit)

    def sep(eng, n=1):
        for _ in range(n):
            if eng == "scalar":
                P.o("scalar", lambda s: s.activation(DUMP[:, 0:64], MASK[:, 0:64], AF.Copy))
            elif eng == "vector":
                P.o("vector", lambda v: v.tensor_copy(DUMP[:, 0:64], MASK[:, 0:64]))
            else:
                P.o("gpsimd", lambda g: g.tensor_copy(DUMP[:, 0:64], MASK[:, 0:64]))

    def finish_tile(t, yth, yt_dep, xo, dst_tile, prep, scratch_f32):
        if yth is not None:
            P.wd("vector", yt_dep)
            v1 = P.o("vector", lambda v, a=yth, b=xo: v.tensor_tensor(b[:], a[:], b[:], ALU.add), "sDVE")
        else:
            P.wd("vector", yt_dep)
            v1 = None
        if dst_tile is not None:
            if v1:
                P.w("sync", "sDVE", v1)
            st = P.d("sync", dst_tile, xo[:], "qSt")
        else:
            st = None
        v2 = P.o("vector", lambda v, b=xo: v.tensor_reduce(STAT[:, 0:1], b[:], AX.X, ALU.add), "sDVE")
        P.w("gpsimd", "sDVE", v2)
        g1 = P.o("gpsimd", lambda g: g.tensor_single_scalar(
            STAT[:, 1:2], STAT[:, 0:1], -1.0 / D, ALU.mult), "sGP")
        P.w("scalar", "sGP", g1)
        P.o("scalar", lambda s, b=xo, d=scratch_f32: s.activation(
            d[:], b[:], AF.Square, bias=STAT[:, 1:2], accum_out=STAT[:, 2:3]))
        sep("scalar", 2)
        a3 = P.o("scalar", lambda s: s.activation(STAT[:, 7:8], STAT[:, 2:3], AF.Sqrt,
                                                  scale=1.0 / D, bias=CEPS[:, 0:1]), "sACT")
        P.w("vector", "sACT", a3)
        P.o("vector", lambda v: v.reciprocal(STAT[:, 3:4], STAT[:, 7:8]))
        sep("vector", 2)
        rdr = [("sACT", a3)]
        if prep == "xnT":
            P.wd("vector", K["xnb_g"])
            v3 = P.o("vector", lambda v, b=xo: v.tensor_scalar(
                XNB[:], b[:], STAT[:, 1:2], STAT[:, 3:4], ALU.add, ALU.mult), "sDVE")
            rdr.append(("sDVE", v3))
            last = None
            Av = A[:].rearrange("p (c x) -> p c x", x=2048)
            for c0 in range(0, 16, 4):
                srcs = [XNB[:, c * 128:(c + 1) * 128] for c in range(c0, c0 + 4)]
                last = transpose_group(srcs, Av[:, c0:c0 + 4, t * 128:(t + 1) * 128],
                                       pe_dep=("sDVE", v3) if c0 == 0 else None)
            K["a_rdy"][t] = last
            K["xnb_g"] = last
        elif prep == "R":
            P.wd("vector", K["xnb_g"])
            v3 = P.o("vector", lambda v, b=xo, d=scratch_f32: v.tensor_scalar(
                d[:], b[:], STAT[:, 1:2], STAT[:, 3:4], ALU.add, ALU.mult), "sDVE")
            v4 = P.o("vector", lambda v, d=scratch_f32: v.tensor_reduce(
                STAT[:, 4:5], d[:], AX.X, ALU.max, apply_absolute_value=True), "sDVE")
            sep("vector", 2)
            P.o("vector", lambda v: v.tensor_single_scalar(
                STAT[:, 5:6], STAT[:, 4:5], QEPS, ALU.max))
            sep("vector", 2)
            rc = P.o("vector", lambda v: v.reciprocal(STAT[:, 8:9], STAT[:, 5:6]), "sDVE")
            P.w("gpsimd", "sDVE", rc)
            v5 = P.o("gpsimd", lambda g: g.tensor_single_scalar(
                STAT[:, 6:7], STAT[:, 8:9], 127.0, ALU.mult), "sGP")
            rdr.append(("sDVE", v4))
            P.w("scalar", "sGP", v5)
            sxr = P.o("scalar", lambda s, tt=t: s.activation(SX[:, tt:tt + 1], STAT[:, 5:6],
                                                             AF.Copy, scale=1.0 / 127.0), "sACT")
            K["sx_rdy"][t] = ("sACT", sxr)
            last = None
            Av = A[:].rearrange("p (c x) -> p c x", x=2048)
            for half in range(2):
                P.o("scalar", lambda s, d=scratch_f32, q=half: s.activation(
                    I32S[:], d[:, q * 1024:(q + 1) * 1024], AF.Copy, scale=STAT[:, 6:7]))
                ra = P.o("scalar", lambda s, q=half: s.activation(
                    XNB[:, q * 1024:(q + 1) * 1024], I32S[:], AF.Copy), "sACT")
                for cc in range(0, 8, 4):
                    c0 = half * 8 + cc
                    srcs = [XNB[:, c * 128:(c + 1) * 128] for c in range(c0, c0 + 4)]
                    last = transpose_group(srcs, Av[:, c0:c0 + 4, t * 128:(t + 1) * 128],
                                           pe_dep=("sACT", ra) if cc == 0 else None)
            K["a_rdy"][t] = last
            K["xnb_g"] = last
        rdr2 = list(rdr)
        if st is not None:
            rdr2.append(("qSt", st))
        return st, rdr2

    def prologue():
        for e in ("tensor", "scalar", "vector", "gpsimd"):
            P.w(e, "qC", consts_rdy)
        for t in range(NT):
            par = t % 2
            P.wd("sync", K["xo_g"][par])
            qx = "qX0" if par == 0 else "qX1"
            ld = P.d("sync", XO[par][:], DT["x0"][t * 128:(t + 1) * 128, :], qx)
            _, rdr = finish_tile(t, None, [(qx, ld)], XO[par], None, "xnT", YTE)
            K["xo_g"][par] = rdr

    WSRC = [DT["wgT"], DT["wuT"], DT["wdT"]]
    TDST = [tg, tu, td]

    def w_tile_aps(m, db, fb):
        src = WSRC[m][db * 1024:(db + 1) * 1024, fb * 256:(fb + 1) * 256]
        src = src.rearrange("(c p) f -> p c f", p=128)
        dst = TDST[m][db * 1024:(db + 1) * 1024, fb * 256:(fb + 1) * 256]
        dst = dst.rearrange("(c p) f -> p c f", p=128)
        return src, dst

    pp_state = {"stage_g": None, "lo_g": None, "out_g": None, "wsum_last": [None] * 3,
                "pi": 0, "ld_pend": None, "stage_g2": [None, None],
                "out_g2": [None, None]}
    PRESB = [PRES, PRES2]
    PREOUTB = [PREOUT, PREOUT2]

    def pp_flush():
        if pp_state["ld_pend"] is not None:
            fn = pp_state["ld_pend"][0]
            pp_state["ld_pend"] = None
            fn()

    def mean_units():
        units = []
        for m in range(3):
            nb = (2, 32) if m < 2 else (8, 8)
            zi = P.o("gpsimd", lambda g, mm=m: g.memset(WSUM[:, mm:mm + 1], 0.0), "sGP")
            pp_state["wsum_last"][m] = ("sGP", zi)
            for db in range(nb[0]):
                for fb in range(nb[1]):
                    units.append((m, db, fb))
        def compute(u, p, ld):
            m, db, fb = u
            P.w("scalar", "qP", ld)
            P.wd("scalar", pp_state.get("gv_g"))
            aa = P.o("scalar", lambda s, pp=p: s.activation(
                PRELO[:], PRESB[pp][:], AF.Abs, accum_out=STAT[:, 25:26]), "sACT")
            P.w("gpsimd", "sACT", aa)
            gv = P.o("gpsimd", lambda g, mm=m: g.tensor_tensor(
                WSUM[:, mm:mm + 1], WSUM[:, mm:mm + 1], STAT[:, 25:26], ALU.add), "sGP")
            pp_state["gv_g"] = ("sGP", gv)
            pp_state["stage_g2"][p] = [("sACT", aa)]
            pp_state["lo_g"] = [("sACT", aa)]
            pp_state["wsum_last"][m] = ("sGP", gv)
        def emit(u):
            m, db, fb = u
            src, _ = w_tile_aps(m, db, fb)
            p = pp_state["pi"] % 2
            pp_state["pi"] += 1
            P.wd("sync", pp_state["stage_g2"][p])
            ld = P.d("sync", PRESB[p][:].rearrange("p (c f) -> p c f", f=256), src, "qP")
            pend = pp_state["ld_pend"]
            pp_state["ld_pend"] = (lambda uu=u, pp=p, ll=ld: compute(uu, pp, ll),)
            if pend is not None:
                pend[0]()
        return units, emit

    def wscale_finalize():
        last = None
        for m in range(3):
            bank_wait_w("tensor", ["B1"])
            P.wd("tensor", pp_state["wsum_last"][m])
            pe = P.o("tensor", lambda t, mm=m: t.matmul(
                psB[:, 512 + mm:512 + mm + 1], ONESF[:], WSUM[:, mm:mm + 1],
                start=True, stop=True), "sPE")
            bank_set_w(["B1"], pe)
            P.w("scalar", "sPE", pe)
            P.o("scalar", lambda s, mm=m: s.activation(WSC[:, mm:mm + 1],
                psB[:, 512 + mm:512 + mm + 1], AF.Identity, scale=1.0 / WDEN, bias=CEPS[:, 0:1]))
            P.o("scalar", lambda s, mm=m: s.activation(THR[:, mm:mm + 1],
                psB[:, 512 + mm:512 + mm + 1], AF.Identity, scale=0.5 / WDEN, bias=CEPS[:, 1:2]))
            last = P.o("scalar", lambda s, mm=m: s.activation(NTHR[:, mm:mm + 1],
                psB[:, 512 + mm:512 + mm + 1], AF.Identity, scale=-0.5 / WDEN,
                bias=CEPS[:, 2:3]), "sACT")
            bank_set_r(["B1"], ("sACT", last))
        K["thr_rdy"] = ("sACT", last)

    def tern_units():
        units = []
        for m in range(3):
            nb = (2, 32) if m < 2 else (8, 8)
            for fb in range(nb[1]):
                for db in range(nb[0]):
                    units.append((m, db, fb, db == nb[0] - 1))
        def compute(u, p, ld):
            m, db, fb, is_last_db = u
            _, dst = w_tile_aps(m, db, fb)
            P.w("vector", "qP", ld)
            P.wd("vector", K["thr_rdy"])
            P.wd("vector", pp_state["out_g2"][p])
            P.o("vector", lambda v, mm=m, pp=p: v.tensor_single_scalar(
                PRELO[:], PRESB[pp][:], NTHR[:, mm:mm + 1], ALU.is_le))
            v1 = P.o("vector", lambda v, mm=m, pp=p: v.scalar_tensor_tensor(
                PREOUTB[pp][:], PRESB[pp][:], THR[:, mm:mm + 1], PRELO[:],
                ALU.is_ge, ALU.subtract), "sDVE")
            P.w("sync", "sDVE", v1)
            st = P.d("sync", dst, PREOUTB[p][:].rearrange("p (c f) -> p c f", f=256),
                     "qT")
            pp_state["stage_g2"][p] = [("sDVE", v1)]
            pp_state["out_g2"][p] = [("qT", st)]
            if m < 2 and is_last_db:
                K["tern_st"][m][fb] = st
            if m == 2 and is_last_db and fb == 7:
                K["td_all"] = st
        def emit(u):
            m, db, fb, is_last_db = u
            src, _ = w_tile_aps(m, db, fb)
            p = pp_state["pi"] % 2
            pp_state["pi"] += 1
            P.wd("sync", pp_state["stage_g2"][p])
            ld = P.d("sync", PRESB[p][:].rearrange("p (c f) -> p c f", f=256), src, "qP")
            pend = pp_state["ld_pend"]
            pp_state["ld_pend"] = (lambda uu=u, pp=p, ll=ld: compute(uu, pp, ll),)
            if pend is not None:
                pend[0]()
        return units, emit

    def load_head_weights(h, wsrcs, bufs, guard_pe):
        outs = []
        for src, buf in zip(wsrcs, bufs):
            P.wd("gpsimd", guard_pe)
            sap = src.rearrange("(c p) e -> p c e", p=128)[:, :, h * 128:(h + 1) * 128]
            dap = buf[:].rearrange("p (c e) -> p c e", e=128)
            outs.append(P.d("gpsimd", dap, sap, "qG0"))
        return outs[-1]

    def proj4(wbuf, out_sb, bias_col, wdep, sig_out=None, zinc=False):
        rdy = None
        sig_rdy = None
        for tc in range(4):
            bk = "B0" if tc % 2 == 0 else "B1"
            bank_wait_w("tensor", [bk])
            if tc == 0:
                P.w("tensor", "qG0", wdep)
            P.wd("tensor", K["a_rdy"][tc * 4 + 3])
            P.wd("tensor", K["a_rdy"][tc * 4 + 2])
            P.wd("tensor", K["a_rdy"][tc * 4 + 1])
            P.wd("tensor", K["a_rdy"][tc * 4 + 0])
            pe = None
            for c in range(16):
                pe = P.o("tensor", lambda t, cc=c, w=wbuf, b=bk, tcc=tc: t.matmul(
                    bank_ap(b), w[:, cc * 128:(cc + 1) * 128],
                    A[:, cc * 2048 + tcc * 512: cc * 2048 + (tcc + 1) * 512],
                    start=(cc == 0), stop=(cc == 15)), "sPE" if c == 15 else None)
            bank_set_w([bk], pe)
            P.w("scalar", "sPE", pe)
            if bias_col is not None:
                ac = P.o("scalar", lambda s, o=out_sb, b=bk, tcc=tc, bc=bias_col: s.activation(
                    o[:, tcc * 512:(tcc + 1) * 512], bank_ap(b), AF.Identity, bias=bc), "sACT")
            else:
                ac = P.o("scalar", lambda s, o=out_sb, b=bk, tcc=tc: s.copy(
                    o[:, tcc * 512:(tcc + 1) * 512], bank_ap(b)), "sACT")
            deps = [("sACT", ac)]
            if sig_out is not None:
                P.w("vector", "sPE", pe)
                P.w("vector", "sACT", ac)
                sm0 = P.o("vector", lambda v, b=bk: v.tensor_single_scalar(
                    SS0[:, 0:512], bank_ap(b), 0.0, ALU.min), "sDVE")
                P.w("scalar", "sDVE", sm0)
                e1 = P.o("scalar", lambda s: s.activation(SS1[:, 0:512], SS0[:, 0:512],
                                                          AF.Exp), "sACT")
                P.w("vector", "sACT", e1)
                if zinc:
                    sv = P.o("vector", lambda v, o=sig_out, b=bk, tcc=tc: v.scalar_tensor_tensor(
                        o[:, tcc * 512:(tcc + 1) * 512], bank_ap(b), 0.0, SS1[:, 0:512],
                        ALU.max, ALU.add, accum_out=ZINC[:, tcc:tcc + 1]), "sDVE")
                else:
                    sv = P.o("vector", lambda v, o=sig_out, b=bk, tcc=tc: v.scalar_tensor_tensor(
                        o[:, tcc * 512:(tcc + 1) * 512], bank_ap(b), 0.0, SS1[:, 0:512],
                        ALU.max, ALU.add), "sDVE")
                deps.append(("sDVE", sv))
                sig_rdy = sv
            bank_set_r([bk], deps)
            rdy = ac
        return rdy, sig_rdy

    def vproj(wbuf, wdep, bias_col):
        vr, _ = proj4(wbuf, VtT, bias_col, wdep)
        last = None
        for t0 in range(0, NT, 4):
            srcs = [VtT[:, tt * 128:(tt + 1) * 128] for tt in range(t0, t0 + 4)]
            last = transpose_group(srcs, Vtok[:, t0 * 128:(t0 + 4) * 128],
                                   pe_dep=("sACT", vr) if t0 == 0 else None)
        return last

    def attn_a(pp_iter):
        bol = P.d("sync", bObc[:], DT["bo_bc"], "qC")
        woq_guard = [None, None]
        zsl_guard = [None, None]
        wq_guard = None
        qkv_rdy = None
        nheads = _DBG_NH if _DBG_NH is not None else H
        wsrcs = (DT["wq_a"], DT["wk_a"], DT["wv_a"])
        wbufs = [(Wqh, Wkh, Wvh), (Wq2, Wk2, Wv2)]
        wdeps = {}
        K["zl_hist"] = [None, None]
        for h in range(nheads):
            if h == 0:
                wdeps[0] = load_head_weights(0, wsrcs, wbufs[0], None)
                if nheads > 1:
                    wdeps[1] = load_head_weights(1, wsrcs, wbufs[1], None)
            elif h + 1 < nheads:
                wdeps[h + 1] = load_head_weights(h + 1, wsrcs, wbufs[(h + 1) % 2],
                                                 wq_guard)
            wdep = wdeps.pop(h)
            wb = wbufs[h % 2]
            qr, _ = proj4(wb[0], Qt, BQC[:, h:h + 1], wdep)
            kr, _ = proj4(wb[1], Kt, BKC[:, h:h + 1], wdep)
            vr = vproj(wb[2], wdep, BVC[:, h:h + 1])
            wq_guard = ("sPE", P.cnt["sPE"])
            if _DBG_HOOKS and h == 0:
                P.w("sync", "sACT", qr)
                P.w("sync", "sACT", kr)
                P.wd("sync", vr)
                P.d("sync", dbgA, A[:], "qSt")
                P.d("sync", dbgq, Qt[:], "qSt")
                P.d("sync", dbgk, Kt[:], "qSt")
                P.d("sync", dbgv, Vtok[:], "qSt")
            zlast = None

            def emit_scores_softmax(i):
                pb = Pbuf if i % 2 == 0 else PbufB
                so = 8 + (i % 2) * 8
                klen = (i + 1) * 128
                chunks = [(0, min(klen, 1024))]
                if klen > 1024:
                    chunks.append((1024, klen - 1024))
                nch = len(chunks)
                if K["zl_hist"][i % 2] is not None:
                    P.w("vector", "sACT", K["zl_hist"][i % 2])
                cinfo = []
                for ci, (c0, wdt) in enumerate(chunks):
                    pair = K["spair_i"] % 2
                    K["spair_i"] += 1
                    pbanks = ["A0", "A1"] if pair == 0 else ["A2", "A3"]
                    pb0 = pair * 1024
                    bank_wait_w("tensor", pbanks)
                    if i == 0 and ci == 0:
                        P.w("tensor", "sACT", qr)
                        P.w("tensor", "sACT", kr)
                        P.wd("tensor", vr)
                    pe = None
                    nmm = (wdt + 511) // 512
                    for mi in range(nmm):
                        mw = min(512, wdt - mi * 512)
                        pe = P.o("tensor", lambda t, ii=i, b0=pb0, m=mi, w=mw, cc0=c0: t.matmul(
                            psA[:, b0 + m * 512: b0 + m * 512 + w],
                            Qt[:, ii * 128:(ii + 1) * 128],
                            Kt[:, cc0 + m * 512: cc0 + m * 512 + w],
                            start=True, stop=True), "sPE" if mi == nmm - 1 else None)
                    bank_set_w(pbanks, pe)
                    P.w("vector", "sPE", pe)
                    if c0 <= i * 128 < c0 + wdt:
                        off = pb0 + i * 128 - c0
                        P.o("vector", lambda v, o=off: v.tensor_tensor(
                            psA[:, o:o + 128], psA[:, o:o + 128], MASK[:], ALU.add))
                    vv = P.o("vector", lambda v, b0=pb0, w=wdt, nm=so + ci: v.tensor_reduce(
                        STAT[:, nm:nm + 1], psA[:, b0:b0 + w], AX.X, ALU.max, negate=True),
                        "sDVE")
                    cinfo.append((c0, wdt, pb0, pbanks, vv))
                if nch == 2:
                    sep("vector", 2)
                    gdep = P.o("vector", lambda v, s_=so: v.tensor_tensor(
                        STAT[:, s_ + 2:s_ + 3], STAT[:, s_:s_ + 1],
                        STAT[:, s_ + 1:s_ + 2], ALU.min), "sDVE")
                    gcol = so + 2
                else:
                    gdep = cinfo[0][4]
                    gcol = so
                P.w("gpsimd", "sDVE", gdep)
                gb = P.o("gpsimd", lambda g, gc=gcol, bm=so + 3: g.tensor_single_scalar(
                    STAT[:, bm:bm + 1], STAT[:, gc:gc + 1], ISC, ALU.mult), "sGP")
                P.w("scalar", "sGP", gb)
                exl = None
                for ci, (c0, wdt, pb0, pbanks, vv) in enumerate(cinfo):
                    exl = P.o("scalar", lambda s, b0=pb0, w=wdt, cc0=c0, bm=so + 3,
                              lc=so + 4 + ci, pb_=pb:
                              s.activation(pb_[:, cc0:cc0 + w], psA[:, b0:b0 + w], AF.Exp,
                                           bias=STAT[:, bm:bm + 1], scale=ISC,
                                           accum_out=STAT[:, lc:lc + 1]), "sACT")
                    bank_set_r(pbanks, [("sACT", exl), ("sDVE", vv)])
                def mk_recip():
                    P.w("vector", "sACT", exl)
                    if nch == 2:
                        P.o("vector", lambda v, s_=so: v.tensor_tensor(
                            STAT[:, s_ + 6:s_ + 7], STAT[:, s_ + 4:s_ + 5],
                            STAT[:, s_ + 5:s_ + 6], ALU.add))
                        sep("vector", 2)
                        return P.o("vector", lambda v, s_=so: v.reciprocal(
                            STAT[:, s_ + 7:s_ + 8], STAT[:, s_ + 6:s_ + 7]), "sDVE")
                    return P.o("vector", lambda v, s_=so: v.reciprocal(
                        STAT[:, s_ + 7:s_ + 8], STAT[:, s_ + 4:s_ + 5]), "sDVE")
                return exl, pb, so + 7, mk_recip

            def emit_ztrans(zp):
                nonlocal zlast
                zi, ev1 = zp
                zlast = transpose_to(ZTOK[:, (zi % 2) * 128:(zi % 2) * 128 + 128],
                                     Ztb[:, zi * 128:(zi + 1) * 128],
                                     pe_dep=("sACT", ev1),
                                     act_dep=("qZ", K["zb_st"]) if zi == 0 else None)

            def emit_trans_av(i, exl, pb, rcol, mk_recip):
                q4 = i % 4
                if K["zpend"] is not None:
                    emit_ztrans(K["zpend"])
                    K["zpend"] = None
                pts = {}
                for gi, j0 in enumerate(range(0, i + 1, 4)):
                    n = min(4, i + 1 - j0)
                    srcs = [pb[:, j * 128:(j + 1) * 128] for j in range(j0, j0 + n)]
                    pt = transpose_group(srcs, Ptb[:, j0 * 128:(j0 + n) * 128],
                                         pe_dep=("sACT", exl) if j0 == 0 else None,
                                         eng="scalar" if gi % 2 == 0 else "vector")
                    pts[pt[0]] = pt[1]
                rc = mk_recip()
                bank_wait_w("tensor", ["B0"])
                for sem_, val_ in pts.items():
                    P.w("tensor", sem_, val_)
                for j in range(i + 1):
                    P.o("tensor", lambda t, jj=j, q=q4, ii=i: t.matmul(
                        psB[:, q * 128: (q + 1) * 128],
                        Ptb[:, jj * 128:(jj + 1) * 128],
                        Vtok[:, jj * 128:(jj + 1) * 128],
                        start=(jj == 0), stop=(jj == ii)),
                        "sPE" if j == i else None)
                avdone = P.cnt["sPE"]
                bank_set_w(["B0"], avdone)
                P.w("scalar", "sPE", avdone)
                P.w("scalar", "sDVE", rc)
                ev1 = P.o("scalar", lambda s, ii=i, q=q4, rcc=rcol: s.activation(
                    ZTOK[:, (ii % 2) * 128:(ii % 2) * 128 + 128],
                    psB[:, q * 128: (q + 1) * 128], AF.Identity,
                    scale=STAT[:, rcc:rcc + 1]), "sACT")
                bank_set_r(["B0"], ("sACT", ev1))
                K["zl_hist"][i % 2] = ev1
                K["zpend"] = (i, ev1)

            pend = None
            K["zpend"] = None
            for i in range(NT):
                info = emit_scores_softmax(i)
                if pend is not None:
                    emit_trans_av(*pend)
                pend = (i,) + info
            emit_trans_av(*pend)
            emit_ztrans(K["zpend"])
            K["zpend"] = None
            if _DBG_HOOKS and h == 0:
                P.w("sync", "sDVE", P.cnt["sDVE"])
                P.d("sync", dbgp, Pbuf[:], "qSt")
            P.w("sync", "sACT", zlast)
            K["zb_st"] = P.d("sync", zbank[h * 128:(h + 1) * 128, :], Ztb[:], "qZ")
            for _ in range(pp_iter.get("per_head", 0)):
                if pp_iter["units"]:
                    pp_iter["emit"](pp_iter["units"].pop(0))
                elif pp_iter.get("next"):
                    nxt = pp_iter.pop("next")
                    pp_flush()
                    nxt[0]()
                    pp_iter["units"], pp_iter["emit"] = nxt[1]
        if pp_iter.get("next"):
            nxt = pp_iter.pop("next")
            pp_flush()
            nxt[0]()
            pp_iter["units"], pp_iter["emit"] = nxt[1]
        while pp_iter["units"]:
            pp_iter["emit"](pp_iter["units"].pop(0))
        pp_flush()

        if _DBG_NOOUT:
            return
        obank = ["A0", "A1", "A2", "A3"]
        yq_rdy = [[None] * 4 for _ in range(4)]
        yto_g = [None] * 4
        wls = {}
        zls = {}
        def load_woq(g, dq):
            wpar = dq % 2
            P.wd("gpsimd", woq_guard[wpar])
            sap = DT["wo_a"].rearrange("(hh p) d -> p hh d", p=128)[:, :, dq * 512:(dq + 1) * 512]
            dap = WOQ[wpar][:].rearrange("p (hh d) -> p hh d", d=512)
            qg = "qG0" if wpar == 0 else "qG1"
            wls[(g, dq)] = (qg, P.d("gpsimd", dap, sap, qg))
        def load_zsl(g, dq, ti):
            t = g * 4 + ti
            zpar = t % 2
            P.wd("sync", zsl_guard[zpar])
            P.w("sync", "qZ", K["zb_st"])
            zap = zbank.rearrange("(hh p) t2 -> p hh t2", p=128)[:, :, t * 128:(t + 1) * 128]
            ql = "qL0" if zpar == 0 else "qL1"
            zls[(g, dq, ti)] = (ql, P.d("sync", ZSL[zpar][:].rearrange(
                "p (hh q) -> p hh q", q=128), zap, ql))
        steps = [(g, dq) for g in range(4) for dq in range(4)]
        for _pp in (0, 1):
            P.wd("gpsimd", pp_state["stage_g2"][_pp])
            P.wd("gpsimd", pp_state["out_g2"][_pp])
        load_woq(*steps[0])
        load_zsl(*steps[0], 0)
        zseq = [(g, dq, ti) for (g, dq) in steps for ti in range(4)]
        zi_next = 1
        for si, (g, dq) in enumerate(steps):
            if si + 1 < len(steps):
                load_woq(*steps[si + 1])
            for ti in range(4):
                t = g * 4 + ti
                while zi_next < len(zseq) and zi_next <= si * 4 + ti + 1:
                    load_zsl(*zseq[zi_next]); zi_next += 1
                qg, wl = wls[(g, dq)]
                ql, zl = zls[(g, dq, ti)]
                zpar = t % 2
                wpar = dq % 2
                bk = obank[ti]
                bank_wait_w("tensor", [bk])
                P.w("tensor", qg, wl)
                P.w("tensor", ql, zl)
                pe = None
                for h2 in range(16):
                    pe = P.o("tensor", lambda t_, hh=h2, zp=zpar, wp=wpar, b=bk: t_.matmul(
                        bank_ap(b), ZSL[zp][:, hh * 128:(hh + 1) * 128],
                        WOQ[wp][:, hh * 512:(hh + 1) * 512],
                        start=(hh == 0), stop=(hh == 15)), "sPE" if h2 == 15 else None)
                bank_set_w([bk], pe)
                zsl_guard[zpar] = ("sPE", pe)
                woq_guard[wpar] = ("sPE", pe)
                P.w("scalar", "sPE", pe)
                P.wd("scalar", yto_g[ti])
                ac = P.o("scalar", lambda s, ti_=ti, dq_=dq, b=bk: s.copy(
                    YTO[ti_][:, dq_ * 512:(dq_ + 1) * 512], bank_ap(b)), "sACT")
                bank_set_r([bk], ("sACT", ac))
                yq_rdy[ti][dq] = ac
            if dq == 3:
                for ti in range(4):
                    t = g * 4 + ti
                    par = t % 2
                    P.wd("sync", K["xo_g"][par])
                    qx = "qX0" if par == 0 else "qX1"
                    xl = P.d("sync", XO[par][:], DT["x0"][t * 128:(t + 1) * 128, :], qx)
                    P.w("vector", "sACT", yq_rdy[ti][3])
                    P.w("vector", "qC", bol)
                    bv = P.o("vector", lambda v, ti_=ti: v.tensor_tensor(
                        YTO[ti_][:], YTO[ti_][:], bObc[:], ALU.add), "sDVE")
                    dsta = (xr if stage > 1 else y)
                    _, rdr = finish_tile(t, YTO[ti], [("sDVE", bv), (qx, xl)], XO[par],
                                         dsta[t * 128:(t + 1) * 128, :],
                                         "R" if stage > 1 else None, YTE)
                    K["xo_g"][par] = rdr
                    yto_g[ti] = ("sDVE", P.cnt["sDVE"])

    def ffn(idx, xsrc, xdst, prep_next):
        P.w("sync", "qT", P.cnt["qT"])
        init_dep = [("sDVE", P.cnt["sDVE"]), ("sPE", P.cnt["sPE"]),
                    ("sACT", P.cnt["sACT"])]
        tgu_guard = [list(init_dep), list(init_dep)]
        tdq_guard = [list(init_dep), list(init_dep)]
        ytf_guard = [None] * 4
        gscr_g = None
        last_ga = None
        shd_g = [None] * 4
        for S in range(4):
            sg_rdy = None
            for tloc in range(4):
                t = S * 4 + tloc
                P.wd("vector", K["sx_rdy"][t])
                P.wd("vector", K["thr_rdy"])
                if last_ga is not None and tloc == 0:
                    P.w("vector", "sACT", last_ga)
                P.o("vector", lambda v, tl=tloc, tt=t: v.tensor_tensor(
                    SGU[:, tl:tl + 1], SX[:, tt:tt + 1], WSC[:, 0:1], ALU.mult))
                sg_rdy = P.o("vector", lambda v, tl=tloc, tt=t: v.tensor_tensor(
                    SGU[:, 4 + tl:5 + tl], SX[:, tt:tt + 1], WSC[:, 1:2], ALU.mult), "sDVE")
            hslot_wr = [[] for _ in range(4)]
            def load_tgu(fc):
                par = fc % 2
                ql = "qL0" if par == 0 else "qL1"
                P.wd("sync", tgu_guard[par])
                gdst = TGU[par][:].rearrange("p (c f) -> p c f", f=512)
                gap = tg.rearrange("(c p) f -> p c f", p=128)[:, :, fc * 256:(fc + 1) * 256]
                uap = tu.rearrange("(c p) f -> p c f", p=128)[:, :, fc * 256:(fc + 1) * 256]
                P.d("sync", gdst[:, :, 0:256], gap, ql)
                return ql, P.d("sync", gdst[:, :, 256:512], uap, ql)
            tgul = {0: load_tgu(0)}
            for fc in range(32):
                par = fc % 2
                if fc + 1 < 32:
                    tgul[fc + 1] = load_tgu(fc + 1)
                ql, l2 = tgul.pop(fc)
                for tloc in range(4):
                    t = S * 4 + tloc
                    bk = "B0" if tloc % 2 == 0 else "B1"
                    bank_wait_w("tensor", [bk])
                    if tloc == 0:
                        P.w("tensor", ql, l2)
                    if fc == 0:
                        P.wd("tensor", K["a_rdy"][t])
                    pe = None
                    for c in range(16):
                        pe = P.o("tensor", lambda t_, cc=c, tt=t, p_=par, b=bk: t_.matmul(
                            bank_ap(b), A[:, cc * 2048 + tt * 128: cc * 2048 + (tt + 1) * 128],
                            TGU[p_][:, cc * 512:(cc + 1) * 512],
                            start=(cc == 0), stop=(cc == 15)), "sPE" if c == 15 else None)
                    bank_set_w([bk], pe)
                    tgu_guard[par] = ("sPE", pe)
                    P.w("scalar", "sPE", pe)
                    if sg_rdy:
                        P.w("scalar", "sDVE", sg_rdy)
                    P.wd("scalar", gscr_g)
                    ga = P.o("scalar", lambda s, b=bk, tl=tloc: s.activation(
                        GSCR[:, 0:256], bank_ap(b)[:, 0:256], AF.Sigmoid,
                        scale=SGU[:, tl:tl + 1]), "sACT")
                    last_ga = ga
                    P.w("vector", "sACT", ga)
                    hv = P.o("vector", lambda v, b=bk, tl=tloc, f=fc: v.scalar_tensor_tensor(
                        HSL[tl][:, f * 256:(f + 1) * 256], bank_ap(b)[:, 256:512],
                        SGU[:, 4 + tl:5 + tl], GSCR[:, 0:256], ALU.mult, ALU.mult), "sDVE")
                    hm = P.o("vector", lambda v, tl=tloc, f=fc: v.tensor_reduce(
                        HMXW[:, tl * 32 + f: tl * 32 + f + 1],
                        HSL[tl][:, f * 256:(f + 1) * 256], AX.X,
                        ALU.max, apply_absolute_value=True), "sDVE")
                    gscr_g = ("sDVE", hv)
                    bank_set_r([bk], [("sACT", ga), ("sDVE", hv)])
                    hslot_wr[tloc].append(("sDVE", hm))
            rh_rdy = [None] * 4
            last_ra = None
            for tloc in range(4):
                if hslot_wr[tloc]:
                    P.wd("vector", hslot_wr[tloc][-1])
                if last_ra is not None:
                    P.w("vector", "sACT", last_ra)
                P.o("vector", lambda v, tl=tloc: v.tensor_reduce(
                    STAT[:, 9:10], HMXW[:, tl * 32:(tl + 1) * 32], AX.X, ALU.max))
                sep("vector", 2)
                P.o("vector", lambda v: v.tensor_single_scalar(
                    STAT[:, 27:28], STAT[:, 9:10], QEPS, ALU.max))
                sep("vector", 2)
                rc = P.o("vector", lambda v: v.reciprocal(STAT[:, 26:27], STAT[:, 27:28]),
                         "sDVE")
                P.w("gpsimd", "sDVE", rc)
                v5 = P.o("gpsimd", lambda g: g.tensor_single_scalar(
                    STAT[:, 28:29], STAT[:, 26:27], 127.0, ALU.mult), "sGP")
                P.w("scalar", "sGP", v5)
                sh1 = P.o("scalar", lambda s: s.activation(STAT[:, 29:30], STAT[:, 27:28],
                                                           AF.Copy, scale=1.0 / 127.0), "sACT")
                P.w("vector", "sACT", sh1)
                P.wd("vector", shd_g[tloc])
                shv = P.o("vector", lambda v, tl=tloc: v.tensor_tensor(
                    SHD[:, tl:tl + 1], STAT[:, 29:30], WSC[:, 2:3], ALU.mult), "sDVE")
                shd_w = shv
                last = None
                for fc2 in range(8):
                    P.o("scalar", lambda s, tl=tloc, f2=fc2: s.activation(
                        I32S[:], HSL[tl][:, f2 * 1024:(f2 + 1) * 1024], AF.Copy,
                        scale=STAT[:, 28:29]))
                    ra = P.o("scalar", lambda s: s.activation(RHC[:], I32S[:], AF.Copy),
                             "sACT")
                    last_ra = ra
                    for j0 in range(0, 8, 4):
                        f0 = fc2 * 8 + j0
                        srcs = [RHC[:, j * 128:(j + 1) * 128] for j in range(j0, j0 + 4)]
                        last = transpose_group(srcs,
                                               HSL[tloc][:, f0 * 128:(f0 + 4) * 128],
                                               pe_dep=("sACT", ra) if j0 == 0 else None)
                rh_rdy[tloc] = last
            shd_rdy = P.cnt["sDVE"]
            dn_pe = [[None] * 4 for _ in range(4)]
            dbank = lambda tl, dc: ["A0", "A1", "A2", "A3"][tl]
            def load_tdq(dc, fq):
                qpar = fq % 2
                qd = "qL0" if qpar == 0 else "qL1"
                P.wd("sync", tdq_guard[qpar])
                tap = td[fq * 512:(fq + 1) * 512, dc * 512:(dc + 1) * 512]
                tap = tap.rearrange("(q p) d -> p q d", p=128)
                return qd, P.d("sync", TDQ[qpar][:].rearrange("p (q d) -> p q d", d=512),
                               tap, qd)
            dsteps = [(dc, fq) for dc in range(4) for fq in range(16)]
            tdql = {dsteps[0]: load_tdq(*dsteps[0])}
            for di, (dc, fq) in enumerate(dsteps):
                if di + 1 < len(dsteps):
                    tdql[dsteps[di + 1]] = load_tdq(*dsteps[di + 1])
                qd, tl_ = tdql.pop((dc, fq))
                if True:
                    qpar = fq % 2
                    for q4 in range(4):
                        f = fq * 4 + q4
                        for tloc in range(4):
                            bk = dbank(tloc, dc)
                            if f == 0:
                                bank_wait_w("tensor", [bk])
                                P.wd("tensor", rh_rdy[tloc])
                            if q4 == 0 and tloc == 0:
                                P.w("tensor", qd, tl_)
                            pe = P.o("tensor", lambda t_, ff=f, tl=tloc, qp=qpar, qq=q4, b=bk:
                                     t_.matmul(bank_ap(b),
                                               HSL[tl][:, ff * 128:(ff + 1) * 128],
                                               TDQ[qp][:, qq * 512:(qq + 1) * 512],
                                               start=(ff == 0), stop=(ff == 63)),
                                     "sPE" if (f == 63 or (q4 == 3 and tloc == 3))
                                     else None)
                            if f == 63:
                                bank_set_w([bk], pe)
                                dn_pe[tloc][dc] = pe
                    tdq_guard[qpar] = ("sPE", P.cnt["sPE"])
                if fq != 15:
                    continue
                for tloc in range(4):
                    bk = dbank(tloc, dc)
                    P.w("scalar", "sPE", dn_pe[tloc][dc])
                    P.w("scalar", "sDVE", shd_rdy)
                    P.wd("scalar", ytf_guard[tloc])
                    ac = P.o("scalar", lambda s, tl=tloc, d_=dc, b=bk: s.activation(
                        YTF[tl][:, d_ * 512:(d_ + 1) * 512], bank_ap(b), AF.Copy,
                        scale=SHD[:, tl:tl + 1]), "sACT")
                    bank_set_r([bk], ("sACT", ac))
                    shd_g[tloc] = ("sACT", ac)
                    dn_pe[tloc][dc] = ac
            for tloc in range(4):
                t = S * 4 + tloc
                par = t % 2
                P.wd("sync", K["xo_g"][par])
                qx = "qX0" if par == 0 else "qX1"
                xl = P.d("sync", XO[par][:], xsrc[t * 128:(t + 1) * 128, :], qx)
                ydep = [("sACT", dn_pe[tloc][3]), (qx, xl),
                        ("sACT", dn_pe[tloc][0]), ("sACT", dn_pe[tloc][1]),
                        ("sACT", dn_pe[tloc][2])]
                st, rdr = finish_tile(t, YTF[tloc], ydep, XO[par],
                                      xdst[t * 128:(t + 1) * 128, :], prep_next, YTE)
                K["xo_g"][par] = rdr
                ytf_guard[tloc] = [("sDVE", P.cnt["sDVE"])]
            tgu_guard[0] = [("sDVE", P.cnt["sDVE"])]
            tgu_guard[1] = [("sDVE", P.cnt["sDVE"])]

    def attn_b(xsrc, xdst, prep_next, nh=None, noout=False):
        bl = P.d("sync", BETA[:], DT["beta_in"], "qB")
        P.w("scalar", "qB", bl)
        brdy = P.o("scalar", lambda s: s.activation(BETA[:], BETA[:], AF.Sigmoid), "sACT")
        wq_guard = [("sDVE", P.cnt["sDVE"]), ("sPE", P.cnt["sPE"]),
                    ("sACT", P.cnt["sACT"])]
        woq_guard = [None, None]
        zsl_guard = [None, None]
        for h in range(nh if nh is not None else H):
            wdep = load_head_weights(h, (DT["wq_b"], DT["wk_b"], DT["wv_b"]),
                                     (Wqh, Wkh, Wvh), wq_guard)
            qr, sq_rdy = proj4(Wqh, Qt, None, wdep,
                               sig_out=None if _DBG_NOSIG else SqT)
            kr, sk_rdy = proj4(Wkh, Kt, None, wdep,
                               sig_out=None if _DBG_NOSIG else SkT, zinc=True)
            vr = vproj(Wvh, wdep, None)
            wq_guard = ("sPE", P.cnt["sPE"])
            P.o("vector", lambda v: v.memset(MEMF[:], 0.0))
            P.o("vector", lambda v: v.memset(STAT[:, 30:31], 1.0 / 128.0))
            P.o("vector", lambda v: v.memset(MEMZ[:, 0:128], 0.0))
            mz = P.o("vector", lambda v: v.memset(MEMZ[:, 128:129], 1.0 / 128.0), "sDVE")
            mz_rdy = ("sDVE", mz)
            atlast = None
            for s_ in range(_DBG_NSEG if _DBG_NSEG is not None else 4):
                c0 = s_ * 512
                for qi in range(4):
                    qq = K["spair_i"] % 4
                    K["spair_i"] += 1
                    bk = ["A0", "A1", "A2", "A3"][qq]
                    bank_wait_w("tensor", [bk])
                    if s_ == 0 and qi == 0:
                        P.w("tensor", "sACT", qr)
                        P.w("tensor", "sACT", kr)
                        P.wd("tensor", vr)
                        P.wd("tensor", [("sDVE", sq_rdy), ("sDVE", sk_rdy)])
                    pe = P.o("tensor", lambda t_, ii=qi, cc=c0, b=bk: t_.matmul(
                        bank_ap(b)[:, 0:(ii + 1) * 128], Qt[:, cc + ii * 128: cc + (ii + 1) * 128],
                        Kt[:, cc: cc + (ii + 1) * 128], start=True, stop=True), "sPE")
                    bank_set_w([bk], pe)
                    P.w("vector", "sPE", pe)
                    P.o("vector", lambda v, ii=qi, b=bk: v.tensor_tensor(
                        bank_ap(b)[:, ii * 128:(ii + 1) * 128],
                        bank_ap(b)[:, ii * 128:(ii + 1) * 128], MASK[:], ALU.add))
                    nv = P.o("vector", lambda v, ii=qi, b=bk: v.tensor_reduce(
                        STAT[:, 8:9], bank_ap(b)[:, 0:(ii + 1) * 128], AX.X, ALU.max,
                        negate=True), "sDVE")
                    P.w("gpsimd", "sDVE", nv)
                    gb = P.o("gpsimd", lambda g: g.tensor_single_scalar(
                        STAT[:, 10:11], STAT[:, 8:9], ISC, ALU.mult), "sGP")
                    P.w("scalar", "sGP", gb)
                    ex = P.o("scalar", lambda s, ii=qi, b=bk: s.activation(
                        PbB[:, 0:(ii + 1) * 128], bank_ap(b)[:, 0:(ii + 1) * 128], AF.Exp,
                        bias=STAT[:, 10:11], scale=ISC, accum_out=STAT[:, 12:13]), "sACT")
                    bank_set_r([bk], ("sACT", ex))
                    P.w("vector", "sACT", ex)
                    P.o("vector", lambda v: v.reciprocal(STAT[:, 14:15], STAT[:, 12:13]))
                    sep("vector", 2)
                    pn = P.o("vector", lambda v, ii=qi: v.tensor_scalar(
                        PbB[:, 0:(ii + 1) * 128], PbB[:, 0:(ii + 1) * 128], STAT[:, 14:15],
                        None, ALU.mult), "sDVE")
                    adq = psB[:, qq * 128:(qq + 1) * 128]
                    addone = None
                    for j in range(qi + 1):
                        pt = transpose_to(PbB[:, j * 128:(j + 1) * 128],
                                          PtB[:, j * 128:(j + 1) * 128],
                                          pe_dep=("sDVE", pn) if j == 0 else None)
                        if j == 0:
                            bank_wait_w("tensor", ["B0"])
                        P.w("tensor", "sACT", pt)
                        addone = P.o("tensor", lambda t_, jj=j, ss=s_, ii=qi, a=adq: t_.matmul(
                            a, PtB[:, jj * 128:(jj + 1) * 128],
                            Vtok[:, (ss * 4 + jj) * 128:(ss * 4 + jj + 1) * 128],
                            start=(jj == 0), stop=(jj == ii)), "sPE" if j == qi else None)
                    bank_set_w(["B0"], addone)
                    amb = psB[:, 512:641]
                    bank_wait_w("tensor", ["B1"])
                    P.wd("tensor", mz_rdy)
                    am = P.o("tensor", lambda t_, cc=c0, ii=qi, a=amb: t_.matmul(
                        a, SqT[:, cc + ii * 128: cc + (ii + 1) * 128], MEMZ[:, 0:129],
                        start=True, stop=True), "sPE")
                    bank_set_w(["B1"], am)
                    P.w("vector", "sPE", am)
                    rd = P.o("vector", lambda v: v.reciprocal(STAT[:, 31:32], psB[:, 640:641]),
                             "sDVE")
                    P.w("scalar", "sDVE", rd)
                    amv0 = P.o("scalar", lambda s: s.activation(
                        CMB[:, 0:128], psB[:, 512:640], AF.Identity, scale=STAT[:, 31:32]),
                        "sACT")
                    P.w("vector", "sACT", amv0)
                    amv = P.o("vector", lambda v, a=adq: v.tensor_tensor(
                        CMB[:, 128:256], CMB[:, 0:128], a, ALU.subtract), "sDVE")
                    bank_set_r(["B1"], [("sACT", amv0), ("sDVE", rd)])
                    P.w("gpsimd", "sACT", brdy)
                    P.w("gpsimd", "sDVE", amv)
                    gu2 = P.o("gpsimd", lambda g, hh=h: g.tensor_tensor(
                        CMB[:, 0:128], BETA[:, hh * 128:(hh + 1) * 128], CMB[:, 128:256],
                        ALU.mult), "sGP")
                    P.w("vector", "sGP", gu2)
                    if atlast is not None:
                        P.w("vector", "sACT", atlast)
                    at = P.o("vector", lambda v, a=adq: v.tensor_tensor(
                        XNB[:, 0:128], CMB[:, 0:128], a, ALU.add), "sDVE")
                    bank_set_r(["B0"], ("sDVE", at))
                    atlast = transpose_to(XNB[:, 0:128],
                                          ATTH[:, c0 + qi * 128: c0 + (qi + 1) * 128],
                                          pe_dep=("sDVE", at),
                                          act_dep=(("qZ", K["ab_st"])
                                                   if (s_ == 0 and qi == 0) else None))
                up = psB[:, 0:128]
                upfin = None
                prev_up = None
                for st_ in range(4):
                    smb = psB[:, 512:641]
                    bank_wait_w("tensor", ["B1"])
                    P.wd("tensor", mz_rdy)
                    sm = P.o("tensor", lambda t_, cc=c0, ii=st_, a=smb: t_.matmul(
                        a, SkT[:, cc + ii * 128: cc + (ii + 1) * 128], MEMZ[:, 0:129],
                        start=True, stop=True), "sPE")
                    bank_set_w(["B1"], sm)
                    P.w("vector", "sPE", sm)
                    rk = P.o("vector", lambda v: v.reciprocal(STAT[:, 31:32], psB[:, 640:641]),
                             "sDVE")
                    P.w("scalar", "sDVE", rk)
                    m1 = P.o("scalar", lambda s: s.activation(
                        SS0[:, 0:128], psB[:, 512:640], AF.Identity, scale=STAT[:, 31:32]),
                        "sACT")
                    P.w("vector", "sACT", m1)
                    if prev_up is not None:
                        P.w("vector", "sPE", prev_up)
                    ng = P.o("vector", lambda v, ss=s_, ii=st_: v.tensor_tensor(
                        NGD[:], SS0[:, 0:128],
                        Vtok[:, (ss * 4 + ii) * 128:(ss * 4 + ii + 1) * 128],
                        ALU.subtract), "sDVE")
                    bank_set_r(["B1"], [("sACT", m1), ("sDVE", rk)])
                    stk = transpose_to(SkT[:, c0 + st_ * 128: c0 + (st_ + 1) * 128],
                                       SKTOK[:], pe_dep=None)
                    if st_ == 0:
                        bank_wait_w("tensor", ["B0"])
                    P.w("tensor", "sACT", stk)
                    P.w("tensor", "sDVE", ng)
                    upfin = P.o("tensor", lambda t_, ii=st_, a=up: t_.matmul(
                        a, SKTOK[:], NGD[:], start=(ii == 0), stop=(ii == 3)), "sPE")
                    prev_up = upfin
                bank_set_w(["B0"], upfin)
                P.w("vector", "sPE", upfin)
                P.o("vector", lambda v, a=up: v.tensor_tensor(MEMF[:], MEMF[:], a,
                                                              ALU.subtract))
                zv = P.o("vector", lambda v, ss=s_: v.tensor_tensor(
                    STAT[:, 30:31], STAT[:, 30:31], ZINC[:, ss:ss + 1], ALU.add), "sDVE")
                bank_set_r(["B0"], ("sDVE", zv))
                P.w("scalar", "sDVE", zv)
                P.o("scalar", lambda s: s.copy(MEMZ[:, 0:128], MEMF[:]))
                mzv = P.o("scalar", lambda s: s.copy(MEMZ[:, 128:129], STAT[:, 30:31]),
                          "sACT")
                mz_rdy = ("sACT", mzv)
            P.w("sync", "sACT", atlast)
            K["ab_st"] = P.d("sync", abank[h * 128:(h + 1) * 128, :], ATTH[:], "qZ")
        if noout:
            return
        obank = ["A0", "A1", "A2", "A3"]
        yq_rdy = [[None] * 4 for _ in range(4)]
        yto_g = [None] * 4
        wls = {}
        zls = {}
        def load_woq(g, dq):
            wpar = dq % 2
            P.wd("gpsimd", woq_guard[wpar])
            sap = DT["wo_b"].rearrange("(hh p) d -> p hh d", p=128)[:, :, dq * 512:(dq + 1) * 512]
            dap = WOQ[wpar][:].rearrange("p (hh d) -> p hh d", d=512)
            qg = "qG0" if wpar == 0 else "qG1"
            wls[(g, dq)] = (qg, P.d("gpsimd", dap, sap, qg))
        def load_zsl(g, dq, ti):
            t = g * 4 + ti
            zpar = t % 2
            P.wd("sync", zsl_guard[zpar])
            P.w("sync", "qZ", K["ab_st"])
            h0 = (t % 4) * 4
            seg = t // 4
            aap = abank.rearrange("(hh p) t2 -> p hh t2", p=128)[
                :, h0:h0 + 4, seg * 512:(seg + 1) * 512]
            ql = "qL0" if zpar == 0 else "qL1"
            zls[(g, dq, ti)] = (ql, P.d("sync", ZSL[zpar][:].rearrange(
                "p (a q) -> p a q", q=512), aap, ql))
        steps = [(g, dq) for g in range(4) for dq in range(4)]
        load_woq(*steps[0])
        load_zsl(*steps[0], 0)
        zseq = [(g, dq, ti) for (g, dq) in steps for ti in range(4)]
        zi_next = 1
        for si, (g, dq) in enumerate(steps):
            if si + 1 < len(steps):
                load_woq(*steps[si + 1])
            for ti in range(4):
                t = g * 4 + ti
                while zi_next < len(zseq) and zi_next <= si * 4 + ti + 1:
                    load_zsl(*zseq[zi_next]); zi_next += 1
                qg, wl = wls[(g, dq)]
                ql, zl = zls[(g, dq, ti)]
                zpar = t % 2
                wpar = dq % 2
                bk = obank[ti]
                bank_wait_w("tensor", [bk])
                P.w("tensor", qg, wl)
                P.w("tensor", ql, zl)
                pe = None
                gat = ZSL[zpar][:].rearrange("p (a b c) -> p a b c", a=4, b=32, c=16)
                for h2 in range(16):
                    pe = P.o("tensor", lambda t_, hh=h2, wp=wpar, b=bk, gg=gat: t_.matmul(
                        bank_ap(b), gg[:, :, :, hh],
                        WOQ[wp][:, hh * 512:(hh + 1) * 512],
                        start=(hh == 0), stop=(hh == 15)), "sPE" if h2 == 15 else None)
                bank_set_w([bk], pe)
                zsl_guard[zpar] = ("sPE", pe)
                woq_guard[wpar] = ("sPE", pe)
                P.w("scalar", "sPE", pe)
                P.wd("scalar", yto_g[ti])
                ac = P.o("scalar", lambda s, ti_=ti, dq_=dq, b=bk: s.copy(
                    YTO[ti_][:, dq_ * 512:(dq_ + 1) * 512], bank_ap(b)), "sACT")
                bank_set_r([bk], ("sACT", ac))
                yq_rdy[ti][dq] = ac
            if dq == 3:
                for ti in range(4):
                    t = g * 4 + ti
                    par = t % 2
                    P.wd("sync", K["xo_g"][par])
                    qx = "qX0" if par == 0 else "qX1"
                    xl = P.d("sync", XO[par][:], xsrc[t * 128:(t + 1) * 128, :], qx)
                    ydep = [("sACT", yq_rdy[ti][3]), (qx, xl),
                            ("sACT", yq_rdy[ti][0]), ("sACT", yq_rdy[ti][1]),
                            ("sACT", yq_rdy[ti][2])]
                    _, rdr = finish_tile(t, YTO[ti], ydep, XO[par],
                                         xdst[t * 128:(t + 1) * 128, :], prep_next, YTE)
                    K["xo_g"][par] = rdr
                    yto_g[ti] = ("sDVE", P.cnt["sDVE"])

    prologue()
    mu, me = mean_units()
    tu_, te = tern_units()
    if stage > 1:
        pp = {"units": mu, "emit": me, "per_head": 26,
              "next": (wscale_finalize, (tu_, te))}
    else:
        pp = {"units": [], "emit": me, "per_head": 0}
    attn_a(pp)
    if stage > 1:
        ffn(0, xr, xr if stage > 2 else y,
            ("xnT" if stage > 2 else None))
    if stage > 2:
        attn_b(xr, xr if stage > 3 else y, ("R" if stage > 3 else None),
               nh=_DBG_NH_B, noout=_DBG_NOOUT_B)
    if stage > 3:
        ffn(1, xr, y, None)
    for s in ("qSt", "qZ", "qT", "qC", "qB", "qX0", "qX1", "qL0", "qL1", "qP",
              "qG0", "qG1"):
        P.w("sync", s, P.cnt[s])

    print(f"[kernel build] planned instructions ~{P.ni}, sems {P.cnt}", file=sys.stderr)
    _emit(P, nc, sems)
    pctx.close()
    sctx.close()
    return nc


def _host_inputs(inputs, b):
    x = np.ascontiguousarray(inputs["L0_attn_a_x"][b])
    wq_a = np.ascontiguousarray(inputs["L0_attn_a_W_Q"].transpose(1, 0, 2).reshape(D, D))
    wk_a = np.ascontiguousarray(inputs["L0_attn_a_W_K"].transpose(1, 0, 2).reshape(D, D))
    wv_a = np.ascontiguousarray(inputs["L0_attn_a_W_V"].transpose(1, 0, 2).reshape(D, D))
    wo_a = np.ascontiguousarray(inputs["L0_attn_a_W_O"].reshape(D, D))
    bq = np.ascontiguousarray(inputs["L0_attn_a_b_Q"].T)
    bk = np.ascontiguousarray(inputs["L0_attn_a_b_K"].T)
    bv = np.ascontiguousarray(inputs["L0_attn_a_b_V"].T)
    bo = np.ascontiguousarray(np.broadcast_to(inputs["L0_attn_a_b_O"].reshape(1, D),
                                              (128, D)))
    beta = np.ascontiguousarray(np.broadcast_to(
        inputs["L0_attn_b_betas"].reshape(1, D), (128, D)))
    wgT = np.ascontiguousarray(inputs["L0_ffn_w_gate"].T)
    wuT = np.ascontiguousarray(inputs["L0_ffn_w_up"].T)
    wdT = np.ascontiguousarray(inputs["L0_ffn_w_down"].T)
    ident = np.eye(128, dtype=BF)
    mask = np.where(np.tril(np.ones((128, 128), bool)), np.float32(0.0),
                    np.float32(NEGBIG)).astype(np.float32)
    ones = np.ones((128, 128), np.float32)
    return {
        "x0": x, "wq_a": wq_a, "wk_a": wk_a, "wv_a": wv_a, "wo_a": wo_a,
        "bq_in": bq, "bk_in": bk, "bv_in": bv, "bo_bc": bo,
        "wq_b": np.ascontiguousarray(inputs["L0_attn_b_Wq"]),
        "wk_b": np.ascontiguousarray(inputs["L0_attn_b_Wk"]),
        "wv_b": np.ascontiguousarray(inputs["L0_attn_b_Wv"]),
        "wo_b": np.ascontiguousarray(inputs["L0_attn_b_Wo"]),
        "beta_in": beta, "wgT": wgT, "wuT": wuT, "wdT": wdT,
        "ident_in": ident, "mask_in": mask, "ones_in": ones,
    }


def kernel(**inputs):
    nc = build_program(_STAGE)
    in_maps = [_host_inputs(inputs, b) for b in range(8)]
    res = run_bass_kernel_spmd(nc, in_maps, core_ids=list(range(8)))
    out = np.stack([np.asarray(r["y"]) for r in res.results], axis=0)
    return out.astype(np.float32)


# version 09 (final)
# speedup vs baseline: 1.6610x; 1.0146x over previous
import sys
import numpy as np
import ml_dtypes
from contextlib import ExitStack
import concourse.bass as bass
import concourse.mybir as mybir
from concourse.bass_utils import run_bass_kernel_spmd

F32 = mybir.dt.float32
BF16 = mybir.dt.bfloat16
I32 = mybir.dt.int32
U8 = mybir.dt.uint8
AF = mybir.ActivationFunctionType
ALU = mybir.AluOpType
AX = mybir.AxisListType
BF = ml_dtypes.bfloat16

D = 2048
T = 2048
H = 16
FF = 8192
NT = 16
ISC = float(1.0 / np.sqrt(np.float64(128.0)))
NEGBIG = -1.0e9
LNEPS = 1e-5
QEPS = 1e-5
WDEN = float(D) * float(FF)
KB = 1024

_STAGE = 4
_DBG_ZB = False
_DBG_HOOKS = False
_DBG_NH = None
_DBG_NOOUT = False
_DBG_NH_B = None
_DBG_NOOUT_B = False
_DBG_NSEG = None
_DBG_NOSIG = False
_DBG_RACE = False


class PL:
    ENGS = ("sync", "scalar", "vector", "tensor", "gpsimd")
    SEMS = ("qC", "qB", "qX0", "qX1", "qL0", "qL1", "qP", "qT", "qZ", "qSt",
            "qG0", "qG1", "sPE", "sACT", "sDVE", "sGP")

    def __init__(self):
        self.ops = {e: [] for e in self.ENGS}
        self.cnt = {s: 0 for s in self.SEMS}
        self.ni = 0

    def w(self, eng, sem, val):
        if val is None or val <= 0:
            return
        assert val <= self.cnt[sem], (eng, sem, val, self.cnt[sem])
        self.ops[eng].append(("w", sem, val))

    def wd(self, eng, dep):
        if dep is None:
            return
        if isinstance(dep, tuple):
            dep = [dep]
        for s, v in dep:
            self.w(eng, s, v)

    def o(self, eng, fn, sem=None):
        v = None
        if sem is not None:
            self.cnt[sem] += 1
            v = self.cnt[sem]
        self.ops[eng].append(("o", fn, sem))
        self.ni += 1
        return v

    def d(self, eng, out, in_, sem):
        self.cnt[sem] += 16
        v = self.cnt[sem]
        self.ops[eng].append(("d", out, in_, sem))
        self.ni += 1
        return v


def _emit(P, nc, sems):
    with nc.Block() as block:
        def runner(name):
            def run(eng):
                for op in P.ops[name]:
                    if op[0] == "w":
                        eng.wait_ge(sems[op[1]], op[2])
                    elif op[0] == "d":
                        eng.dma_start(out=op[1], in_=op[2]).then_inc(sems[op[3]], 16)
                    else:
                        ins = op[1](eng)
                        if op[2] is not None:
                            ins.then_inc(sems[op[2]], 1)
            return run
        block.sync(runner("sync"))
        block.scalar(runner("scalar"))
        block.vector(runner("vector"))
        block.tensor(runner("tensor"))
        block.gpsimd(runner("gpsimd"))


def build_program(stage=_STAGE):
    nc = bass.Bass("TRN2", target_bir_lowering=False, detect_race_conditions=_DBG_RACE)

    DT = {}
    def din(name, shape, dt=F32):
        DT[name] = nc.dram_tensor(name, shape, dt, kind="ExternalInput").ap()
    din("x0", [T, D]); din("wq_a", [D, D]); din("wk_a", [D, D]); din("wv_a", [D, D])
    din("wo_a", [D, D]); din("bq_in", [128, H]); din("bk_in", [128, H])
    din("bv_in", [128, H]); din("bo_bc", [128, D])
    din("wq_b", [D, D]); din("wk_b", [D, D]); din("wv_b", [D, D]); din("wo_b", [D, D])
    din("beta_in", [128, D])
    din("wgT", [D, FF]); din("wuT", [D, FF]); din("wdT", [FF, D])
    din("ident_in", [128, 128], BF16); din("mask_in", [128, 128]); din("ones_in", [128, 128])
    y = nc.dram_tensor("y", [T, D], F32, kind="ExternalOutput").ap()
    xr = nc.dram_tensor("xr", [T, D], F32, kind="Internal").ap()
    zkind = "ExternalOutput" if _DBG_ZB else "Internal"
    zbank = nc.dram_tensor("zbank", [H * 128, T], BF16, kind=zkind).ap()
    abank = nc.dram_tensor("abank", [H * 128, T], BF16, kind=zkind).ap()
    if _DBG_HOOKS:
        dbgA = nc.dram_tensor("dbgA", [128, 32 * KB], BF16, kind="ExternalOutput").ap()
        dbgq = nc.dram_tensor("dbgq", [128, T], BF16, kind="ExternalOutput").ap()
        dbgk = nc.dram_tensor("dbgk", [128, T], BF16, kind="ExternalOutput").ap()
        dbgv = nc.dram_tensor("dbgv", [128, T], BF16, kind="ExternalOutput").ap()
        dbgp = nc.dram_tensor("dbgp", [128, T], BF16, kind="ExternalOutput").ap()
    tg = nc.dram_tensor("tg", [D, FF], BF16, kind="Internal").ap()
    tu = nc.dram_tensor("tu", [D, FF], BF16, kind="Internal").ap()
    td = nc.dram_tensor("td", [FF, D], BF16, kind="Internal").ap()

    base = (nc.sbuf_base + 31) & ~31
    assert nc.sbuf_top - base >= 206 * KB, (base, nc.sbuf_top)

    def at(name, off_b, shape, dt):
        assert off_b % 32 == 0, (name, off_b)
        h = nc.alloc_sbuf_tensor_at(name, shape, dt, offset=base + off_b)
        return h

    P2 = 64 * KB
    EO = 172 * KB
    SM = EO + 32 * KB
    A = at("A", 0, [128, 32 * KB], BF16)
    WOQ = [at("WOQ0", P2, [128, 8 * KB], BF16), at("WOQ1", P2 + 16 * KB, [128, 8 * KB], BF16)]
    Qt = at("Qt", P2 + 32 * KB, [128, 2 * KB], BF16)
    Kt = at("Kt", P2 + 36 * KB, [128, 2 * KB], BF16)
    VtT = at("VtT", P2 + 40 * KB, [128, 2 * KB], BF16)
    Vtok = at("Vtok", P2 + 44 * KB, [128, 2 * KB], BF16)
    Pbuf = at("Pbuf", P2 + 48 * KB, [128, 2 * KB], BF16)
    PbufB = at("pbuf2", P2 + 104 * KB, [128, 2 * KB], BF16)
    Ptb = at("Ptb", P2 + 52 * KB, [128, 2 * KB], BF16)
    Ztb = at("Ztb", P2 + 56 * KB, [128, 2 * KB], BF16)
    ZSL = [at("zsl0", P2 + 76 * KB, [128, 2 * KB], BF16), at("zsl1", P2 + 80 * KB, [128, 2 * KB], BF16)]
    bObc = at("bObc", P2 + 68 * KB, [128, 2 * KB], F32)
    Wqh = at("Wqh", P2 + 76 * KB, [128, 2 * KB], BF16)
    Wkh = at("Wkh", P2 + 80 * KB, [128, 2 * KB], BF16)
    Wvh = at("Wvh", P2 + 84 * KB, [128, 2 * KB], BF16)
    Wq2 = at("Wq2", P2 + 0 * KB, [128, 2 * KB], BF16)
    Wk2 = at("Wk2", P2 + 4 * KB, [128, 2 * KB], BF16)
    Wv2 = at("Wv2", P2 + 8 * KB, [128, 2 * KB], BF16)
    ZTOK = at("ztok", P2 + 60 * KB, [128, 256], BF16)
    PRES = at("pres", P2 + 88 * KB, [128, 2 * KB], F32)
    PRES2 = at("pres2", P2 + 16 * KB, [128, 2 * KB], F32)
    PRES3 = at("pres3", P2 + 24 * KB, [128, 2 * KB], F32)
    PRELO = at("prelo", P2 + 96 * KB, [128, 2 * KB], BF16)
    PREOUT = at("preout", P2 + 100 * KB, [128, 2 * KB], BF16)
    PREOUT2 = at("preout2", P2 + 12 * KB, [128, 2 * KB], BF16)
    YTO = [at(f"yto{i}", P2 + (32 + 8 * i) * KB, [128, 2 * KB], F32) for i in range(4)]
    HSL = [at(f"hsl{i}", P2 + 16 * i * KB, [128, 8 * KB], BF16) for i in range(4)]
    TGU = [at("tgu0", P2 + 64 * KB, [128, 8 * KB], BF16), at("tgu1", P2 + 80 * KB, [128, 8 * KB], BF16)]
    TDQ = [at("tdq0", P2 + 96 * KB, [128, 2 * KB], BF16), at("tdq1", P2 + 100 * KB, [128, 2 * KB], BF16)]
    GSCR = at("gscr", P2 + 104 * KB, [128, 256], F32)
    RHC = at("rhc", P2 + 105 * KB, [128, 1 * KB], BF16)
    YTF = [at("ytf0", P2 + 64 * KB, [128, 2 * KB], F32), at("ytf1", P2 + 72 * KB, [128, 2 * KB], F32),
           at("ytf2", P2 + 80 * KB, [128, 2 * KB], F32), at("ytf3", P2 + 88 * KB, [128, 2 * KB], F32)]
    SqT = at("SqT", P2 + 48 * KB, [128, 2 * KB], BF16)
    SkT = at("SkT", P2 + 88 * KB, [128, 2 * KB], BF16)
    ATTH = at("ATTH", P2 + 92 * KB, [128, 2 * KB], BF16)
    SS0 = at("ss0", P2 + 96 * KB, [128, 512], F32)
    SS1 = at("ss1", P2 + 98 * KB, [128, 512], F32)
    CMB = at("cmb", P2 + 100 * KB, [128, 256], F32)
    PbB = at("pbB", P2 + 101 * KB, [128, 512], BF16)
    PtB = at("ptB", P2 + 102 * KB, [128, 512], BF16)
    MEMF = at("memf", P2 + 103 * KB, [128, 128], F32)
    MEMZ = at("memz", P2 + 103 * KB + 512, [128, 132], BF16)
    NGD = at("ngd", P2 + 103 * KB + 800, [128, 128], BF16)
    SKTOK = at("sktok", P2 + 103 * KB + 1056, [128, 128], BF16)
    BETA = at("beta", P2 + 68 * KB, [128, 2 * KB], F32)
    XO = [at("xo0", EO, [128, 2 * KB], F32), at("xo1", EO + 8 * KB, [128, 2 * KB], F32)]
    YTE = at("ytE", EO + 16 * KB, [128, 2 * KB], F32)
    I32S = at("i32s", EO + 24 * KB, [128, 1 * KB], I32)
    XNB = at("xnb", EO + 28 * KB, [128, 2 * KB], BF16)
    MASK = at("maskT", SM, [128, 128], F32)
    IDT = at("idT", SM + 512, [128, 128], BF16)
    ONESF = at("onesT", SM + 768, [128, 128], F32)
    BQC = at("bqc", SM + 1280, [128, H], F32)
    BKC = at("bkc", SM + 1344, [128, H], F32)
    BVC = at("bvc", SM + 1408, [128, H], F32)
    SX = at("sx", SM + 1472, [128, NT], F32)
    STAT = at("stat", SM + 1536, [128, 32], F32)
    WSUM = at("wsum", SM + 1664, [128, 4], F32)
    WSC = at("wsc", SM + 1696, [128, 4], F32)
    THR = at("thr", SM + 1728, [128, 4], F32)
    NTHR = at("nthr", SM + 1760, [128, 4], F32)
    SHD = at("shd", SM + 1792, [128, 4], F32)
    SGU = at("sgu", SM + 1824, [128, 8], F32)
    HMX = at("hmx", SM + 1856, [128, 4], F32)
    ZINC = at("zinc", SM + 1888, [128, 4], F32)
    CEPS = at("ceps", SM + 1920, [128, 4], F32)
    DUMP = at("dump", SM + 1952, [128, 64], F32)
    HMXW = at("hmxw", SM + 2208, [128, 128], F32)

    pctx = ExitStack()
    psA = pctx.enter_context(nc.psum_tensor("psA", [128, 2048], F32))
    psB = pctx.enter_context(nc.psum_tensor("psB", [128, 1024], F32))
    psT = pctx.enter_context(nc.psum_tensor("psT", [128, 1024], BF16))
    psU = pctx.enter_context(nc.psum_tensor("psU", [128, 1024], BF16))
    sctx = ExitStack()
    sems = {s: sctx.enter_context(nc.semaphore(s)) for s in PL.SEMS}

    P = PL()
    K = {
        "bkg": {b: None for b in
                ("A0", "A1", "A2", "A3", "B0", "B1")},
        "bkw": {b: None for b in
                ("A0", "A1", "A2", "A3", "B0", "B1")},
        "psT_i": 0,
        "psT_g": [None, None],
        "xo_g": [None, None],
        "a_rdy": [None] * NT,
        "sx_rdy": [None] * NT,
        "spair_i": 0,
        "tern_st": {m: [0] * 32 for m in range(2)},
        "td_all": 0,
        "thr_rdy": None,
        "zb_st": 0,
        "ab_st": 0,
        "xnb_g": None,
    }

    def bank_ap(name):
        if name.startswith("A"):
            i = int(name[1])
            return psA[:, i * 512:(i + 1) * 512]
        if name.startswith("B"):
            i = int(name[1])
            return psB[:, i * 512:(i + 1) * 512]
        raise KeyError(name)

    def bank_wait_w(eng, names):
        for n in names:
            P.wd(eng, K["bkg"][n])

    def bank_set_w(names, val):
        for n in names:
            K["bkw"][n] = ("sPE", val)

    def bank_wait_r(eng, names):
        for n in names:
            P.wd(eng, K["bkw"][n])

    def bank_set_r(names, dep):
        for n in names:
            old = K["bkg"][n]
            deps = [dep] if isinstance(dep, tuple) else list(dep)
            if old:
                deps = deps + ([old] if isinstance(old, tuple) else list(old))
            best = {}
            for s, v in deps:
                best[s] = max(best.get(s, 0), v)
            K["bkg"][n] = [(s, v) for s, v in best.items()]

    def transpose_to(src_ap, dst_ap, pe_dep=None, act_dep=None):
        i = K["psT_i"] % 2
        K["psT_i"] += 1
        slot = (psT if i == 0 else psU)[:, 0:128]
        P.wd("tensor", K["psT_g"][i])
        P.wd("tensor", pe_dep)
        pe = P.o("tensor", lambda t, s=slot, a=src_ap: t.transpose(s, a, IDT[:]), "sPE")
        P.w("scalar", "sPE", pe)
        P.wd("scalar", act_dep)
        ac = P.o("scalar", lambda s, d=dst_ap, sl=slot: s.copy(d, sl), "sACT")
        K["psT_g"][i] = ("sACT", ac)
        return ac

    def transpose_group(srcs, dst_ap, pe_dep=None, act_dep=None, eng="scalar"):
        n = len(srcs)
        i = K["psT_i"] % 2
        K["psT_i"] += 1
        bank = psT if i == 0 else psU
        P.wd("tensor", K["psT_g"][i])
        P.wd("tensor", pe_dep)
        pe = None
        for k, s_ap in enumerate(srcs):
            pe = P.o("tensor", lambda t, sl=bank[:, k * 128:(k + 1) * 128], a=s_ap:
                     t.transpose(sl, a, IDT[:]), "sPE" if k == n - 1 else None)
        P.w(eng, "sPE", pe)
        P.wd(eng, act_dep)
        if eng == "scalar":
            ac = P.o("scalar", lambda s, d=dst_ap, sl=bank[:, 0:n * 128]: s.copy(d, sl),
                     "sACT")
            dep = ("sACT", ac)
        else:
            ac = P.o("vector", lambda v, d=dst_ap, sl=bank[:, 0:n * 128]:
                     v.tensor_copy(d, sl), "sDVE")
            dep = ("sDVE", ac)
        K["psT_g"][i] = dep
        return dep

    ident = DT["ident_in"]; maskin = DT["mask_in"]; onesin = DT["ones_in"]

    cl = P.d("sync", IDT[:], ident, "qC")
    cl = P.d("sync", MASK[:], maskin, "qC")
    cl = P.d("sync", ONESF[:], onesin, "qC")
    cl = P.d("sync", BQC[:], DT["bq_in"], "qC")
    cl = P.d("sync", BKC[:], DT["bk_in"], "qC")
    cl = P.d("sync", BVC[:], DT["bv_in"], "qC")
    consts_rdy = cl
    P.o("vector", lambda v: v.memset(CEPS[:, 0:1], 1e-5))
    P.o("vector", lambda v: v.memset(CEPS[:, 1:2], 5e-6))
    cinit = P.o("vector", lambda v: v.memset(CEPS[:, 2:3], -5e-6), "sDVE")
    P.w("scalar", "sDVE", cinit)

    def sep(eng, n=1):
        for _ in range(n):
            if eng == "scalar":
                P.o("scalar", lambda s: s.activation(DUMP[:, 0:64], MASK[:, 0:64], AF.Copy))
            elif eng == "vector":
                P.o("vector", lambda v: v.tensor_copy(DUMP[:, 0:64], MASK[:, 0:64]))
            else:
                P.o("gpsimd", lambda g: g.tensor_copy(DUMP[:, 0:64], MASK[:, 0:64]))

    def finish_tile(t, yth, yt_dep, xo, dst_tile, prep, scratch_f32):
        if yth is not None:
            P.wd("vector", yt_dep)
            v1 = P.o("vector", lambda v, a=yth, b=xo: v.tensor_tensor(b[:], a[:], b[:], ALU.add), "sDVE")
        else:
            P.wd("vector", yt_dep)
            v1 = None
        if dst_tile is not None:
            if v1:
                P.w("sync", "sDVE", v1)
            st = P.d("sync", dst_tile, xo[:], "qSt")
        else:
            st = None
        v2 = P.o("vector", lambda v, b=xo: v.tensor_reduce(STAT[:, 0:1], b[:], AX.X, ALU.add), "sDVE")
        P.w("gpsimd", "sDVE", v2)
        g1 = P.o("gpsimd", lambda g: g.tensor_single_scalar(
            STAT[:, 1:2], STAT[:, 0:1], -1.0 / D, ALU.mult), "sGP")
        P.w("scalar", "sGP", g1)
        P.o("scalar", lambda s, b=xo, d=scratch_f32: s.activation(
            d[:], b[:], AF.Square, bias=STAT[:, 1:2], accum_out=STAT[:, 2:3]))
        sep("scalar", 2)
        a3 = P.o("scalar", lambda s: s.activation(STAT[:, 7:8], STAT[:, 2:3], AF.Sqrt,
                                                  scale=1.0 / D, bias=CEPS[:, 0:1]), "sACT")
        P.w("vector", "sACT", a3)
        P.o("vector", lambda v: v.reciprocal(STAT[:, 3:4], STAT[:, 7:8]))
        sep("vector", 2)
        rdr = [("sACT", a3)]
        if prep == "xnT":
            P.wd("vector", K["xnb_g"])
            v3 = P.o("vector", lambda v, b=xo: v.tensor_scalar(
                XNB[:], b[:], STAT[:, 1:2], STAT[:, 3:4], ALU.add, ALU.mult), "sDVE")
            rdr.append(("sDVE", v3))
            last = None
            Av = A[:].rearrange("p (c x) -> p c x", x=2048)
            for c0 in range(0, 16, 4):
                srcs = [XNB[:, c * 128:(c + 1) * 128] for c in range(c0, c0 + 4)]
                last = transpose_group(srcs, Av[:, c0:c0 + 4, t * 128:(t + 1) * 128],
                                       pe_dep=("sDVE", v3) if c0 == 0 else None)
            K["a_rdy"][t] = last
            K["xnb_g"] = last
        elif prep == "R":
            P.wd("vector", K["xnb_g"])
            v3 = P.o("vector", lambda v, b=xo, d=scratch_f32: v.tensor_scalar(
                d[:], b[:], STAT[:, 1:2], STAT[:, 3:4], ALU.add, ALU.mult), "sDVE")
            v4 = P.o("vector", lambda v, d=scratch_f32: v.tensor_reduce(
                STAT[:, 4:5], d[:], AX.X, ALU.max, apply_absolute_value=True), "sDVE")
            sep("vector", 2)
            P.o("vector", lambda v: v.tensor_single_scalar(
                STAT[:, 5:6], STAT[:, 4:5], QEPS, ALU.max))
            sep("vector", 2)
            rc = P.o("vector", lambda v: v.reciprocal(STAT[:, 8:9], STAT[:, 5:6]), "sDVE")
            P.w("gpsimd", "sDVE", rc)
            v5 = P.o("gpsimd", lambda g: g.tensor_single_scalar(
                STAT[:, 6:7], STAT[:, 8:9], 127.0, ALU.mult), "sGP")
            rdr.append(("sDVE", v4))
            P.w("scalar", "sGP", v5)
            sxr = P.o("scalar", lambda s, tt=t: s.activation(SX[:, tt:tt + 1], STAT[:, 5:6],
                                                             AF.Copy, scale=1.0 / 127.0), "sACT")
            K["sx_rdy"][t] = ("sACT", sxr)
            last = None
            Av = A[:].rearrange("p (c x) -> p c x", x=2048)
            for half in range(2):
                P.o("scalar", lambda s, d=scratch_f32, q=half: s.activation(
                    I32S[:], d[:, q * 1024:(q + 1) * 1024], AF.Copy, scale=STAT[:, 6:7]))
                ra = P.o("scalar", lambda s, q=half: s.activation(
                    XNB[:, q * 1024:(q + 1) * 1024], I32S[:], AF.Copy), "sACT")
                for cc in range(0, 8, 4):
                    c0 = half * 8 + cc
                    srcs = [XNB[:, c * 128:(c + 1) * 128] for c in range(c0, c0 + 4)]
                    last = transpose_group(srcs, Av[:, c0:c0 + 4, t * 128:(t + 1) * 128],
                                           pe_dep=("sACT", ra) if cc == 0 else None)
            K["a_rdy"][t] = last
            K["xnb_g"] = last
        rdr2 = list(rdr)
        if st is not None:
            rdr2.append(("qSt", st))
        return st, rdr2

    def prologue():
        for e in ("tensor", "scalar", "vector", "gpsimd"):
            P.w(e, "qC", consts_rdy)
        for t in range(NT):
            par = t % 2
            P.wd("sync", K["xo_g"][par])
            qx = "qX0" if par == 0 else "qX1"
            ld = P.d("sync", XO[par][:], DT["x0"][t * 128:(t + 1) * 128, :], qx)
            _, rdr = finish_tile(t, None, [(qx, ld)], XO[par], None, "xnT", YTE)
            K["xo_g"][par] = rdr

    WSRC = [DT["wgT"], DT["wuT"], DT["wdT"]]
    TDST = [tg, tu, td]

    def w_tile_aps(m, db, fb):
        src = WSRC[m][db * 1024:(db + 1) * 1024, fb * 256:(fb + 1) * 256]
        src = src.rearrange("(c p) f -> p c f", p=128)
        dst = TDST[m][db * 1024:(db + 1) * 1024, fb * 256:(fb + 1) * 256]
        dst = dst.rearrange("(c p) f -> p c f", p=128)
        return src, dst

    pp_state = {"stage_g": None, "lo_g": None, "out_g": None, "wsum_last": [None] * 3,
                "pi": 0, "poi": 0, "ld_pend": [], "stage_g2": [None, None, None],
                "out_g2": [None, None]}
    PRESB = [PRES, PRES2, PRES3]
    PREOUTB = [PREOUT, PREOUT2]

    def pp_flush():
        while pp_state["ld_pend"]:
            pp_state["ld_pend"].pop(0)()

    def mean_units():
        units = []
        for m in range(3):
            nb = (2, 32) if m < 2 else (8, 8)
            zi = P.o("gpsimd", lambda g, mm=m: g.memset(WSUM[:, mm:mm + 1], 0.0), "sGP")
            pp_state["wsum_last"][m] = ("sGP", zi)
            for db in range(nb[0]):
                for fb in range(nb[1]):
                    units.append((m, db, fb))
        def compute(u, p, ld):
            m, db, fb = u
            P.w("scalar", "qP", ld)
            P.wd("scalar", pp_state.get("gv_g"))
            aa = P.o("scalar", lambda s, pp=p: s.activation(
                PRELO[:], PRESB[pp][:], AF.Abs, accum_out=STAT[:, 25:26]), "sACT")
            P.w("gpsimd", "sACT", aa)
            gv = P.o("gpsimd", lambda g, mm=m: g.tensor_tensor(
                WSUM[:, mm:mm + 1], WSUM[:, mm:mm + 1], STAT[:, 25:26], ALU.add), "sGP")
            pp_state["gv_g"] = ("sGP", gv)
            pp_state["stage_g2"][p] = [("sACT", aa)]
            pp_state["lo_g"] = [("sACT", aa)]
            pp_state["wsum_last"][m] = ("sGP", gv)
        def emit(u):
            m, db, fb = u
            src, _ = w_tile_aps(m, db, fb)
            p = pp_state["pi"] % 3
            pp_state["pi"] += 1
            P.wd("sync", pp_state["stage_g2"][p])
            ld = P.d("sync", PRESB[p][:].rearrange("p (c f) -> p c f", f=256), src, "qP")
            pp_state["ld_pend"].append(lambda uu=u, pp=p, ll=ld: compute(uu, pp, ll))
            if len(pp_state["ld_pend"]) > 2:
                pp_state["ld_pend"].pop(0)()
        return units, emit

    def wscale_finalize():
        last = None
        for m in range(3):
            bank_wait_w("tensor", ["B1"])
            P.wd("tensor", pp_state["wsum_last"][m])
            pe = P.o("tensor", lambda t, mm=m: t.matmul(
                psB[:, 512 + mm:512 + mm + 1], ONESF[:], WSUM[:, mm:mm + 1],
                start=True, stop=True), "sPE")
            bank_set_w(["B1"], pe)
            P.w("scalar", "sPE", pe)
            P.o("scalar", lambda s, mm=m: s.activation(WSC[:, mm:mm + 1],
                psB[:, 512 + mm:512 + mm + 1], AF.Identity, scale=1.0 / WDEN, bias=CEPS[:, 0:1]))
            P.o("scalar", lambda s, mm=m: s.activation(THR[:, mm:mm + 1],
                psB[:, 512 + mm:512 + mm + 1], AF.Identity, scale=0.5 / WDEN, bias=CEPS[:, 1:2]))
            last = P.o("scalar", lambda s, mm=m: s.activation(NTHR[:, mm:mm + 1],
                psB[:, 512 + mm:512 + mm + 1], AF.Identity, scale=-0.5 / WDEN,
                bias=CEPS[:, 2:3]), "sACT")
            bank_set_r(["B1"], ("sACT", last))
        K["thr_rdy"] = ("sACT", last)

    def tern_units():
        units = []
        for m in range(3):
            nb = (2, 32) if m < 2 else (8, 8)
            for fb in range(nb[1]):
                for db in range(nb[0]):
                    units.append((m, db, fb, db == nb[0] - 1))
        def compute(u, p, ld):
            m, db, fb, is_last_db = u
            _, dst = w_tile_aps(m, db, fb)
            po = pp_state["poi"] % 2
            pp_state["poi"] += 1
            P.w("vector", "qP", ld)
            P.wd("vector", K["thr_rdy"])
            P.wd("vector", pp_state["out_g2"][po])
            P.o("vector", lambda v, mm=m, pp=p: v.tensor_single_scalar(
                PRELO[:], PRESB[pp][:], NTHR[:, mm:mm + 1], ALU.is_le))
            v1 = P.o("vector", lambda v, mm=m, pp=p, qq=po: v.scalar_tensor_tensor(
                PREOUTB[qq][:], PRESB[pp][:], THR[:, mm:mm + 1], PRELO[:],
                ALU.is_ge, ALU.subtract), "sDVE")
            P.w("sync", "sDVE", v1)
            st = P.d("sync", dst, PREOUTB[po][:].rearrange("p (c f) -> p c f", f=256),
                     "qT")
            pp_state["stage_g2"][p] = [("sDVE", v1)]
            pp_state["out_g2"][po] = [("qT", st)]
            if m < 2 and is_last_db:
                K["tern_st"][m][fb] = st
            if m == 2 and is_last_db and fb == 7:
                K["td_all"] = st
        def emit(u):
            m, db, fb, is_last_db = u
            src, _ = w_tile_aps(m, db, fb)
            p = pp_state["pi"] % 3
            pp_state["pi"] += 1
            P.wd("sync", pp_state["stage_g2"][p])
            ld = P.d("sync", PRESB[p][:].rearrange("p (c f) -> p c f", f=256), src, "qP")
            pp_state["ld_pend"].append(lambda uu=u, pp=p, ll=ld: compute(uu, pp, ll))
            if len(pp_state["ld_pend"]) > 2:
                pp_state["ld_pend"].pop(0)()
        return units, emit

    def load_head_weights(h, wsrcs, bufs, guard_pe):
        outs = []
        for src, buf in zip(wsrcs, bufs):
            P.wd("gpsimd", guard_pe)
            sap = src.rearrange("(c p) e -> p c e", p=128)[:, :, h * 128:(h + 1) * 128]
            dap = buf[:].rearrange("p (c e) -> p c e", e=128)
            outs.append(P.d("gpsimd", dap, sap, "qG0"))
        return outs[-1]

    def proj4(wbuf, out_sb, bias_col, wdep, sig_out=None, zinc=False):
        rdy = None
        sig_rdy = None
        for tc in range(4):
            bk = "B0" if tc % 2 == 0 else "B1"
            bank_wait_w("tensor", [bk])
            if tc == 0:
                P.w("tensor", "qG0", wdep)
            P.wd("tensor", K["a_rdy"][tc * 4 + 3])
            P.wd("tensor", K["a_rdy"][tc * 4 + 2])
            P.wd("tensor", K["a_rdy"][tc * 4 + 1])
            P.wd("tensor", K["a_rdy"][tc * 4 + 0])
            pe = None
            for c in range(16):
                pe = P.o("tensor", lambda t, cc=c, w=wbuf, b=bk, tcc=tc: t.matmul(
                    bank_ap(b), w[:, cc * 128:(cc + 1) * 128],
                    A[:, cc * 2048 + tcc * 512: cc * 2048 + (tcc + 1) * 512],
                    start=(cc == 0), stop=(cc == 15)), "sPE" if c == 15 else None)
            bank_set_w([bk], pe)
            P.w("scalar", "sPE", pe)
            if bias_col is not None:
                ac = P.o("scalar", lambda s, o=out_sb, b=bk, tcc=tc, bc=bias_col: s.activation(
                    o[:, tcc * 512:(tcc + 1) * 512], bank_ap(b), AF.Identity, bias=bc), "sACT")
            else:
                ac = P.o("scalar", lambda s, o=out_sb, b=bk, tcc=tc: s.copy(
                    o[:, tcc * 512:(tcc + 1) * 512], bank_ap(b)), "sACT")
            deps = [("sACT", ac)]
            if sig_out is not None:
                P.w("vector", "sPE", pe)
                P.w("vector", "sACT", ac)
                sm0 = P.o("vector", lambda v, b=bk: v.tensor_single_scalar(
                    SS0[:, 0:512], bank_ap(b), 0.0, ALU.min), "sDVE")
                P.w("scalar", "sDVE", sm0)
                e1 = P.o("scalar", lambda s: s.activation(SS1[:, 0:512], SS0[:, 0:512],
                                                          AF.Exp), "sACT")
                P.w("vector", "sACT", e1)
                if zinc:
                    sv = P.o("vector", lambda v, o=sig_out, b=bk, tcc=tc: v.scalar_tensor_tensor(
                        o[:, tcc * 512:(tcc + 1) * 512], bank_ap(b), 0.0, SS1[:, 0:512],
                        ALU.max, ALU.add, accum_out=ZINC[:, tcc:tcc + 1]), "sDVE")
                else:
                    sv = P.o("vector", lambda v, o=sig_out, b=bk, tcc=tc: v.scalar_tensor_tensor(
                        o[:, tcc * 512:(tcc + 1) * 512], bank_ap(b), 0.0, SS1[:, 0:512],
                        ALU.max, ALU.add), "sDVE")
                deps.append(("sDVE", sv))
                sig_rdy = sv
            bank_set_r([bk], deps)
            rdy = ac
        return rdy, sig_rdy

    def vproj(wbuf, wdep, bias_col):
        vr, _ = proj4(wbuf, VtT, bias_col, wdep)
        last = None
        for t0 in range(0, NT, 4):
            srcs = [VtT[:, tt * 128:(tt + 1) * 128] for tt in range(t0, t0 + 4)]
            last = transpose_group(srcs, Vtok[:, t0 * 128:(t0 + 4) * 128],
                                   pe_dep=("sACT", vr) if t0 == 0 else None)
        return last

    def attn_a(pp_iter):
        bol = P.d("sync", bObc[:], DT["bo_bc"], "qC")
        woq_guard = [None, None]
        zsl_guard = [None, None]
        wq_guard = None
        qkv_rdy = None
        nheads = _DBG_NH if _DBG_NH is not None else H
        wsrcs = (DT["wq_a"], DT["wk_a"], DT["wv_a"])
        wbufs = [(Wqh, Wkh, Wvh), (Wq2, Wk2, Wv2)]
        wdeps = {}
        K["zl_hist"] = [None, None]
        for h in range(nheads):
            if h == 0:
                wdeps[0] = load_head_weights(0, wsrcs, wbufs[0], None)
                if nheads > 1:
                    wdeps[1] = load_head_weights(1, wsrcs, wbufs[1], None)
            elif h + 1 < nheads:
                wdeps[h + 1] = load_head_weights(h + 1, wsrcs, wbufs[(h + 1) % 2],
                                                 wq_guard)
            wdep = wdeps.pop(h)
            wb = wbufs[h % 2]
            qr, _ = proj4(wb[0], Qt, BQC[:, h:h + 1], wdep)
            kr, _ = proj4(wb[1], Kt, BKC[:, h:h + 1], wdep)
            vr = vproj(wb[2], wdep, BVC[:, h:h + 1])
            wq_guard = ("sPE", P.cnt["sPE"])
            if _DBG_HOOKS and h == 0:
                P.w("sync", "sACT", qr)
                P.w("sync", "sACT", kr)
                P.wd("sync", vr)
                P.d("sync", dbgA, A[:], "qSt")
                P.d("sync", dbgq, Qt[:], "qSt")
                P.d("sync", dbgk, Kt[:], "qSt")
                P.d("sync", dbgv, Vtok[:], "qSt")
            zlast = None

            def emit_scores_softmax(i):
                pb = Pbuf if i % 2 == 0 else PbufB
                so = 8 + (i % 2) * 8
                klen = (i + 1) * 128
                chunks = [(0, min(klen, 1024))]
                if klen > 1024:
                    chunks.append((1024, klen - 1024))
                nch = len(chunks)
                if K["zl_hist"][i % 2] is not None:
                    P.w("vector", "sACT", K["zl_hist"][i % 2])
                cinfo = []
                for ci, (c0, wdt) in enumerate(chunks):
                    pair = K["spair_i"] % 2
                    K["spair_i"] += 1
                    pbanks = ["A0", "A1"] if pair == 0 else ["A2", "A3"]
                    pb0 = pair * 1024
                    bank_wait_w("tensor", pbanks)
                    if i == 0 and ci == 0:
                        P.w("tensor", "sACT", qr)
                        P.w("tensor", "sACT", kr)
                        P.wd("tensor", vr)
                    pe = None
                    nmm = (wdt + 511) // 512
                    for mi in range(nmm):
                        mw = min(512, wdt - mi * 512)
                        pe = P.o("tensor", lambda t, ii=i, b0=pb0, m=mi, w=mw, cc0=c0: t.matmul(
                            psA[:, b0 + m * 512: b0 + m * 512 + w],
                            Qt[:, ii * 128:(ii + 1) * 128],
                            Kt[:, cc0 + m * 512: cc0 + m * 512 + w],
                            start=True, stop=True), "sPE" if mi == nmm - 1 else None)
                    bank_set_w(pbanks, pe)
                    P.w("vector", "sPE", pe)
                    if c0 <= i * 128 < c0 + wdt:
                        off = pb0 + i * 128 - c0
                        P.o("vector", lambda v, o=off: v.tensor_tensor(
                            psA[:, o:o + 128], psA[:, o:o + 128], MASK[:], ALU.add))
                    vv = P.o("vector", lambda v, b0=pb0, w=wdt, nm=so + ci: v.tensor_reduce(
                        STAT[:, nm:nm + 1], psA[:, b0:b0 + w], AX.X, ALU.max, negate=True),
                        "sDVE")
                    cinfo.append((c0, wdt, pb0, pbanks, vv))
                if nch == 2:
                    sep("vector", 2)
                    gdep = P.o("vector", lambda v, s_=so: v.tensor_tensor(
                        STAT[:, s_ + 2:s_ + 3], STAT[:, s_:s_ + 1],
                        STAT[:, s_ + 1:s_ + 2], ALU.min), "sDVE")
                    gcol = so + 2
                else:
                    gdep = cinfo[0][4]
                    gcol = so
                P.w("gpsimd", "sDVE", gdep)
                gb = P.o("gpsimd", lambda g, gc=gcol, bm=so + 3: g.tensor_single_scalar(
                    STAT[:, bm:bm + 1], STAT[:, gc:gc + 1], ISC, ALU.mult), "sGP")
                P.w("scalar", "sGP", gb)
                exl = None
                for ci, (c0, wdt, pb0, pbanks, vv) in enumerate(cinfo):
                    exl = P.o("scalar", lambda s, b0=pb0, w=wdt, cc0=c0, bm=so + 3,
                              lc=so + 4 + ci, pb_=pb:
                              s.activation(pb_[:, cc0:cc0 + w], psA[:, b0:b0 + w], AF.Exp,
                                           bias=STAT[:, bm:bm + 1], scale=ISC,
                                           accum_out=STAT[:, lc:lc + 1]), "sACT")
                    bank_set_r(pbanks, [("sACT", exl), ("sDVE", vv)])
                def mk_recip():
                    P.w("vector", "sACT", exl)
                    if nch == 2:
                        P.o("vector", lambda v, s_=so: v.tensor_tensor(
                            STAT[:, s_ + 6:s_ + 7], STAT[:, s_ + 4:s_ + 5],
                            STAT[:, s_ + 5:s_ + 6], ALU.add))
                        sep("vector", 2)
                        return P.o("vector", lambda v, s_=so: v.reciprocal(
                            STAT[:, s_ + 7:s_ + 8], STAT[:, s_ + 6:s_ + 7]), "sDVE")
                    return P.o("vector", lambda v, s_=so: v.reciprocal(
                        STAT[:, s_ + 7:s_ + 8], STAT[:, s_ + 4:s_ + 5]), "sDVE")
                return exl, pb, so + 7, mk_recip

            def emit_ztrans(zp):
                nonlocal zlast
                zi, ev1 = zp
                zlast = transpose_to(ZTOK[:, (zi % 2) * 128:(zi % 2) * 128 + 128],
                                     Ztb[:, zi * 128:(zi + 1) * 128],
                                     pe_dep=("sACT", ev1),
                                     act_dep=("qZ", K["zb_st"]) if zi == 0 else None)

            def emit_trans_av(i, exl, pb, rcol, mk_recip):
                q4 = i % 4
                if K["zpend"] is not None:
                    emit_ztrans(K["zpend"])
                    K["zpend"] = None
                pts = {}
                for gi, j0 in enumerate(range(0, i + 1, 4)):
                    n = min(4, i + 1 - j0)
                    srcs = [pb[:, j * 128:(j + 1) * 128] for j in range(j0, j0 + n)]
                    pt = transpose_group(srcs, Ptb[:, j0 * 128:(j0 + n) * 128],
                                         pe_dep=("sACT", exl) if j0 == 0 else None,
                                         eng="scalar" if gi % 2 == 0 else "vector")
                    pts[pt[0]] = pt[1]
                rc = mk_recip()
                bank_wait_w("tensor", ["B0"])
                for sem_, val_ in pts.items():
                    P.w("tensor", sem_, val_)
                for j in range(i + 1):
                    P.o("tensor", lambda t, jj=j, q=q4, ii=i: t.matmul(
                        psB[:, q * 128: (q + 1) * 128],
                        Ptb[:, jj * 128:(jj + 1) * 128],
                        Vtok[:, jj * 128:(jj + 1) * 128],
                        start=(jj == 0), stop=(jj == ii)),
                        "sPE" if j == i else None)
                avdone = P.cnt["sPE"]
                bank_set_w(["B0"], avdone)
                P.w("scalar", "sPE", avdone)
                P.w("scalar", "sDVE", rc)
                ev1 = P.o("scalar", lambda s, ii=i, q=q4, rcc=rcol: s.activation(
                    ZTOK[:, (ii % 2) * 128:(ii % 2) * 128 + 128],
                    psB[:, q * 128: (q + 1) * 128], AF.Identity,
                    scale=STAT[:, rcc:rcc + 1]), "sACT")
                bank_set_r(["B0"], ("sACT", ev1))
                K["zl_hist"][i % 2] = ev1
                K["zpend"] = (i, ev1)

            pend = None
            K["zpend"] = None
            for i in range(NT):
                info = emit_scores_softmax(i)
                if pend is not None:
                    emit_trans_av(*pend)
                pend = (i,) + info
            emit_trans_av(*pend)
            emit_ztrans(K["zpend"])
            K["zpend"] = None
            if _DBG_HOOKS and h == 0:
                P.w("sync", "sDVE", P.cnt["sDVE"])
                P.d("sync", dbgp, Pbuf[:], "qSt")
            P.w("sync", "sACT", zlast)
            K["zb_st"] = P.d("sync", zbank[h * 128:(h + 1) * 128, :], Ztb[:], "qZ")
            for _ in range(pp_iter.get("per_head", 0)):
                if pp_iter["units"]:
                    pp_iter["emit"](pp_iter["units"].pop(0))
                elif pp_iter.get("next"):
                    nxt = pp_iter.pop("next")
                    pp_flush()
                    nxt[0]()
                    pp_iter["units"], pp_iter["emit"] = nxt[1]
        if pp_iter.get("next"):
            nxt = pp_iter.pop("next")
            pp_flush()
            nxt[0]()
            pp_iter["units"], pp_iter["emit"] = nxt[1]
        while pp_iter["units"]:
            pp_iter["emit"](pp_iter["units"].pop(0))
        pp_flush()

        if _DBG_NOOUT:
            return
        obank = ["A0", "A1", "A2", "A3"]
        yq_rdy = [[None] * 4 for _ in range(4)]
        yto_g = [None] * 4
        wls = {}
        zls = {}
        def load_woq(g, dq):
            wpar = dq % 2
            P.wd("gpsimd", woq_guard[wpar])
            sap = DT["wo_a"].rearrange("(hh p) d -> p hh d", p=128)[:, :, dq * 512:(dq + 1) * 512]
            dap = WOQ[wpar][:].rearrange("p (hh d) -> p hh d", d=512)
            qg = "qG0" if wpar == 0 else "qG1"
            wls[(g, dq)] = (qg, P.d("gpsimd", dap, sap, qg))
        def load_zsl(g, dq, ti):
            t = g * 4 + ti
            zpar = t % 2
            P.wd("sync", zsl_guard[zpar])
            P.w("sync", "qZ", K["zb_st"])
            zap = zbank.rearrange("(hh p) t2 -> p hh t2", p=128)[:, :, t * 128:(t + 1) * 128]
            ql = "qL0" if zpar == 0 else "qL1"
            zls[(g, dq, ti)] = (ql, P.d("sync", ZSL[zpar][:].rearrange(
                "p (hh q) -> p hh q", q=128), zap, ql))
        steps = [(g, dq) for g in range(4) for dq in range(4)]
        for _pp in (0, 1, 2):
            P.wd("gpsimd", pp_state["stage_g2"][_pp])
        for _pp in (0, 1):
            P.wd("gpsimd", pp_state["out_g2"][_pp])
        load_woq(*steps[0])
        load_zsl(*steps[0], 0)
        zseq = [(g, dq, ti) for (g, dq) in steps for ti in range(4)]
        zi_next = 1
        for si, (g, dq) in enumerate(steps):
            if si + 1 < len(steps):
                load_woq(*steps[si + 1])
            for ti in range(4):
                t = g * 4 + ti
                while zi_next < len(zseq) and zi_next <= si * 4 + ti + 1:
                    load_zsl(*zseq[zi_next]); zi_next += 1
                qg, wl = wls[(g, dq)]
                ql, zl = zls[(g, dq, ti)]
                zpar = t % 2
                wpar = dq % 2
                bk = obank[ti]
                bank_wait_w("tensor", [bk])
                P.w("tensor", qg, wl)
                P.w("tensor", ql, zl)
                pe = None
                for h2 in range(16):
                    pe = P.o("tensor", lambda t_, hh=h2, zp=zpar, wp=wpar, b=bk: t_.matmul(
                        bank_ap(b), ZSL[zp][:, hh * 128:(hh + 1) * 128],
                        WOQ[wp][:, hh * 512:(hh + 1) * 512],
                        start=(hh == 0), stop=(hh == 15)), "sPE" if h2 == 15 else None)
                bank_set_w([bk], pe)
                zsl_guard[zpar] = ("sPE", pe)
                woq_guard[wpar] = ("sPE", pe)
                P.w("scalar", "sPE", pe)
                P.wd("scalar", yto_g[ti])
                ac = P.o("scalar", lambda s, ti_=ti, dq_=dq, b=bk: s.copy(
                    YTO[ti_][:, dq_ * 512:(dq_ + 1) * 512], bank_ap(b)), "sACT")
                bank_set_r([bk], ("sACT", ac))
                yq_rdy[ti][dq] = ac
            if dq == 3:
                for ti in range(4):
                    t = g * 4 + ti
                    par = t % 2
                    P.wd("sync", K["xo_g"][par])
                    qx = "qX0" if par == 0 else "qX1"
                    xl = P.d("sync", XO[par][:], DT["x0"][t * 128:(t + 1) * 128, :], qx)
                    P.w("vector", "sACT", yq_rdy[ti][3])
                    P.w("vector", "qC", bol)
                    bv = P.o("vector", lambda v, ti_=ti: v.tensor_tensor(
                        YTO[ti_][:], YTO[ti_][:], bObc[:], ALU.add), "sDVE")
                    dsta = (xr if stage > 1 else y)
                    _, rdr = finish_tile(t, YTO[ti], [("sDVE", bv), (qx, xl)], XO[par],
                                         dsta[t * 128:(t + 1) * 128, :],
                                         "R" if stage > 1 else None, YTE)
                    K["xo_g"][par] = rdr
                    yto_g[ti] = ("sDVE", P.cnt["sDVE"])

    def ffn(idx, xsrc, xdst, prep_next):
        P.w("sync", "qT", P.cnt["qT"])
        init_dep = [("sDVE", P.cnt["sDVE"]), ("sPE", P.cnt["sPE"]),
                    ("sACT", P.cnt["sACT"])]
        tgu_guard = [list(init_dep), list(init_dep)]
        tdq_guard = [list(init_dep), list(init_dep)]
        ytf_guard = [None] * 4
        gscr_g = None
        last_ga = None
        shd_g = [None] * 4
        for S in range(4):
            sg_rdy = None
            for tloc in range(4):
                t = S * 4 + tloc
                P.wd("vector", K["sx_rdy"][t])
                P.wd("vector", K["thr_rdy"])
                if last_ga is not None and tloc == 0:
                    P.w("vector", "sACT", last_ga)
                P.o("vector", lambda v, tl=tloc, tt=t: v.tensor_tensor(
                    SGU[:, tl:tl + 1], SX[:, tt:tt + 1], WSC[:, 0:1], ALU.mult))
                sg_rdy = P.o("vector", lambda v, tl=tloc, tt=t: v.tensor_tensor(
                    SGU[:, 4 + tl:5 + tl], SX[:, tt:tt + 1], WSC[:, 1:2], ALU.mult), "sDVE")
            hslot_wr = [[] for _ in range(4)]
            def load_tgu(fc):
                par = fc % 2
                ql = "qL0" if par == 0 else "qL1"
                P.wd("sync", tgu_guard[par])
                gdst = TGU[par][:].rearrange("p (c f) -> p c f", f=512)
                gap = tg.rearrange("(c p) f -> p c f", p=128)[:, :, fc * 256:(fc + 1) * 256]
                uap = tu.rearrange("(c p) f -> p c f", p=128)[:, :, fc * 256:(fc + 1) * 256]
                P.d("sync", gdst[:, :, 0:256], gap, ql)
                return ql, P.d("sync", gdst[:, :, 256:512], uap, ql)
            tgul = {0: load_tgu(0)}
            for fc in range(32):
                par = fc % 2
                if fc + 1 < 32:
                    tgul[fc + 1] = load_tgu(fc + 1)
                ql, l2 = tgul.pop(fc)
                for tloc in range(4):
                    t = S * 4 + tloc
                    bk = "B0" if tloc % 2 == 0 else "B1"
                    bank_wait_w("tensor", [bk])
                    if tloc == 0:
                        P.w("tensor", ql, l2)
                    if fc == 0:
                        P.wd("tensor", K["a_rdy"][t])
                    pe = None
                    for c in range(16):
                        pe = P.o("tensor", lambda t_, cc=c, tt=t, p_=par, b=bk: t_.matmul(
                            bank_ap(b), A[:, cc * 2048 + tt * 128: cc * 2048 + (tt + 1) * 128],
                            TGU[p_][:, cc * 512:(cc + 1) * 512],
                            start=(cc == 0), stop=(cc == 15)), "sPE" if c == 15 else None)
                    bank_set_w([bk], pe)
                    tgu_guard[par] = ("sPE", pe)
                    P.w("scalar", "sPE", pe)
                    if sg_rdy:
                        P.w("scalar", "sDVE", sg_rdy)
                    P.wd("scalar", gscr_g)
                    ga = P.o("scalar", lambda s, b=bk, tl=tloc: s.activation(
                        GSCR[:, 0:256], bank_ap(b)[:, 0:256], AF.Sigmoid,
                        scale=SGU[:, tl:tl + 1]), "sACT")
                    last_ga = ga
                    P.w("vector", "sACT", ga)
                    hv = P.o("vector", lambda v, b=bk, tl=tloc, f=fc: v.scalar_tensor_tensor(
                        HSL[tl][:, f * 256:(f + 1) * 256], bank_ap(b)[:, 256:512],
                        SGU[:, 4 + tl:5 + tl], GSCR[:, 0:256], ALU.mult, ALU.mult), "sDVE")
                    hm = P.o("vector", lambda v, tl=tloc, f=fc: v.tensor_reduce(
                        HMXW[:, tl * 32 + f: tl * 32 + f + 1],
                        HSL[tl][:, f * 256:(f + 1) * 256], AX.X,
                        ALU.max, apply_absolute_value=True), "sDVE")
                    gscr_g = ("sDVE", hv)
                    bank_set_r([bk], [("sACT", ga), ("sDVE", hv)])
                    hslot_wr[tloc].append(("sDVE", hm))
            rh_rdy = [None] * 4
            last_ra = None
            for tloc in range(4):
                if hslot_wr[tloc]:
                    P.wd("vector", hslot_wr[tloc][-1])
                if last_ra is not None:
                    P.w("vector", "sACT", last_ra)
                P.o("vector", lambda v, tl=tloc: v.tensor_reduce(
                    STAT[:, 9:10], HMXW[:, tl * 32:(tl + 1) * 32], AX.X, ALU.max))
                sep("vector", 2)
                P.o("vector", lambda v: v.tensor_single_scalar(
                    STAT[:, 27:28], STAT[:, 9:10], QEPS, ALU.max))
                sep("vector", 2)
                rc = P.o("vector", lambda v: v.reciprocal(STAT[:, 26:27], STAT[:, 27:28]),
                         "sDVE")
                P.w("gpsimd", "sDVE", rc)
                v5 = P.o("gpsimd", lambda g: g.tensor_single_scalar(
                    STAT[:, 28:29], STAT[:, 26:27], 127.0, ALU.mult), "sGP")
                P.w("scalar", "sGP", v5)
                sh1 = P.o("scalar", lambda s: s.activation(STAT[:, 29:30], STAT[:, 27:28],
                                                           AF.Copy, scale=1.0 / 127.0), "sACT")
                P.w("vector", "sACT", sh1)
                P.wd("vector", shd_g[tloc])
                shv = P.o("vector", lambda v, tl=tloc: v.tensor_tensor(
                    SHD[:, tl:tl + 1], STAT[:, 29:30], WSC[:, 2:3], ALU.mult), "sDVE")
                shd_w = shv
                last = None
                for fc2 in range(8):
                    P.o("scalar", lambda s, tl=tloc, f2=fc2: s.activation(
                        I32S[:], HSL[tl][:, f2 * 1024:(f2 + 1) * 1024], AF.Copy,
                        scale=STAT[:, 28:29]))
                    ra = P.o("scalar", lambda s: s.activation(RHC[:], I32S[:], AF.Copy),
                             "sACT")
                    last_ra = ra
                    for j0 in range(0, 8, 4):
                        f0 = fc2 * 8 + j0
                        srcs = [RHC[:, j * 128:(j + 1) * 128] for j in range(j0, j0 + 4)]
                        last = transpose_group(srcs,
                                               HSL[tloc][:, f0 * 128:(f0 + 4) * 128],
                                               pe_dep=("sACT", ra) if j0 == 0 else None)
                rh_rdy[tloc] = last
            shd_rdy = P.cnt["sDVE"]
            dn_pe = [[None] * 4 for _ in range(4)]
            dbank = lambda tl, dc: ["A0", "A1", "A2", "A3"][tl]
            def load_tdq(dc, fq):
                qpar = fq % 2
                qd = "qL0" if qpar == 0 else "qL1"
                P.wd("sync", tdq_guard[qpar])
                tap = td[fq * 512:(fq + 1) * 512, dc * 512:(dc + 1) * 512]
                tap = tap.rearrange("(q p) d -> p q d", p=128)
                return qd, P.d("sync", TDQ[qpar][:].rearrange("p (q d) -> p q d", d=512),
                               tap, qd)
            dsteps = [(dc, fq) for dc in range(4) for fq in range(16)]
            tdql = {dsteps[0]: load_tdq(*dsteps[0])}
            for di, (dc, fq) in enumerate(dsteps):
                if di + 1 < len(dsteps):
                    tdql[dsteps[di + 1]] = load_tdq(*dsteps[di + 1])
                qd, tl_ = tdql.pop((dc, fq))
                if True:
                    qpar = fq % 2
                    for q4 in range(4):
                        f = fq * 4 + q4
                        for tloc in range(4):
                            bk = dbank(tloc, dc)
                            if f == 0:
                                bank_wait_w("tensor", [bk])
                                P.wd("tensor", rh_rdy[tloc])
                            if q4 == 0 and tloc == 0:
                                P.w("tensor", qd, tl_)
                            pe = P.o("tensor", lambda t_, ff=f, tl=tloc, qp=qpar, qq=q4, b=bk:
                                     t_.matmul(bank_ap(b),
                                               HSL[tl][:, ff * 128:(ff + 1) * 128],
                                               TDQ[qp][:, qq * 512:(qq + 1) * 512],
                                               start=(ff == 0), stop=(ff == 63)),
                                     "sPE" if (f == 63 or (q4 == 3 and tloc == 3))
                                     else None)
                            if f == 63:
                                bank_set_w([bk], pe)
                                dn_pe[tloc][dc] = pe
                    tdq_guard[qpar] = ("sPE", P.cnt["sPE"])
                if fq != 15:
                    continue
                for tloc in range(4):
                    bk = dbank(tloc, dc)
                    P.w("scalar", "sPE", dn_pe[tloc][dc])
                    P.w("scalar", "sDVE", shd_rdy)
                    P.wd("scalar", ytf_guard[tloc])
                    ac = P.o("scalar", lambda s, tl=tloc, d_=dc, b=bk: s.activation(
                        YTF[tl][:, d_ * 512:(d_ + 1) * 512], bank_ap(b), AF.Copy,
                        scale=SHD[:, tl:tl + 1]), "sACT")
                    bank_set_r([bk], ("sACT", ac))
                    shd_g[tloc] = ("sACT", ac)
                    dn_pe[tloc][dc] = ac
            for tloc in range(4):
                t = S * 4 + tloc
                par = t % 2
                P.wd("sync", K["xo_g"][par])
                qx = "qX0" if par == 0 else "qX1"
                xl = P.d("sync", XO[par][:], xsrc[t * 128:(t + 1) * 128, :], qx)
                ydep = [("sACT", dn_pe[tloc][3]), (qx, xl),
                        ("sACT", dn_pe[tloc][0]), ("sACT", dn_pe[tloc][1]),
                        ("sACT", dn_pe[tloc][2])]
                st, rdr = finish_tile(t, YTF[tloc], ydep, XO[par],
                                      xdst[t * 128:(t + 1) * 128, :], prep_next, YTE)
                K["xo_g"][par] = rdr
                ytf_guard[tloc] = [("sDVE", P.cnt["sDVE"])]
            tgu_guard[0] = [("sDVE", P.cnt["sDVE"])]
            tgu_guard[1] = [("sDVE", P.cnt["sDVE"])]

    def attn_b(xsrc, xdst, prep_next, nh=None, noout=False):
        bl = P.d("sync", BETA[:], DT["beta_in"], "qB")
        P.w("scalar", "qB", bl)
        brdy = P.o("scalar", lambda s: s.activation(BETA[:], BETA[:], AF.Sigmoid), "sACT")
        wq_guard = [("sDVE", P.cnt["sDVE"]), ("sPE", P.cnt["sPE"]),
                    ("sACT", P.cnt["sACT"])]
        woq_guard = [None, None]
        zsl_guard = [None, None]
        for h in range(nh if nh is not None else H):
            wdep = load_head_weights(h, (DT["wq_b"], DT["wk_b"], DT["wv_b"]),
                                     (Wqh, Wkh, Wvh), wq_guard)
            qr, sq_rdy = proj4(Wqh, Qt, None, wdep,
                               sig_out=None if _DBG_NOSIG else SqT)
            kr, sk_rdy = proj4(Wkh, Kt, None, wdep,
                               sig_out=None if _DBG_NOSIG else SkT, zinc=True)
            vr = vproj(Wvh, wdep, None)
            wq_guard = ("sPE", P.cnt["sPE"])
            P.o("vector", lambda v: v.memset(MEMF[:], 0.0))
            P.o("vector", lambda v: v.memset(STAT[:, 30:31], 1.0 / 128.0))
            P.o("vector", lambda v: v.memset(MEMZ[:, 0:128], 0.0))
            mz = P.o("vector", lambda v: v.memset(MEMZ[:, 128:129], 1.0 / 128.0), "sDVE")
            mz_rdy = ("sDVE", mz)
            atlast = None
            for s_ in range(_DBG_NSEG if _DBG_NSEG is not None else 4):
                c0 = s_ * 512
                for qi in range(4):
                    qq = K["spair_i"] % 4
                    K["spair_i"] += 1
                    bk = ["A0", "A1", "A2", "A3"][qq]
                    bank_wait_w("tensor", [bk])
                    if s_ == 0 and qi == 0:
                        P.w("tensor", "sACT", qr)
                        P.w("tensor", "sACT", kr)
                        P.wd("tensor", vr)
                        P.wd("tensor", [("sDVE", sq_rdy), ("sDVE", sk_rdy)])
                    pe = P.o("tensor", lambda t_, ii=qi, cc=c0, b=bk: t_.matmul(
                        bank_ap(b)[:, 0:(ii + 1) * 128], Qt[:, cc + ii * 128: cc + (ii + 1) * 128],
                        Kt[:, cc: cc + (ii + 1) * 128], start=True, stop=True), "sPE")
                    bank_set_w([bk], pe)
                    P.w("vector", "sPE", pe)
                    P.o("vector", lambda v, ii=qi, b=bk: v.tensor_tensor(
                        bank_ap(b)[:, ii * 128:(ii + 1) * 128],
                        bank_ap(b)[:, ii * 128:(ii + 1) * 128], MASK[:], ALU.add))
                    nv = P.o("vector", lambda v, ii=qi, b=bk: v.tensor_reduce(
                        STAT[:, 8:9], bank_ap(b)[:, 0:(ii + 1) * 128], AX.X, ALU.max,
                        negate=True), "sDVE")
                    P.w("gpsimd", "sDVE", nv)
                    gb = P.o("gpsimd", lambda g: g.tensor_single_scalar(
                        STAT[:, 10:11], STAT[:, 8:9], ISC, ALU.mult), "sGP")
                    P.w("scalar", "sGP", gb)
                    ex = P.o("scalar", lambda s, ii=qi, b=bk: s.activation(
                        PbB[:, 0:(ii + 1) * 128], bank_ap(b)[:, 0:(ii + 1) * 128], AF.Exp,
                        bias=STAT[:, 10:11], scale=ISC, accum_out=STAT[:, 12:13]), "sACT")
                    bank_set_r([bk], ("sACT", ex))
                    P.w("vector", "sACT", ex)
                    P.o("vector", lambda v: v.reciprocal(STAT[:, 14:15], STAT[:, 12:13]))
                    sep("vector", 2)
                    pn = P.o("vector", lambda v, ii=qi: v.tensor_scalar(
                        PbB[:, 0:(ii + 1) * 128], PbB[:, 0:(ii + 1) * 128], STAT[:, 14:15],
                        None, ALU.mult), "sDVE")
                    adq = psB[:, qq * 128:(qq + 1) * 128]
                    addone = None
                    for j in range(qi + 1):
                        pt = transpose_to(PbB[:, j * 128:(j + 1) * 128],
                                          PtB[:, j * 128:(j + 1) * 128],
                                          pe_dep=("sDVE", pn) if j == 0 else None)
                        if j == 0:
                            bank_wait_w("tensor", ["B0"])
                        P.w("tensor", "sACT", pt)
                        addone = P.o("tensor", lambda t_, jj=j, ss=s_, ii=qi, a=adq: t_.matmul(
                            a, PtB[:, jj * 128:(jj + 1) * 128],
                            Vtok[:, (ss * 4 + jj) * 128:(ss * 4 + jj + 1) * 128],
                            start=(jj == 0), stop=(jj == ii)), "sPE" if j == qi else None)
                    bank_set_w(["B0"], addone)
                    amb = psB[:, 512:641]
                    bank_wait_w("tensor", ["B1"])
                    P.wd("tensor", mz_rdy)
                    am = P.o("tensor", lambda t_, cc=c0, ii=qi, a=amb: t_.matmul(
                        a, SqT[:, cc + ii * 128: cc + (ii + 1) * 128], MEMZ[:, 0:129],
                        start=True, stop=True), "sPE")
                    bank_set_w(["B1"], am)
                    P.w("vector", "sPE", am)
                    rd = P.o("vector", lambda v: v.reciprocal(STAT[:, 31:32], psB[:, 640:641]),
                             "sDVE")
                    P.w("scalar", "sDVE", rd)
                    amv0 = P.o("scalar", lambda s: s.activation(
                        CMB[:, 0:128], psB[:, 512:640], AF.Identity, scale=STAT[:, 31:32]),
                        "sACT")
                    P.w("vector", "sACT", amv0)
                    amv = P.o("vector", lambda v, a=adq: v.tensor_tensor(
                        CMB[:, 128:256], CMB[:, 0:128], a, ALU.subtract), "sDVE")
                    bank_set_r(["B1"], [("sACT", amv0), ("sDVE", rd)])
                    P.w("gpsimd", "sACT", brdy)
                    P.w("gpsimd", "sDVE", amv)
                    gu2 = P.o("gpsimd", lambda g, hh=h: g.tensor_tensor(
                        CMB[:, 0:128], BETA[:, hh * 128:(hh + 1) * 128], CMB[:, 128:256],
                        ALU.mult), "sGP")
                    P.w("vector", "sGP", gu2)
                    if atlast is not None:
                        P.w("vector", "sACT", atlast)
                    at = P.o("vector", lambda v, a=adq: v.tensor_tensor(
                        XNB[:, 0:128], CMB[:, 0:128], a, ALU.add), "sDVE")
                    bank_set_r(["B0"], ("sDVE", at))
                    atlast = transpose_to(XNB[:, 0:128],
                                          ATTH[:, c0 + qi * 128: c0 + (qi + 1) * 128],
                                          pe_dep=("sDVE", at),
                                          act_dep=(("qZ", K["ab_st"])
                                                   if (s_ == 0 and qi == 0) else None))
                up = psB[:, 0:128]
                upfin = None
                prev_up = None
                for st_ in range(4):
                    smb = psB[:, 512:641]
                    bank_wait_w("tensor", ["B1"])
                    P.wd("tensor", mz_rdy)
                    sm = P.o("tensor", lambda t_, cc=c0, ii=st_, a=smb: t_.matmul(
                        a, SkT[:, cc + ii * 128: cc + (ii + 1) * 128], MEMZ[:, 0:129],
                        start=True, stop=True), "sPE")
                    bank_set_w(["B1"], sm)
                    P.w("vector", "sPE", sm)
                    rk = P.o("vector", lambda v: v.reciprocal(STAT[:, 31:32], psB[:, 640:641]),
                             "sDVE")
                    P.w("scalar", "sDVE", rk)
                    m1 = P.o("scalar", lambda s: s.activation(
                        SS0[:, 0:128], psB[:, 512:640], AF.Identity, scale=STAT[:, 31:32]),
                        "sACT")
                    P.w("vector", "sACT", m1)
                    if prev_up is not None:
                        P.w("vector", "sPE", prev_up)
                    ng = P.o("vector", lambda v, ss=s_, ii=st_: v.tensor_tensor(
                        NGD[:], SS0[:, 0:128],
                        Vtok[:, (ss * 4 + ii) * 128:(ss * 4 + ii + 1) * 128],
                        ALU.subtract), "sDVE")
                    bank_set_r(["B1"], [("sACT", m1), ("sDVE", rk)])
                    stk = transpose_to(SkT[:, c0 + st_ * 128: c0 + (st_ + 1) * 128],
                                       SKTOK[:], pe_dep=None)
                    if st_ == 0:
                        bank_wait_w("tensor", ["B0"])
                    P.w("tensor", "sACT", stk)
                    P.w("tensor", "sDVE", ng)
                    upfin = P.o("tensor", lambda t_, ii=st_, a=up: t_.matmul(
                        a, SKTOK[:], NGD[:], start=(ii == 0), stop=(ii == 3)), "sPE")
                    prev_up = upfin
                bank_set_w(["B0"], upfin)
                P.w("vector", "sPE", upfin)
                P.o("vector", lambda v, a=up: v.tensor_tensor(MEMF[:], MEMF[:], a,
                                                              ALU.subtract))
                zv = P.o("vector", lambda v, ss=s_: v.tensor_tensor(
                    STAT[:, 30:31], STAT[:, 30:31], ZINC[:, ss:ss + 1], ALU.add), "sDVE")
                bank_set_r(["B0"], ("sDVE", zv))
                P.w("scalar", "sDVE", zv)
                P.o("scalar", lambda s: s.copy(MEMZ[:, 0:128], MEMF[:]))
                mzv = P.o("scalar", lambda s: s.copy(MEMZ[:, 128:129], STAT[:, 30:31]),
                          "sACT")
                mz_rdy = ("sACT", mzv)
            P.w("sync", "sACT", atlast)
            K["ab_st"] = P.d("sync", abank[h * 128:(h + 1) * 128, :], ATTH[:], "qZ")
        if noout:
            return
        obank = ["A0", "A1", "A2", "A3"]
        yq_rdy = [[None] * 4 for _ in range(4)]
        yto_g = [None] * 4
        wls = {}
        zls = {}
        def load_woq(g, dq):
            wpar = dq % 2
            P.wd("gpsimd", woq_guard[wpar])
            sap = DT["wo_b"].rearrange("(hh p) d -> p hh d", p=128)[:, :, dq * 512:(dq + 1) * 512]
            dap = WOQ[wpar][:].rearrange("p (hh d) -> p hh d", d=512)
            qg = "qG0" if wpar == 0 else "qG1"
            wls[(g, dq)] = (qg, P.d("gpsimd", dap, sap, qg))
        def load_zsl(g, dq, ti):
            t = g * 4 + ti
            zpar = t % 2
            P.wd("sync", zsl_guard[zpar])
            P.w("sync", "qZ", K["ab_st"])
            h0 = (t % 4) * 4
            seg = t // 4
            aap = abank.rearrange("(hh p) t2 -> p hh t2", p=128)[
                :, h0:h0 + 4, seg * 512:(seg + 1) * 512]
            ql = "qL0" if zpar == 0 else "qL1"
            zls[(g, dq, ti)] = (ql, P.d("sync", ZSL[zpar][:].rearrange(
                "p (a q) -> p a q", q=512), aap, ql))
        steps = [(g, dq) for g in range(4) for dq in range(4)]
        load_woq(*steps[0])
        load_zsl(*steps[0], 0)
        zseq = [(g, dq, ti) for (g, dq) in steps for ti in range(4)]
        zi_next = 1
        for si, (g, dq) in enumerate(steps):
            if si + 1 < len(steps):
                load_woq(*steps[si + 1])
            for ti in range(4):
                t = g * 4 + ti
                while zi_next < len(zseq) and zi_next <= si * 4 + ti + 1:
                    load_zsl(*zseq[zi_next]); zi_next += 1
                qg, wl = wls[(g, dq)]
                ql, zl = zls[(g, dq, ti)]
                zpar = t % 2
                wpar = dq % 2
                bk = obank[ti]
                bank_wait_w("tensor", [bk])
                P.w("tensor", qg, wl)
                P.w("tensor", ql, zl)
                pe = None
                gat = ZSL[zpar][:].rearrange("p (a b c) -> p a b c", a=4, b=32, c=16)
                for h2 in range(16):
                    pe = P.o("tensor", lambda t_, hh=h2, wp=wpar, b=bk, gg=gat: t_.matmul(
                        bank_ap(b), gg[:, :, :, hh],
                        WOQ[wp][:, hh * 512:(hh + 1) * 512],
                        start=(hh == 0), stop=(hh == 15)), "sPE" if h2 == 15 else None)
                bank_set_w([bk], pe)
                zsl_guard[zpar] = ("sPE", pe)
                woq_guard[wpar] = ("sPE", pe)
                P.w("scalar", "sPE", pe)
                P.wd("scalar", yto_g[ti])
                ac = P.o("scalar", lambda s, ti_=ti, dq_=dq, b=bk: s.copy(
                    YTO[ti_][:, dq_ * 512:(dq_ + 1) * 512], bank_ap(b)), "sACT")
                bank_set_r([bk], ("sACT", ac))
                yq_rdy[ti][dq] = ac
            if dq == 3:
                for ti in range(4):
                    t = g * 4 + ti
                    par = t % 2
                    P.wd("sync", K["xo_g"][par])
                    qx = "qX0" if par == 0 else "qX1"
                    xl = P.d("sync", XO[par][:], xsrc[t * 128:(t + 1) * 128, :], qx)
                    ydep = [("sACT", yq_rdy[ti][3]), (qx, xl),
                            ("sACT", yq_rdy[ti][0]), ("sACT", yq_rdy[ti][1]),
                            ("sACT", yq_rdy[ti][2])]
                    _, rdr = finish_tile(t, YTO[ti], ydep, XO[par],
                                         xdst[t * 128:(t + 1) * 128, :], prep_next, YTE)
                    K["xo_g"][par] = rdr
                    yto_g[ti] = ("sDVE", P.cnt["sDVE"])

    prologue()
    mu, me = mean_units()
    tu_, te = tern_units()
    if stage > 1:
        pp = {"units": mu, "emit": me, "per_head": 26,
              "next": (wscale_finalize, (tu_, te))}
    else:
        pp = {"units": [], "emit": me, "per_head": 0}
    attn_a(pp)
    if stage > 1:
        ffn(0, xr, xr if stage > 2 else y,
            ("xnT" if stage > 2 else None))
    if stage > 2:
        attn_b(xr, xr if stage > 3 else y, ("R" if stage > 3 else None),
               nh=_DBG_NH_B, noout=_DBG_NOOUT_B)
    if stage > 3:
        ffn(1, xr, y, None)
    for s in ("qSt", "qZ", "qT", "qC", "qB", "qX0", "qX1", "qL0", "qL1", "qP",
              "qG0", "qG1"):
        P.w("sync", s, P.cnt[s])

    print(f"[kernel build] planned instructions ~{P.ni}, sems {P.cnt}", file=sys.stderr)
    _emit(P, nc, sems)
    pctx.close()
    sctx.close()
    return nc


def _host_inputs(inputs, b):
    x = np.ascontiguousarray(inputs["L0_attn_a_x"][b])
    wq_a = np.ascontiguousarray(inputs["L0_attn_a_W_Q"].transpose(1, 0, 2).reshape(D, D))
    wk_a = np.ascontiguousarray(inputs["L0_attn_a_W_K"].transpose(1, 0, 2).reshape(D, D))
    wv_a = np.ascontiguousarray(inputs["L0_attn_a_W_V"].transpose(1, 0, 2).reshape(D, D))
    wo_a = np.ascontiguousarray(inputs["L0_attn_a_W_O"].reshape(D, D))
    bq = np.ascontiguousarray(inputs["L0_attn_a_b_Q"].T)
    bk = np.ascontiguousarray(inputs["L0_attn_a_b_K"].T)
    bv = np.ascontiguousarray(inputs["L0_attn_a_b_V"].T)
    bo = np.ascontiguousarray(np.broadcast_to(inputs["L0_attn_a_b_O"].reshape(1, D),
                                              (128, D)))
    beta = np.ascontiguousarray(np.broadcast_to(
        inputs["L0_attn_b_betas"].reshape(1, D), (128, D)))
    wgT = np.ascontiguousarray(inputs["L0_ffn_w_gate"].T)
    wuT = np.ascontiguousarray(inputs["L0_ffn_w_up"].T)
    wdT = np.ascontiguousarray(inputs["L0_ffn_w_down"].T)
    ident = np.eye(128, dtype=BF)
    mask = np.where(np.tril(np.ones((128, 128), bool)), np.float32(0.0),
                    np.float32(NEGBIG)).astype(np.float32)
    ones = np.ones((128, 128), np.float32)
    return {
        "x0": x, "wq_a": wq_a, "wk_a": wk_a, "wv_a": wv_a, "wo_a": wo_a,
        "bq_in": bq, "bk_in": bk, "bv_in": bv, "bo_bc": bo,
        "wq_b": np.ascontiguousarray(inputs["L0_attn_b_Wq"]),
        "wk_b": np.ascontiguousarray(inputs["L0_attn_b_Wk"]),
        "wv_b": np.ascontiguousarray(inputs["L0_attn_b_Wv"]),
        "wo_b": np.ascontiguousarray(inputs["L0_attn_b_Wo"]),
        "beta_in": beta, "wgT": wgT, "wuT": wuT, "wdT": wdT,
        "ident_in": ident, "mask_in": mask, "ones_in": ones,
    }


def kernel(**inputs):
    nc = build_program(_STAGE)
    in_maps = [_host_inputs(inputs, b) for b in range(8)]
    res = run_bass_kernel_spmd(nc, in_maps, core_ids=list(range(8)))
    out = np.stack([np.asarray(r["y"]) for r in res.results], axis=0)
    return out.astype(np.float32)
